# version 05
# baseline (speedup 1.0000x reference)
import numpy as np
import ml_dtypes
import concourse.bass as bass
import concourse.mybir as mybir
from concourse.bass_utils import run_bass_kernel_spmd

F32 = mybir.dt.float32
BF16 = mybir.dt.bfloat16
AF = mybir.ActivationFunctionType
AL = mybir.AluOpType
AX = mybir.AxisListType
BF = ml_dtypes.bfloat16

D, T, NC_ = 1024, 2048, 8
PI = float(np.pi)
TWO_PI = float(2.0 * np.pi)

DEBUG_TAP = None


def _host_consts():
    c = {}
    c['ident'] = np.eye(128, dtype=np.float32)
    c['ones32'] = np.ones((128, 128), dtype=np.float32)
    c['ones_bf'] = np.ones((128, 128), dtype=BF)
    sel = np.zeros((33, 128), dtype=np.float32)
    sel[0, :64] = 1.0
    sel[32, 64:] = 1.0
    c['sel33'] = sel
    b = np.zeros((128, 128), dtype=np.float32)
    for j in range(4):
        b[32 * j:32 * j + 16, 32 * j] = 1.0
    c['bsumL0'] = b.astype(BF)
    a = np.zeros((128, 128), dtype=np.float32)
    a[0:64, 0] = 1.0
    a[64:128, 32] = 1.0
    c['bsumA'] = a.astype(BF)
    a = np.zeros((128, 128), dtype=np.float32)
    a[0:64, 64] = 1.0
    a[64:128, 96] = 1.0
    c['bsumB'] = a.astype(BF)
    c['pos_row'] = np.arange(1, T + 1, dtype=np.float32).reshape(1, T)
    r = np.arange(128)
    exq = np.zeros((128, 8), dtype=np.float32)
    for ch in range(8):
        exq[:, ch] = -(((128 * ch + r) // 2).astype(np.float64) / 512.0)
    c['exq'] = exq
    exk = np.zeros((128, 8), dtype=np.float32)
    for ch in range(8):
        exk[:, ch] = -(((128 * ch + r) // 4).astype(np.float64) / 256.0)
    c['exk'] = exk
    c['exf'] = (-(((r % 64) % 32).astype(np.float64) / 32.0)).astype(np.float32).reshape(128, 1)
    c['sgq'] = np.where(r % 2 == 0, -1.0, 1.0).astype(np.float32).reshape(128, 1)
    c['sgk4'] = np.where(r % 4 < 2, -1.0, 1.0).astype(np.float32).reshape(128, 1)
    c['sgrh'] = np.where(r % 64 < 32, -1.0, 1.0).astype(np.float32).reshape(128, 1)
    c['tenk'] = np.full((128, 1), 10000.0, dtype=np.float32)
    s8 = np.zeros((8, 1024), dtype=np.float32)
    for e in range(8):
        s8[e, 128 * e:128 * (e + 1)] = 1.0
    c['sel8'] = s8.astype(BF)
    return c


def _host_weights(I):
    w = {}
    qp = np.asarray(I['L0_attn_query_proj'])
    kp = np.asarray(I['L0_attn_key_proj'])
    qstr = np.zeros((D, 512), dtype=np.float32)
    kstr = np.zeros((D, 512), dtype=np.float32)
    for h in range(16):
        qstr[:, 32 * h:32 * h + 16] = qp[:, h, :].T
        kstr[:, 32 * h:32 * h + 16] = kp[h, :, :].T
    w['l0_qstr'] = qstr
    w['l0_kstr'] = kstr
    w['l0_vT'] = np.ascontiguousarray(np.asarray(I['L0_attn_value_proj']).reshape(1024, D).T)
    w['l0_opT'] = np.ascontiguousarray(np.asarray(I['L0_attn_output_proj']).reshape(D, 1024).T)
    w['l0_rtr'] = np.asarray(I['L0_ffn_router_w'])
    w['l0_rtb'] = np.asarray(I['L0_ffn_router_b']).reshape(8, 1)
    w['l0_W1'] = np.asarray(I['L0_ffn_W1'])
    w['l0_b1c'] = np.ascontiguousarray(
        np.asarray(I['L0_ffn_b1']).reshape(8, 4, 128).transpose(2, 0, 1).reshape(128, 32))
    w['l0_W2'] = np.asarray(I['L0_ffn_W2'])
    w['l0_b2r'] = np.asarray(I['L0_ffn_b2'])
    Wq = np.asarray(I['L1_attn_Wq'])
    Wk = np.asarray(I['L1_attn_Wk'])
    Wv = np.asarray(I['L1_attn_Wv'])
    rep = np.repeat(np.arange(512), 2)
    permq = np.arange(1024).reshape(-1, 2)[:, ::-1].reshape(-1)
    perm4 = (np.arange(1024).reshape(-1, 4)[:, [2, 3, 0, 1]]).reshape(-1)
    w['l1a_wq'] = Wq
    w['l1a_wqp'] = np.ascontiguousarray(Wq[:, permq])
    Wkr = Wk[:, rep]
    w['l1a_wkr'] = np.ascontiguousarray(Wkr)
    w['l1a_wkrp'] = np.ascontiguousarray(Wkr[:, perm4])
    w['l1a_wvr'] = np.ascontiguousarray(Wv[:, rep])
    w['l1a_wo'] = np.asarray(I['L1_attn_Wo'])
    qkvT = np.ascontiguousarray(np.asarray(I['L1_ffn_qkv_w']).T)
    w['l1f_qkvT'] = qkvT
    permrh = (np.arange(2048).reshape(-1, 64)[:, list(range(32, 64)) + list(range(32))]).reshape(-1)
    w['l1f_qkvTp'] = np.ascontiguousarray(qkvT[:, :2048][:, permrh])
    qkvb = np.asarray(I['L1_ffn_qkv_b'])
    w['l1f_qkvbc'] = np.ascontiguousarray(qkvb.reshape(24, 128).T)
    w['l1f_qkvbcp'] = np.ascontiguousarray(qkvb[:2048][permrh].reshape(16, 128).T)
    w['l1f_outwT'] = np.ascontiguousarray(np.asarray(I['L1_ffn_out_w']).T)
    w['l1f_outbc'] = np.ascontiguousarray(np.asarray(I['L1_ffn_out_b']).reshape(8, 128).T)
    w['l1f_gateT'] = np.ascontiguousarray(np.asarray(I['L1_ffn_gate_w']).T)
    w['l1f_w1T'] = np.ascontiguousarray(np.asarray(I['L1_ffn_w1']).transpose(0, 2, 1))
    w['l1f_w2T'] = np.ascontiguousarray(np.asarray(I['L1_ffn_w2']).transpose(0, 2, 1))
    w['l1f_w3T'] = np.ascontiguousarray(np.asarray(I['L1_ffn_w3']).transpose(0, 2, 1))
    w['l1f_b1c'] = np.ascontiguousarray(
        np.asarray(I['L1_ffn_b1']).reshape(8, 16, 128).transpose(2, 0, 1).reshape(128, 128))
    w['l1f_b2r'] = np.ascontiguousarray(np.asarray(I['L1_ffn_b2']).reshape(1, 8 * 2048))
    w['l1f_b3r'] = np.asarray(I['L1_ffn_b3'])
    w['l1f_vb'] = np.ascontiguousarray(qkvb[2048:3072].reshape(1, 1024))
    w['l1f_n1c'] = np.ascontiguousarray(np.asarray(I['L1_ffn_norm1_w']).reshape(8, 128).T)
    w['l1f_n2c'] = np.ascontiguousarray(np.asarray(I['L1_ffn_norm2_w']).reshape(8, 128).T)
    return w


_WSHAPES = {
    'l0_qstr': (D, 512), 'l0_kstr': (D, 512), 'l0_vT': (D, 1024), 'l0_opT': (1024, D),
    'l0_rtr': (D, 8), 'l0_rtb': (8, 1), 'l0_W1': (8, D, 512), 'l0_b1c': (128, 32),
    'l0_W2': (8, 512, D), 'l0_b2r': (8, D),
    'l1a_wq': (D, D), 'l1a_wqp': (D, D), 'l1a_wkr': (D, D), 'l1a_wkrp': (D, D),
    'l1a_wvr': (D, D), 'l1a_wo': (D, D),
    'l1f_qkvT': (D, 3072), 'l1f_qkvTp': (D, 2048), 'l1f_qkvbc': (128, 24),
    'l1f_qkvbcp': (128, 16), 'l1f_outwT': (D, D), 'l1f_outbc': (128, 8),
    'l1f_gateT': (D, 8), 'l1f_w1T': (8, D, 2048), 'l1f_b1c': (128, 128),
    'l1f_w2T': (8, D, 2048), 'l1f_b2r': (1, 8 * 2048), 'l1f_w3T': (8, 2048, D),
    'l1f_b3r': (8, D), 'l1f_n1c': (128, 8), 'l1f_n2c': (128, 8), 'l1f_vb': (1, 1024),
}


class Ctx:
    pass


def build_program(tap=None):
    nc = bass.Bass('TRN2', target_bir_lowering=False)
    g = Ctx()
    g.nc = nc
    g.tap = tap
    g.x_fm = nc.dram_tensor('x_fm', [D, T], F32, kind='ExternalInput')
    CST = _host_consts()
    g.cst = {k: nc.dram_tensor('c_' + k, list(v.shape), BF16 if v.dtype == BF else F32,
                               kind='ExternalInput') for k, v in CST.items()}
    g.wt = {k: nc.dram_tensor('w_' + k, list(sh), F32, kind='ExternalInput')
            for k, sh in _WSHAPES.items()}
    g.fvals = []
    g.fv_dram = nc.dram_tensor('c_fvals', [128, 64], F32, kind='ExternalInput')
    g.hx1 = nc.dram_tensor('hx1', [D, T], F32)
    g.hx2 = nc.dram_tensor('hx2', [D, T], F32)
    g.hs = nc.dram_tensor('hs', [D, T], BF16)
    g.ho = nc.dram_tensor('ho', [D, T], BF16)
    g.ha = nc.dram_tensor('ha', [D, T], BF16)
    g.out = nc.dram_tensor('out', [D, T], F32, kind='ExternalOutput')
    g.dbg = nc.dram_tensor('dbg', [D, T], F32, kind='ExternalOutput')

    import contextlib
    with contextlib.ExitStack() as st:
        g.xn = st.enter_context(nc.sbuf_tensor('xn', [128, 8 * T], BF16))
        g.xm = st.enter_context(nc.sbuf_tensor('xm', [128, 8 * T], BF16))
        g.wb = st.enter_context(nc.sbuf_tensor('wb', [128, 52 * 1024 - 128], BF16))
        g.fb = st.enter_context(nc.sbuf_tensor('fb', [128, 8 * 1024], F32))
        g.mc_ = st.enter_context(nc.sbuf_tensor('mc_', [128, 64], F32))
        g.mc2 = st.enter_context(nc.sbuf_tensor('mc2', [128, 192], F32))
        g.ic = st.enter_context(nc.sbuf_tensor('ic', [128, 512], mybir.dt.int32))
        g.fv = st.enter_context(nc.sbuf_tensor('fv', [128, 64], F32))

        def fc(v):
            v = float(np.float32(v))
            if v not in g.fvals:
                assert len(g.fvals) < 64
                g.fvals.append(v)
                nc.const_aps.register(F32, v, g.fv[:, len(g.fvals) - 1:len(g.fvals)])
            return v
        g.fc = fc
        g.cs = {k: st.enter_context(nc.sbuf_tensor('s_' + k, list(v.shape),
                                                   BF16 if v.dtype == BF else F32))
                for k, v in CST.items() if k != 'pos_row'}
        g.psA = st.enter_context(nc.psum_tensor('psA', [128, 2048], F32))
        g.psB = st.enter_context(nc.psum_tensor('psB', [128, 2048], F32))
        g.ps = [g.psA[:, 512 * i:512 * (i + 1)] for i in range(4)] + \
               [g.psB[:, 512 * i:512 * (i + 1)] for i in range(4)]
        g.sem = {k: st.enter_context(nc.semaphore('sm' + k))
                 for k in ('W', 'X', 'T', 'S', 'V', 'O', 'Wa', 'Wb', 'Wc', 'Wd', 'D')}
        g.cnt = {k: 0 for k in g.sem}
        g.q = {'t': [], 's': [], 'v': [], 'g': [], 'y': []}
        g.done = False

        emit_all(g)
        kernel._fvals = list(g.fvals)

        with nc.Block() as blk:
            @blk.tensor
            def _(e):
                for fn in g.q['t']:
                    fn(e)

            @blk.scalar
            def _(e):
                for fn in g.q['s']:
                    fn(e)

            @blk.vector
            def _(e):
                for fn in g.q['v']:
                    fn(e)

            @blk.gpsimd
            def _(e):
                for fn in g.q['g']:
                    fn(e)

            @blk.sync
            def _(e):
                for fn in g.q['y']:
                    fn(e)
    return nc


def _mk(g):
    sem, cnt, q = g.sem, g.cnt, g.q

    def w8(eng, s, val):
        q[eng].append(lambda e, S=sem[s], V=val: e.wait_ge(S, V))

    def _run(eng, fn, s=None, n=1):
        if s is None:
            q[eng].append(fn)
            return None
        cnt[s] += n
        v = cnt[s]
        q[eng].append(lambda e, F=fn, S=sem[s], N=n: F(e).then_inc(S, N))
        if eng in ('t', 's', 'v'):
            q[eng].append(lambda e, S=sem[s], V=v: e.wait_ge(S, V))
        return v

    g.dmawait = {}

    def _dma(eng, dst, src, s):
        v = _run(eng, lambda e, d=dst, r=src: e.dma_start(out=d, in_=r), s, 16)
        key = (eng, s)
        if v - g.dmawait.get(key, 0) >= 7 * 16:
            q[eng].append(lambda e, S=sem[s], V=v: e.wait_ge(S, V))
            g.dmawait[key] = v
        return v

    def wgroup(items, s):
        if cnt[s] > 0:
            q['g'].append(lambda e, S=sem[s], V=cnt[s]: e.wait_ge(S, V))
            g.dmawait[('g', s)] = cnt[s]
        v = None
        for dst, srcap in items:
            v = _dma('g', dst, srcap, s)
        return v

    def ydma(dst, src, s='X'):
        return _dma('y', dst, src, s)

    def gdma(dst, src, s='W'):
        return _dma('g', dst, src, s)

    def mm(o, l, r, start, stop, tp=None, sgc=False, s=None):
        return _run('t', lambda e, o=o, l=l, r=r, a=start, b=stop, tp=tp, k=sgc:
                    e.matmul(o, l, r, start=a, stop=b, tile_position=tp,
                             skip_group_check=k), s)

    def tr(o, i, ident, s=None):
        return _run('t', lambda e, o=o, i=i, d=ident: e.transpose(o, i, d), s)

    def act(o, i, f, bias=0.0, scale=1.0, accum=None, s='S'):
        return _run('s', lambda e, o=o, i=i, f=f, b=bias, sc=scale, ac=accum:
                    e.activation(o, i, f, bias=b, scale=sc, accum_out=ac), s)

    def tt(o, a, b, op, s='V'):
        return _run('v', lambda e, o=o, a=a, b=b, op=op: e.tensor_tensor(o, a, b, op), s)

    def ts(o, a, s1, op0, s2=None, op1=None, accum=None, s='V'):
        if op1 is None:
            return _run('v', lambda e, o=o, a=a, s1=s1, ac=accum, op0=op0:
                        e.tensor_scalar(o, a, s1, None, op0, accum_out=ac), s)
        return _run('v', lambda e, o=o, a=a, s1=s1, s2=s2, op0=op0, op1=op1, ac=accum:
                    e.tensor_scalar(o, a, s1, s2, op0, op1, accum_out=ac), s)

    def vcp(o, i, s='V'):
        return _run('v', lambda e, o=o, i=i: e.tensor_copy(o, i), s)

    def vred(o, i, op, negate=None, s='V'):
        return _run('v', lambda e, o=o, i=i, op=op, n=negate:
                    e.tensor_reduce(o, i, AX.X, op, negate=n), s)

    def vrec(o, i, s='V'):
        return _run('v', lambda e, o=o, i=i: e.reciprocal(o, i), s)

    def vms(ap, c, s='V'):
        return _run('v', lambda e, a=ap, c=c: e.memset(a, c), s)

    def bar():
        tm = cnt['T']
        sv = act(g.mc_[:, 62:63], g.mc_[:, 62:63], AF.Copy, s='S')
        w8('v', 'S', sv)
        vv = vcp(g.mc_[:, 63:64], g.mc_[:, 62:63], s='V')
        w8('t', 'S', sv)
        w8('t', 'V', vv)
        w8('s', 'V', vv)
        w8('s', 'T', tm)
        w8('v', 'T', tm)
        for eng in ('g', 'y'):
            w8(eng, 'T', tm)
            w8(eng, 'S', sv)
            w8(eng, 'V', vv)
        return {'T': tm, 'S': sv, 'V': vv}

    g.w8, g.run = w8, _run
    g.wgroup = wgroup
    g.ydma, g.gdma, g.mm, g.tr = ydma, gdma, mm, tr
    g.act, g.tt, g.ts, g.vcp, g.vred, g.vrec, g.vms = act, tt, ts, vcp, vred, vrec, vms
    g.bar = bar


def _tap_out(g, tiles, rows=128):
    for s in ('O', 'X', 'T', 'S', 'V', 'W', 'Wa', 'Wb', 'Wc', 'Wd'):
        g.w8('g', s, g.cnt[s])
    for dst, srcap in tiles:
        g.gdma(dst, srcap, s='D')
    g.w8('g', 'D', g.cnt['D'])
    g.done = True


FPOS, FA, FB_, FC = 0, 2048, 4096, 6144
MLN, MTHQ, MTHK, MTHF, MSC = 0, 1, 9, 17, 18


def emit_all(g):
    _mk(g)
    nc, cs, fb, wb, xn, ps = g.nc, g.cs, g.fb, g.wb, g.xn, g.ps
    mm, act, tt, ts, vcp = g.mm, g.act, g.tt, g.ts, g.vcp
    ydma, w8, bar = g.ydma, g.w8, g.bar
    mc_ = g.mc_

    for k in cs:
        ydma(cs[k][:, :], g.cst[k][:, :])
    ydma(fb[0:1, FA:FA + T], g.cst['pos_row'][0:1, :])
    ydma(g.fv[:, :], g.fv_dram[:, :])
    w8('t', 'X', g.cnt['X'])
    w8('s', 'X', g.cnt['X'])
    w8('v', 'X', g.cnt['X'])

    act(mc_[:, 62:63], cs['tenk'][:, :], AF.Copy, s='S')

    for nt in range(4):
        mm(ps[nt][:, :], cs['ones32'][0:1, :], fb[0:1, FA + 512 * nt:FA + 512 * (nt + 1)],
           True, True, s='T')
    w8('s', 'T', g.cnt['T'])
    for nt in range(4):
        act(fb[:, FPOS + 512 * nt:FPOS + 512 * (nt + 1)], ps[nt][:, :], AF.Copy, s='S')
    act(mc_[:, MLN:MLN + 1], cs['tenk'][:, :], AF.Ln, s='S')
    for c in range(8):
        act(mc_[:, MTHQ + c:MTHQ + c + 1], cs['exq'][:, c:c + 1], AF.Exp,
            scale=mc_[:, MLN:MLN + 1], s='S')
        act(mc_[:, MTHK + c:MTHK + c + 1], cs['exk'][:, c:c + 1], AF.Exp,
            scale=mc_[:, MLN:MLN + 1], s='S')
    act(mc_[:, MTHF:MTHF + 1], cs['exf'][:, :], AF.Exp, scale=mc_[:, MLN:MLN + 1], s='S')
    w8('v', 'S', g.cnt['S'])
    for cols, exc in ((slice(MTHQ, MTHQ + 8), 'exq'), (slice(MTHK, MTHK + 8), 'exk'),
                      (slice(MTHF, MTHF + 1), 'exf')):
        n = cols.stop - cols.start
        ts(mc_[:, 40:40 + n], cs[exc][:, :], mc_[:, MLN:MLN + 1], AL.mult, s='V')
        w8('s', 'V', g.cnt['V'])
        act(mc_[:, 48:48 + n], mc_[:, cols], AF.Ln, s='S')
        w8('v', 'S', g.cnt['S'])
        tt(mc_[:, 40:40 + n], mc_[:, 40:40 + n], mc_[:, 48:48 + n], AL.subtract, s='V')
        ts(mc_[:, 40:40 + n], mc_[:, 40:40 + n], 1.0, AL.add, s='V')
        tt(mc_[:, cols], mc_[:, cols], mc_[:, 40:40 + n], AL.mult, s='V')
    ts(mc_[:, MTHQ:MTHQ + 8], mc_[:, MTHQ:MTHQ + 8], cs['sgq'][:, :], AL.mult, s='V')
    ts(mc_[:, MTHK:MTHK + 8], mc_[:, MTHK:MTHK + 8], cs['sgk4'][:, :], AL.mult, s='V')
    ts(mc_[:, MTHF:MTHF + 1], mc_[:, MTHF:MTHF + 1], cs['sgrh'][:, :], AL.mult, s='V')
    ts(mc_[:, 18:19], mc_[:, MTHF:MTHF + 1], -1.0, AL.mult, s='V')
    ts(mc_[:, 19:20], mc_[:, 18:19], 1.0, AL.mult, PI / 2.0, AL.add, s='V')
    bar()

    if g.tap == 'pos':
        _tap_out(g, [(g.dbg[0:128, 0:T], fb[:, FPOS:FPOS + T]),
                     (g.dbg[128:256, 0:32], mc_[:, 0:32])])
        return

    emit_norm(g, g.x_fm, 'ln', dst=g.xn)
    if g.tap == 'xn0':
        _tap_out(g, [(g.dbg[128 * c:128 * (c + 1), :], xn[:, T * c:T * (c + 1)])
                     for c in range(8)])
        return

    if emit_l0_attn(g):
        return
    if g.tap == 'l0s0':
        _tap_out(g, [(g.dbg[:, :], g.hs[:, :])])
        return

    if emit_l0_ffn_tail(g):
        return
    if g.tap == 'x1':
        _tap_out(g, [(g.dbg[:, :], g.hx1[:, :])])
        return

    emit_norm(g, g.hx1, 'ln', dst=g.xn)
    emit_pos_bcast(g)
    if g.tap == 'xn1':
        _tap_out(g, [(g.dbg[128 * c:128 * (c + 1), :], xn[:, T * c:T * (c + 1)])
                     for c in range(8)])
        return

    P = dict(src=(g.xn, 0), wv_moving=g.wt['l1a_wvr'],
             wq=lambda oc: g.wt['l1a_wq'][:, 128 * oc:128 * (oc + 1)],
             wqp=lambda oc: g.wt['l1a_wqp'][:, 128 * oc:128 * (oc + 1)],
             wk=lambda oc: g.wt['l1a_wkr'][:, 128 * oc:128 * (oc + 1)],
             wkp=lambda oc: g.wt['l1a_wkrp'][:, 128 * oc:128 * (oc + 1)],
             qb=None, qpb=None, kb=None, kpb=None, vb_row=None,
             theta_q=lambda oc: g.mc_[:, MTHQ + oc:MTHQ + oc + 1],
             theta_k=lambda oc: g.mc_[:, MTHK + oc:MTHK + oc + 1],
             pos_s2_sin=0.0, pos_s2_cos=PI / 2.0,
             shared_tables=False, scale=0.5, tau=8.0, dsh=40.0,
             wo=g.wt['l1a_wo'], ob=None, dst=g.hs,
             tap_qk='l1aqk', tap_o='l1ao', tap_tbl='l1atbl')
    if emit_l1_attention(g, P):
        return
    if g.tap == 'l1as':
        _tap_out(g, [(g.dbg[:, :], g.hs[:, :])])
        return

    lv = g.ydma(g.mc2[:, 0:24], g.wt['l1f_qkvbc'][:, :])
    lv = g.ydma(g.mc2[:, 24:40], g.wt['l1f_qkvbcp'][:, :])
    lv = g.ydma(g.mc2[:, 40:48], g.wt['l1f_outbc'][:, :])
    lv = g.ydma(g.mc2[:, 48:56], g.wt['l1f_n1c'][:, :])
    lv = g.ydma(g.mc2[:, 56:64], g.wt['l1f_n2c'][:, :])
    g.w8('s', 'X', lv)
    g.w8('v', 'X', lv)
    emit_norm(g, (g.xn, 0), 'rms', wcol=g.mc2[:, 48:56], dst=g.xm)
    if g.tap == 'xn1f':
        _tap_out(g, [(g.dbg[128 * c:128 * (c + 1), :], g.xm[:, T * c:T * (c + 1)])
                     for c in range(8)])
        return
    P = dict(src=(g.xm, 0), wv_moving=g.wt['l1f_qkvT'][:, 2048:3072],
             wq=lambda oc: g.wt['l1f_qkvT'][:, 128 * oc:128 * (oc + 1)],
             wqp=lambda oc: g.wt['l1f_qkvTp'][:, 128 * oc:128 * (oc + 1)],
             wk=lambda oc: g.wt['l1f_qkvT'][:, 1024 + 128 * oc:1024 + 128 * (oc + 1)],
             wkp=lambda oc: g.wt['l1f_qkvTp'][:, 1024 + 128 * oc:1024 + 128 * (oc + 1)],
             qb=lambda oc: g.mc2[:, oc:oc + 1],
             qpb=lambda oc: g.mc2[:, 24 + oc:25 + oc],
             kb=lambda oc: g.mc2[:, 8 + oc:9 + oc],
             kpb=lambda oc: g.mc2[:, 32 + oc:33 + oc],
             vb_row=g.wt['l1f_vb'][0:1, :],
             theta_q=lambda oc: g.mc_[:, MTHF:MTHF + 1],
             theta_k=lambda oc: g.mc_[:, MTHF:MTHF + 1],
             pos_s2_sin=g.mc_[:, 18:19], pos_s2_cos=g.mc_[:, 19:20],
             shared_tables=True, scale=0.125, tau=8.0, dsh=40.0,
             wo=g.wt['l1f_outwT'], ob=lambda dc: g.mc2[:, 40 + dc:41 + dc], dst=g.ha,
             tap_qk='l1fqk', tap_o='l1fo', tap_tbl='l1ftbl')
    if emit_l1_attention(g, P):
        return
    emit_l1f_post(g)
    if g.tap == 'xn2':
        _tap_out(g, [(g.dbg[128 * c:128 * (c + 1), :], g.xm[:, T * c:T * (c + 1)])
                     for c in range(8)])
        return

    emit_l1_moe_tail(g)
    if g.tap is None:
        g.w8('y', 'O', g.cnt['O'])


def emit_norm(g, xsrc, kind, wcol=None, dst=None, dst_off=0, stat_src_bf=None):
    cs, fb, wb, ps = g.cs, g.fb, g.wb, g.ps
    mm, act, tt, ts, vcp, vrec = g.mm, g.act, g.tt, g.ts, g.vcp, g.vrec
    ydma, w8, bar = g.ydma, g.w8, g.bar
    from_dram = not isinstance(xsrc, tuple)
    eps = 1e-5 if kind == 'ln' else 1e-6
    ln = kind == 'ln'

    g.w8('y', 'O', g.cnt['O'])
    tmarks = []
    vmark_cast = None
    for c in range(8):
        pp = c % 2
        if from_dram:
            xf = fb[:, FA:FA + T]
            if c > 0:
                w8('y', 'V', vmark_cast)
            lv = ydma(xf, xsrc[128 * c:128 * (c + 1), :])
            w8('v', 'X', lv)
            xb = wb[:, 2048 * pp:2048 * pp + T]
            if c >= 2:
                w8('v', 'T', tmarks[c - 2])
            vmark_cast = vcp(xb, xf, s='V')
        else:
            xb = xsrc[0][:, xsrc[1] + T * c:xsrc[1] + T * (c + 1)]
        sq = wb[:, 4096 + 2048 * pp:4096 + 2048 * pp + T]
        if c >= 2:
            w8('v', 'T', tmarks[c - 2])
        sv = tt(sq, xb, xb, AL.mult, s='V')
        w8('t', 'V', sv)
        for nt in range(4):
            sl = slice(512 * nt, 512 * (nt + 1))
            if ln:
                mm(ps[4 + nt][0:1, :], cs['ones_bf'][:, 0:1], xb[:, sl], c == 0, c == 7)
            mm(ps[nt][0:1, :], cs['ones_bf'][:, 0:1], sq[:, sl], c == 0, c == 7,
               s='T' if nt == 3 else None)
        tmarks.append(g.cnt['T'])
    w8('v', 'T', tmarks[-1])
    w8('s', 'T', tmarks[-1])

    arow = fb[0:1, FA:FA + T]
    brow = fb[0:1, FB_:FB_ + T]
    crow = fb[0:1, FC:FC + T]
    for nt in range(4):
        sl = slice(512 * nt, 512 * (nt + 1))
        ts(arow[:, sl], ps[nt][0:1, :], 1.0 / D, AL.mult)
        if ln:
            ts(crow[:, sl], ps[4 + nt][0:1, :], 1.0 / D, AL.mult)
    if ln:
        tt(brow, crow, crow, AL.mult)
        sv = tt(brow, arow, brow, AL.subtract, s='V')
    else:
        sv = vcp(brow, arow, s='V')
    w8('s', 'V', sv)
    act(brow, brow, AF.Sqrt, bias=g.fc(eps), s='S')
    w8('v', 'S', g.cnt['S'])
    vrec(brow, brow)
    if ln:
        tt(crow, crow, brow, AL.mult)
        sv = ts(crow, crow, -1.0, AL.mult, s='V')
    else:
        sv = vcp(g.mc_[0:1, 61:62], brow[0:1, 0:1], s='V')
    w8('t', 'V', sv)

    for nt in range(4):
        sl = slice(512 * nt, 512 * (nt + 1))
        tv = mm(ps[nt][:, :], cs['ones32'][0:1, :], brow[:, sl], True, True, s='T')
        w8('s', 'T', tv)
        act(fb[:, FB_ + 512 * nt:FB_ + 512 * (nt + 1)], ps[nt][:, :], AF.Copy, s='S')
        if ln:
            tv = mm(ps[4 + nt][:, :], cs['ones32'][0:1, :], crow[:, sl], True, True, s='T')
            w8('s', 'T', tv)
            act(fb[:, FC + 512 * nt:FC + 512 * (nt + 1)], ps[4 + nt][:, :], AF.Copy, s='S')
    w8('v', 'S', g.cnt['S'])

    vbar = g.cnt['V']
    for c in range(8):
        xo = dst[:, dst_off + T * c:dst_off + T * (c + 1)]
        if from_dram:
            xf = fb[:, FA:FA + T]
            w8('y', 'V', vbar if c == 0 else vmark_cast)
            lv = ydma(xf, xsrc[128 * c:128 * (c + 1), :])
            w8('v', 'X', lv)
            vmark_cast = tt(xo, xf, fb[:, FB_:FB_ + T], AL.mult, s='V')
        else:
            xi = xsrc[0][:, xsrc[1] + T * c:xsrc[1] + T * (c + 1)]
            tt(xo, xi, fb[:, FB_:FB_ + T], AL.mult)
        if ln:
            tt(xo, xo, fb[:, FC:FC + T], AL.add, s='V')
        else:
            ts(xo, xo, wcol[:, c:c + 1], AL.mult, s='V')
    bar()


def emit_l0_attn(g):
    cs, fb, wb, xn, xm, ps = g.cs, g.fb, g.wb, g.xn, g.xm, g.ps
    psA, psB = g.psA, g.psB
    mm, act, tt, ts, vcp, vred, vrec, tr = g.mm, g.act, g.tt, g.ts, g.vcp, g.vred, g.vrec, g.tr
    ydma, gdma, w8, bar = g.ydma, g.gdma, g.w8, g.bar
    mc_ = g.mc_
    QS, KS, VP, WA, WB2, WC, SQ, S0 = 0, 8192, 16384, 32768, 36864, 40960, 49152, 51200
    SALL, NEGB, BIAS2 = 6144, 6400, 6656

    wq_t = g.wgroup([(wb[:, WA + 512 * kc:WA + 512 * (kc + 1)],
                      g.wt['l0_qstr'][128 * kc:128 * (kc + 1), :]) for kc in range(8)], 'Wa')
    wk_t = g.wgroup([(wb[:, WB2 + 512 * kc:WB2 + 512 * (kc + 1)],
                      g.wt['l0_kstr'][128 * kc:128 * (kc + 1), :]) for kc in range(8)], 'Wb')
    wv_t = g.wgroup([(wb[:, WC + 1024 * kc:WC + 1024 * (kc + 1)],
                      g.wt['l0_vT'][128 * kc:128 * (kc + 1), :]) for kc in range(8)], 'Wc')

    for side, Woff, Aoff in (('q', WA, QS), ('k', WB2, KS)):
        w8('t', 'Wa' if side == 'q' else 'Wb', wq_t if side == 'q' else wk_t)
        for c4 in range(4):
            for nt in range(4):
                bank = ps[(c4 * 4 + nt) % 2]
                if not (side == 'q' and c4 == 0 and nt < 2):
                    w8('t', 'V', g.cnt['V'])
                for kc in range(8):
                    mm(bank, wb[:, Woff + 512 * kc + 128 * c4:Woff + 512 * kc + 128 * (c4 + 1)],
                       xn[:, T * kc + 512 * nt:T * kc + 512 * (nt + 1)],
                       kc == 0, kc == 7, s='T' if kc == 7 else None)
                w8('v', 'T', g.cnt['T'])
                vcp(wb[:, Aoff + 2048 * c4 + 512 * nt:Aoff + 2048 * c4 + 512 * (nt + 1)], bank)
    w8('t', 'Wc', wv_t)
    for mc in range(16):
        for vh in range(2):
            bank = ps[2 + (mc * 2 + vh) % 2]
            w8('t', 'V', g.cnt['V'])
            for kc in range(8):
                mm(bank, xn[:, T * kc + 128 * mc:T * kc + 128 * (mc + 1)],
                   wb[:, WC + 1024 * kc + 512 * vh:WC + 1024 * kc + 512 * (vh + 1)],
                   kc == 0, kc == 7, s='T' if kc == 7 else None)
            w8('v', 'T', g.cnt['T'])
            vcp(wb[:, VP + 1024 * mc + 512 * vh:VP + 1024 * mc + 512 * (vh + 1)], bank)
    vdone = g.cnt['V']
    w8('g', 'V', vdone)
    wo_t = g.wgroup([(wb[:, WC + 1024 * hc:WC + 1024 * (hc + 1)],
                      g.wt['l0_opT'][128 * hc:128 * (hc + 1), :]) for hc in range(8)], 'Wd')

    for c4 in range(4):
        sq = wb[:, SQ:SQ + T]
        qs = wb[:, QS + 2048 * c4:QS + 2048 * (c4 + 1)]
        if c4 > 0:
            w8('v', 'T', g.cnt['T'])
        tt(sq, qs, qs, AL.mult, s='V')
        w8('t', 'V', g.cnt['V'])
        for nt in range(4):
            if c4 > 0 and nt == 0:
                w8('t', 'V', g.cnt['V'])
            mm(ps[4 + nt], cs['bsumL0'][:, :], sq[:, 512 * nt:512 * (nt + 1)],
               True, True, s='T' if nt == 3 else None)
        w8('v', 'T', g.cnt['T'])
        for nt in range(4):
            vred(mc_[:, 30 + nt:31 + nt], ps[4 + nt], AL.max)
        tt(mc_[:, 30:31], mc_[:, 30:31], mc_[:, 31:32], AL.max)
        tt(mc_[:, 32:33], mc_[:, 32:33], mc_[:, 33:34], AL.max)
        tt(mc_[:, 20 + c4:21 + c4], mc_[:, 30:31], mc_[:, 32:33], AL.max)
        ts(mc_[:, 20 + c4:21 + c4], mc_[:, 20 + c4:21 + c4], 0.0625, AL.mult, s='V')
    for c4 in range(4):
        sq = wb[:, SQ:SQ + T]
        ks = wb[:, KS + 2048 * c4:KS + 2048 * (c4 + 1)]
        w8('v', 'T', g.cnt['T'])
        tt(sq, ks, ks, AL.mult, s='V')
        w8('t', 'V', g.cnt['V'])
        for nt in range(4):
            if nt == 0:
                w8('t', 'S', g.cnt['S'])
            mm(ps[4 + nt], cs['bsumL0'][:, :], sq[:, 512 * nt:512 * (nt + 1)],
               True, True, s='T' if nt == 3 else None)
        w8('s', 'T', g.cnt['T'])
        for nt in range(4):
            act(fb[:, FB_ + 512 * nt:FB_ + 512 * (nt + 1)], ps[4 + nt], AF.Sqrt,
                scale=mc_[:, 20 + c4:21 + c4], s='S')
        w8('t', 'S', g.cnt['S'])
        for mc in range(16):
            tv = tr(ps[mc % 2][:, 0:128], fb[:, FB_ + 128 * mc:FB_ + 128 * (mc + 1)],
                    cs['ident'][:, :], s='T')
            w8('s', 'T', tv)
            for j in range(4):
                act(fb[:, NEGB + 16 * mc + 4 * c4 + j:NEGB + 16 * mc + 4 * c4 + j + 1],
                    ps[mc % 2][:, 32 * j:32 * j + 1], AF.Copy,
                    scale=g.fc(-1.0), bias=g.fc(25.0), s='S')
            w8('t', 'S', g.cnt['S'])
    bar()
    if g.tap == 'l0bnd':
        _tap_out(g, [(g.dbg[0:128, 0:256], fb[:, NEGB:NEGB + 256]),
                     (g.dbg[128:256, 0:1024], wb[:, VP:VP + 1024]),
                     (g.dbg[384:512, 0:2048], wb[:, QS:QS + 2048]),
                     (g.dbg[512:640, 0:2048], wb[:, KS:KS + 2048])])
        return True

    sprev = {}
    vlast = None
    for mc in range(16):
        for q4 in range(4):
            bs = 0 if q4 % 2 == 0 else 4
            c4 = q4
            for nt in range(4):
                for j in range(4):
                    key = bs + j
                    if key in sprev:
                        w8('t', 'S', sprev[key])
                    mm(ps[key],
                       wb[32 * j:32 * j + 16,
                          KS + 2048 * c4 + 128 * mc:KS + 2048 * c4 + 128 * (mc + 1)],
                       wb[32 * j:32 * j + 16,
                          QS + 2048 * c4 + 512 * nt:QS + 2048 * c4 + 512 * (nt + 1)],
                       True, True, tp=(32 * j, 0), s='T' if j == 3 else None)
                w8('s', 'T', g.cnt['T'])
                if nt == 0 and vlast is not None:
                    w8('s', 'V', vlast)
                for j in range(4):
                    h = 4 * q4 + j
                    sprev[bs + j] = act(
                        xm[:, 512 * j:512 * (j + 1)], ps[bs + j], AF.Exp,
                        bias=fb[:, NEGB + 16 * mc + h:NEGB + 16 * mc + h + 1],
                        scale=g.fc(0.25),
                        accum=mc_[:, 36 + 4 * j + nt:37 + 4 * j + nt], s='S')
            w8('v', 'S', g.cnt['S'])
            for j in range(4):
                h = 4 * q4 + j
                vred(fb[:, SALL + 16 * mc + h:SALL + 16 * mc + h + 1],
                     mc_[:, 36 + 4 * j:36 + 4 * j + 4], AL.add, s='V')
            vlast = g.cnt['V']
    w8('s', 'V', g.cnt['V'])
    act(fb[:, BIAS2:BIAS2 + 256], fb[:, SALL:SALL + 256], AF.Ln, s='S')
    w8('v', 'S', g.cnt['S'])
    tt(fb[:, BIAS2:BIAS2 + 256], fb[:, NEGB:NEGB + 256], fb[:, BIAS2:BIAS2 + 256],
       AL.subtract, s='V')
    bar()
    if g.tap == 'l0S':
        _tap_out(g, [(g.dbg[0:128, 0:256], fb[:, BIAS2:BIAS2 + 256]),
                     (g.dbg[128:256, 0:1024], wb[:, VP:VP + 1024]),
                     (g.dbg[256:384, 0:256], fb[:, SALL:SALL + 256])])
        return True

    sprevB = {}
    for q4 in range(4):
        c4 = q4
        for nt in range(4):
            for mc in range(16):
                for j in range(4):
                    h = 4 * q4 + j
                    if (4 + j) in sprevB:
                        w8('t', 'S', sprevB[4 + j])
                    mm(ps[4 + j],
                       wb[32 * j:32 * j + 16,
                          KS + 2048 * c4 + 128 * mc:KS + 2048 * c4 + 128 * (mc + 1)],
                       wb[32 * j:32 * j + 16,
                          QS + 2048 * c4 + 512 * nt:QS + 2048 * c4 + 512 * (nt + 1)],
                       True, True, tp=(32 * j, 0), s='T' if j == 3 else None)
                w8('s', 'T', g.cnt['T'])
                exv = {}
                for j in range(4):
                    h = 4 * q4 + j
                    exv[j] = act(xm[:, 512 * j:512 * (j + 1)], ps[4 + j], AF.Exp,
                                 bias=fb[:, BIAS2 + 16 * mc + h:BIAS2 + 16 * mc + h + 1],
                                 scale=g.fc(0.25), s='S')
                    sprevB[4 + j] = exv[j]
                for j in range(4):
                    h = 4 * q4 + j
                    pb = 64 * (h - (h % 2))
                    w8('t', 'S', exv[j])
                    mm(ps[j], wb[:, VP + 1024 * mc + pb:VP + 1024 * mc + pb + 128],
                       xm[:, 512 * j:512 * (j + 1)],
                       mc == 0, mc == 15, sgc=True,
                       s='T' if (mc == 15 and j == 3) else None)
            w8('s', 'T', g.cnt['T'])
            for j in range(4):
                h = 4 * q4 + j
                r0 = 64 * (h % 2)
                ev = act(wb[r0:r0 + 64, S0:S0 + 512], ps[j][r0:r0 + 64, :], AF.Copy, s='S')
                w8('y', 'S', ev)
                ov = ydma(g.ho[64 * h:64 * (h + 1), 512 * nt:512 * (nt + 1)],
                          wb[r0:r0 + 64, S0:S0 + 512], s='O')
                g.q['s'].append(lambda e, S=g.sem['O'], V=ov: e.wait_ge(S, V))
            w8('t', 'S', g.cnt['S'])
    bar()
    if g.tap == 'l0o':
        _tap_out(g, [(g.dbg[:, :], g.ho[:, :])])
        return True

    w8('g', 'O', g.cnt['O'])
    lo_t = g.wgroup([(wb[:, QS + 2048 * hc:QS + 2048 * (hc + 1)],
                      g.ho[128 * hc:128 * (hc + 1), :]) for hc in range(8)], 'Wa')
    w8('t', 'Wd', wo_t)
    w8('t', 'Wa', lo_t)
    for nt in range(4):
        for dc in range(8):
            bank = ps[(nt * 8 + dc) % 2]
            if nt + dc > 0:
                w8('t', 'S', g.cnt['S'])
            for hc in range(8):
                mm(bank, wb[:, WC + 1024 * hc + 128 * dc:WC + 1024 * hc + 128 * (dc + 1)],
                   wb[:, QS + 2048 * hc + 512 * nt:QS + 2048 * hc + 512 * (nt + 1)],
                   hc == 0, hc == 7, s='T' if hc == 7 else None)
            w8('s', 'T', g.cnt['T'])
            ev = act(wb[:, S0:S0 + 512], bank, AF.Copy, s='S')
            w8('y', 'S', ev)
            ov = ydma(g.hs[128 * dc:128 * (dc + 1), 512 * nt:512 * (nt + 1)],
                      wb[:, S0:S0 + 512], s='O')
            g.q['s'].append(lambda e, S=g.sem['O'], V=ov: e.wait_ge(S, V))
    bar()
    return False


def emit_l0_ffn_tail(g):
    cs, fb, wb, xn, xm, ps = g.cs, g.fb, g.wb, g.xn, g.xm, g.ps
    mm, act, tt, ts, vcp, vred, vrec, tr = g.mm, g.act, g.tt, g.ts, g.vcp, g.vred, g.vrec, g.tr
    ydma, gdma, w8, bar = g.ydma, g.gdma, g.w8, g.bar
    mc_ = g.mc_
    W1s = (0, 4096)
    W2s = (8192, 12288)
    H1, CB, CBC, B2, RT, SB, GG = 16384, 20480, 22528, 23552, 24576, 24704, 25728
    bv = ydma(mc_[:, 20:52], g.wt['l0_b1c'][:, :])
    bv = ydma(mc_[0:8, 52:53], g.wt['l0_rtb'][:, :])
    rtv = g.wgroup([(wb[:, RT + 8 * kc:RT + 8 * (kc + 1)],
                     g.wt['l0_rtr'][128 * kc:128 * (kc + 1), :]) for kc in range(8)], 'Wa')
    b2v = g.wgroup([(wb[0:8, B2:B2 + 1024], g.wt['l0_b2r'][:, :])], 'Wb')

    w8('t', 'Wa', rtv)
    for nt in range(4):
        bank = ps[4 + nt]
        for kc in range(8):
            mm(bank[0:8, :], wb[:, RT + 8 * kc:RT + 8 * (kc + 1)],
               xn[:, T * kc + 512 * nt:T * kc + 512 * (nt + 1)],
               kc == 0, kc == 7, s='T' if kc == 7 else None)
        w8('s', 'T', g.cnt['T'])
        w8('s', 'X', bv)
        act(fb[0:8, FA + 512 * nt:FA + 512 * (nt + 1)], bank[0:8, :], AF.Identity,
            bias=mc_[0:8, 52:53], s='S')
    LG = 6144
    for tsub in range(16):
        w8('t', 'S', g.cnt['S'])
        tv = tr(ps[0][:, 0:8], fb[0:8, FA + 128 * tsub:FA + 128 * (tsub + 1)],
                cs['ident'][0:8, 0:8], s='T')
        w8('v', 'T', tv)
        lg = fb[:, LG:LG + 8]
        vcp(lg, ps[0][:, 0:8])
        m1n = mc_[:, 53:54]
        vred(m1n, lg, AL.max, negate=True)
        ts(fb[:, LG + 8:LG + 16], lg, m1n, AL.add, s='V')
        ts(fb[:, LG + 16:LG + 24], fb[:, LG + 8:LG + 16], 0.0, AL.is_ge, s='V')
        ts(fb[:, LG + 24:LG + 32], fb[:, LG + 16:LG + 24], g.fc(-1e30), AL.mult, s='V')
        tt(fb[:, LG + 24:LG + 32], fb[:, LG + 8:LG + 16], fb[:, LG + 24:LG + 32], AL.add)
        vred(mc_[:, 54:55], fb[:, LG + 24:LG + 32], AL.max)
        w8('s', 'V', g.cnt['V'])
        act(mc_[:, 55:56], mc_[:, 54:55], AF.Exp, s='S')
        act(fb[:, LG + 32:LG + 40], fb[:, LG + 8:LG + 16], AF.Exp, s='S')
        w8('v', 'S', g.cnt['S'])
        ts(mc_[:, 55:56], mc_[:, 55:56], 1.0, AL.add, s='V')
        vrec(mc_[:, 55:56], mc_[:, 55:56])
        ts(fb[:, LG + 40:LG + 48], fb[:, LG + 8:LG + 16], mc_[:, 54:55], AL.is_ge, s='V')
        tt(fb[:, LG + 32:LG + 40], fb[:, LG + 32:LG + 40], fb[:, LG + 40:LG + 48], AL.mult)
        ts(fb[:, LG:LG + 8], fb[:, LG + 32:LG + 40], mc_[:, 55:56], AL.mult, s='V')
        w8('t', 'V', g.cnt['V'])
        tv = tr(ps[1][0:8, 0:128], fb[:, LG:LG + 8], cs['ident'][:, :], s='T')
        w8('s', 'T', tv)
        act(wb[0:8, CB + 128 * tsub:CB + 128 * (tsub + 1)], ps[1][0:8, 0:128],
            AF.Copy, s='S')
    bar()
    if g.tap == 'l0comb':
        _tap_out(g, [(g.dbg[0:8, 0:T], wb[0:8, CB:CB + T])])
        return True

    for half in range(2):
        bar()
        for eng in ('t', 's', 'v', 'g', 'y'):
            w8(eng, 'O', g.cnt['O'])
        t0 = 1024 * half
        w8('t', 'Wb', b2v)
        for dc in range(8):
            for t2 in range(2):
                bank = ps[(dc * 2 + t2) % 4]
                if dc + t2 > 0:
                    w8('t', 'S', g.cnt['S'])
                mm(bank, wb[0:8, B2 + 128 * dc:B2 + 128 * (dc + 1)],
                   wb[0:8, CB + t0 + 512 * t2:CB + t0 + 512 * (t2 + 1)],
                   True, True, s='T')
                w8('s', 'T', g.cnt['T'])
                act(fb[:, 1024 * dc + 512 * t2:1024 * dc + 512 * (t2 + 1)],
                    bank, AF.Copy, s='S')
        w8('v', 'S', g.cnt['S'])
        wsem = {}
        tm_w1, tm_w2 = {}, {}
        for e in range(8):
            sl = e % 2
            if e >= 2:
                w8('g', 'T', tm_w1[e - 2])
                w8('g', 'T', tm_w2[e - 2])
            elif half == 1:
                w8('g', 'T', g.l0_tm_last)
            wsem[('w1', e)] = g.wgroup(
                [(wb[:, W1s[sl] + 512 * kc:W1s[sl] + 512 * (kc + 1)],
                  g.wt['l0_W1'][e, 128 * kc:128 * (kc + 1), :]) for kc in range(8)],
                'Wa' if sl == 0 else 'Wb')
            wsem[('w2', e)] = g.wgroup(
                [(wb[:, W2s[sl] + 1024 * fc:W2s[sl] + 1024 * (fc + 1)],
                  g.wt['l0_W2'][e, 128 * fc:128 * (fc + 1), :]) for fc in range(4)],
                'Wc' if sl == 0 else 'Wd')
            for t2 in range(2):
                bank = ps[4 + t2]
                w8('t', 'V', g.cnt['V'])
                mm(bank, cs['sel8'][:, 128 * e:128 * (e + 1)],
                   wb[0:8, CB + t0 + 512 * t2:CB + t0 + 512 * (t2 + 1)],
                   True, True, s='T')
                w8('v', 'T', g.cnt['T'])
                vcp(wb[:, CBC + 512 * t2:CBC + 512 * (t2 + 1)], bank)
            w8('t', 'Wa' if sl == 0 else 'Wb', wsem[('w1', e)])
            for fc in range(4):
                for t2 in range(2):
                    bank = ps[4 + 2 + (fc * 2 + t2) % 2]
                    w8('t', 'S', g.cnt['S'])
                    for kc in range(8):
                        mm(bank, wb[:, W1s[sl] + 512 * kc + 128 * fc:W1s[sl] + 512 * kc + 128 * (fc + 1)],
                           xn[:, T * kc + t0 + 512 * t2:T * kc + t0 + 512 * (t2 + 1)],
                           kc == 0, kc == 7, s='T' if kc == 7 else None)
                    w8('s', 'T', g.cnt['T'])
                    act(wb[:, H1 + 1024 * fc + 512 * t2:H1 + 1024 * fc + 512 * (t2 + 1)],
                        bank, AF.Gelu, bias=mc_[:, 20 + 4 * e + fc:21 + 4 * e + fc], s='S')
            tm_w1[e] = g.cnt['T']
            w8('v', 'S', g.cnt['S'])
            for fc in range(4):
                for t2 in range(2):
                    tt(wb[:, H1 + 1024 * fc + 512 * t2:H1 + 1024 * fc + 512 * (t2 + 1)],
                       wb[:, H1 + 1024 * fc + 512 * t2:H1 + 1024 * fc + 512 * (t2 + 1)],
                       wb[:, CBC + 512 * t2:CBC + 512 * (t2 + 1)], AL.mult, s='V')
            hv = g.cnt['V']
            w8('t', 'V', hv)
            w8('t', 'Wc' if sl == 0 else 'Wd', wsem[('w2', e)])
            for dc in range(8):
                for t2 in range(2):
                    bank = ps[(dc * 2 + t2) % 4]
                    w8('t', 'V', g.cnt['V'])
                    for fc in range(4):
                        mm(bank, wb[:, W2s[sl] + 1024 * fc + 128 * dc:W2s[sl] + 1024 * fc + 128 * (dc + 1)],
                           wb[:, H1 + 1024 * fc + 512 * t2:H1 + 1024 * fc + 512 * (t2 + 1)],
                           fc == 0, fc == 3, s='T' if fc == 3 else None)
                    w8('v', 'T', g.cnt['T'])
                    tt(fb[:, 1024 * dc + 512 * t2:1024 * dc + 512 * (t2 + 1)],
                       fb[:, 1024 * dc + 512 * t2:1024 * dc + 512 * (t2 + 1)],
                       bank, AL.add, s='V')
            tm_w2[e] = g.cnt['T']
        g.l0_tm_last = g.cnt['T']
        bar()
        if g.tap == 'l0ffn' and half == 0:
            _tap_out(g, [(g.dbg[128 * dc:128 * (dc + 1), 0:1024],
                          fb[:, 1024 * dc:1024 * (dc + 1)]) for dc in range(8)])
            return True
        w8('v', 'T', g.cnt['T'])
        for dc in range(8):
            for t2 in range(2):
                osl = fb[:, 1024 * dc + 512 * t2:1024 * dc + 512 * (t2 + 1)]
                gsl = wb[:, GG:GG + 512]
                ssl = wb[:, GG + 512:GG + 1024]
                w8('s', 'V', g.cnt['V'])
                w8('s', 'O', g.cnt['O'])
                sv = act(gsl, osl, AF.Silu, s='S')
                w8('y', 'S', sv)
                lv1 = ydma(osl, g.x_fm[128 * dc:128 * (dc + 1),
                                       t0 + 512 * t2:t0 + 512 * (t2 + 1)])
                lv2 = ydma(ssl, g.hs[128 * dc:128 * (dc + 1),
                                     t0 + 512 * t2:t0 + 512 * (t2 + 1)])
                w8('v', 'X', lv2)
                tt(ps[6], gsl, ssl, AL.mult)
                tt(osl, osl, ps[6], AL.add, s='V')
                w8('y', 'V', g.cnt['V'])
                ydma(g.hx1[128 * dc:128 * (dc + 1), t0 + 512 * t2:t0 + 512 * (t2 + 1)],
                     osl, s='O')
    w8('y', 'O', g.cnt['O'])
    bar()
    return False


def emit_pos_bcast(g):
    cs, fb, ps = g.cs, g.fb, g.ps
    lv = g.ydma(fb[0:1, FA:FA + T], g.cst['pos_row'][0:1, :])
    g.w8('t', 'X', lv)
    g.w8('t', 'S', g.cnt['S'])
    for nt in range(4):
        tv = g.mm(ps[nt][:, :], cs['ones32'][0:1, :], fb[0:1, FA + 512 * nt:FA + 512 * (nt + 1)],
                  True, True, s='T')
        g.w8('s', 'T', tv)
        g.act(fb[:, FPOS + 512 * nt:FPOS + 512 * (nt + 1)], ps[nt][:, :], AF.Copy, s='S')
    g.bar()


def emit_l1_attention(g, P):
    cs, fb, wb, xm, ps = g.cs, g.fb, g.wb, g.xm, g.ps
    psA, psB = g.psA, g.psB
    mm, act, tt, ts, vcp, vred, vrec, tr = g.mm, g.act, g.tt, g.ts, g.vcp, g.vred, g.vrec, g.tr
    ydma, gdma, w8, bar = g.ydma, g.gdma, g.w8, g.bar
    mc_ = g.mc_
    QR, KR, VP, WS = 16384, 32768, 0, 49152
    OS = 51200
    NB = 0
    SC1, SC2, MSK, EX = 8192, 10240, 12288, 12416
    src = P['src']
    scale, tau, dsh = P['scale'], P['tau'], P['dsh']

    wvv = g.wgroup([(wb[:, QR + 1024 * kc:QR + 1024 * (kc + 1)],
                     P['wv_moving'][128 * kc:128 * (kc + 1), :]) for kc in range(8)], 'Wa')
    if P.get('vb_row') is not None:
        vbv = g.wgroup([(wb[0:1, WS:WS + 1024], P['vb_row'])], 'Wb')
        w8('t', 'Wb', vbv)
    w8('t', 'Wa', wvv)
    g._ws_tmark = None
    hasvb = P.get('vb_row') is not None
    for mc in range(16):
        for vh in range(2):
            bank = ps[4 + (mc * 2 + vh) % 2]
            w8('t', 'V', g.cnt['V'])
            for kc in range(8):
                mm(bank, src[0][:, src[1] + T * kc + 128 * mc:src[1] + T * kc + 128 * (mc + 1)],
                   wb[:, QR + 1024 * kc + 512 * vh:QR + 1024 * kc + 512 * (vh + 1)],
                   kc == 0, kc == 7,
                   s='T' if (kc == 7 and not hasvb) else None)
            if hasvb:
                mm(bank, cs['ones_bf'][0:1, :], wb[0:1, WS + 512 * vh:WS + 512 * (vh + 1)],
                   False, True, sgc=True, s='T')
            w8('v', 'T', g.cnt['T'])
            vcp(wb[:, VP + 1024 * mc + 512 * vh:VP + 1024 * mc + 512 * (vh + 1)], bank)
    bar()

    if P['shared_tables']:
        _build_tables(g, P, 0)
    for side, Aoff, wsel, wpsel, bsel, bpsel, thsel in (
            ('q', QR, P['wq'], P['wqp'], P.get('qb'), P.get('qpb'), 'theta_q'),
            ('k', KR, P['wk'], P['wkp'], P.get('kb'), P.get('kpb'), 'theta_k')):
        for oc in range(8):
            sems = ('Wa', 'Wb') if side == 'q' else ('Wc', 'Wd')
            if g._ws_tmark is not None:
                w8('g', 'T', g._ws_tmark)
            lv1 = g.wgroup([(wb[:, WS + 256 * kc:WS + 256 * kc + 128],
                             wsel(oc)[128 * kc:128 * (kc + 1), :]) for kc in range(8)], sems[0])
            lv2 = g.wgroup([(wb[:, WS + 256 * kc + 128:WS + 256 * kc + 256],
                             wpsel(oc)[128 * kc:128 * (kc + 1), :]) for kc in range(8)], sems[1])
            if not P['shared_tables']:
                _build_tables(g, P, oc, thsel)
            w8('t', sems[0], lv1)
            w8('t', sems[1], lv2)
            w8('t', 'S', g.cnt['S'])
            for pp, bias in ((0, bsel), (1, bpsel)):
                for nt in range(4):
                    bank = (psA if pp == 0 else psB)[:, 512 * nt:512 * (nt + 1)]
                    for kc in range(8):
                        mm(bank, wb[:, WS + 256 * kc + 128 * pp:WS + 256 * kc + 128 * (pp + 1)],
                           src[0][:, src[1] + T * kc + 512 * nt:src[1] + T * kc + 512 * (nt + 1)],
                           kc == 0, kc == 7, s='T' if kc == 7 else None)
            w8('s', 'T', g.cnt['T'])
            src_is_xm = src[0] is g.xm
            for pp, bias in ((0, bsel), (1, bpsel)):
                pst = psA if pp == 0 else psB
                if src_is_xm:
                    dstc = fb[:, (FA if pp == 0 else FPOS):(FA if pp == 0 else FPOS) + 2048]
                else:
                    dstc = g.xm[:, (SC1 if pp == 0 else SC2):(SC1 if pp == 0 else SC2) + 2048]
                if bias is not None:
                    act(dstc, pst[:, :], AF.Identity, bias=bias(oc), s='S')
                else:
                    act(dstc, pst[:, :], AF.Copy, s='S')
            if g.tap == P.get('tap_tbl') and side == 'q' and oc == 0:
                _tap_out(g, [(g.dbg[0:128, :], g.xm[:, SC1:SC1 + 2048]),
                             (g.dbg[128:256, :], g.xm[:, SC2:SC2 + 2048]),
                             (g.dbg[256:384, :], fb[:, FB_:FB_ + T]),
                             (g.dbg[384:512, :], fb[:, FC:FC + T])])
                return True
            w8('v', 'S', g.cnt['S'])
            if src_is_xm:
                t0 = fb[:, FA:FA + 2048]
                t1 = fb[:, FPOS:FPOS + 2048]
            else:
                t0 = g.xm[:, SC1:SC1 + 2048]
                t1 = g.xm[:, SC2:SC2 + 2048]
            out = wb[:, Aoff + 2048 * oc:Aoff + 2048 * (oc + 1)]
            tt(out, t0, fb[:, FC:FC + T], AL.mult)
            tt(t1, t1, fb[:, FB_:FB_ + T], AL.mult)
            tt(out, out, t1, AL.add, s='V')
            w8('t', 'V', g.cnt['V'])
            g._ws_tmark = g.cnt['T']
    bar()
    if g.tap == P.get('tap_qk'):
        _tap_out(g, [(g.dbg[0:128, 0:T], wb[:, QR:QR + T]),
                     (g.dbg[128:256, 0:T], wb[:, KR:KR + T]),
                     (g.dbg[256:384, 0:1024], wb[:, VP:VP + 1024])])
        return True

    for c in range(8):
        bst = cs['bsumA'] if c % 2 == 0 else cs['bsumB']
        rbase = 64 * (c % 2)
        xtile = NB + 2048 * (c // 2)
        kch = wb[:, KR + 2048 * c:KR + 2048 * (c + 1)]
        qch = wb[:, QR + 2048 * c:QR + 2048 * (c + 1)]
        vred(mc_[:, 20:21], kch, AL.add)
        ts(mc_[:, 20:21], mc_[:, 20:21], 1.0 / T, AL.mult, s='V')
        w8('v', 'T', g.cnt['T'])
        ts(g.xm[:, SC1:SC1 + T], kch, mc_[:, 20:21], AL.subtract, s='V')
        tt(g.xm[:, SC1:SC1 + T], g.xm[:, SC1:SC1 + T], g.xm[:, SC1:SC1 + T], AL.mult, s='V')
        w8('t', 'V', g.cnt['V'])
        for nt in range(4):
            mm(ps[4 + nt], bst[:, :], g.xm[:, SC1 + 512 * nt:SC1 + 512 * (nt + 1)],
               True, True, s='T' if nt == 3 else None)
        w8('v', 'T', g.cnt['T'])
        for nt in range(4):
            vred(mc_[:, 30 + nt:31 + nt], ps[4 + nt], AL.max)
        tt(mc_[:, 30:31], mc_[:, 30:31], mc_[:, 31:32], AL.max)
        tt(mc_[:, 32:33], mc_[:, 32:33], mc_[:, 33:34], AL.max)
        tt(mc_[:, 21:22], mc_[:, 30:31], mc_[:, 32:33], AL.max, s='V')
        w8('s', 'V', g.cnt['V'])
        act(mc_[:, 21:22], mc_[:, 21:22], AF.Sqrt, s='S')
        g.vms(g.xm[:, MSK:MSK + 128], 0.0)
        w8('v', 'S', g.cnt['S'])
        vcp(g.xm[0:64, MSK + rbase:MSK + rbase + 1], mc_[0:64, 20:21])
        vcp(g.xm[64:128, MSK + rbase + 32:MSK + rbase + 33], mc_[64:128, 20:21], s='V')
        w8('t', 'V', g.cnt['V'])
        for nt in range(4):
            mm(ps[nt], g.xm[:, MSK:MSK + 128], qch[:, 512 * nt:512 * (nt + 1)],
               True, True, s='T' if nt == 3 else None)
        tq = g.cnt['T']
        w8('v', 'T', tq)
        tt(g.xm[:, SC2:SC2 + T], qch, qch, AL.mult, s='V')
        w8('t', 'V', g.cnt['V'])
        for nt in range(4):
            mm(ps[4 + nt], bst[:, :], g.xm[:, SC2 + 512 * nt:SC2 + 512 * (nt + 1)],
               True, True, s='T' if nt == 3 else None)
        w8('s', 'T', g.cnt['T'])
        w8('v', 'T', g.cnt['T'])
        for nt in range(4):
            act(fb[:, FA + 512 * nt:FA + 512 * (nt + 1)], ps[4 + nt], AF.Sqrt, s='S')
        w8('v', 'S', g.cnt['S'])
        for nt in range(4):
            sl = slice(512 * nt, 512 * (nt + 1))
            ts(fb[:, FA:FA + T][:, sl], fb[:, FA:FA + T][:, sl], mc_[:, 21:22], AL.mult)
            tt(fb[:, FA:FA + T][:, sl], fb[:, FA:FA + T][:, sl], ps[nt], AL.add)
            for j in range(2):
                rh = rbase + 32 * j
                ts(g.xm[rh:rh + 1, xtile:xtile + T][:, sl],
                   fb[rh:rh + 1, FA:FA + T][:, sl], -1.0, AL.mult, s='V')
        w8('t', 'V', g.cnt['V'])
    bar()

    for qt in range(4):
        for c in range(8):
            rbase = 64 * (c % 2)
            xtile = NB + 2048 * (c // 2)
            for j in range(2):
                h = 2 * c + j
                rh = rbase + 32 * j
                sab = ps[2 + j]
                for mc in range(16):
                    bank = ps[mc % 2]
                    if True:
                        w8('t', 'S', g.cnt['S'])
                    mm(bank, cs['ones_bf'][rh:rh + 1, :],
                       g.xm[rh:rh + 1, xtile + 512 * qt:xtile + 512 * (qt + 1)],
                       True, True, tp=(rh, 0), sgc=True)
                    mm(bank, wb[64 * j:64 * (j + 1), KR + 2048 * c + 128 * mc:KR + 2048 * c + 128 * (mc + 1)],
                       wb[64 * j:64 * (j + 1), QR + 2048 * c + 512 * qt:QR + 2048 * c + 512 * (qt + 1)],
                       False, True, sgc=True, s='T')
                    w8('s', 'T', g.cnt['T'])
                    ev = act(g.xm[:, EX:EX + 512], bank, AF.Exp, scale=g.fc(scale / tau), s='S')
                    w8('t', 'S', ev)
                    mm(sab, cs['ones_bf'][:, :], g.xm[:, EX:EX + 512],
                       mc == 0, mc == 15, sgc=True, s='T' if mc == 15 else None)
                w8('s', 'T', g.cnt['T'])
                act(fb[rh:rh + 1, FA:FA + 512], sab[rh:rh + 1, :], AF.Ln, s='S')
                w8('v', 'S', g.cnt['S'])
                ts(fb[rh:rh + 1, FA:FA + 512], fb[rh:rh + 1, FA:FA + 512],
                   g.fc(-tau / scale), AL.mult, g.fc(dsh / scale), AL.add, s='V')
                tt(g.xm[rh:rh + 1, xtile + 512 * qt:xtile + 512 * (qt + 1)],
                   fb[rh:rh + 1, FA:FA + 512],
                   g.xm[rh:rh + 1, xtile + 512 * qt:xtile + 512 * (qt + 1)], AL.add, s='V')
                w8('t', 'V', g.cnt['V'])
    bar()

    for qt in range(4):
        for c in range(8):
            rbase = 64 * (c % 2)
            xtile = NB + 2048 * (c // 2)
            for j in range(2):
                h = 2 * c + j
                rh = rbase + 32 * j
                ob = ps[2 + j]
                sbb = ps[4 + j]
                for mc in range(16):
                    bank = ps[mc % 2]
                    w8('t', 'S', g.cnt['S'])
                    mm(bank, cs['ones_bf'][rh:rh + 1, :],
                       g.xm[rh:rh + 1, xtile + 512 * qt:xtile + 512 * (qt + 1)],
                       True, True, tp=(rh, 0), sgc=True)
                    mm(bank, wb[64 * j:64 * (j + 1), KR + 2048 * c + 128 * mc:KR + 2048 * c + 128 * (mc + 1)],
                       wb[64 * j:64 * (j + 1), QR + 2048 * c + 512 * qt:QR + 2048 * c + 512 * (qt + 1)],
                       False, True, sgc=True, s='T')
                    w8('s', 'T', g.cnt['T'])
                    ev = act(g.xm[:, EX:EX + 512], bank, AF.Exp, scale=g.fc(scale), s='S')
                    w8('t', 'S', ev)
                    pb = 64 * (h - (h % 2))
                    mm(ob, wb[:, VP + 1024 * mc + pb:VP + 1024 * mc + pb + 128],
                       g.xm[:, EX:EX + 512], mc == 0, mc == 15, sgc=True)
                    mm(sbb, cs['ones_bf'][:, :], g.xm[:, EX:EX + 512],
                       mc == 0, mc == 15, sgc=True, s='T' if mc == 15 else None)
                w8('v', 'T', g.cnt['T'])
                r0 = 64 * (h % 2)
                vrec(fb[:, FA:FA + 512], sbb)
                tt(wb[r0:r0 + 64, OS:OS + 512],
                   ob[r0:r0 + 64, :], fb[r0:r0 + 64, FA:FA + 512], AL.mult, s='V')
                w8('y', 'V', g.cnt['V'])
                ov = ydma(g.ho[64 * h:64 * (h + 1), 512 * qt:512 * (qt + 1)],
                          wb[r0:r0 + 64, OS:OS + 512], s='O')
                g.q['v'].append(lambda e, S=g.sem['O'], V=ov: e.wait_ge(S, V))
                w8('t', 'V', g.cnt['V'])
    bar()
    if g.tap == P.get('tap_o'):
        _tap_out(g, [(g.dbg[:, :], g.ho[:, :])])
        return True

    w8('g', 'O', g.cnt['O'])
    lo_t = g.wgroup([(wb[:, QR + 2048 * hc:QR + 2048 * (hc + 1)],
                      g.ho[128 * hc:128 * (hc + 1), :]) for hc in range(8)], 'Wa')
    wo_t = g.wgroup([(wb[:, KR + 1024 * hc:KR + 1024 * (hc + 1)],
                      P['wo'][128 * hc:128 * (hc + 1), :]) for hc in range(8)], 'Wb')
    w8('t', 'Wa', lo_t)
    w8('t', 'Wb', wo_t)
    ob = P.get('ob')
    for nt in range(4):
        for dc in range(8):
            bank = ps[(nt * 8 + dc) % 2]
            if nt + dc > 0:
                w8('t', 'S', g.cnt['S'])
            for hc in range(8):
                mm(bank, wb[:, KR + 1024 * hc + 128 * dc:KR + 1024 * hc + 128 * (dc + 1)],
                   wb[:, QR + 2048 * hc + 512 * nt:QR + 2048 * hc + 512 * (nt + 1)],
                   hc == 0, hc == 7, s='T' if hc == 7 else None)
            w8('s', 'T', g.cnt['T'])
            if ob is not None:
                ev = act(wb[:, OS:OS + 512], bank, AF.Identity, bias=ob(dc), s='S')
            else:
                ev = act(wb[:, OS:OS + 512], bank, AF.Copy, s='S')
            w8('y', 'S', ev)
            ov = ydma(P['dst'][128 * dc:128 * (dc + 1), 512 * nt:512 * (nt + 1)],
                      wb[:, OS:OS + 512], s='O')
            g.q['s'].append(lambda e, S=g.sem['O'], V=ov: e.wait_ge(S, V))
    bar()
    return False


def _build_tables(g, P, oc, thsel='theta_q'):
    fb, ic = g.fb, g.ic
    th = P[thsel](oc)
    g.w8('v', 'T', g.cnt['T'])
    g.w8('v', 'S', g.cnt['S'])
    for dst, s2 in ((FB_, P['pos_s2_sin']), (FC, P['pos_s2_cos'])):
        g.ts(fb[:, dst:dst + T], fb[:, FPOS:FPOS + T], th, AL.mult, s2, AL.add, s='V')
        g.ts(fb[:, dst:dst + T], fb[:, dst:dst + T], 1.0 / TWO_PI, AL.mult,
             1024.0, AL.add, s='V')
        for nt in range(4):
            sl = slice(dst + 512 * nt, dst + 512 * (nt + 1))
            g.vcp(ic[:, :], fb[:, sl])
            g.vcp(fb[:, FA:FA + 512], ic[:, :])
            g.tt(fb[:, sl], fb[:, sl], fb[:, FA:FA + 512], AL.subtract, s='V')
        g.ts(fb[:, FA:FA + T], fb[:, dst:dst + T], 0.5, AL.is_ge, s='V')
        g.tt(fb[:, dst:dst + T], fb[:, dst:dst + T], fb[:, FA:FA + T],
             AL.subtract, s='V')
        g.ts(fb[:, FA:FA + T], fb[:, dst:dst + T], -0.5, AL.is_lt, s='V')
        g.tt(fb[:, dst:dst + T], fb[:, dst:dst + T], fb[:, FA:FA + T],
             AL.add, s='V')
        g.ts(fb[:, dst:dst + T], fb[:, dst:dst + T], TWO_PI, AL.mult, s='V')
    g.w8('s', 'V', g.cnt['V'])
    g.act(fb[:, FB_:FB_ + T], fb[:, FB_:FB_ + T], AF.Sin, s='S')
    g.act(fb[:, FC:FC + T], fb[:, FC:FC + T], AF.Sin, s='S')
    g.w8('v', 'S', g.cnt['S'])


def emit_l1f_post(g):
    fb, wb, xn = g.fb, g.wb, g.xn
    tt, w8, ydma, gdma = g.tt, g.w8, g.ydma, g.gdma
    for dc in range(8):
        for t2 in range(4):
            sl = slice(512 * t2, 512 * (t2 + 1))
            w8('g', 'V', g.cnt['V'])
            hav = gdma(wb[:, 512:1024], g.ha[128 * dc:128 * (dc + 1), sl], s='Wc')
            w8('v', 'Wc', hav)
            w8('v', 'O', g.cnt['O'])
            tt(fb[:, FA:FA + 512], xn[:, T * dc:T * (dc + 1)][:, sl],
               wb[:, 512:1024], AL.add, s='V')
            w8('y', 'V', g.cnt['V'])
            ov = ydma(g.hx2[128 * dc:128 * (dc + 1), sl], fb[:, FA:FA + 512], s='O')
            g.q['v'].append(lambda e, S=g.sem['O'], V=ov: e.wait_ge(S, V))
    for eng in ('t', 's', 'v'):
        w8(eng, 'O', g.cnt['O'])
    emit_norm(g, g.hx2, 'rms', wcol=g.mc2[:, 56:64], dst=g.xm)


def emit_l1_moe_tail(g):
    cs, fb, wb, xn, xm, ps = g.cs, g.fb, g.wb, g.xn, g.xm, g.ps
    mm, act, tt, ts, vcp, vred, vrec, tr = g.mm, g.act, g.tt, g.ts, g.vcp, g.vred, g.vrec, g.tr
    ydma, gdma, w8, bar = g.ydma, g.gdma, g.w8, g.bar
    mc_, mc2 = g.mc_, g.mc2
    W1, W2, W3 = 0, 16384, 32768
    CB = 49152
    RT = 51200
    H1 = 0
    CBC, ST1, ST2, B3, B2E = 8192, 9216, 9728, 10240, 11264
    wbx = xn
    bv = ydma(mc2[:, 64:192], g.wt['l1f_b1c'][:, :])
    rtv = g.wgroup([(wb[:, RT + 8 * kc:RT + 8 * (kc + 1)],
                     g.wt['l1f_gateT'][128 * kc:128 * (kc + 1), :]) for kc in range(8)], 'Wa')
    w8('t', 'Wa', rtv)
    for nt in range(4):
        bank = ps[4 + nt]
        for kc in range(8):
            mm(bank[0:8, :], wb[:, RT + 8 * kc:RT + 8 * (kc + 1)],
               xm[:, T * kc + 512 * nt:T * kc + 512 * (nt + 1)],
               kc == 0, kc == 7, s='T' if kc == 7 else None)
        w8('s', 'T', g.cnt['T'])
        act(fb[0:8, FA + 512 * nt:FA + 512 * (nt + 1)], bank[0:8, :], AF.Copy, s='S')
    LG = 6144
    for tsub in range(16):
        w8('t', 'S', g.cnt['S'])
        tv = tr(ps[0][:, 0:8], fb[0:8, FA + 128 * tsub:FA + 128 * (tsub + 1)],
                cs['ident'][0:8, 0:8], s='T')
        w8('v', 'T', tv)
        lg = fb[:, LG:LG + 8]
        vcp(lg, ps[0][:, 0:8])
        m1n = mc_[:, 53:54]
        vred(m1n, lg, AL.max, negate=True)
        ts(fb[:, LG + 8:LG + 16], lg, m1n, AL.add, s='V')
        ts(fb[:, LG + 16:LG + 24], fb[:, LG + 8:LG + 16], 0.0, AL.is_ge, s='V')
        ts(fb[:, LG + 24:LG + 32], fb[:, LG + 16:LG + 24], g.fc(-1e30), AL.mult, s='V')
        tt(fb[:, LG + 24:LG + 32], fb[:, LG + 8:LG + 16], fb[:, LG + 24:LG + 32], AL.add)
        vred(mc_[:, 54:55], fb[:, LG + 24:LG + 32], AL.max)
        w8('s', 'V', g.cnt['V'])
        act(mc_[:, 55:56], mc_[:, 54:55], AF.Exp, s='S')
        act(fb[:, LG + 32:LG + 40], fb[:, LG + 8:LG + 16], AF.Exp, s='S')
        w8('v', 'S', g.cnt['S'])
        ts(mc_[:, 55:56], mc_[:, 55:56], 1.0, AL.add, s='V')
        vrec(mc_[:, 55:56], mc_[:, 55:56])
        ts(fb[:, LG + 40:LG + 48], fb[:, LG + 8:LG + 16], mc_[:, 54:55], AL.is_ge, s='V')
        tt(fb[:, LG + 32:LG + 40], fb[:, LG + 32:LG + 40], fb[:, LG + 40:LG + 48], AL.mult)
        ts(fb[:, LG:LG + 8], fb[:, LG + 32:LG + 40], mc_[:, 55:56], AL.mult, s='V')
        w8('t', 'V', g.cnt['V'])
        tv = tr(ps[1][0:8, 0:128], fb[:, LG:LG + 8], cs['ident'][:, :], s='T')
        w8('s', 'T', tv)
        act(wb[0:8, CB + 128 * tsub:CB + 128 * (tsub + 1)], ps[1][0:8, 0:128],
            AF.Copy, s='S')
    bar()
    if g.tap == 'l1comb':
        _tap_out(g, [(g.dbg[0:8, 0:T], wb[0:8, CB:CB + T])])
        return True

    b3v = g.wgroup([(wbx[0:8, B3:B3 + 1024], g.wt['l1f_b3r'][:, :])], 'Wb')
    g.vms(wbx[0:1, 13312:13824], 1.0, s='V')
    w8('t', 'V', g.cnt['V'])
    w8('s', 'X', bv)
    tm_w = {}
    for half in range(2):
        bar()
        for eng in ('t', 's', 'v', 'g', 'y'):
            w8(eng, 'O', g.cnt['O'])
        t0 = 1024 * half
        w8('t', 'Wb', b3v)
        for dc in range(8):
            for t2 in range(2):
                bank = ps[(dc * 2 + t2) % 4]
                if dc + t2 > 0:
                    w8('t', 'S', g.cnt['S'])
                mm(bank, wbx[0:8, B3 + 128 * dc:B3 + 128 * (dc + 1)],
                   wb[0:8, CB + t0 + 512 * t2:CB + t0 + 512 * (t2 + 1)],
                   True, True, s='T')
                w8('s', 'T', g.cnt['T'])
                act(fb[:, 1024 * dc + 512 * t2:1024 * dc + 512 * (t2 + 1)],
                    bank, AF.Copy, s='S')
        w8('v', 'S', g.cnt['S'])
        for e in range(8):
            if (e, half) > (0, 0):
                w8('g', 'T', tm_w.get((e, half), g.cnt['T']))
            w1v = g.wgroup([(wb[:, W1 + 2048 * kc:W1 + 2048 * (kc + 1)],
                             g.wt['l1f_w1T'][e, 128 * kc:128 * (kc + 1), :])
                            for kc in range(8)], 'Wa')
            w2v = g.wgroup([(wb[:, W2 + 2048 * kc:W2 + 2048 * (kc + 1)],
                             g.wt['l1f_w2T'][e, 128 * kc:128 * (kc + 1), :])
                            for kc in range(8)], 'Wc')
            w3v = g.wgroup([(wb[:, W3 + 1024 * fc:W3 + 1024 * (fc + 1)],
                             g.wt['l1f_w3T'][e, 128 * fc:128 * (fc + 1), :])
                            for fc in range(16)], 'Wd')
            b2v = g.wgroup([(wbx[0:1, B2E:B2E + 2048],
                             g.wt['l1f_b2r'][0:1, 2048 * e:2048 * (e + 1)])], 'Wb')
            for t2 in range(2):
                bank = ps[4 + t2]
                w8('t', 'V', g.cnt['V'])
                mm(bank, cs['sel8'][:, 128 * e:128 * (e + 1)],
                   wb[0:8, CB + t0 + 512 * t2:CB + t0 + 512 * (t2 + 1)],
                   True, True, s='T')
                w8('v', 'T', g.cnt['T'])
                vcp(wbx[:, CBC + 512 * t2:CBC + 512 * (t2 + 1)], bank)
            w8('t', 'Wa', w1v)
            w8('t', 'Wc', w2v)
            w8('t', 'Wb', b2v)
            w8('t', 'Wd', w3v)
            for t2 in range(2):
                for fc in range(16):
                    bank = ps[4 + 2 + fc % 2]
                    w8('t', 'S', g.cnt['S'])
                    for kc in range(8):
                        mm(bank, wb[:, W1 + 2048 * kc + 128 * fc:W1 + 2048 * kc + 128 * (fc + 1)],
                           xm[:, T * kc + t0 + 512 * t2:T * kc + t0 + 512 * (t2 + 1)],
                           kc == 0, kc == 7, s='T' if kc == 7 else None)
                    w8('s', 'T', g.cnt['T'])
                    w8('s', 'V', g.cnt['V'])
                    act(wbx[:, ST1:ST1 + 512], bank, AF.Silu,
                        bias=mc2[:, 64 + 16 * e + fc:65 + 16 * e + fc], s='S')
                    bank2 = ps[fc % 2]
                    w8('t', 'V', g.cnt['V'])
                    for kc in range(8):
                        mm(bank2, wb[:, W2 + 2048 * kc + 128 * fc:W2 + 2048 * kc + 128 * (fc + 1)],
                           xm[:, T * kc + t0 + 512 * t2:T * kc + t0 + 512 * (t2 + 1)],
                           kc == 0, kc == 7, s=None)
                    mm(bank2, wbx[0:1, B2E + 128 * fc:B2E + 128 * (fc + 1)],
                       wbx[0:1, 13312:13824],
                       False, True, sgc=True, s='T')
                    w8('v', 'S', g.cnt['S'])
                    w8('v', 'T', g.cnt['T'])
                    tt(wbx[:, ST2:ST2 + 512], wbx[:, ST1:ST1 + 512],
                       wbx[:, CBC + 512 * t2:CBC + 512 * (t2 + 1)], AL.mult)
                    tt(xn[:, H1 + 512 * fc:H1 + 512 * (fc + 1)],
                       wbx[:, ST2:ST2 + 512], bank2, AL.mult, s='V')
                if g.tap == 'l1h' and e == 0 and half == 0 and t2 == 0:
                    g.bar()
                    _tap_out(g, [(g.dbg[128 * fc:128 * (fc + 1), 0:512],
                                  xn[:, H1 + 512 * fc:H1 + 512 * (fc + 1)])
                                 for fc in range(16)])
                    return True
                for dc in range(8):
                    bank = ps[(dc + t2) % 4]
                    w8('t', 'V', g.cnt['V'])
                    for fc in range(16):
                        mm(bank, wb[:, W3 + 1024 * fc + 128 * dc:W3 + 1024 * fc + 128 * (dc + 1)],
                           xn[:, H1 + 512 * fc:H1 + 512 * (fc + 1)],
                           fc == 0, fc == 15, s='T' if fc == 15 else None)
                    w8('v', 'T', g.cnt['T'])
                    tt(fb[:, 1024 * dc + 512 * t2:1024 * dc + 512 * (t2 + 1)],
                       fb[:, 1024 * dc + 512 * t2:1024 * dc + 512 * (t2 + 1)],
                       bank, AL.add, s='V')
            tm_w[(e + 1, half) if e < 7 else (0, 1)] = g.cnt['T']
        bar()
        if g.tap == 'l1moe' and half == 0:
            _tap_out(g, [(g.dbg[128 * dc:128 * (dc + 1), 0:1024],
                          fb[:, 1024 * dc:1024 * (dc + 1)]) for dc in range(8)])
            return True
        w8('v', 'T', g.cnt['T'])
        for dc in range(8):
            for t2 in range(2):
                sl = slice(t0 + 512 * t2, t0 + 512 * (t2 + 1))
                osl = fb[:, 1024 * dc + 512 * t2:1024 * dc + 512 * (t2 + 1)]
                w8('g', 'V', g.cnt['V'])
                x2v = gdma(wbx[:, ST1:ST1 + 512], g.hx2[128 * dc:128 * (dc + 1), sl], s='Wc')
                w8('y', 'V', g.cnt['V'])
                sv_ = ydma(wbx[:, ST2:ST2 + 512], g.hs[128 * dc:128 * (dc + 1), sl])
                w8('v', 'Wc', x2v)
                w8('v', 'X', sv_)
                w8('v', 'S', g.cnt['S'])
                tt(ps[6], osl, wbx[:, ST1:ST1 + 512], AL.add, s='V')
                w8('s', 'V', g.cnt['V'])
                gv = act(wbx[:, ST1:ST1 + 512], ps[6], AF.Silu, s='S')
                w8('y', 'V', g.cnt['V'])
                lv1 = ydma(osl, g.hx1[128 * dc:128 * (dc + 1), sl])
                w8('v', 'X', lv1)
                w8('v', 'S', gv)
                tt(ps[7], wbx[:, ST1:ST1 + 512], wbx[:, ST2:ST2 + 512], AL.mult)
                tt(osl, osl, ps[7], AL.add, s='V')
                w8('y', 'V', g.cnt['V'])
                ov = ydma(g.out[128 * dc:128 * (dc + 1), sl], osl, s='O')
                g.q['v'].append(lambda e, S=g.sem['O'], V=ov: e.wait_ge(S, V))
                g.q['s'].append(lambda e, S=g.sem['O'], V=ov: e.wait_ge(S, V))
    w8('y', 'O', g.cnt['O'])
    bar()
    return False


def finish(g):
    for c in range(8):
        g.ydma(g.out[128 * c:128 * (c + 1), :], g.xn[:, T * c:T * (c + 1)], s='O')
    g.w8('y', 'O', g.cnt['O'])


def kernel(**inputs):
    x = np.asarray(inputs['L0_attn_x'])
    CST = _host_consts()
    W = _host_weights(inputs)
    nc = build_program(tap=DEBUG_TAP)
    fvals = np.zeros((128, 64), dtype=np.float32)
    for i, v in enumerate(kernel._fvals):
        fvals[:, i] = v
    in_maps = []
    for c in range(NC_):
        m = {'x_fm': np.ascontiguousarray(x[c].T), 'c_fvals': fvals}
        for k, v in CST.items():
            m['c_' + k] = v
        for k, v in W.items():
            m['w_' + k] = np.ascontiguousarray(v, dtype=np.float32)
        in_maps.append(m)
    res = run_bass_kernel_spmd(nc, in_maps, core_ids=list(range(NC_)))
    outs = [np.asarray(r['out']).T for r in res.results]
    if DEBUG_TAP is not None:
        kernel.dbg = [np.asarray(r['dbg']) for r in res.results]
    return np.stack(outs).astype(np.float32)


# version 06
# speedup vs baseline: 1.0048x; 1.0048x over previous
import numpy as np
import ml_dtypes
import concourse.bass as bass
import concourse.mybir as mybir
from concourse.bass_utils import run_bass_kernel_spmd

F32 = mybir.dt.float32
BF16 = mybir.dt.bfloat16
AF = mybir.ActivationFunctionType
AL = mybir.AluOpType
AX = mybir.AxisListType
BF = ml_dtypes.bfloat16

D, T, NC_ = 1024, 2048, 8
PI = float(np.pi)
TWO_PI = float(2.0 * np.pi)

DEBUG_TAP = None


def _host_consts():
    c = {}
    c['ident'] = np.eye(128, dtype=np.float32)
    c['ones32'] = np.ones((128, 128), dtype=np.float32)
    c['ones_bf'] = np.ones((128, 128), dtype=BF)
    sel = np.zeros((33, 128), dtype=np.float32)
    sel[0, :64] = 1.0
    sel[32, 64:] = 1.0
    c['sel33'] = sel
    b = np.zeros((128, 128), dtype=np.float32)
    for j in range(4):
        b[32 * j:32 * j + 16, 32 * j] = 1.0
    c['bsumL0'] = b.astype(BF)
    a = np.zeros((128, 128), dtype=np.float32)
    a[0:64, 0] = 1.0
    a[64:128, 32] = 1.0
    c['bsumA'] = a.astype(BF)
    a = np.zeros((128, 128), dtype=np.float32)
    a[0:64, 64] = 1.0
    a[64:128, 96] = 1.0
    c['bsumB'] = a.astype(BF)
    c['pos_row'] = np.arange(1, T + 1, dtype=np.float32).reshape(1, T)
    r = np.arange(128)
    exq = np.zeros((128, 8), dtype=np.float32)
    for ch in range(8):
        exq[:, ch] = -(((128 * ch + r) // 2).astype(np.float64) / 512.0)
    c['exq'] = exq
    exk = np.zeros((128, 8), dtype=np.float32)
    for ch in range(8):
        exk[:, ch] = -(((128 * ch + r) // 4).astype(np.float64) / 256.0)
    c['exk'] = exk
    c['exf'] = (-(((r % 64) % 32).astype(np.float64) / 32.0)).astype(np.float32).reshape(128, 1)
    c['sgq'] = np.where(r % 2 == 0, -1.0, 1.0).astype(np.float32).reshape(128, 1)
    c['sgk4'] = np.where(r % 4 < 2, -1.0, 1.0).astype(np.float32).reshape(128, 1)
    c['sgrh'] = np.where(r % 64 < 32, -1.0, 1.0).astype(np.float32).reshape(128, 1)
    c['tenk'] = np.full((128, 1), 10000.0, dtype=np.float32)
    s8 = np.zeros((8, 1024), dtype=np.float32)
    for e in range(8):
        s8[e, 128 * e:128 * (e + 1)] = 1.0
    c['sel8'] = s8.astype(BF)
    return c


def _host_weights(I):
    w = {}
    qp = np.asarray(I['L0_attn_query_proj'])
    kp = np.asarray(I['L0_attn_key_proj'])
    qstr = np.zeros((D, 512), dtype=np.float32)
    kstr = np.zeros((D, 512), dtype=np.float32)
    for h in range(16):
        qstr[:, 32 * h:32 * h + 16] = qp[:, h, :].T
        kstr[:, 32 * h:32 * h + 16] = kp[h, :, :].T
    w['l0_qstr'] = qstr
    w['l0_kstr'] = kstr
    w['l0_vT'] = np.ascontiguousarray(np.asarray(I['L0_attn_value_proj']).reshape(1024, D).T)
    w['l0_opT'] = np.ascontiguousarray(np.asarray(I['L0_attn_output_proj']).reshape(D, 1024).T)
    w['l0_rtr'] = np.asarray(I['L0_ffn_router_w'])
    w['l0_rtb'] = np.asarray(I['L0_ffn_router_b']).reshape(8, 1)
    w['l0_W1'] = np.asarray(I['L0_ffn_W1'])
    w['l0_b1c'] = np.ascontiguousarray(
        np.asarray(I['L0_ffn_b1']).reshape(8, 4, 128).transpose(2, 0, 1).reshape(128, 32))
    w['l0_W2'] = np.asarray(I['L0_ffn_W2'])
    w['l0_b2r'] = np.asarray(I['L0_ffn_b2'])
    Wq = np.asarray(I['L1_attn_Wq'])
    Wk = np.asarray(I['L1_attn_Wk'])
    Wv = np.asarray(I['L1_attn_Wv'])
    rep = np.repeat(np.arange(512), 2)
    permq = np.arange(1024).reshape(-1, 2)[:, ::-1].reshape(-1)
    perm4 = (np.arange(1024).reshape(-1, 4)[:, [2, 3, 0, 1]]).reshape(-1)
    w['l1a_wq'] = Wq
    w['l1a_wqp'] = np.ascontiguousarray(Wq[:, permq])
    Wkr = Wk[:, rep]
    w['l1a_wkr'] = np.ascontiguousarray(Wkr)
    w['l1a_wkrp'] = np.ascontiguousarray(Wkr[:, perm4])
    w['l1a_wvr'] = np.ascontiguousarray(Wv[:, rep])
    w['l1a_wo'] = np.asarray(I['L1_attn_Wo'])
    qkvT = np.ascontiguousarray(np.asarray(I['L1_ffn_qkv_w']).T)
    w['l1f_qkvT'] = qkvT
    permrh = (np.arange(2048).reshape(-1, 64)[:, list(range(32, 64)) + list(range(32))]).reshape(-1)
    w['l1f_qkvTp'] = np.ascontiguousarray(qkvT[:, :2048][:, permrh])
    qkvb = np.asarray(I['L1_ffn_qkv_b'])
    w['l1f_qkvbc'] = np.ascontiguousarray(qkvb.reshape(24, 128).T)
    w['l1f_qkvbcp'] = np.ascontiguousarray(qkvb[:2048][permrh].reshape(16, 128).T)
    w['l1f_outwT'] = np.ascontiguousarray(np.asarray(I['L1_ffn_out_w']).T)
    w['l1f_outbc'] = np.ascontiguousarray(np.asarray(I['L1_ffn_out_b']).reshape(8, 128).T)
    w['l1f_gateT'] = np.ascontiguousarray(np.asarray(I['L1_ffn_gate_w']).T)
    w['l1f_w1T'] = np.ascontiguousarray(np.asarray(I['L1_ffn_w1']).transpose(0, 2, 1))
    w['l1f_w2T'] = np.ascontiguousarray(np.asarray(I['L1_ffn_w2']).transpose(0, 2, 1))
    w['l1f_w3T'] = np.ascontiguousarray(np.asarray(I['L1_ffn_w3']).transpose(0, 2, 1))
    w['l1f_b1c'] = np.ascontiguousarray(
        np.asarray(I['L1_ffn_b1']).reshape(8, 16, 128).transpose(2, 0, 1).reshape(128, 128))
    w['l1f_b2r'] = np.ascontiguousarray(np.asarray(I['L1_ffn_b2']).reshape(1, 8 * 2048))
    w['l1f_b3r'] = np.asarray(I['L1_ffn_b3'])
    w['l1f_vb'] = np.ascontiguousarray(qkvb[2048:3072].reshape(1, 1024))
    w['l1f_n1c'] = np.ascontiguousarray(np.asarray(I['L1_ffn_norm1_w']).reshape(8, 128).T)
    w['l1f_n2c'] = np.ascontiguousarray(np.asarray(I['L1_ffn_norm2_w']).reshape(8, 128).T)
    return w


_WSHAPES = {
    'l0_qstr': (D, 512), 'l0_kstr': (D, 512), 'l0_vT': (D, 1024), 'l0_opT': (1024, D),
    'l0_rtr': (D, 8), 'l0_rtb': (8, 1), 'l0_W1': (8, D, 512), 'l0_b1c': (128, 32),
    'l0_W2': (8, 512, D), 'l0_b2r': (8, D),
    'l1a_wq': (D, D), 'l1a_wqp': (D, D), 'l1a_wkr': (D, D), 'l1a_wkrp': (D, D),
    'l1a_wvr': (D, D), 'l1a_wo': (D, D),
    'l1f_qkvT': (D, 3072), 'l1f_qkvTp': (D, 2048), 'l1f_qkvbc': (128, 24),
    'l1f_qkvbcp': (128, 16), 'l1f_outwT': (D, D), 'l1f_outbc': (128, 8),
    'l1f_gateT': (D, 8), 'l1f_w1T': (8, D, 2048), 'l1f_b1c': (128, 128),
    'l1f_w2T': (8, D, 2048), 'l1f_b2r': (1, 8 * 2048), 'l1f_w3T': (8, 2048, D),
    'l1f_b3r': (8, D), 'l1f_n1c': (128, 8), 'l1f_n2c': (128, 8), 'l1f_vb': (1, 1024),
}


class Ctx:
    pass


def build_program(tap=None):
    nc = bass.Bass('TRN2', target_bir_lowering=False)
    g = Ctx()
    g.nc = nc
    g.tap = tap
    g.x_fm = nc.dram_tensor('x_fm', [D, T], F32, kind='ExternalInput')
    CST = _host_consts()
    g.cst = {k: nc.dram_tensor('c_' + k, list(v.shape), BF16 if v.dtype == BF else F32,
                               kind='ExternalInput') for k, v in CST.items()}
    g.wt = {k: nc.dram_tensor('w_' + k, list(sh), F32, kind='ExternalInput')
            for k, sh in _WSHAPES.items()}
    g.fvals = []
    g.fv_dram = nc.dram_tensor('c_fvals', [128, 64], F32, kind='ExternalInput')
    g.hx1 = nc.dram_tensor('hx1', [D, T], F32)
    g.hx2 = nc.dram_tensor('hx2', [D, T], F32)
    g.hs = nc.dram_tensor('hs', [D, T], BF16)
    g.ho = nc.dram_tensor('ho', [D, T], BF16)
    g.ha = nc.dram_tensor('ha', [D, T], BF16)
    g.out = nc.dram_tensor('out', [D, T], F32, kind='ExternalOutput')
    g.dbg = nc.dram_tensor('dbg', [D, T], F32, kind='ExternalOutput')

    import contextlib
    with contextlib.ExitStack() as st:
        g.xn = st.enter_context(nc.sbuf_tensor('xn', [128, 8 * T], BF16))
        g.xm = st.enter_context(nc.sbuf_tensor('xm', [128, 8 * T], BF16))
        g.wb = st.enter_context(nc.sbuf_tensor('wb', [128, 52 * 1024 - 128], BF16))
        g.fb = st.enter_context(nc.sbuf_tensor('fb', [128, 8 * 1024], F32))
        g.mc_ = st.enter_context(nc.sbuf_tensor('mc_', [128, 64], F32))
        g.mc2 = st.enter_context(nc.sbuf_tensor('mc2', [128, 192], F32))
        g.ic = st.enter_context(nc.sbuf_tensor('ic', [128, 512], mybir.dt.int32))
        g.fv = st.enter_context(nc.sbuf_tensor('fv', [128, 64], F32))

        def fc(v):
            v = float(np.float32(v))
            if v not in g.fvals:
                assert len(g.fvals) < 64
                g.fvals.append(v)
                nc.const_aps.register(F32, v, g.fv[:, len(g.fvals) - 1:len(g.fvals)])
            return v
        g.fc = fc
        g.cs = {k: st.enter_context(nc.sbuf_tensor('s_' + k, list(v.shape),
                                                   BF16 if v.dtype == BF else F32))
                for k, v in CST.items() if k != 'pos_row'}
        g.psA = st.enter_context(nc.psum_tensor('psA', [128, 2048], F32))
        g.psB = st.enter_context(nc.psum_tensor('psB', [128, 2048], F32))
        g.ps = [g.psA[:, 512 * i:512 * (i + 1)] for i in range(4)] + \
               [g.psB[:, 512 * i:512 * (i + 1)] for i in range(4)]
        g.sem = {k: st.enter_context(nc.semaphore('sm' + k))
                 for k in ('W', 'X', 'T', 'S', 'V', 'O', 'Wa', 'Wb', 'Wc', 'Wd', 'D')}
        g.cnt = {k: 0 for k in g.sem}
        g.q = {'t': [], 's': [], 'v': [], 'g': [], 'y': []}
        g.done = False

        emit_all(g)
        kernel._fvals = list(g.fvals)

        with nc.Block() as blk:
            @blk.tensor
            def _(e):
                for fn in g.q['t']:
                    fn(e)

            @blk.scalar
            def _(e):
                for fn in g.q['s']:
                    fn(e)

            @blk.vector
            def _(e):
                for fn in g.q['v']:
                    fn(e)

            @blk.gpsimd
            def _(e):
                for fn in g.q['g']:
                    fn(e)

            @blk.sync
            def _(e):
                for fn in g.q['y']:
                    fn(e)
    return nc


def _mk(g):
    sem, cnt, q = g.sem, g.cnt, g.q

    def w8(eng, s, val):
        q[eng].append(lambda e, S=sem[s], V=val: e.wait_ge(S, V))

    def _run(eng, fn, s=None, n=1):
        if s is None:
            q[eng].append(fn)
            return None
        cnt[s] += n
        v = cnt[s]
        q[eng].append(lambda e, F=fn, S=sem[s], N=n: F(e).then_inc(S, N))
        if eng in ('t', 's', 'v'):
            q[eng].append(lambda e, S=sem[s], V=v: e.wait_ge(S, V))
        return v

    g.dmawait = {}

    def _dma(eng, dst, src, s):
        v = _run(eng, lambda e, d=dst, r=src: e.dma_start(out=d, in_=r), s, 16)
        key = (eng, s)
        if v - g.dmawait.get(key, 0) >= 7 * 16:
            q[eng].append(lambda e, S=sem[s], V=v: e.wait_ge(S, V))
            g.dmawait[key] = v
        return v

    def wgroup(items, s):
        if cnt[s] > 0:
            q['g'].append(lambda e, S=sem[s], V=cnt[s]: e.wait_ge(S, V))
            g.dmawait[('g', s)] = cnt[s]
        v = None
        for dst, srcap in items:
            v = _dma('g', dst, srcap, s)
        return v

    def ydma(dst, src, s='X'):
        return _dma('y', dst, src, s)

    def gdma(dst, src, s='W'):
        return _dma('g', dst, src, s)

    def mm(o, l, r, start, stop, tp=None, sgc=False, s=None):
        return _run('t', lambda e, o=o, l=l, r=r, a=start, b=stop, tp=tp, k=sgc:
                    e.matmul(o, l, r, start=a, stop=b, tile_position=tp,
                             skip_group_check=k), s)

    def tr(o, i, ident, s=None):
        return _run('t', lambda e, o=o, i=i, d=ident: e.transpose(o, i, d), s)

    def act(o, i, f, bias=0.0, scale=1.0, accum=None, s='S'):
        return _run('s', lambda e, o=o, i=i, f=f, b=bias, sc=scale, ac=accum:
                    e.activation(o, i, f, bias=b, scale=sc, accum_out=ac), s)

    def tt(o, a, b, op, s='V'):
        return _run('v', lambda e, o=o, a=a, b=b, op=op: e.tensor_tensor(o, a, b, op), s)

    def ts(o, a, s1, op0, s2=None, op1=None, accum=None, s='V'):
        if op1 is None:
            return _run('v', lambda e, o=o, a=a, s1=s1, ac=accum, op0=op0:
                        e.tensor_scalar(o, a, s1, None, op0, accum_out=ac), s)
        return _run('v', lambda e, o=o, a=a, s1=s1, s2=s2, op0=op0, op1=op1, ac=accum:
                    e.tensor_scalar(o, a, s1, s2, op0, op1, accum_out=ac), s)

    def vcp(o, i, s='V'):
        return _run('v', lambda e, o=o, i=i: e.tensor_copy(o, i), s)

    def vred(o, i, op, negate=None, s='V'):
        return _run('v', lambda e, o=o, i=i, op=op, n=negate:
                    e.tensor_reduce(o, i, AX.X, op, negate=n), s)

    def vrec(o, i, s='V'):
        return _run('v', lambda e, o=o, i=i: e.reciprocal(o, i), s)

    def vms(ap, c, s='V'):
        return _run('v', lambda e, a=ap, c=c: e.memset(a, c), s)

    def bar():
        tm = cnt['T']
        sv = act(g.mc_[:, 62:63], g.mc_[:, 62:63], AF.Copy, s='S')
        w8('v', 'S', sv)
        vv = vcp(g.mc_[:, 63:64], g.mc_[:, 62:63], s='V')
        w8('t', 'S', sv)
        w8('t', 'V', vv)
        w8('s', 'V', vv)
        w8('s', 'T', tm)
        w8('v', 'T', tm)
        for eng in ('g', 'y'):
            w8(eng, 'T', tm)
            w8(eng, 'S', sv)
            w8(eng, 'V', vv)
        return {'T': tm, 'S': sv, 'V': vv}

    g.w8, g.run = w8, _run
    g.wgroup = wgroup
    g.ydma, g.gdma, g.mm, g.tr = ydma, gdma, mm, tr
    g.act, g.tt, g.ts, g.vcp, g.vred, g.vrec, g.vms = act, tt, ts, vcp, vred, vrec, vms
    g.bar = bar


def _tap_out(g, tiles, rows=128):
    for s in ('O', 'X', 'T', 'S', 'V', 'W', 'Wa', 'Wb', 'Wc', 'Wd'):
        g.w8('g', s, g.cnt[s])
    for dst, srcap in tiles:
        g.gdma(dst, srcap, s='D')
    g.w8('g', 'D', g.cnt['D'])
    g.done = True


FPOS, FA, FB_, FC = 0, 2048, 4096, 6144
MLN, MTHQ, MTHK, MTHF, MSC = 0, 1, 9, 17, 18


def emit_all(g):
    _mk(g)
    nc, cs, fb, wb, xn, ps = g.nc, g.cs, g.fb, g.wb, g.xn, g.ps
    mm, act, tt, ts, vcp = g.mm, g.act, g.tt, g.ts, g.vcp
    ydma, w8, bar = g.ydma, g.w8, g.bar
    mc_ = g.mc_

    for k in cs:
        ydma(cs[k][:, :], g.cst[k][:, :])
    ydma(fb[0:1, FA:FA + T], g.cst['pos_row'][0:1, :])
    ydma(g.fv[:, :], g.fv_dram[:, :])
    w8('t', 'X', g.cnt['X'])
    w8('s', 'X', g.cnt['X'])
    w8('v', 'X', g.cnt['X'])

    act(mc_[:, 62:63], cs['tenk'][:, :], AF.Copy, s='S')

    for nt in range(4):
        mm(ps[nt][:, :], cs['ones32'][0:1, :], fb[0:1, FA + 512 * nt:FA + 512 * (nt + 1)],
           True, True, s='T')
    w8('s', 'T', g.cnt['T'])
    for nt in range(4):
        act(fb[:, FPOS + 512 * nt:FPOS + 512 * (nt + 1)], ps[nt][:, :], AF.Copy, s='S')
    act(mc_[:, MLN:MLN + 1], cs['tenk'][:, :], AF.Ln, s='S')
    for c in range(8):
        act(mc_[:, MTHQ + c:MTHQ + c + 1], cs['exq'][:, c:c + 1], AF.Exp,
            scale=mc_[:, MLN:MLN + 1], s='S')
        act(mc_[:, MTHK + c:MTHK + c + 1], cs['exk'][:, c:c + 1], AF.Exp,
            scale=mc_[:, MLN:MLN + 1], s='S')
    act(mc_[:, MTHF:MTHF + 1], cs['exf'][:, :], AF.Exp, scale=mc_[:, MLN:MLN + 1], s='S')
    w8('v', 'S', g.cnt['S'])
    for cols, exc in ((slice(MTHQ, MTHQ + 8), 'exq'), (slice(MTHK, MTHK + 8), 'exk'),
                      (slice(MTHF, MTHF + 1), 'exf')):
        n = cols.stop - cols.start
        ts(mc_[:, 40:40 + n], cs[exc][:, :], mc_[:, MLN:MLN + 1], AL.mult, s='V')
        w8('s', 'V', g.cnt['V'])
        act(mc_[:, 48:48 + n], mc_[:, cols], AF.Ln, s='S')
        w8('v', 'S', g.cnt['S'])
        tt(mc_[:, 40:40 + n], mc_[:, 40:40 + n], mc_[:, 48:48 + n], AL.subtract, s='V')
        ts(mc_[:, 40:40 + n], mc_[:, 40:40 + n], 1.0, AL.add, s='V')
        tt(mc_[:, cols], mc_[:, cols], mc_[:, 40:40 + n], AL.mult, s='V')
    ts(mc_[:, MTHQ:MTHQ + 8], mc_[:, MTHQ:MTHQ + 8], cs['sgq'][:, :], AL.mult, s='V')
    ts(mc_[:, MTHK:MTHK + 8], mc_[:, MTHK:MTHK + 8], cs['sgk4'][:, :], AL.mult, s='V')
    ts(mc_[:, MTHF:MTHF + 1], mc_[:, MTHF:MTHF + 1], cs['sgrh'][:, :], AL.mult, s='V')
    ts(mc_[:, 18:19], mc_[:, MTHF:MTHF + 1], -1.0, AL.mult, s='V')
    ts(mc_[:, 19:20], mc_[:, 18:19], 1.0, AL.mult, PI / 2.0, AL.add, s='V')
    bar()

    if g.tap == 'pos':
        _tap_out(g, [(g.dbg[0:128, 0:T], fb[:, FPOS:FPOS + T]),
                     (g.dbg[128:256, 0:32], mc_[:, 0:32])])
        return

    emit_norm(g, g.x_fm, 'ln', dst=g.xn)
    if g.tap == 'xn0':
        _tap_out(g, [(g.dbg[128 * c:128 * (c + 1), :], xn[:, T * c:T * (c + 1)])
                     for c in range(8)])
        return

    if emit_l0_attn(g):
        return
    if g.tap == 'l0s0':
        _tap_out(g, [(g.dbg[:, :], g.hs[:, :])])
        return

    if emit_l0_ffn_tail(g):
        return
    if g.tap == 'x1':
        _tap_out(g, [(g.dbg[:, :], g.hx1[:, :])])
        return

    emit_norm(g, g.hx1, 'ln', dst=g.xn)
    emit_pos_bcast(g)
    if g.tap == 'xn1':
        _tap_out(g, [(g.dbg[128 * c:128 * (c + 1), :], xn[:, T * c:T * (c + 1)])
                     for c in range(8)])
        return

    P = dict(src=(g.xn, 0), wv_moving=g.wt['l1a_wvr'],
             wq=lambda oc: g.wt['l1a_wq'][:, 128 * oc:128 * (oc + 1)],
             wqp=lambda oc: g.wt['l1a_wqp'][:, 128 * oc:128 * (oc + 1)],
             wk=lambda oc: g.wt['l1a_wkr'][:, 128 * oc:128 * (oc + 1)],
             wkp=lambda oc: g.wt['l1a_wkrp'][:, 128 * oc:128 * (oc + 1)],
             qb=None, qpb=None, kb=None, kpb=None, vb_row=None,
             theta_q=lambda oc: g.mc_[:, MTHQ + oc:MTHQ + oc + 1],
             theta_k=lambda oc: g.mc_[:, MTHK + oc:MTHK + oc + 1],
             pos_s2_sin=0.0, pos_s2_cos=PI / 2.0,
             shared_tables=False, scale=0.5, tau=8.0, dsh=40.0,
             wo=g.wt['l1a_wo'], ob=None, dst=g.hs,
             tap_qk='l1aqk', tap_o='l1ao', tap_tbl='l1atbl')
    if emit_l1_attention(g, P):
        return
    if g.tap == 'l1as':
        _tap_out(g, [(g.dbg[:, :], g.hs[:, :])])
        return

    lv = g.ydma(g.mc2[:, 0:24], g.wt['l1f_qkvbc'][:, :])
    lv = g.ydma(g.mc2[:, 24:40], g.wt['l1f_qkvbcp'][:, :])
    lv = g.ydma(g.mc2[:, 40:48], g.wt['l1f_outbc'][:, :])
    lv = g.ydma(g.mc2[:, 48:56], g.wt['l1f_n1c'][:, :])
    lv = g.ydma(g.mc2[:, 56:64], g.wt['l1f_n2c'][:, :])
    g.w8('s', 'X', lv)
    g.w8('v', 'X', lv)
    emit_norm(g, (g.xn, 0), 'rms', wcol=g.mc2[:, 48:56], dst=g.xm)
    if g.tap == 'xn1f':
        _tap_out(g, [(g.dbg[128 * c:128 * (c + 1), :], g.xm[:, T * c:T * (c + 1)])
                     for c in range(8)])
        return
    P = dict(src=(g.xm, 0), wv_moving=g.wt['l1f_qkvT'][:, 2048:3072],
             wq=lambda oc: g.wt['l1f_qkvT'][:, 128 * oc:128 * (oc + 1)],
             wqp=lambda oc: g.wt['l1f_qkvTp'][:, 128 * oc:128 * (oc + 1)],
             wk=lambda oc: g.wt['l1f_qkvT'][:, 1024 + 128 * oc:1024 + 128 * (oc + 1)],
             wkp=lambda oc: g.wt['l1f_qkvTp'][:, 1024 + 128 * oc:1024 + 128 * (oc + 1)],
             qb=lambda oc: g.mc2[:, oc:oc + 1],
             qpb=lambda oc: g.mc2[:, 24 + oc:25 + oc],
             kb=lambda oc: g.mc2[:, 8 + oc:9 + oc],
             kpb=lambda oc: g.mc2[:, 32 + oc:33 + oc],
             vb_row=g.wt['l1f_vb'][0:1, :],
             theta_q=lambda oc: g.mc_[:, MTHF:MTHF + 1],
             theta_k=lambda oc: g.mc_[:, MTHF:MTHF + 1],
             pos_s2_sin=g.mc_[:, 18:19], pos_s2_cos=g.mc_[:, 19:20],
             shared_tables=True, scale=0.125, tau=8.0, dsh=40.0,
             wo=g.wt['l1f_outwT'], ob=lambda dc: g.mc2[:, 40 + dc:41 + dc], dst=g.ha,
             tap_qk='l1fqk', tap_o='l1fo', tap_tbl='l1ftbl')
    if emit_l1_attention(g, P):
        return
    emit_l1f_post(g)
    if g.tap == 'xn2':
        _tap_out(g, [(g.dbg[128 * c:128 * (c + 1), :], g.xm[:, T * c:T * (c + 1)])
                     for c in range(8)])
        return

    emit_l1_moe_tail(g)
    if g.tap is None:
        g.w8('y', 'O', g.cnt['O'])


def emit_norm(g, xsrc, kind, wcol=None, dst=None, dst_off=0, stat_src_bf=None):
    cs, fb, wb, ps = g.cs, g.fb, g.wb, g.ps
    mm, act, tt, ts, vcp, vrec = g.mm, g.act, g.tt, g.ts, g.vcp, g.vrec
    ydma, w8, bar = g.ydma, g.w8, g.bar
    from_dram = not isinstance(xsrc, tuple)
    eps = 1e-5 if kind == 'ln' else 1e-6
    ln = kind == 'ln'

    g.w8('y', 'O', g.cnt['O'])
    tmarks = []
    vmark_cast = None
    for c in range(8):
        pp = c % 2
        if from_dram:
            xf = fb[:, FA:FA + T]
            if c > 0:
                w8('y', 'V', vmark_cast)
            lv = ydma(xf, xsrc[128 * c:128 * (c + 1), :])
            w8('v', 'X', lv)
            xb = wb[:, 2048 * pp:2048 * pp + T]
            if c >= 2:
                w8('v', 'T', tmarks[c - 2])
            vmark_cast = vcp(xb, xf, s='V')
        else:
            xb = xsrc[0][:, xsrc[1] + T * c:xsrc[1] + T * (c + 1)]
        sq = wb[:, 4096 + 2048 * pp:4096 + 2048 * pp + T]
        if c >= 2:
            w8('v', 'T', tmarks[c - 2])
        sv = tt(sq, xb, xb, AL.mult, s='V')
        w8('t', 'V', sv)
        for nt in range(4):
            sl = slice(512 * nt, 512 * (nt + 1))
            if ln:
                mm(ps[4 + nt][0:1, :], cs['ones_bf'][:, 0:1], xb[:, sl], c == 0, c == 7)
            mm(ps[nt][0:1, :], cs['ones_bf'][:, 0:1], sq[:, sl], c == 0, c == 7,
               s='T' if nt == 3 else None)
        tmarks.append(g.cnt['T'])
    w8('v', 'T', tmarks[-1])
    w8('s', 'T', tmarks[-1])

    arow = fb[0:1, FA:FA + T]
    brow = fb[0:1, FB_:FB_ + T]
    crow = fb[0:1, FC:FC + T]
    for nt in range(4):
        sl = slice(512 * nt, 512 * (nt + 1))
        ts(arow[:, sl], ps[nt][0:1, :], 1.0 / D, AL.mult)
        if ln:
            ts(crow[:, sl], ps[4 + nt][0:1, :], 1.0 / D, AL.mult)
    if ln:
        tt(brow, crow, crow, AL.mult)
        sv = tt(brow, arow, brow, AL.subtract, s='V')
    else:
        sv = vcp(brow, arow, s='V')
    w8('s', 'V', sv)
    act(brow, brow, AF.Sqrt, bias=g.fc(eps), s='S')
    w8('v', 'S', g.cnt['S'])
    vrec(brow, brow)
    if ln:
        tt(crow, crow, brow, AL.mult)
        sv = ts(crow, crow, -1.0, AL.mult, s='V')
    else:
        sv = vcp(g.mc_[0:1, 61:62], brow[0:1, 0:1], s='V')
    w8('t', 'V', sv)

    for nt in range(4):
        sl = slice(512 * nt, 512 * (nt + 1))
        tv = mm(ps[nt][:, :], cs['ones32'][0:1, :], brow[:, sl], True, True, s='T')
        w8('s', 'T', tv)
        act(fb[:, FB_ + 512 * nt:FB_ + 512 * (nt + 1)], ps[nt][:, :], AF.Copy, s='S')
        if ln:
            tv = mm(ps[4 + nt][:, :], cs['ones32'][0:1, :], crow[:, sl], True, True, s='T')
            w8('s', 'T', tv)
            act(fb[:, FC + 512 * nt:FC + 512 * (nt + 1)], ps[4 + nt][:, :], AF.Copy, s='S')
    w8('v', 'S', g.cnt['S'])

    vbar = g.cnt['V']
    for c in range(8):
        xo = dst[:, dst_off + T * c:dst_off + T * (c + 1)]
        if from_dram:
            xf = fb[:, FA:FA + T]
            w8('y', 'V', vbar if c == 0 else vmark_cast)
            lv = ydma(xf, xsrc[128 * c:128 * (c + 1), :])
            w8('v', 'X', lv)
            vmark_cast = tt(xo, xf, fb[:, FB_:FB_ + T], AL.mult, s='V')
        else:
            xi = xsrc[0][:, xsrc[1] + T * c:xsrc[1] + T * (c + 1)]
            tt(xo, xi, fb[:, FB_:FB_ + T], AL.mult)
        if ln:
            tt(xo, xo, fb[:, FC:FC + T], AL.add, s='V')
        else:
            ts(xo, xo, wcol[:, c:c + 1], AL.mult, s='V')
    bar()


def emit_l0_attn(g):
    cs, fb, wb, xn, xm, ps = g.cs, g.fb, g.wb, g.xn, g.xm, g.ps
    psA, psB = g.psA, g.psB
    mm, act, tt, ts, vcp, vred, vrec, tr = g.mm, g.act, g.tt, g.ts, g.vcp, g.vred, g.vrec, g.tr
    ydma, gdma, w8, bar = g.ydma, g.gdma, g.w8, g.bar
    mc_ = g.mc_
    QS, KS, VP, WA, WB2, WC, SQ, S0 = 0, 8192, 16384, 32768, 36864, 40960, 49152, 51200
    SALL, NEGB, BIAS2 = 6144, 6400, 6656

    wq_t = g.wgroup([(wb[:, WA + 512 * kc:WA + 512 * (kc + 1)],
                      g.wt['l0_qstr'][128 * kc:128 * (kc + 1), :]) for kc in range(8)], 'Wa')
    wk_t = g.wgroup([(wb[:, WB2 + 512 * kc:WB2 + 512 * (kc + 1)],
                      g.wt['l0_kstr'][128 * kc:128 * (kc + 1), :]) for kc in range(8)], 'Wb')
    wv_t = g.wgroup([(wb[:, WC + 1024 * kc:WC + 1024 * (kc + 1)],
                      g.wt['l0_vT'][128 * kc:128 * (kc + 1), :]) for kc in range(8)], 'Wc')

    for side, Woff, Aoff in (('q', WA, QS), ('k', WB2, KS)):
        w8('t', 'Wa' if side == 'q' else 'Wb', wq_t if side == 'q' else wk_t)
        for c4 in range(4):
            for nt in range(4):
                bank = ps[(c4 * 4 + nt) % 2]
                if not (side == 'q' and c4 == 0 and nt < 2):
                    w8('t', 'V', g.cnt['V'])
                for kc in range(8):
                    mm(bank, wb[:, Woff + 512 * kc + 128 * c4:Woff + 512 * kc + 128 * (c4 + 1)],
                       xn[:, T * kc + 512 * nt:T * kc + 512 * (nt + 1)],
                       kc == 0, kc == 7, s='T' if kc == 7 else None)
                w8('v', 'T', g.cnt['T'])
                vcp(wb[:, Aoff + 2048 * c4 + 512 * nt:Aoff + 2048 * c4 + 512 * (nt + 1)], bank)
    w8('t', 'Wc', wv_t)
    for mc in range(16):
        for vh in range(2):
            bank = ps[2 + (mc * 2 + vh) % 2]
            w8('t', 'V', g.cnt['V'])
            for kc in range(8):
                mm(bank, xn[:, T * kc + 128 * mc:T * kc + 128 * (mc + 1)],
                   wb[:, WC + 1024 * kc + 512 * vh:WC + 1024 * kc + 512 * (vh + 1)],
                   kc == 0, kc == 7, s='T' if kc == 7 else None)
            w8('v', 'T', g.cnt['T'])
            vcp(wb[:, VP + 1024 * mc + 512 * vh:VP + 1024 * mc + 512 * (vh + 1)], bank)
    vdone = g.cnt['V']
    w8('g', 'V', vdone)
    wo_t = g.wgroup([(wb[:, WC + 1024 * hc:WC + 1024 * (hc + 1)],
                      g.wt['l0_opT'][128 * hc:128 * (hc + 1), :]) for hc in range(8)], 'Wd')

    for c4 in range(4):
        sq = wb[:, SQ:SQ + T]
        qs = wb[:, QS + 2048 * c4:QS + 2048 * (c4 + 1)]
        if c4 > 0:
            w8('v', 'T', g.cnt['T'])
        tt(sq, qs, qs, AL.mult, s='V')
        w8('t', 'V', g.cnt['V'])
        for nt in range(4):
            if c4 > 0 and nt == 0:
                w8('t', 'V', g.cnt['V'])
            mm(ps[4 + nt], cs['bsumL0'][:, :], sq[:, 512 * nt:512 * (nt + 1)],
               True, True, s='T' if nt == 3 else None)
        w8('v', 'T', g.cnt['T'])
        for nt in range(4):
            vred(mc_[:, 30 + nt:31 + nt], ps[4 + nt], AL.max)
        tt(mc_[:, 30:31], mc_[:, 30:31], mc_[:, 31:32], AL.max)
        tt(mc_[:, 32:33], mc_[:, 32:33], mc_[:, 33:34], AL.max)
        tt(mc_[:, 20 + c4:21 + c4], mc_[:, 30:31], mc_[:, 32:33], AL.max)
        ts(mc_[:, 20 + c4:21 + c4], mc_[:, 20 + c4:21 + c4], 0.0625, AL.mult, s='V')
    for c4 in range(4):
        sq = wb[:, SQ:SQ + T]
        ks = wb[:, KS + 2048 * c4:KS + 2048 * (c4 + 1)]
        w8('v', 'T', g.cnt['T'])
        tt(sq, ks, ks, AL.mult, s='V')
        w8('t', 'V', g.cnt['V'])
        for nt in range(4):
            if nt == 0:
                w8('t', 'S', g.cnt['S'])
            mm(ps[4 + nt], cs['bsumL0'][:, :], sq[:, 512 * nt:512 * (nt + 1)],
               True, True, s='T' if nt == 3 else None)
        w8('s', 'T', g.cnt['T'])
        for nt in range(4):
            act(fb[:, FB_ + 512 * nt:FB_ + 512 * (nt + 1)], ps[4 + nt], AF.Sqrt,
                scale=mc_[:, 20 + c4:21 + c4], s='S')
        w8('t', 'S', g.cnt['S'])
        for mc in range(16):
            tv = tr(ps[mc % 2][:, 0:128], fb[:, FB_ + 128 * mc:FB_ + 128 * (mc + 1)],
                    cs['ident'][:, :], s='T')
            w8('s', 'T', tv)
            for j in range(4):
                act(fb[:, NEGB + 16 * mc + 4 * c4 + j:NEGB + 16 * mc + 4 * c4 + j + 1],
                    ps[mc % 2][:, 32 * j:32 * j + 1], AF.Copy,
                    scale=g.fc(-1.0), bias=g.fc(25.0), s='S')
            w8('t', 'S', g.cnt['S'])
    bar()
    if g.tap == 'l0bnd':
        _tap_out(g, [(g.dbg[0:128, 0:256], fb[:, NEGB:NEGB + 256]),
                     (g.dbg[128:256, 0:1024], wb[:, VP:VP + 1024]),
                     (g.dbg[384:512, 0:2048], wb[:, QS:QS + 2048]),
                     (g.dbg[512:640, 0:2048], wb[:, KS:KS + 2048])])
        return True

    sprev = {}
    vlast = None
    for mc in range(16):
        for q4 in range(4):
            bs = 0 if q4 % 2 == 0 else 4
            c4 = q4
            for nt in range(4):
                for j in range(4):
                    key = bs + j
                    if key in sprev:
                        w8('t', 'S', sprev[key])
                    mm(ps[key],
                       wb[32 * j:32 * j + 16,
                          KS + 2048 * c4 + 128 * mc:KS + 2048 * c4 + 128 * (mc + 1)],
                       wb[32 * j:32 * j + 16,
                          QS + 2048 * c4 + 512 * nt:QS + 2048 * c4 + 512 * (nt + 1)],
                       True, True, tp=(32 * j, 0), s='T' if j == 3 else None)
                w8('s', 'T', g.cnt['T'])
                if nt == 0 and vlast is not None:
                    w8('s', 'V', vlast)
                for j in range(4):
                    h = 4 * q4 + j
                    sprev[bs + j] = act(
                        xm[:, 512 * j:512 * (j + 1)], ps[bs + j], AF.Exp,
                        bias=fb[:, NEGB + 16 * mc + h:NEGB + 16 * mc + h + 1],
                        scale=g.fc(0.25),
                        accum=mc_[:, 36 + 4 * j + nt:37 + 4 * j + nt], s='S')
            w8('v', 'S', g.cnt['S'])
            for j in range(4):
                h = 4 * q4 + j
                vred(fb[:, SALL + 16 * mc + h:SALL + 16 * mc + h + 1],
                     mc_[:, 36 + 4 * j:36 + 4 * j + 4], AL.add, s='V')
            vlast = g.cnt['V']
    w8('s', 'V', g.cnt['V'])
    act(fb[:, BIAS2:BIAS2 + 256], fb[:, SALL:SALL + 256], AF.Ln, s='S')
    w8('v', 'S', g.cnt['S'])
    tt(fb[:, BIAS2:BIAS2 + 256], fb[:, NEGB:NEGB + 256], fb[:, BIAS2:BIAS2 + 256],
       AL.subtract, s='V')
    bar()
    if g.tap == 'l0S':
        _tap_out(g, [(g.dbg[0:128, 0:256], fb[:, BIAS2:BIAS2 + 256]),
                     (g.dbg[128:256, 0:1024], wb[:, VP:VP + 1024]),
                     (g.dbg[256:384, 0:256], fb[:, SALL:SALL + 256])])
        return True

    sprevB = {}
    for q4 in range(4):
        c4 = q4
        for nt in range(4):
            for mc in range(16):
                for j in range(4):
                    h = 4 * q4 + j
                    if (4 + j) in sprevB:
                        w8('t', 'S', sprevB[4 + j])
                    mm(ps[4 + j],
                       wb[32 * j:32 * j + 16,
                          KS + 2048 * c4 + 128 * mc:KS + 2048 * c4 + 128 * (mc + 1)],
                       wb[32 * j:32 * j + 16,
                          QS + 2048 * c4 + 512 * nt:QS + 2048 * c4 + 512 * (nt + 1)],
                       True, True, tp=(32 * j, 0), s='T' if j == 3 else None)
                w8('s', 'T', g.cnt['T'])
                exv = {}
                for j in range(4):
                    h = 4 * q4 + j
                    exv[j] = act(xm[:, 512 * j:512 * (j + 1)], ps[4 + j], AF.Exp,
                                 bias=fb[:, BIAS2 + 16 * mc + h:BIAS2 + 16 * mc + h + 1],
                                 scale=g.fc(0.25), s='S')
                    sprevB[4 + j] = exv[j]
                for j in range(4):
                    h = 4 * q4 + j
                    pb = 64 * (h - (h % 2))
                    w8('t', 'S', exv[j])
                    mm(ps[j], wb[:, VP + 1024 * mc + pb:VP + 1024 * mc + pb + 128],
                       xm[:, 512 * j:512 * (j + 1)],
                       mc == 0, mc == 15, sgc=True,
                       s='T' if (mc == 15 and j == 3) else None)
            w8('s', 'T', g.cnt['T'])
            for j in range(4):
                h = 4 * q4 + j
                r0 = 64 * (h % 2)
                ev = act(wb[r0:r0 + 64, S0:S0 + 512], ps[j][r0:r0 + 64, :], AF.Copy, s='S')
                w8('y', 'S', ev)
                ov = ydma(g.ho[64 * h:64 * (h + 1), 512 * nt:512 * (nt + 1)],
                          wb[r0:r0 + 64, S0:S0 + 512], s='O')
                g.q['s'].append(lambda e, S=g.sem['O'], V=ov: e.wait_ge(S, V))
            w8('t', 'S', g.cnt['S'])
    bar()
    if g.tap == 'l0o':
        _tap_out(g, [(g.dbg[:, :], g.ho[:, :])])
        return True

    w8('g', 'O', g.cnt['O'])
    lo_t = g.wgroup([(wb[:, QS + 2048 * hc:QS + 2048 * (hc + 1)],
                      g.ho[128 * hc:128 * (hc + 1), :]) for hc in range(8)], 'Wa')
    w8('t', 'Wd', wo_t)
    w8('t', 'Wa', lo_t)
    for nt in range(4):
        for dc in range(8):
            bank = ps[(nt * 8 + dc) % 2]
            if nt + dc > 0:
                w8('t', 'S', g.cnt['S'])
            for hc in range(8):
                mm(bank, wb[:, WC + 1024 * hc + 128 * dc:WC + 1024 * hc + 128 * (dc + 1)],
                   wb[:, QS + 2048 * hc + 512 * nt:QS + 2048 * hc + 512 * (nt + 1)],
                   hc == 0, hc == 7, s='T' if hc == 7 else None)
            w8('s', 'T', g.cnt['T'])
            ev = act(wb[:, S0:S0 + 512], bank, AF.Copy, s='S')
            w8('y', 'S', ev)
            ov = ydma(g.hs[128 * dc:128 * (dc + 1), 512 * nt:512 * (nt + 1)],
                      wb[:, S0:S0 + 512], s='O')
            g.q['s'].append(lambda e, S=g.sem['O'], V=ov: e.wait_ge(S, V))
    bar()
    return False


def emit_l0_ffn_tail(g):
    cs, fb, wb, xn, xm, ps = g.cs, g.fb, g.wb, g.xn, g.xm, g.ps
    mm, act, tt, ts, vcp, vred, vrec, tr = g.mm, g.act, g.tt, g.ts, g.vcp, g.vred, g.vrec, g.tr
    ydma, gdma, w8, bar = g.ydma, g.gdma, g.w8, g.bar
    mc_ = g.mc_
    W1s = (0, 4096)
    W2s = (8192, 12288)
    H1, CB, CBC, B2, RT, SB, GG = 16384, 20480, 22528, 23552, 24576, 24704, 25728
    bv = ydma(mc_[:, 20:52], g.wt['l0_b1c'][:, :])
    bv = ydma(mc_[0:8, 52:53], g.wt['l0_rtb'][:, :])
    rtv = g.wgroup([(wb[:, RT + 8 * kc:RT + 8 * (kc + 1)],
                     g.wt['l0_rtr'][128 * kc:128 * (kc + 1), :]) for kc in range(8)], 'Wa')
    b2v = g.wgroup([(wb[0:8, B2:B2 + 1024], g.wt['l0_b2r'][:, :])], 'Wb')

    w8('t', 'Wa', rtv)
    for nt in range(4):
        bank = ps[4 + nt]
        for kc in range(8):
            mm(bank[0:8, :], wb[:, RT + 8 * kc:RT + 8 * (kc + 1)],
               xn[:, T * kc + 512 * nt:T * kc + 512 * (nt + 1)],
               kc == 0, kc == 7, s='T' if kc == 7 else None)
        w8('s', 'T', g.cnt['T'])
        w8('s', 'X', bv)
        act(fb[0:8, FA + 512 * nt:FA + 512 * (nt + 1)], bank[0:8, :], AF.Identity,
            bias=mc_[0:8, 52:53], s='S')
    LG = 6144
    for tsub in range(16):
        w8('t', 'S', g.cnt['S'])
        tv = tr(ps[0][:, 0:8], fb[0:8, FA + 128 * tsub:FA + 128 * (tsub + 1)],
                cs['ident'][0:8, 0:8], s='T')
        w8('v', 'T', tv)
        lg = fb[:, LG:LG + 8]
        vcp(lg, ps[0][:, 0:8])
        m1n = mc_[:, 53:54]
        vred(m1n, lg, AL.max, negate=True)
        ts(fb[:, LG + 8:LG + 16], lg, m1n, AL.add, s='V')
        ts(fb[:, LG + 16:LG + 24], fb[:, LG + 8:LG + 16], 0.0, AL.is_ge, s='V')
        ts(fb[:, LG + 24:LG + 32], fb[:, LG + 16:LG + 24], g.fc(-1e30), AL.mult, s='V')
        tt(fb[:, LG + 24:LG + 32], fb[:, LG + 8:LG + 16], fb[:, LG + 24:LG + 32], AL.add)
        vred(mc_[:, 54:55], fb[:, LG + 24:LG + 32], AL.max)
        w8('s', 'V', g.cnt['V'])
        act(mc_[:, 55:56], mc_[:, 54:55], AF.Exp, s='S')
        act(fb[:, LG + 32:LG + 40], fb[:, LG + 8:LG + 16], AF.Exp, s='S')
        w8('v', 'S', g.cnt['S'])
        ts(mc_[:, 55:56], mc_[:, 55:56], 1.0, AL.add, s='V')
        vrec(mc_[:, 55:56], mc_[:, 55:56])
        ts(fb[:, LG + 40:LG + 48], fb[:, LG + 8:LG + 16], mc_[:, 54:55], AL.is_ge, s='V')
        tt(fb[:, LG + 32:LG + 40], fb[:, LG + 32:LG + 40], fb[:, LG + 40:LG + 48], AL.mult)
        ts(fb[:, LG:LG + 8], fb[:, LG + 32:LG + 40], mc_[:, 55:56], AL.mult, s='V')
        w8('t', 'V', g.cnt['V'])
        tv = tr(ps[1][0:8, 0:128], fb[:, LG:LG + 8], cs['ident'][:, :], s='T')
        w8('s', 'T', tv)
        act(wb[0:8, CB + 128 * tsub:CB + 128 * (tsub + 1)], ps[1][0:8, 0:128],
            AF.Copy, s='S')
    bar()
    if g.tap == 'l0comb':
        _tap_out(g, [(g.dbg[0:8, 0:T], wb[0:8, CB:CB + T])])
        return True

    for half in range(2):
        bar()
        for eng in ('t', 's', 'v', 'g', 'y'):
            w8(eng, 'O', g.cnt['O'])
        t0 = 1024 * half
        w8('t', 'Wb', b2v)
        for dc in range(8):
            for t2 in range(2):
                bank = ps[(dc * 2 + t2) % 4]
                if dc + t2 > 0:
                    w8('t', 'S', g.cnt['S'])
                mm(bank, wb[0:8, B2 + 128 * dc:B2 + 128 * (dc + 1)],
                   wb[0:8, CB + t0 + 512 * t2:CB + t0 + 512 * (t2 + 1)],
                   True, True, s='T')
                w8('s', 'T', g.cnt['T'])
                act(fb[:, 1024 * dc + 512 * t2:1024 * dc + 512 * (t2 + 1)],
                    bank, AF.Copy, s='S')
        w8('v', 'S', g.cnt['S'])
        wsem = {}
        tm_w1, tm_w2 = {}, {}
        for e in range(8):
            sl = e % 2
            if e >= 2:
                w8('g', 'T', tm_w1[e - 2])
                w8('g', 'T', tm_w2[e - 2])
            elif half == 1:
                w8('g', 'T', g.l0_tm_last)
            wsem[('w1', e)] = g.wgroup(
                [(wb[:, W1s[sl] + 512 * kc:W1s[sl] + 512 * (kc + 1)],
                  g.wt['l0_W1'][e, 128 * kc:128 * (kc + 1), :]) for kc in range(8)],
                'Wa' if sl == 0 else 'Wb')
            wsem[('w2', e)] = g.wgroup(
                [(wb[:, W2s[sl] + 1024 * fc:W2s[sl] + 1024 * (fc + 1)],
                  g.wt['l0_W2'][e, 128 * fc:128 * (fc + 1), :]) for fc in range(4)],
                'Wc' if sl == 0 else 'Wd')
            for t2 in range(2):
                bank = ps[4 + t2]
                w8('t', 'V', g.cnt['V'])
                mm(bank, cs['sel8'][:, 128 * e:128 * (e + 1)],
                   wb[0:8, CB + t0 + 512 * t2:CB + t0 + 512 * (t2 + 1)],
                   True, True, s='T')
                w8('v', 'T', g.cnt['T'])
                vcp(wb[:, CBC + 512 * t2:CBC + 512 * (t2 + 1)], bank)
            w8('t', 'Wa' if sl == 0 else 'Wb', wsem[('w1', e)])
            for fc in range(4):
                for t2 in range(2):
                    bank = ps[4 + 2 + (fc * 2 + t2) % 2]
                    w8('t', 'S', g.cnt['S'])
                    for kc in range(8):
                        mm(bank, wb[:, W1s[sl] + 512 * kc + 128 * fc:W1s[sl] + 512 * kc + 128 * (fc + 1)],
                           xn[:, T * kc + t0 + 512 * t2:T * kc + t0 + 512 * (t2 + 1)],
                           kc == 0, kc == 7, s='T' if kc == 7 else None)
                    w8('s', 'T', g.cnt['T'])
                    act(wb[:, H1 + 1024 * fc + 512 * t2:H1 + 1024 * fc + 512 * (t2 + 1)],
                        bank, AF.Gelu, bias=mc_[:, 20 + 4 * e + fc:21 + 4 * e + fc], s='S')
            tm_w1[e] = g.cnt['T']
            w8('v', 'S', g.cnt['S'])
            for fc in range(4):
                for t2 in range(2):
                    tt(wb[:, H1 + 1024 * fc + 512 * t2:H1 + 1024 * fc + 512 * (t2 + 1)],
                       wb[:, H1 + 1024 * fc + 512 * t2:H1 + 1024 * fc + 512 * (t2 + 1)],
                       wb[:, CBC + 512 * t2:CBC + 512 * (t2 + 1)], AL.mult, s='V')
            hv = g.cnt['V']
            w8('t', 'V', hv)
            w8('t', 'Wc' if sl == 0 else 'Wd', wsem[('w2', e)])
            for dc in range(8):
                for t2 in range(2):
                    bank = ps[(dc * 2 + t2) % 4]
                    w8('t', 'V', g.cnt['V'])
                    for fc in range(4):
                        mm(bank, wb[:, W2s[sl] + 1024 * fc + 128 * dc:W2s[sl] + 1024 * fc + 128 * (dc + 1)],
                           wb[:, H1 + 1024 * fc + 512 * t2:H1 + 1024 * fc + 512 * (t2 + 1)],
                           fc == 0, fc == 3, s='T' if fc == 3 else None)
                    w8('v', 'T', g.cnt['T'])
                    tt(fb[:, 1024 * dc + 512 * t2:1024 * dc + 512 * (t2 + 1)],
                       fb[:, 1024 * dc + 512 * t2:1024 * dc + 512 * (t2 + 1)],
                       bank, AL.add, s='V')
            tm_w2[e] = g.cnt['T']
        g.l0_tm_last = g.cnt['T']
        bar()
        if g.tap == 'l0ffn' and half == 0:
            _tap_out(g, [(g.dbg[128 * dc:128 * (dc + 1), 0:1024],
                          fb[:, 1024 * dc:1024 * (dc + 1)]) for dc in range(8)])
            return True
        w8('v', 'T', g.cnt['T'])
        for dc in range(8):
            for t2 in range(2):
                osl = fb[:, 1024 * dc + 512 * t2:1024 * dc + 512 * (t2 + 1)]
                gsl = wb[:, GG:GG + 512]
                ssl = wb[:, GG + 512:GG + 1024]
                w8('s', 'V', g.cnt['V'])
                w8('s', 'O', g.cnt['O'])
                sv = act(gsl, osl, AF.Silu, s='S')
                w8('y', 'S', sv)
                lv1 = ydma(osl, g.x_fm[128 * dc:128 * (dc + 1),
                                       t0 + 512 * t2:t0 + 512 * (t2 + 1)])
                lv2 = ydma(ssl, g.hs[128 * dc:128 * (dc + 1),
                                     t0 + 512 * t2:t0 + 512 * (t2 + 1)])
                w8('v', 'X', lv2)
                tt(ps[6], gsl, ssl, AL.mult)
                tt(osl, osl, ps[6], AL.add, s='V')
                w8('y', 'V', g.cnt['V'])
                ydma(g.hx1[128 * dc:128 * (dc + 1), t0 + 512 * t2:t0 + 512 * (t2 + 1)],
                     osl, s='O')
    w8('y', 'O', g.cnt['O'])
    bar()
    return False


def emit_pos_bcast(g):
    cs, fb, ps = g.cs, g.fb, g.ps
    lv = g.ydma(fb[0:1, FA:FA + T], g.cst['pos_row'][0:1, :])
    g.w8('t', 'X', lv)
    g.w8('t', 'S', g.cnt['S'])
    for nt in range(4):
        tv = g.mm(ps[nt][:, :], cs['ones32'][0:1, :], fb[0:1, FA + 512 * nt:FA + 512 * (nt + 1)],
                  True, True, s='T')
        g.w8('s', 'T', tv)
        g.act(fb[:, FPOS + 512 * nt:FPOS + 512 * (nt + 1)], ps[nt][:, :], AF.Copy, s='S')
    g.bar()


def emit_l1_attention(g, P):
    cs, fb, wb, xm, ps = g.cs, g.fb, g.wb, g.xm, g.ps
    psA, psB = g.psA, g.psB
    mm, act, tt, ts, vcp, vred, vrec, tr = g.mm, g.act, g.tt, g.ts, g.vcp, g.vred, g.vrec, g.tr
    ydma, gdma, w8, bar = g.ydma, g.gdma, g.w8, g.bar
    mc_ = g.mc_
    QR, KR, VP, WS = 16384, 32768, 0, 49152
    OS = 51200
    NB = 0
    SC1, SC2, MSK, EX = 8192, 10240, 12288, 12416
    src = P['src']
    scale, tau, dsh = P['scale'], P['tau'], P['dsh']

    wvv = g.wgroup([(wb[:, QR + 1024 * kc:QR + 1024 * (kc + 1)],
                     P['wv_moving'][128 * kc:128 * (kc + 1), :]) for kc in range(8)], 'Wa')
    if P.get('vb_row') is not None:
        vbv = g.wgroup([(wb[0:1, WS:WS + 1024], P['vb_row'])], 'Wb')
        w8('t', 'Wb', vbv)
    w8('t', 'Wa', wvv)
    g._ws_tmark = None
    g._examk = {}
    hasvb = P.get('vb_row') is not None
    for mc in range(16):
        for vh in range(2):
            bank = ps[4 + (mc * 2 + vh) % 2]
            w8('t', 'V', g.cnt['V'])
            for kc in range(8):
                mm(bank, src[0][:, src[1] + T * kc + 128 * mc:src[1] + T * kc + 128 * (mc + 1)],
                   wb[:, QR + 1024 * kc + 512 * vh:QR + 1024 * kc + 512 * (vh + 1)],
                   kc == 0, kc == 7,
                   s='T' if (kc == 7 and not hasvb) else None)
            if hasvb:
                mm(bank, cs['ones_bf'][0:1, :], wb[0:1, WS + 512 * vh:WS + 512 * (vh + 1)],
                   False, True, sgc=True, s='T')
            w8('v', 'T', g.cnt['T'])
            vcp(wb[:, VP + 1024 * mc + 512 * vh:VP + 1024 * mc + 512 * (vh + 1)], bank)
    bar()

    if P['shared_tables']:
        _build_tables(g, P, 0)
    for side, Aoff, wsel, wpsel, bsel, bpsel, thsel in (
            ('q', QR, P['wq'], P['wqp'], P.get('qb'), P.get('qpb'), 'theta_q'),
            ('k', KR, P['wk'], P['wkp'], P.get('kb'), P.get('kpb'), 'theta_k')):
        for oc in range(8):
            sems = ('Wa', 'Wb') if side == 'q' else ('Wc', 'Wd')
            if g._ws_tmark is not None:
                w8('g', 'T', g._ws_tmark)
            lv1 = g.wgroup([(wb[:, WS + 256 * kc:WS + 256 * kc + 128],
                             wsel(oc)[128 * kc:128 * (kc + 1), :]) for kc in range(8)], sems[0])
            lv2 = g.wgroup([(wb[:, WS + 256 * kc + 128:WS + 256 * kc + 256],
                             wpsel(oc)[128 * kc:128 * (kc + 1), :]) for kc in range(8)], sems[1])
            if not P['shared_tables']:
                _build_tables(g, P, oc, thsel)
            w8('t', sems[0], lv1)
            w8('t', sems[1], lv2)
            w8('t', 'S', g.cnt['S'])
            for pp, bias in ((0, bsel), (1, bpsel)):
                for nt in range(4):
                    bank = (psA if pp == 0 else psB)[:, 512 * nt:512 * (nt + 1)]
                    for kc in range(8):
                        mm(bank, wb[:, WS + 256 * kc + 128 * pp:WS + 256 * kc + 128 * (pp + 1)],
                           src[0][:, src[1] + T * kc + 512 * nt:src[1] + T * kc + 512 * (nt + 1)],
                           kc == 0, kc == 7, s='T' if kc == 7 else None)
            w8('s', 'T', g.cnt['T'])
            src_is_xm = src[0] is g.xm
            for pp, bias in ((0, bsel), (1, bpsel)):
                pst = psA if pp == 0 else psB
                if src_is_xm:
                    dstc = fb[:, (FA if pp == 0 else FPOS):(FA if pp == 0 else FPOS) + 2048]
                else:
                    dstc = g.xm[:, (SC1 if pp == 0 else SC2):(SC1 if pp == 0 else SC2) + 2048]
                if bias is not None:
                    act(dstc, pst[:, :], AF.Identity, bias=bias(oc), s='S')
                else:
                    act(dstc, pst[:, :], AF.Copy, s='S')
            if g.tap == P.get('tap_tbl') and side == 'q' and oc == 0:
                _tap_out(g, [(g.dbg[0:128, :], g.xm[:, SC1:SC1 + 2048]),
                             (g.dbg[128:256, :], g.xm[:, SC2:SC2 + 2048]),
                             (g.dbg[256:384, :], fb[:, FB_:FB_ + T]),
                             (g.dbg[384:512, :], fb[:, FC:FC + T])])
                return True
            w8('v', 'S', g.cnt['S'])
            if src_is_xm:
                t0 = fb[:, FA:FA + 2048]
                t1 = fb[:, FPOS:FPOS + 2048]
            else:
                t0 = g.xm[:, SC1:SC1 + 2048]
                t1 = g.xm[:, SC2:SC2 + 2048]
            out = wb[:, Aoff + 2048 * oc:Aoff + 2048 * (oc + 1)]
            tt(out, t0, fb[:, FC:FC + T], AL.mult)
            tt(t1, t1, fb[:, FB_:FB_ + T], AL.mult)
            tt(out, out, t1, AL.add, s='V')
            w8('t', 'V', g.cnt['V'])
            g._ws_tmark = g.cnt['T']
    bar()
    if g.tap == P.get('tap_qk'):
        _tap_out(g, [(g.dbg[0:128, 0:T], wb[:, QR:QR + T]),
                     (g.dbg[128:256, 0:T], wb[:, KR:KR + T]),
                     (g.dbg[256:384, 0:1024], wb[:, VP:VP + 1024])])
        return True

    for c in range(8):
        bst = cs['bsumA'] if c % 2 == 0 else cs['bsumB']
        rbase = 64 * (c % 2)
        xtile = NB + 2048 * (c // 2)
        kch = wb[:, KR + 2048 * c:KR + 2048 * (c + 1)]
        qch = wb[:, QR + 2048 * c:QR + 2048 * (c + 1)]
        vred(mc_[:, 20:21], kch, AL.add)
        ts(mc_[:, 20:21], mc_[:, 20:21], 1.0 / T, AL.mult, s='V')
        w8('v', 'T', g.cnt['T'])
        ts(g.xm[:, SC1:SC1 + T], kch, mc_[:, 20:21], AL.subtract, s='V')
        tt(g.xm[:, SC1:SC1 + T], g.xm[:, SC1:SC1 + T], g.xm[:, SC1:SC1 + T], AL.mult, s='V')
        w8('t', 'V', g.cnt['V'])
        for nt in range(4):
            mm(ps[4 + nt], bst[:, :], g.xm[:, SC1 + 512 * nt:SC1 + 512 * (nt + 1)],
               True, True, s='T' if nt == 3 else None)
        w8('v', 'T', g.cnt['T'])
        for nt in range(4):
            vred(mc_[:, 30 + nt:31 + nt], ps[4 + nt], AL.max)
        tt(mc_[:, 30:31], mc_[:, 30:31], mc_[:, 31:32], AL.max)
        tt(mc_[:, 32:33], mc_[:, 32:33], mc_[:, 33:34], AL.max)
        tt(mc_[:, 21:22], mc_[:, 30:31], mc_[:, 32:33], AL.max, s='V')
        w8('s', 'V', g.cnt['V'])
        act(mc_[:, 21:22], mc_[:, 21:22], AF.Sqrt, s='S')
        g.vms(g.xm[:, MSK:MSK + 128], 0.0)
        w8('v', 'S', g.cnt['S'])
        vcp(g.xm[0:64, MSK + rbase:MSK + rbase + 1], mc_[0:64, 20:21])
        vcp(g.xm[64:128, MSK + rbase + 32:MSK + rbase + 33], mc_[64:128, 20:21], s='V')
        w8('t', 'V', g.cnt['V'])
        for nt in range(4):
            mm(ps[nt], g.xm[:, MSK:MSK + 128], qch[:, 512 * nt:512 * (nt + 1)],
               True, True, s='T' if nt == 3 else None)
        tq = g.cnt['T']
        w8('v', 'T', tq)
        tt(g.xm[:, SC2:SC2 + T], qch, qch, AL.mult, s='V')
        w8('t', 'V', g.cnt['V'])
        for nt in range(4):
            mm(ps[4 + nt], bst[:, :], g.xm[:, SC2 + 512 * nt:SC2 + 512 * (nt + 1)],
               True, True, s='T' if nt == 3 else None)
        w8('s', 'T', g.cnt['T'])
        w8('v', 'T', g.cnt['T'])
        for nt in range(4):
            act(fb[:, FA + 512 * nt:FA + 512 * (nt + 1)], ps[4 + nt], AF.Sqrt, s='S')
        w8('v', 'S', g.cnt['S'])
        for nt in range(4):
            sl = slice(512 * nt, 512 * (nt + 1))
            ts(fb[:, FA:FA + T][:, sl], fb[:, FA:FA + T][:, sl], mc_[:, 21:22], AL.mult)
            tt(fb[:, FA:FA + T][:, sl], fb[:, FA:FA + T][:, sl], ps[nt], AL.add)
            for j in range(2):
                rh = rbase + 32 * j
                ts(g.xm[rh:rh + 1, xtile:xtile + T][:, sl],
                   fb[rh:rh + 1, FA:FA + T][:, sl], -1.0, AL.mult, s='V')
        w8('t', 'V', g.cnt['V'])
    bar()

    for qt in range(4):
        for c in range(8):
            rbase = 64 * (c % 2)
            xtile = NB + 2048 * (c // 2)
            for j in range(2):
                h = 2 * c + j
                rh = rbase + 32 * j
                sab = ps[2 + j]
                for mc in range(16):
                    bank = ps[mc % 2]
                    if g._examk.get(mc % 2):
                        w8('t', 'S', g._examk[mc % 2])
                    mm(bank, cs['ones_bf'][rh:rh + 1, :],
                       g.xm[rh:rh + 1, xtile + 512 * qt:xtile + 512 * (qt + 1)],
                       True, True, tp=(rh, 0), sgc=True)
                    mm(bank, wb[64 * j:64 * (j + 1), KR + 2048 * c + 128 * mc:KR + 2048 * c + 128 * (mc + 1)],
                       wb[64 * j:64 * (j + 1), QR + 2048 * c + 512 * qt:QR + 2048 * c + 512 * (qt + 1)],
                       False, True, sgc=True, s='T')
                    w8('s', 'T', g.cnt['T'])
                    ev = act(g.xm[:, EX + 512 * (mc % 2):EX + 512 * (mc % 2 + 1)],
                             bank, AF.Exp, scale=g.fc(scale / tau), s='S')
                    g._examk[mc % 2] = ev
                    w8('t', 'S', ev)
                    mm(sab, cs['ones_bf'][:, :],
                       g.xm[:, EX + 512 * (mc % 2):EX + 512 * (mc % 2 + 1)],
                       mc == 0, mc == 15, sgc=True, s='T' if mc == 15 else None)
                w8('s', 'T', g.cnt['T'])
                act(fb[rh:rh + 1, FA:FA + 512], sab[rh:rh + 1, :], AF.Ln, s='S')
                w8('v', 'S', g.cnt['S'])
                ts(fb[rh:rh + 1, FA:FA + 512], fb[rh:rh + 1, FA:FA + 512],
                   g.fc(-tau / scale), AL.mult, g.fc(dsh / scale), AL.add, s='V')
                tt(g.xm[rh:rh + 1, xtile + 512 * qt:xtile + 512 * (qt + 1)],
                   fb[rh:rh + 1, FA:FA + 512],
                   g.xm[rh:rh + 1, xtile + 512 * qt:xtile + 512 * (qt + 1)], AL.add, s='V')
                w8('t', 'V', g.cnt['V'])
    bar()

    for qt in range(4):
        for c in range(8):
            rbase = 64 * (c % 2)
            xtile = NB + 2048 * (c // 2)
            for j in range(2):
                h = 2 * c + j
                rh = rbase + 32 * j
                ob = ps[2 + j]
                sbb = ps[4 + j]
                for mc in range(16):
                    bank = ps[mc % 2]
                    if g._examk.get(mc % 2):
                        w8('t', 'S', g._examk[mc % 2])
                    mm(bank, cs['ones_bf'][rh:rh + 1, :],
                       g.xm[rh:rh + 1, xtile + 512 * qt:xtile + 512 * (qt + 1)],
                       True, True, tp=(rh, 0), sgc=True)
                    mm(bank, wb[64 * j:64 * (j + 1), KR + 2048 * c + 128 * mc:KR + 2048 * c + 128 * (mc + 1)],
                       wb[64 * j:64 * (j + 1), QR + 2048 * c + 512 * qt:QR + 2048 * c + 512 * (qt + 1)],
                       False, True, sgc=True, s='T')
                    w8('s', 'T', g.cnt['T'])
                    ev = act(g.xm[:, EX + 512 * (mc % 2):EX + 512 * (mc % 2 + 1)],
                             bank, AF.Exp, scale=g.fc(scale), s='S')
                    g._examk[mc % 2] = ev
                    w8('t', 'S', ev)
                    pb = 64 * (h - (h % 2))
                    mm(ob, wb[:, VP + 1024 * mc + pb:VP + 1024 * mc + pb + 128],
                       g.xm[:, EX + 512 * (mc % 2):EX + 512 * (mc % 2 + 1)],
                       mc == 0, mc == 15, sgc=True)
                    mm(sbb, cs['ones_bf'][:, :],
                       g.xm[:, EX + 512 * (mc % 2):EX + 512 * (mc % 2 + 1)],
                       mc == 0, mc == 15, sgc=True, s='T' if mc == 15 else None)
                w8('v', 'T', g.cnt['T'])
                r0 = 64 * (h % 2)
                vrec(fb[:, FA:FA + 512], sbb)
                tt(wb[r0:r0 + 64, OS:OS + 512],
                   ob[r0:r0 + 64, :], fb[r0:r0 + 64, FA:FA + 512], AL.mult, s='V')
                w8('y', 'V', g.cnt['V'])
                ov = ydma(g.ho[64 * h:64 * (h + 1), 512 * qt:512 * (qt + 1)],
                          wb[r0:r0 + 64, OS:OS + 512], s='O')
                g.q['v'].append(lambda e, S=g.sem['O'], V=ov: e.wait_ge(S, V))
                w8('t', 'V', g.cnt['V'])
    bar()
    if g.tap == P.get('tap_o'):
        _tap_out(g, [(g.dbg[:, :], g.ho[:, :])])
        return True

    w8('g', 'O', g.cnt['O'])
    lo_t = g.wgroup([(wb[:, QR + 2048 * hc:QR + 2048 * (hc + 1)],
                      g.ho[128 * hc:128 * (hc + 1), :]) for hc in range(8)], 'Wa')
    wo_t = g.wgroup([(wb[:, KR + 1024 * hc:KR + 1024 * (hc + 1)],
                      P['wo'][128 * hc:128 * (hc + 1), :]) for hc in range(8)], 'Wb')
    w8('t', 'Wa', lo_t)
    w8('t', 'Wb', wo_t)
    ob = P.get('ob')
    for nt in range(4):
        for dc in range(8):
            bank = ps[(nt * 8 + dc) % 2]
            if nt + dc > 0:
                w8('t', 'S', g.cnt['S'])
            for hc in range(8):
                mm(bank, wb[:, KR + 1024 * hc + 128 * dc:KR + 1024 * hc + 128 * (dc + 1)],
                   wb[:, QR + 2048 * hc + 512 * nt:QR + 2048 * hc + 512 * (nt + 1)],
                   hc == 0, hc == 7, s='T' if hc == 7 else None)
            w8('s', 'T', g.cnt['T'])
            if ob is not None:
                ev = act(wb[:, OS:OS + 512], bank, AF.Identity, bias=ob(dc), s='S')
            else:
                ev = act(wb[:, OS:OS + 512], bank, AF.Copy, s='S')
            w8('y', 'S', ev)
            ov = ydma(P['dst'][128 * dc:128 * (dc + 1), 512 * nt:512 * (nt + 1)],
                      wb[:, OS:OS + 512], s='O')
            g.q['s'].append(lambda e, S=g.sem['O'], V=ov: e.wait_ge(S, V))
    bar()
    return False


def _build_tables(g, P, oc, thsel='theta_q'):
    fb, ic = g.fb, g.ic
    th = P[thsel](oc)
    g.w8('v', 'T', g.cnt['T'])
    g.w8('v', 'S', g.cnt['S'])
    for dst, s2 in ((FB_, P['pos_s2_sin']), (FC, P['pos_s2_cos'])):
        g.ts(fb[:, dst:dst + T], fb[:, FPOS:FPOS + T], th, AL.mult, s2, AL.add, s='V')
        g.ts(fb[:, dst:dst + T], fb[:, dst:dst + T], 1.0 / TWO_PI, AL.mult,
             1024.0, AL.add, s='V')
        for nt in range(4):
            sl = slice(dst + 512 * nt, dst + 512 * (nt + 1))
            g.vcp(ic[:, :], fb[:, sl])
            g.vcp(fb[:, FA:FA + 512], ic[:, :])
            g.tt(fb[:, sl], fb[:, sl], fb[:, FA:FA + 512], AL.subtract, s='V')
        g.ts(fb[:, FA:FA + T], fb[:, dst:dst + T], 0.5, AL.is_ge, s='V')
        g.tt(fb[:, dst:dst + T], fb[:, dst:dst + T], fb[:, FA:FA + T],
             AL.subtract, s='V')
        g.ts(fb[:, FA:FA + T], fb[:, dst:dst + T], -0.5, AL.is_lt, s='V')
        g.tt(fb[:, dst:dst + T], fb[:, dst:dst + T], fb[:, FA:FA + T],
             AL.add, s='V')
        g.ts(fb[:, dst:dst + T], fb[:, dst:dst + T], TWO_PI, AL.mult, s='V')
    g.w8('s', 'V', g.cnt['V'])
    g.act(fb[:, FB_:FB_ + T], fb[:, FB_:FB_ + T], AF.Sin, s='S')
    g.act(fb[:, FC:FC + T], fb[:, FC:FC + T], AF.Sin, s='S')
    g.w8('v', 'S', g.cnt['S'])


def emit_l1f_post(g):
    fb, wb, xn = g.fb, g.wb, g.xn
    tt, w8, ydma, gdma = g.tt, g.w8, g.ydma, g.gdma
    for dc in range(8):
        for t2 in range(4):
            sl = slice(512 * t2, 512 * (t2 + 1))
            w8('g', 'V', g.cnt['V'])
            hav = gdma(wb[:, 512:1024], g.ha[128 * dc:128 * (dc + 1), sl], s='Wc')
            w8('v', 'Wc', hav)
            w8('v', 'O', g.cnt['O'])
            tt(fb[:, FA:FA + 512], xn[:, T * dc:T * (dc + 1)][:, sl],
               wb[:, 512:1024], AL.add, s='V')
            w8('y', 'V', g.cnt['V'])
            ov = ydma(g.hx2[128 * dc:128 * (dc + 1), sl], fb[:, FA:FA + 512], s='O')
            g.q['v'].append(lambda e, S=g.sem['O'], V=ov: e.wait_ge(S, V))
    for eng in ('t', 's', 'v'):
        w8(eng, 'O', g.cnt['O'])
    emit_norm(g, g.hx2, 'rms', wcol=g.mc2[:, 56:64], dst=g.xm)


def emit_l1_moe_tail(g):
    cs, fb, wb, xn, xm, ps = g.cs, g.fb, g.wb, g.xn, g.xm, g.ps
    mm, act, tt, ts, vcp, vred, vrec, tr = g.mm, g.act, g.tt, g.ts, g.vcp, g.vred, g.vrec, g.tr
    ydma, gdma, w8, bar = g.ydma, g.gdma, g.w8, g.bar
    mc_, mc2 = g.mc_, g.mc2
    W1, W2, W3 = 0, 16384, 32768
    CB = 49152
    RT = 51200
    H1 = 0
    CBC, ST1, ST2, B3, B2E = 8192, 9216, 10240, 11264, 12288
    wbx = xn
    bv = ydma(mc2[:, 64:192], g.wt['l1f_b1c'][:, :])
    rtv = g.wgroup([(wb[:, RT + 8 * kc:RT + 8 * (kc + 1)],
                     g.wt['l1f_gateT'][128 * kc:128 * (kc + 1), :]) for kc in range(8)], 'Wa')
    w8('t', 'Wa', rtv)
    for nt in range(4):
        bank = ps[4 + nt]
        for kc in range(8):
            mm(bank[0:8, :], wb[:, RT + 8 * kc:RT + 8 * (kc + 1)],
               xm[:, T * kc + 512 * nt:T * kc + 512 * (nt + 1)],
               kc == 0, kc == 7, s='T' if kc == 7 else None)
        w8('s', 'T', g.cnt['T'])
        act(fb[0:8, FA + 512 * nt:FA + 512 * (nt + 1)], bank[0:8, :], AF.Copy, s='S')
    LG = 6144
    for tsub in range(16):
        w8('t', 'S', g.cnt['S'])
        tv = tr(ps[0][:, 0:8], fb[0:8, FA + 128 * tsub:FA + 128 * (tsub + 1)],
                cs['ident'][0:8, 0:8], s='T')
        w8('v', 'T', tv)
        lg = fb[:, LG:LG + 8]
        vcp(lg, ps[0][:, 0:8])
        m1n = mc_[:, 53:54]
        vred(m1n, lg, AL.max, negate=True)
        ts(fb[:, LG + 8:LG + 16], lg, m1n, AL.add, s='V')
        ts(fb[:, LG + 16:LG + 24], fb[:, LG + 8:LG + 16], 0.0, AL.is_ge, s='V')
        ts(fb[:, LG + 24:LG + 32], fb[:, LG + 16:LG + 24], g.fc(-1e30), AL.mult, s='V')
        tt(fb[:, LG + 24:LG + 32], fb[:, LG + 8:LG + 16], fb[:, LG + 24:LG + 32], AL.add)
        vred(mc_[:, 54:55], fb[:, LG + 24:LG + 32], AL.max)
        w8('s', 'V', g.cnt['V'])
        act(mc_[:, 55:56], mc_[:, 54:55], AF.Exp, s='S')
        act(fb[:, LG + 32:LG + 40], fb[:, LG + 8:LG + 16], AF.Exp, s='S')
        w8('v', 'S', g.cnt['S'])
        ts(mc_[:, 55:56], mc_[:, 55:56], 1.0, AL.add, s='V')
        vrec(mc_[:, 55:56], mc_[:, 55:56])
        ts(fb[:, LG + 40:LG + 48], fb[:, LG + 8:LG + 16], mc_[:, 54:55], AL.is_ge, s='V')
        tt(fb[:, LG + 32:LG + 40], fb[:, LG + 32:LG + 40], fb[:, LG + 40:LG + 48], AL.mult)
        ts(fb[:, LG:LG + 8], fb[:, LG + 32:LG + 40], mc_[:, 55:56], AL.mult, s='V')
        w8('t', 'V', g.cnt['V'])
        tv = tr(ps[1][0:8, 0:128], fb[:, LG:LG + 8], cs['ident'][:, :], s='T')
        w8('s', 'T', tv)
        act(wb[0:8, CB + 128 * tsub:CB + 128 * (tsub + 1)], ps[1][0:8, 0:128],
            AF.Copy, s='S')
    bar()
    if g.tap == 'l1comb':
        _tap_out(g, [(g.dbg[0:8, 0:T], wb[0:8, CB:CB + T])])
        return True

    b3v = g.wgroup([(wbx[0:8, B3:B3 + 1024], g.wt['l1f_b3r'][:, :])], 'Wb')
    g.vms(wb[0:1, 51200:51712], 1.0, s='V')
    w8('t', 'V', g.cnt['V'])
    w8('s', 'X', bv)
    tm_w = {}
    for half in range(2):
        bar()
        for eng in ('t', 's', 'v', 'g', 'y'):
            w8(eng, 'O', g.cnt['O'])
        t0 = 1024 * half
        w8('t', 'Wb', b3v)
        for dc in range(8):
            for t2 in range(2):
                bank = ps[(dc * 2 + t2) % 4]
                if dc + t2 > 0:
                    w8('t', 'S', g.cnt['S'])
                mm(bank, wbx[0:8, B3 + 128 * dc:B3 + 128 * (dc + 1)],
                   wb[0:8, CB + t0 + 512 * t2:CB + t0 + 512 * (t2 + 1)],
                   True, True, s='T')
                w8('s', 'T', g.cnt['T'])
                act(fb[:, 1024 * dc + 512 * t2:1024 * dc + 512 * (t2 + 1)],
                    bank, AF.Copy, s='S')
        w8('v', 'S', g.cnt['S'])
        wv_marks = {}

        def load_expert(e2):
            wv_marks[e2] = (
                g.wgroup([(wb[:, W1 + 2048 * kc:W1 + 2048 * (kc + 1)],
                           g.wt['l1f_w1T'][e2, 128 * kc:128 * (kc + 1), :])
                          for kc in range(8)], 'Wa'),
                g.wgroup([(wb[:, W2 + 2048 * kc:W2 + 2048 * (kc + 1)],
                           g.wt['l1f_w2T'][e2, 128 * kc:128 * (kc + 1), :])
                          for kc in range(8)], 'Wc'),
                g.wgroup([(wb[:, W3 + 1024 * fc:W3 + 1024 * (fc + 1)],
                           g.wt['l1f_w3T'][e2, 128 * fc:128 * (fc + 1), :])
                          for fc in range(16)], 'Wd'),
                g.wgroup([(wbx[0:1, B2E + 2048 * (e2 % 2):B2E + 2048 * (e2 % 2) + 2048],
                           g.wt['l1f_b2r'][0:1, 2048 * e2:2048 * (e2 + 1)])], 'Wb'))
        smk = {0: 0, 1: 0}
        vmk = {0: 0, 1: 0}
        hmk = {0: 0, 1: 0}
        omk = {0: 0, 1: 0, 2: 0, 3: 0}
        cbk = 0
        w3mk = 0
        for e in range(8):
            if (e, half) > (0, 0):
                w8('g', 'T', tm_w.get((e, half), g.cnt['T']))
            load_expert(e)
            w1v, w2v, w3v, b2v = wv_marks[e]
            for t2 in range(2):
                bank = ps[4 + t2]
                if cbk:
                    w8('t', 'V', cbk)
                mm(bank, cs['sel8'][:, 128 * e:128 * (e + 1)],
                   wb[0:8, CB + t0 + 512 * t2:CB + t0 + 512 * (t2 + 1)],
                   True, True, s='T')
                w8('v', 'T', g.cnt['T'])
                cbk = vcp(wbx[:, CBC + 512 * t2:CBC + 512 * (t2 + 1)], bank, s='V')
            w8('t', 'Wa', w1v)
            w8('t', 'Wc', w2v)
            w8('t', 'Wb', b2v)
            w8('t', 'Wd', w3v)
            for t2 in range(2):
                for fc in range(16):
                    pp = fc % 2
                    bank = ps[4 + 2 + pp]
                    if smk[pp]:
                        w8('t', 'S', smk[pp])
                    for kc in range(8):
                        mm(bank, wb[:, W1 + 2048 * kc + 128 * fc:W1 + 2048 * kc + 128 * (fc + 1)],
                           xm[:, T * kc + t0 + 512 * t2:T * kc + t0 + 512 * (t2 + 1)],
                           kc == 0, kc == 7, s='T' if kc == 7 else None)
                    tmka = g.cnt['T']
                    bank2 = ps[pp]
                    if hmk[pp]:
                        w8('t', 'V', hmk[pp])
                    for kc in range(8):
                        mm(bank2, wb[:, W2 + 2048 * kc + 128 * fc:W2 + 2048 * kc + 128 * (fc + 1)],
                           xm[:, T * kc + t0 + 512 * t2:T * kc + t0 + 512 * (t2 + 1)],
                           kc == 0, kc == 7, s=None)
                    mm(bank2, wbx[0:1, B2E + 2048 * (e % 2) + 128 * fc:B2E + 2048 * (e % 2) + 128 * (fc + 1)],
                       wb[0:1, 51200:51712],
                       False, True, sgc=True, s='T')
                    tmkb = g.cnt['T']
                    w8('s', 'T', tmka)
                    if hmk[pp]:
                        w8('s', 'V', hmk[pp])
                    smk[pp] = act(wbx[:, ST1 + 512 * pp:ST1 + 512 * (pp + 1)], bank, AF.Silu,
                                  bias=mc2[:, 64 + 16 * e + fc:65 + 16 * e + fc], s='S')
                    w8('v', 'S', smk[pp])
                    w8('v', 'T', tmkb)
                    if fc == 0 and w3mk:
                        w8('v', 'T', w3mk)
                    tt(wbx[:, ST2 + 512 * pp:ST2 + 512 * (pp + 1)],
                       wbx[:, ST1 + 512 * pp:ST1 + 512 * (pp + 1)],
                       wbx[:, CBC + 512 * t2:CBC + 512 * (t2 + 1)], AL.mult)
                    hmk[pp] = tt(xn[:, H1 + 512 * fc:H1 + 512 * (fc + 1)],
                                 wbx[:, ST2 + 512 * pp:ST2 + 512 * (pp + 1)],
                                 bank2, AL.mult, s='V')
                if g.tap == 'l1h' and e == 0 and half == 0 and t2 == 0:
                    g.bar()
                    _tap_out(g, [(g.dbg[128 * fc:128 * (fc + 1), 0:512],
                                  xn[:, H1 + 512 * fc:H1 + 512 * (fc + 1)])
                                 for fc in range(16)])
                    return True
                w8('t', 'V', max(hmk[0], hmk[1]))
                for dc in range(8):
                    bank = ps[(dc + t2) % 4]
                    if omk[(dc + t2) % 4]:
                        w8('t', 'V', omk[(dc + t2) % 4])
                    for fc in range(16):
                        mm(bank, wb[:, W3 + 1024 * fc + 128 * dc:W3 + 1024 * fc + 128 * (dc + 1)],
                           xn[:, H1 + 512 * fc:H1 + 512 * (fc + 1)],
                           fc == 0, fc == 15, s='T' if fc == 15 else None)
                    w8('v', 'T', g.cnt['T'])
                    omk[(dc + t2) % 4] = tt(
                        fb[:, 1024 * dc + 512 * t2:1024 * dc + 512 * (t2 + 1)],
                        fb[:, 1024 * dc + 512 * t2:1024 * dc + 512 * (t2 + 1)],
                        bank, AL.add, s='V')
                w3mk = g.cnt['T']
            tm_w[(e + 1, half) if e < 7 else (0, 1)] = g.cnt['T']
        bar()
        if g.tap == 'l1moe' and half == 0:
            _tap_out(g, [(g.dbg[128 * dc:128 * (dc + 1), 0:1024],
                          fb[:, 1024 * dc:1024 * (dc + 1)]) for dc in range(8)])
            return True
        w8('v', 'T', g.cnt['T'])
        for dc in range(8):
            for t2 in range(2):
                sl = slice(t0 + 512 * t2, t0 + 512 * (t2 + 1))
                osl = fb[:, 1024 * dc + 512 * t2:1024 * dc + 512 * (t2 + 1)]
                w8('g', 'V', g.cnt['V'])
                x2v = gdma(wbx[:, ST1:ST1 + 512], g.hx2[128 * dc:128 * (dc + 1), sl], s='Wc')
                w8('y', 'V', g.cnt['V'])
                sv_ = ydma(wbx[:, ST2:ST2 + 512], g.hs[128 * dc:128 * (dc + 1), sl])
                w8('v', 'Wc', x2v)
                w8('v', 'X', sv_)
                w8('v', 'S', g.cnt['S'])
                tt(ps[6], osl, wbx[:, ST1:ST1 + 512], AL.add, s='V')
                w8('s', 'V', g.cnt['V'])
                gv = act(wbx[:, ST1:ST1 + 512], ps[6], AF.Silu, s='S')
                w8('y', 'V', g.cnt['V'])
                lv1 = ydma(osl, g.hx1[128 * dc:128 * (dc + 1), sl])
                w8('v', 'X', lv1)
                w8('v', 'S', gv)
                tt(ps[7], wbx[:, ST1:ST1 + 512], wbx[:, ST2:ST2 + 512], AL.mult)
                tt(osl, osl, ps[7], AL.add, s='V')
                w8('y', 'V', g.cnt['V'])
                ov = ydma(g.out[128 * dc:128 * (dc + 1), sl], osl, s='O')
                g.q['v'].append(lambda e, S=g.sem['O'], V=ov: e.wait_ge(S, V))
                g.q['s'].append(lambda e, S=g.sem['O'], V=ov: e.wait_ge(S, V))
    w8('y', 'O', g.cnt['O'])
    bar()
    return False


def finish(g):
    for c in range(8):
        g.ydma(g.out[128 * c:128 * (c + 1), :], g.xn[:, T * c:T * (c + 1)], s='O')
    g.w8('y', 'O', g.cnt['O'])


def kernel(**inputs):
    x = np.asarray(inputs['L0_attn_x'])
    CST = _host_consts()
    W = _host_weights(inputs)
    nc = build_program(tap=DEBUG_TAP)
    fvals = np.zeros((128, 64), dtype=np.float32)
    for i, v in enumerate(kernel._fvals):
        fvals[:, i] = v
    in_maps = []
    for c in range(NC_):
        m = {'x_fm': np.ascontiguousarray(x[c].T), 'c_fvals': fvals}
        for k, v in CST.items():
            m['c_' + k] = v
        for k, v in W.items():
            m['w_' + k] = np.ascontiguousarray(v, dtype=np.float32)
        in_maps.append(m)
    res = run_bass_kernel_spmd(nc, in_maps, core_ids=list(range(NC_)))
    outs = [np.asarray(r['out']).T for r in res.results]
    if DEBUG_TAP is not None:
        kernel.dbg = [np.asarray(r['dbg']) for r in res.results]
    return np.stack(outs).astype(np.float32)


# version 07
# speedup vs baseline: 1.1275x; 1.1222x over previous
import numpy as np
import ml_dtypes
import concourse.bass as bass
import concourse.mybir as mybir
from concourse.bass_utils import run_bass_kernel_spmd

F32 = mybir.dt.float32
BF16 = mybir.dt.bfloat16
AF = mybir.ActivationFunctionType
AL = mybir.AluOpType
AX = mybir.AxisListType
BF = ml_dtypes.bfloat16

D, T, NC_ = 1024, 2048, 8
PI = float(np.pi)
TWO_PI = float(2.0 * np.pi)

DEBUG_TAP = None


def _host_consts():
    c = {}
    c['ident'] = np.eye(128, dtype=np.float32)
    c['ones32'] = np.ones((128, 128), dtype=np.float32)
    c['ones_bf'] = np.ones((128, 128), dtype=BF)
    sel = np.zeros((33, 128), dtype=np.float32)
    sel[0, :64] = 1.0
    sel[32, 64:] = 1.0
    c['sel33'] = sel
    b = np.zeros((128, 128), dtype=np.float32)
    for j in range(4):
        b[32 * j:32 * j + 16, 32 * j] = 1.0
    c['bsumL0'] = b.astype(BF)
    a = np.zeros((128, 128), dtype=np.float32)
    a[0:64, 0] = 1.0
    a[64:128, 32] = 1.0
    c['bsumA'] = a.astype(BF)
    a = np.zeros((128, 128), dtype=np.float32)
    a[0:64, 64] = 1.0
    a[64:128, 96] = 1.0
    c['bsumB'] = a.astype(BF)
    c['pos_row'] = np.arange(1, T + 1, dtype=np.float32).reshape(1, T)
    r = np.arange(128)
    exq = np.zeros((128, 8), dtype=np.float32)
    for ch in range(8):
        exq[:, ch] = -(((128 * ch + r) // 2).astype(np.float64) / 512.0)
    c['exq'] = exq
    exk = np.zeros((128, 8), dtype=np.float32)
    for ch in range(8):
        exk[:, ch] = -(((128 * ch + r) // 4).astype(np.float64) / 256.0)
    c['exk'] = exk
    c['exf'] = (-(((r % 64) % 32).astype(np.float64) / 32.0)).astype(np.float32).reshape(128, 1)
    c['sgq'] = np.where(r % 2 == 0, -1.0, 1.0).astype(np.float32).reshape(128, 1)
    c['sgk4'] = np.where(r % 4 < 2, -1.0, 1.0).astype(np.float32).reshape(128, 1)
    c['sgrh'] = np.where(r % 64 < 32, -1.0, 1.0).astype(np.float32).reshape(128, 1)
    c['tenk'] = np.full((128, 1), 10000.0, dtype=np.float32)
    s8 = np.zeros((8, 1024), dtype=np.float32)
    for e in range(8):
        s8[e, 128 * e:128 * (e + 1)] = 1.0
    c['sel8'] = s8.astype(BF)
    return c


def _host_weights(I):
    w = {}
    qp = np.asarray(I['L0_attn_query_proj'])
    kp = np.asarray(I['L0_attn_key_proj'])
    qstr = np.zeros((D, 512), dtype=np.float32)
    kstr = np.zeros((D, 512), dtype=np.float32)
    for h in range(16):
        qstr[:, 32 * h:32 * h + 16] = qp[:, h, :].T
        kstr[:, 32 * h:32 * h + 16] = kp[h, :, :].T
    w['l0_qstr'] = qstr
    w['l0_kstr'] = kstr
    w['l0_vT'] = np.ascontiguousarray(np.asarray(I['L0_attn_value_proj']).reshape(1024, D).T)
    w['l0_opT'] = np.ascontiguousarray(np.asarray(I['L0_attn_output_proj']).reshape(D, 1024).T)
    w['l0_rtr'] = np.asarray(I['L0_ffn_router_w'])
    w['l0_rtb'] = np.asarray(I['L0_ffn_router_b']).reshape(8, 1)
    w['l0_W1'] = np.asarray(I['L0_ffn_W1'])
    w['l0_b1c'] = np.ascontiguousarray(
        np.asarray(I['L0_ffn_b1']).reshape(8, 4, 128).transpose(2, 0, 1).reshape(128, 32))
    w['l0_W2'] = np.asarray(I['L0_ffn_W2'])
    w['l0_b2r'] = np.asarray(I['L0_ffn_b2'])
    Wq = np.asarray(I['L1_attn_Wq'])
    Wk = np.asarray(I['L1_attn_Wk'])
    Wv = np.asarray(I['L1_attn_Wv'])
    rep = np.repeat(np.arange(512), 2)
    permq = np.arange(1024).reshape(-1, 2)[:, ::-1].reshape(-1)
    perm4 = (np.arange(1024).reshape(-1, 4)[:, [2, 3, 0, 1]]).reshape(-1)
    w['l1a_wq'] = Wq
    w['l1a_wqp'] = np.ascontiguousarray(Wq[:, permq])
    Wkr = Wk[:, rep]
    w['l1a_wkr'] = np.ascontiguousarray(Wkr)
    w['l1a_wkrp'] = np.ascontiguousarray(Wkr[:, perm4])
    w['l1a_wvr'] = np.ascontiguousarray(Wv[:, rep])
    w['l1a_wo'] = np.asarray(I['L1_attn_Wo'])
    qkvT = np.ascontiguousarray(np.asarray(I['L1_ffn_qkv_w']).T)
    w['l1f_qkvT'] = qkvT
    permrh = (np.arange(2048).reshape(-1, 64)[:, list(range(32, 64)) + list(range(32))]).reshape(-1)
    w['l1f_qkvTp'] = np.ascontiguousarray(qkvT[:, :2048][:, permrh])
    qkvb = np.asarray(I['L1_ffn_qkv_b'])
    w['l1f_qkvbc'] = np.ascontiguousarray(qkvb.reshape(24, 128).T)
    w['l1f_qkvbcp'] = np.ascontiguousarray(qkvb[:2048][permrh].reshape(16, 128).T)
    w['l1f_outwT'] = np.ascontiguousarray(np.asarray(I['L1_ffn_out_w']).T)
    w['l1f_outbc'] = np.ascontiguousarray(np.asarray(I['L1_ffn_out_b']).reshape(8, 128).T)
    w['l1f_gateT'] = np.ascontiguousarray(np.asarray(I['L1_ffn_gate_w']).T)
    w['l1f_w1T'] = np.ascontiguousarray(np.asarray(I['L1_ffn_w1']).transpose(0, 2, 1))
    w['l1f_w2T'] = np.ascontiguousarray(np.asarray(I['L1_ffn_w2']).transpose(0, 2, 1))
    w['l1f_w3T'] = np.ascontiguousarray(np.asarray(I['L1_ffn_w3']).transpose(0, 2, 1))
    w['l1f_b1c'] = np.ascontiguousarray(
        np.asarray(I['L1_ffn_b1']).reshape(8, 16, 128).transpose(2, 0, 1).reshape(128, 128))
    w['l1f_b2r'] = np.ascontiguousarray(np.asarray(I['L1_ffn_b2']).reshape(1, 8 * 2048))
    w['l1f_b3r'] = np.asarray(I['L1_ffn_b3'])
    w['l1f_vb'] = np.ascontiguousarray(qkvb[2048:3072].reshape(1, 1024))
    w['l1f_n1c'] = np.ascontiguousarray(np.asarray(I['L1_ffn_norm1_w']).reshape(8, 128).T)
    w['l1f_n2c'] = np.ascontiguousarray(np.asarray(I['L1_ffn_norm2_w']).reshape(8, 128).T)
    return w


_WSHAPES = {
    'l0_qstr': (D, 512), 'l0_kstr': (D, 512), 'l0_vT': (D, 1024), 'l0_opT': (1024, D),
    'l0_rtr': (D, 8), 'l0_rtb': (8, 1), 'l0_W1': (8, D, 512), 'l0_b1c': (128, 32),
    'l0_W2': (8, 512, D), 'l0_b2r': (8, D),
    'l1a_wq': (D, D), 'l1a_wqp': (D, D), 'l1a_wkr': (D, D), 'l1a_wkrp': (D, D),
    'l1a_wvr': (D, D), 'l1a_wo': (D, D),
    'l1f_qkvT': (D, 3072), 'l1f_qkvTp': (D, 2048), 'l1f_qkvbc': (128, 24),
    'l1f_qkvbcp': (128, 16), 'l1f_outwT': (D, D), 'l1f_outbc': (128, 8),
    'l1f_gateT': (D, 8), 'l1f_w1T': (8, D, 2048), 'l1f_b1c': (128, 128),
    'l1f_w2T': (8, D, 2048), 'l1f_b2r': (1, 8 * 2048), 'l1f_w3T': (8, 2048, D),
    'l1f_b3r': (8, D), 'l1f_n1c': (128, 8), 'l1f_n2c': (128, 8), 'l1f_vb': (1, 1024),
}


class Ctx:
    pass


def build_program(tap=None):
    nc = bass.Bass('TRN2', target_bir_lowering=False)
    g = Ctx()
    g.nc = nc
    g.tap = tap
    g.x_fm = nc.dram_tensor('x_fm', [D, T], F32, kind='ExternalInput')
    CST = _host_consts()
    g.cst = {k: nc.dram_tensor('c_' + k, list(v.shape), BF16 if v.dtype == BF else F32,
                               kind='ExternalInput') for k, v in CST.items()}
    g.wt = {k: nc.dram_tensor('w_' + k, list(sh), F32, kind='ExternalInput')
            for k, sh in _WSHAPES.items()}
    g.fvals = []
    g.fv_dram = nc.dram_tensor('c_fvals', [128, 64], F32, kind='ExternalInput')
    g.hx1 = nc.dram_tensor('hx1', [D, T], F32)
    g.hx2 = nc.dram_tensor('hx2', [D, T], F32)
    g.hs = nc.dram_tensor('hs', [D, T], BF16)
    g.ho = nc.dram_tensor('ho', [D, T], BF16)
    g.ha = nc.dram_tensor('ha', [D, T], BF16)
    g.out = nc.dram_tensor('out', [D, T], F32, kind='ExternalOutput')
    g.dbg = nc.dram_tensor('dbg', [D, T], F32, kind='ExternalOutput')

    import contextlib
    with contextlib.ExitStack() as st:
        g.xn = st.enter_context(nc.sbuf_tensor('xn', [128, 8 * T], BF16))
        g.xm = st.enter_context(nc.sbuf_tensor('xm', [128, 8 * T], BF16))
        g.wb = st.enter_context(nc.sbuf_tensor('wb', [128, 52 * 1024 - 128], BF16))
        g.fb = st.enter_context(nc.sbuf_tensor('fb', [128, 8 * 1024], F32))
        g.mc_ = st.enter_context(nc.sbuf_tensor('mc_', [128, 64], F32))
        g.mc2 = st.enter_context(nc.sbuf_tensor('mc2', [128, 192], F32))
        g.ic = st.enter_context(nc.sbuf_tensor('ic', [128, 512], mybir.dt.int32))
        g.fv = st.enter_context(nc.sbuf_tensor('fv', [128, 64], F32))

        def fc(v):
            v = float(np.float32(v))
            if v not in g.fvals:
                assert len(g.fvals) < 64
                g.fvals.append(v)
                nc.const_aps.register(F32, v, g.fv[:, len(g.fvals) - 1:len(g.fvals)])
            return v
        g.fc = fc
        g.cs = {k: st.enter_context(nc.sbuf_tensor('s_' + k, list(v.shape),
                                                   BF16 if v.dtype == BF else F32))
                for k, v in CST.items() if k != 'pos_row'}
        g.psA = st.enter_context(nc.psum_tensor('psA', [128, 2048], F32))
        g.psB = st.enter_context(nc.psum_tensor('psB', [128, 2048], F32))
        g.ps = [g.psA[:, 512 * i:512 * (i + 1)] for i in range(4)] + \
               [g.psB[:, 512 * i:512 * (i + 1)] for i in range(4)]
        g.sem = {k: st.enter_context(nc.semaphore('sm' + k))
                 for k in ('W', 'X', 'T', 'S', 'V', 'O', 'Wa', 'Wb', 'Wc', 'Wd', 'D')}
        g.cnt = {k: 0 for k in g.sem}
        g.q = {'t': [], 's': [], 'v': [], 'g': [], 'y': []}
        g.done = False

        emit_all(g)
        kernel._fvals = list(g.fvals)

        with nc.Block() as blk:
            @blk.tensor
            def _(e):
                for fn in g.q['t']:
                    fn(e)

            @blk.scalar
            def _(e):
                for fn in g.q['s']:
                    fn(e)

            @blk.vector
            def _(e):
                for fn in g.q['v']:
                    fn(e)

            @blk.gpsimd
            def _(e):
                for fn in g.q['g']:
                    fn(e)

            @blk.sync
            def _(e):
                for fn in g.q['y']:
                    fn(e)
    return nc


def _mk(g):
    sem, cnt, q = g.sem, g.cnt, g.q

    def w8(eng, s, val):
        q[eng].append(lambda e, S=sem[s], V=val: e.wait_ge(S, V))

    def _run(eng, fn, s=None, n=1):
        if s is None:
            q[eng].append(fn)
            return None
        cnt[s] += n
        v = cnt[s]
        q[eng].append(lambda e, F=fn, S=sem[s], N=n: F(e).then_inc(S, N))
        if eng in ('t', 's', 'v'):
            q[eng].append(lambda e, S=sem[s], V=v: e.wait_ge(S, V))
        return v

    g.dmawait = {}

    def _dma(eng, dst, src, s):
        v = _run(eng, lambda e, d=dst, r=src: e.dma_start(out=d, in_=r), s, 16)
        key = (eng, s)
        if v - g.dmawait.get(key, 0) >= 7 * 16:
            q[eng].append(lambda e, S=sem[s], V=v: e.wait_ge(S, V))
            g.dmawait[key] = v
        return v

    def wgroup(items, s):
        if cnt[s] > 0:
            q['g'].append(lambda e, S=sem[s], V=cnt[s]: e.wait_ge(S, V))
            g.dmawait[('g', s)] = cnt[s]
        v = None
        for dst, srcap in items:
            v = _dma('g', dst, srcap, s)
        return v

    def ydma(dst, src, s='X'):
        return _dma('y', dst, src, s)

    def gdma(dst, src, s='W'):
        return _dma('g', dst, src, s)

    def mm(o, l, r, start, stop, tp=None, sgc=False, s=None):
        return _run('t', lambda e, o=o, l=l, r=r, a=start, b=stop, tp=tp, k=sgc:
                    e.matmul(o, l, r, start=a, stop=b, tile_position=tp,
                             skip_group_check=k), s)

    def tr(o, i, ident, s=None):
        return _run('t', lambda e, o=o, i=i, d=ident: e.transpose(o, i, d), s)

    def act(o, i, f, bias=0.0, scale=1.0, accum=None, s='S'):
        return _run('s', lambda e, o=o, i=i, f=f, b=bias, sc=scale, ac=accum:
                    e.activation(o, i, f, bias=b, scale=sc, accum_out=ac), s)

    def tt(o, a, b, op, s='V'):
        return _run('v', lambda e, o=o, a=a, b=b, op=op: e.tensor_tensor(o, a, b, op), s)

    def ts(o, a, s1, op0, s2=None, op1=None, accum=None, s='V'):
        if op1 is None:
            return _run('v', lambda e, o=o, a=a, s1=s1, ac=accum, op0=op0:
                        e.tensor_scalar(o, a, s1, None, op0, accum_out=ac), s)
        return _run('v', lambda e, o=o, a=a, s1=s1, s2=s2, op0=op0, op1=op1, ac=accum:
                    e.tensor_scalar(o, a, s1, s2, op0, op1, accum_out=ac), s)

    def vcp(o, i, s='V'):
        return _run('v', lambda e, o=o, i=i: e.tensor_copy(o, i), s)

    def vred(o, i, op, negate=None, s='V'):
        return _run('v', lambda e, o=o, i=i, op=op, n=negate:
                    e.tensor_reduce(o, i, AX.X, op, negate=n), s)

    def vrec(o, i, s='V'):
        return _run('v', lambda e, o=o, i=i: e.reciprocal(o, i), s)

    def vms(ap, c, s='V'):
        return _run('v', lambda e, a=ap, c=c: e.memset(a, c), s)

    def bar():
        tm = cnt['T']
        sv = act(g.mc_[:, 62:63], g.mc_[:, 62:63], AF.Copy, s='S')
        w8('v', 'S', sv)
        vv = vcp(g.mc_[:, 63:64], g.mc_[:, 62:63], s='V')
        w8('t', 'S', sv)
        w8('t', 'V', vv)
        w8('s', 'V', vv)
        w8('s', 'T', tm)
        w8('v', 'T', tm)
        for eng in ('g', 'y'):
            w8(eng, 'T', tm)
            w8(eng, 'S', sv)
            w8(eng, 'V', vv)
        return {'T': tm, 'S': sv, 'V': vv}

    g.w8, g.run = w8, _run
    g.wgroup = wgroup
    g.ydma, g.gdma, g.mm, g.tr = ydma, gdma, mm, tr
    g.act, g.tt, g.ts, g.vcp, g.vred, g.vrec, g.vms = act, tt, ts, vcp, vred, vrec, vms
    g.bar = bar


def _tap_out(g, tiles, rows=128):
    for s in ('O', 'X', 'T', 'S', 'V', 'W', 'Wa', 'Wb', 'Wc', 'Wd'):
        g.w8('g', s, g.cnt[s])
    for dst, srcap in tiles:
        g.gdma(dst, srcap, s='D')
    g.w8('g', 'D', g.cnt['D'])
    g.done = True


FPOS, FA, FB_, FC = 0, 2048, 4096, 6144
MLN, MTHQ, MTHK, MTHF, MSC = 0, 1, 9, 17, 18


def emit_all(g):
    _mk(g)
    nc, cs, fb, wb, xn, ps = g.nc, g.cs, g.fb, g.wb, g.xn, g.ps
    mm, act, tt, ts, vcp = g.mm, g.act, g.tt, g.ts, g.vcp
    ydma, w8, bar = g.ydma, g.w8, g.bar
    mc_ = g.mc_

    for k in cs:
        ydma(cs[k][:, :], g.cst[k][:, :])
    ydma(fb[0:1, FA:FA + T], g.cst['pos_row'][0:1, :])
    ydma(g.fv[:, :], g.fv_dram[:, :])
    w8('t', 'X', g.cnt['X'])
    w8('s', 'X', g.cnt['X'])
    w8('v', 'X', g.cnt['X'])

    act(mc_[:, 62:63], cs['tenk'][:, :], AF.Copy, s='S')

    for nt in range(4):
        mm(ps[nt][:, :], cs['ones32'][0:1, :], fb[0:1, FA + 512 * nt:FA + 512 * (nt + 1)],
           True, True, s='T')
    w8('s', 'T', g.cnt['T'])
    for nt in range(4):
        act(fb[:, FPOS + 512 * nt:FPOS + 512 * (nt + 1)], ps[nt][:, :], AF.Copy, s='S')
    act(mc_[:, MLN:MLN + 1], cs['tenk'][:, :], AF.Ln, s='S')
    for c in range(8):
        act(mc_[:, MTHQ + c:MTHQ + c + 1], cs['exq'][:, c:c + 1], AF.Exp,
            scale=mc_[:, MLN:MLN + 1], s='S')
        act(mc_[:, MTHK + c:MTHK + c + 1], cs['exk'][:, c:c + 1], AF.Exp,
            scale=mc_[:, MLN:MLN + 1], s='S')
    act(mc_[:, MTHF:MTHF + 1], cs['exf'][:, :], AF.Exp, scale=mc_[:, MLN:MLN + 1], s='S')
    w8('v', 'S', g.cnt['S'])
    for cols, exc in ((slice(MTHQ, MTHQ + 8), 'exq'), (slice(MTHK, MTHK + 8), 'exk'),
                      (slice(MTHF, MTHF + 1), 'exf')):
        n = cols.stop - cols.start
        ts(mc_[:, 40:40 + n], cs[exc][:, :], mc_[:, MLN:MLN + 1], AL.mult, s='V')
        w8('s', 'V', g.cnt['V'])
        act(mc_[:, 48:48 + n], mc_[:, cols], AF.Ln, s='S')
        w8('v', 'S', g.cnt['S'])
        tt(mc_[:, 40:40 + n], mc_[:, 40:40 + n], mc_[:, 48:48 + n], AL.subtract, s='V')
        ts(mc_[:, 40:40 + n], mc_[:, 40:40 + n], 1.0, AL.add, s='V')
        tt(mc_[:, cols], mc_[:, cols], mc_[:, 40:40 + n], AL.mult, s='V')
    ts(mc_[:, MTHQ:MTHQ + 8], mc_[:, MTHQ:MTHQ + 8], cs['sgq'][:, :], AL.mult, s='V')
    ts(mc_[:, MTHK:MTHK + 8], mc_[:, MTHK:MTHK + 8], cs['sgk4'][:, :], AL.mult, s='V')
    ts(mc_[:, MTHF:MTHF + 1], mc_[:, MTHF:MTHF + 1], cs['sgrh'][:, :], AL.mult, s='V')
    ts(mc_[:, 18:19], mc_[:, MTHF:MTHF + 1], -1.0, AL.mult, s='V')
    ts(mc_[:, 19:20], mc_[:, 18:19], 1.0, AL.mult, PI / 2.0, AL.add, s='V')
    bar()

    if g.tap == 'pos':
        _tap_out(g, [(g.dbg[0:128, 0:T], fb[:, FPOS:FPOS + T]),
                     (g.dbg[128:256, 0:32], mc_[:, 0:32])])
        return

    emit_norm(g, g.x_fm, 'ln', dst=g.xn)
    if g.tap == 'xn0':
        _tap_out(g, [(g.dbg[128 * c:128 * (c + 1), :], xn[:, T * c:T * (c + 1)])
                     for c in range(8)])
        return

    if emit_l0_attn(g):
        return
    if g.tap == 'l0s0':
        _tap_out(g, [(g.dbg[:, :], g.hs[:, :])])
        return

    if emit_l0_ffn_tail(g):
        return
    if g.tap == 'x1':
        _tap_out(g, [(g.dbg[:, :], g.hx1[:, :])])
        return

    emit_norm(g, g.hx1, 'ln', dst=g.xn)
    emit_pos_bcast(g)
    if g.tap == 'xn1':
        _tap_out(g, [(g.dbg[128 * c:128 * (c + 1), :], xn[:, T * c:T * (c + 1)])
                     for c in range(8)])
        return

    P = dict(src=(g.xn, 0), wv_moving=g.wt['l1a_wvr'],
             wq=lambda oc: g.wt['l1a_wq'][:, 128 * oc:128 * (oc + 1)],
             wqp=lambda oc: g.wt['l1a_wqp'][:, 128 * oc:128 * (oc + 1)],
             wk=lambda oc: g.wt['l1a_wkr'][:, 128 * oc:128 * (oc + 1)],
             wkp=lambda oc: g.wt['l1a_wkrp'][:, 128 * oc:128 * (oc + 1)],
             qb=None, qpb=None, kb=None, kpb=None, vb_row=None,
             theta_q=lambda oc: g.mc_[:, MTHQ + oc:MTHQ + oc + 1],
             theta_k=lambda oc: g.mc_[:, MTHK + oc:MTHK + oc + 1],
             pos_s2_sin=0.0, pos_s2_cos=PI / 2.0,
             shared_tables=False, scale=0.5, tau=8.0, dsh=40.0,
             wo=g.wt['l1a_wo'], ob=None, dst=g.hs,
             tap_qk='l1aqk', tap_o='l1ao', tap_tbl='l1atbl')
    if emit_l1_attention(g, P):
        return
    if g.tap == 'l1as':
        _tap_out(g, [(g.dbg[:, :], g.hs[:, :])])
        return

    lv = g.ydma(g.mc2[:, 0:24], g.wt['l1f_qkvbc'][:, :])
    lv = g.ydma(g.mc2[:, 24:40], g.wt['l1f_qkvbcp'][:, :])
    lv = g.ydma(g.mc2[:, 40:48], g.wt['l1f_outbc'][:, :])
    lv = g.ydma(g.mc2[:, 48:56], g.wt['l1f_n1c'][:, :])
    lv = g.ydma(g.mc2[:, 56:64], g.wt['l1f_n2c'][:, :])
    g.w8('s', 'X', lv)
    g.w8('v', 'X', lv)
    emit_norm(g, (g.xn, 0), 'rms', wcol=g.mc2[:, 48:56], dst=g.xm)
    if g.tap == 'xn1f':
        _tap_out(g, [(g.dbg[128 * c:128 * (c + 1), :], g.xm[:, T * c:T * (c + 1)])
                     for c in range(8)])
        return
    P = dict(src=(g.xm, 0), wv_moving=g.wt['l1f_qkvT'][:, 2048:3072],
             wq=lambda oc: g.wt['l1f_qkvT'][:, 128 * oc:128 * (oc + 1)],
             wqp=lambda oc: g.wt['l1f_qkvTp'][:, 128 * oc:128 * (oc + 1)],
             wk=lambda oc: g.wt['l1f_qkvT'][:, 1024 + 128 * oc:1024 + 128 * (oc + 1)],
             wkp=lambda oc: g.wt['l1f_qkvTp'][:, 1024 + 128 * oc:1024 + 128 * (oc + 1)],
             qb=lambda oc: g.mc2[:, oc:oc + 1],
             qpb=lambda oc: g.mc2[:, 24 + oc:25 + oc],
             kb=lambda oc: g.mc2[:, 8 + oc:9 + oc],
             kpb=lambda oc: g.mc2[:, 32 + oc:33 + oc],
             vb_row=g.wt['l1f_vb'][0:1, :],
             theta_q=lambda oc: g.mc_[:, MTHF:MTHF + 1],
             theta_k=lambda oc: g.mc_[:, MTHF:MTHF + 1],
             pos_s2_sin=g.mc_[:, 18:19], pos_s2_cos=g.mc_[:, 19:20],
             shared_tables=True, scale=0.125, tau=8.0, dsh=40.0,
             wo=g.wt['l1f_outwT'], ob=lambda dc: g.mc2[:, 40 + dc:41 + dc], dst=g.ha,
             tap_qk='l1fqk', tap_o='l1fo', tap_tbl='l1ftbl')
    if emit_l1_attention(g, P):
        return
    emit_l1f_post(g)
    if g.tap == 'xn2':
        _tap_out(g, [(g.dbg[128 * c:128 * (c + 1), :], g.xm[:, T * c:T * (c + 1)])
                     for c in range(8)])
        return

    emit_l1_moe_tail(g)
    if g.tap is None:
        g.w8('y', 'O', g.cnt['O'])


def emit_norm(g, xsrc, kind, wcol=None, dst=None, dst_off=0, stat_src_bf=None):
    cs, fb, wb, ps = g.cs, g.fb, g.wb, g.ps
    mm, act, tt, ts, vcp, vrec = g.mm, g.act, g.tt, g.ts, g.vcp, g.vrec
    ydma, w8, bar = g.ydma, g.w8, g.bar
    from_dram = not isinstance(xsrc, tuple)
    eps = 1e-5 if kind == 'ln' else 1e-6
    ln = kind == 'ln'

    g.w8('y', 'O', g.cnt['O'])
    tmarks = []
    vmark_cast = None
    for c in range(8):
        pp = c % 2
        if from_dram:
            xf = fb[:, FA:FA + T]
            if c > 0:
                w8('y', 'V', vmark_cast)
            lv = ydma(xf, xsrc[128 * c:128 * (c + 1), :])
            w8('v', 'X', lv)
            xb = wb[:, 2048 * pp:2048 * pp + T]
            if c >= 2:
                w8('v', 'T', tmarks[c - 2])
            vmark_cast = vcp(xb, xf, s='V')
        else:
            xb = xsrc[0][:, xsrc[1] + T * c:xsrc[1] + T * (c + 1)]
        sq = wb[:, 4096 + 2048 * pp:4096 + 2048 * pp + T]
        if c >= 2:
            w8('v', 'T', tmarks[c - 2])
        sv = tt(sq, xb, xb, AL.mult, s='V')
        w8('t', 'V', sv)
        for nt in range(4):
            sl = slice(512 * nt, 512 * (nt + 1))
            if ln:
                mm(ps[4 + nt][0:1, :], cs['ones_bf'][:, 0:1], xb[:, sl], c == 0, c == 7)
            mm(ps[nt][0:1, :], cs['ones_bf'][:, 0:1], sq[:, sl], c == 0, c == 7,
               s='T' if nt == 3 else None)
        tmarks.append(g.cnt['T'])
    w8('v', 'T', tmarks[-1])
    w8('s', 'T', tmarks[-1])

    arow = fb[0:1, FA:FA + T]
    brow = fb[0:1, FB_:FB_ + T]
    crow = fb[0:1, FC:FC + T]
    for nt in range(4):
        sl = slice(512 * nt, 512 * (nt + 1))
        ts(arow[:, sl], ps[nt][0:1, :], 1.0 / D, AL.mult)
        if ln:
            ts(crow[:, sl], ps[4 + nt][0:1, :], 1.0 / D, AL.mult)
    if ln:
        tt(brow, crow, crow, AL.mult)
        sv = tt(brow, arow, brow, AL.subtract, s='V')
    else:
        sv = vcp(brow, arow, s='V')
    w8('s', 'V', sv)
    act(brow, brow, AF.Sqrt, bias=g.fc(eps), s='S')
    w8('v', 'S', g.cnt['S'])
    vrec(brow, brow)
    if ln:
        tt(crow, crow, brow, AL.mult)
        sv = ts(crow, crow, -1.0, AL.mult, s='V')
    else:
        sv = vcp(g.mc_[0:1, 61:62], brow[0:1, 0:1], s='V')
    w8('t', 'V', sv)

    for nt in range(4):
        sl = slice(512 * nt, 512 * (nt + 1))
        tv = mm(ps[nt][:, :], cs['ones32'][0:1, :], brow[:, sl], True, True, s='T')
        w8('s', 'T', tv)
        act(fb[:, FB_ + 512 * nt:FB_ + 512 * (nt + 1)], ps[nt][:, :], AF.Copy, s='S')
        if ln:
            tv = mm(ps[4 + nt][:, :], cs['ones32'][0:1, :], crow[:, sl], True, True, s='T')
            w8('s', 'T', tv)
            act(fb[:, FC + 512 * nt:FC + 512 * (nt + 1)], ps[4 + nt][:, :], AF.Copy, s='S')
    w8('v', 'S', g.cnt['S'])

    vbar = g.cnt['V']
    for c in range(8):
        xo = dst[:, dst_off + T * c:dst_off + T * (c + 1)]
        if from_dram:
            xf = fb[:, FA:FA + T]
            w8('y', 'V', vbar if c == 0 else vmark_cast)
            lv = ydma(xf, xsrc[128 * c:128 * (c + 1), :])
            w8('v', 'X', lv)
            vmark_cast = tt(xo, xf, fb[:, FB_:FB_ + T], AL.mult, s='V')
        else:
            xi = xsrc[0][:, xsrc[1] + T * c:xsrc[1] + T * (c + 1)]
            tt(xo, xi, fb[:, FB_:FB_ + T], AL.mult)
        if ln:
            tt(xo, xo, fb[:, FC:FC + T], AL.add, s='V')
        else:
            ts(xo, xo, wcol[:, c:c + 1], AL.mult, s='V')
    bar()


def emit_l0_attn(g):
    cs, fb, wb, xn, xm, ps = g.cs, g.fb, g.wb, g.xn, g.xm, g.ps
    psA, psB = g.psA, g.psB
    mm, act, tt, ts, vcp, vred, vrec, tr = g.mm, g.act, g.tt, g.ts, g.vcp, g.vred, g.vrec, g.tr
    ydma, gdma, w8, bar = g.ydma, g.gdma, g.w8, g.bar
    mc_ = g.mc_
    QS, KS, VP, WA, WB2, WC, SQ, S0 = 0, 8192, 16384, 32768, 36864, 40960, 49152, 51200
    SALL, NEGB, BIAS2 = 6144, 6400, 6656

    wq_t = g.wgroup([(wb[:, WA + 512 * kc:WA + 512 * (kc + 1)],
                      g.wt['l0_qstr'][128 * kc:128 * (kc + 1), :]) for kc in range(8)], 'Wa')
    wk_t = g.wgroup([(wb[:, WB2 + 512 * kc:WB2 + 512 * (kc + 1)],
                      g.wt['l0_kstr'][128 * kc:128 * (kc + 1), :]) for kc in range(8)], 'Wb')
    wv_t = g.wgroup([(wb[:, WC + 1024 * kc:WC + 1024 * (kc + 1)],
                      g.wt['l0_vT'][128 * kc:128 * (kc + 1), :]) for kc in range(8)], 'Wc')

    for side, Woff, Aoff in (('q', WA, QS), ('k', WB2, KS)):
        w8('t', 'Wa' if side == 'q' else 'Wb', wq_t if side == 'q' else wk_t)
        for c4 in range(4):
            for nt in range(4):
                bank = ps[(c4 * 4 + nt) % 2]
                if not (side == 'q' and c4 == 0 and nt < 2):
                    w8('t', 'V', g.cnt['V'])
                for kc in range(8):
                    mm(bank, wb[:, Woff + 512 * kc + 128 * c4:Woff + 512 * kc + 128 * (c4 + 1)],
                       xn[:, T * kc + 512 * nt:T * kc + 512 * (nt + 1)],
                       kc == 0, kc == 7, s='T' if kc == 7 else None)
                w8('v', 'T', g.cnt['T'])
                vcp(wb[:, Aoff + 2048 * c4 + 512 * nt:Aoff + 2048 * c4 + 512 * (nt + 1)], bank)
    w8('t', 'Wc', wv_t)
    for mc in range(16):
        for vh in range(2):
            bank = ps[2 + (mc * 2 + vh) % 2]
            w8('t', 'V', g.cnt['V'])
            for kc in range(8):
                mm(bank, xn[:, T * kc + 128 * mc:T * kc + 128 * (mc + 1)],
                   wb[:, WC + 1024 * kc + 512 * vh:WC + 1024 * kc + 512 * (vh + 1)],
                   kc == 0, kc == 7, s='T' if kc == 7 else None)
            w8('v', 'T', g.cnt['T'])
            vcp(wb[:, VP + 1024 * mc + 512 * vh:VP + 1024 * mc + 512 * (vh + 1)], bank)
    vdone = g.cnt['V']
    w8('g', 'V', vdone)
    wo_t = g.wgroup([(wb[:, WC + 1024 * hc:WC + 1024 * (hc + 1)],
                      g.wt['l0_opT'][128 * hc:128 * (hc + 1), :]) for hc in range(8)], 'Wd')

    for c4 in range(4):
        sq = wb[:, SQ:SQ + T]
        qs = wb[:, QS + 2048 * c4:QS + 2048 * (c4 + 1)]
        if c4 > 0:
            w8('v', 'T', g.cnt['T'])
        tt(sq, qs, qs, AL.mult, s='V')
        w8('t', 'V', g.cnt['V'])
        for nt in range(4):
            if c4 > 0 and nt == 0:
                w8('t', 'V', g.cnt['V'])
            mm(ps[4 + nt], cs['bsumL0'][:, :], sq[:, 512 * nt:512 * (nt + 1)],
               True, True, s='T' if nt == 3 else None)
        w8('v', 'T', g.cnt['T'])
        for nt in range(4):
            vred(mc_[:, 30 + nt:31 + nt], ps[4 + nt], AL.max)
        tt(mc_[:, 30:31], mc_[:, 30:31], mc_[:, 31:32], AL.max)
        tt(mc_[:, 32:33], mc_[:, 32:33], mc_[:, 33:34], AL.max)
        tt(mc_[:, 20 + c4:21 + c4], mc_[:, 30:31], mc_[:, 32:33], AL.max)
        ts(mc_[:, 20 + c4:21 + c4], mc_[:, 20 + c4:21 + c4], 0.0625, AL.mult, s='V')
    for c4 in range(4):
        sq = wb[:, SQ:SQ + T]
        ks = wb[:, KS + 2048 * c4:KS + 2048 * (c4 + 1)]
        w8('v', 'T', g.cnt['T'])
        tt(sq, ks, ks, AL.mult, s='V')
        w8('t', 'V', g.cnt['V'])
        for nt in range(4):
            if nt == 0:
                w8('t', 'S', g.cnt['S'])
            mm(ps[4 + nt], cs['bsumL0'][:, :], sq[:, 512 * nt:512 * (nt + 1)],
               True, True, s='T' if nt == 3 else None)
        w8('s', 'T', g.cnt['T'])
        for nt in range(4):
            act(fb[:, FB_ + 512 * nt:FB_ + 512 * (nt + 1)], ps[4 + nt], AF.Sqrt,
                scale=mc_[:, 20 + c4:21 + c4], s='S')
        w8('t', 'S', g.cnt['S'])
        for mc in range(16):
            tv = tr(ps[mc % 2][:, 0:128], fb[:, FB_ + 128 * mc:FB_ + 128 * (mc + 1)],
                    cs['ident'][:, :], s='T')
            w8('s', 'T', tv)
            for j in range(4):
                act(fb[:, NEGB + 16 * mc + 4 * c4 + j:NEGB + 16 * mc + 4 * c4 + j + 1],
                    ps[mc % 2][:, 32 * j:32 * j + 1], AF.Copy,
                    scale=g.fc(-1.0), bias=g.fc(25.0), s='S')
            w8('t', 'S', g.cnt['S'])
    bar()
    if g.tap == 'l0bnd':
        _tap_out(g, [(g.dbg[0:128, 0:256], fb[:, NEGB:NEGB + 256]),
                     (g.dbg[128:256, 0:1024], wb[:, VP:VP + 1024]),
                     (g.dbg[384:512, 0:2048], wb[:, QS:QS + 2048]),
                     (g.dbg[512:640, 0:2048], wb[:, KS:KS + 2048])])
        return True

    sprev = {}
    vlast = None
    for mc in range(16):
        for q4 in range(4):
            bs = 0 if q4 % 2 == 0 else 4
            c4 = q4
            for nt in range(4):
                for j in range(4):
                    key = bs + j
                    if key in sprev:
                        w8('t', 'S', sprev[key])
                    mm(ps[key],
                       wb[32 * j:32 * j + 16,
                          KS + 2048 * c4 + 128 * mc:KS + 2048 * c4 + 128 * (mc + 1)],
                       wb[32 * j:32 * j + 16,
                          QS + 2048 * c4 + 512 * nt:QS + 2048 * c4 + 512 * (nt + 1)],
                       True, True, tp=(32 * j, 0), s='T' if j == 3 else None)
                w8('s', 'T', g.cnt['T'])
                if nt == 0 and vlast is not None:
                    w8('s', 'V', vlast)
                for j in range(4):
                    h = 4 * q4 + j
                    sprev[bs + j] = act(
                        xm[:, 512 * j:512 * (j + 1)], ps[bs + j], AF.Exp,
                        bias=fb[:, NEGB + 16 * mc + h:NEGB + 16 * mc + h + 1],
                        scale=g.fc(0.25),
                        accum=mc_[:, 36 + 4 * j + nt:37 + 4 * j + nt], s='S')
            w8('v', 'S', g.cnt['S'])
            for j in range(4):
                h = 4 * q4 + j
                vred(fb[:, SALL + 16 * mc + h:SALL + 16 * mc + h + 1],
                     mc_[:, 36 + 4 * j:36 + 4 * j + 4], AL.add, s='V')
            vlast = g.cnt['V']
    w8('s', 'V', g.cnt['V'])
    act(fb[:, BIAS2:BIAS2 + 256], fb[:, SALL:SALL + 256], AF.Ln, s='S')
    w8('v', 'S', g.cnt['S'])
    tt(fb[:, BIAS2:BIAS2 + 256], fb[:, NEGB:NEGB + 256], fb[:, BIAS2:BIAS2 + 256],
       AL.subtract, s='V')
    bar()
    if g.tap == 'l0S':
        _tap_out(g, [(g.dbg[0:128, 0:256], fb[:, BIAS2:BIAS2 + 256]),
                     (g.dbg[128:256, 0:1024], wb[:, VP:VP + 1024]),
                     (g.dbg[256:384, 0:256], fb[:, SALL:SALL + 256])])
        return True

    sprevB = {}
    for q4 in range(4):
        c4 = q4
        for nt in range(4):
            evq = {}
            for mc in range(17):
                if mc < 16:
                    pp2 = mc % 2
                    for j in range(4):
                        if j in sprevB:
                            w8('t', 'S', sprevB[j])
                        mm(ps[4 + j],
                           wb[32 * j:32 * j + 16,
                              KS + 2048 * c4 + 128 * mc:KS + 2048 * c4 + 128 * (mc + 1)],
                           wb[32 * j:32 * j + 16,
                              QS + 2048 * c4 + 512 * nt:QS + 2048 * c4 + 512 * (nt + 1)],
                           True, True, tp=(32 * j, 0), s='T' if j == 3 else None)
                    w8('s', 'T', g.cnt['T'])
                    for j in range(4):
                        h = 4 * q4 + j
                        ev = act(xm[:, 2048 * pp2 + 512 * j:2048 * pp2 + 512 * (j + 1)],
                                 ps[4 + j], AF.Exp,
                                 bias=fb[:, BIAS2 + 16 * mc + h:BIAS2 + 16 * mc + h + 1],
                                 scale=g.fc(0.25), s='S')
                        sprevB[j] = ev
                        evq[(mc, j)] = ev
                if mc > 0:
                    pm = mc - 1
                    pp2p = pm % 2
                    for j in range(4):
                        h = 4 * q4 + j
                        pb = 64 * (h - (h % 2))
                        w8('t', 'S', evq[(pm, j)])
                        mm(ps[j], wb[:, VP + 1024 * pm + pb:VP + 1024 * pm + pb + 128],
                           xm[:, 2048 * pp2p + 512 * j:2048 * pp2p + 512 * (j + 1)],
                           pm == 0, pm == 15, sgc=True,
                           s='T' if (pm == 15 and j == 3) else None)
            w8('s', 'T', g.cnt['T'])
            for j in range(4):
                h = 4 * q4 + j
                r0 = 64 * (h % 2)
                ev = act(wb[r0:r0 + 64, S0:S0 + 512], ps[j][r0:r0 + 64, :], AF.Copy, s='S')
                w8('y', 'S', ev)
                ov = ydma(g.ho[64 * h:64 * (h + 1), 512 * nt:512 * (nt + 1)],
                          wb[r0:r0 + 64, S0:S0 + 512], s='O')
                g.q['s'].append(lambda e, S=g.sem['O'], V=ov: e.wait_ge(S, V))
            w8('t', 'S', g.cnt['S'])
    bar()
    if g.tap == 'l0o':
        _tap_out(g, [(g.dbg[:, :], g.ho[:, :])])
        return True

    w8('g', 'O', g.cnt['O'])
    lo_t = g.wgroup([(wb[:, QS + 2048 * hc:QS + 2048 * (hc + 1)],
                      g.ho[128 * hc:128 * (hc + 1), :]) for hc in range(8)], 'Wa')
    w8('t', 'Wd', wo_t)
    w8('t', 'Wa', lo_t)
    for nt in range(4):
        for dc in range(8):
            bank = ps[(nt * 8 + dc) % 2]
            if nt + dc > 0:
                w8('t', 'S', g.cnt['S'])
            for hc in range(8):
                mm(bank, wb[:, WC + 1024 * hc + 128 * dc:WC + 1024 * hc + 128 * (dc + 1)],
                   wb[:, QS + 2048 * hc + 512 * nt:QS + 2048 * hc + 512 * (nt + 1)],
                   hc == 0, hc == 7, s='T' if hc == 7 else None)
            w8('s', 'T', g.cnt['T'])
            ev = act(wb[:, S0:S0 + 512], bank, AF.Copy, s='S')
            w8('y', 'S', ev)
            ov = ydma(g.hs[128 * dc:128 * (dc + 1), 512 * nt:512 * (nt + 1)],
                      wb[:, S0:S0 + 512], s='O')
            g.q['s'].append(lambda e, S=g.sem['O'], V=ov: e.wait_ge(S, V))
    bar()
    return False


def emit_l0_ffn_tail(g):
    cs, fb, wb, xn, xm, ps = g.cs, g.fb, g.wb, g.xn, g.xm, g.ps
    mm, act, tt, ts, vcp, vred, vrec, tr = g.mm, g.act, g.tt, g.ts, g.vcp, g.vred, g.vrec, g.tr
    ydma, gdma, w8, bar = g.ydma, g.gdma, g.w8, g.bar
    mc_ = g.mc_
    W1s = (0, 4096)
    W2s = (8192, 12288)
    H1, CB, CBC, B2, RT, SB, GG = 16384, 20480, 22528, 23552, 24576, 24704, 25728
    bv = ydma(mc_[:, 20:52], g.wt['l0_b1c'][:, :])
    bv = ydma(mc_[0:8, 52:53], g.wt['l0_rtb'][:, :])
    rtv = g.wgroup([(wb[:, RT + 8 * kc:RT + 8 * (kc + 1)],
                     g.wt['l0_rtr'][128 * kc:128 * (kc + 1), :]) for kc in range(8)], 'Wa')
    b2v = g.wgroup([(wb[0:8, B2:B2 + 1024], g.wt['l0_b2r'][:, :])], 'Wb')

    w8('t', 'Wa', rtv)
    for nt in range(4):
        bank = ps[4 + nt]
        for kc in range(8):
            mm(bank[0:8, :], wb[:, RT + 8 * kc:RT + 8 * (kc + 1)],
               xn[:, T * kc + 512 * nt:T * kc + 512 * (nt + 1)],
               kc == 0, kc == 7, s='T' if kc == 7 else None)
        w8('s', 'T', g.cnt['T'])
        w8('s', 'X', bv)
        act(fb[0:8, FA + 512 * nt:FA + 512 * (nt + 1)], bank[0:8, :], AF.Identity,
            bias=mc_[0:8, 52:53], s='S')
    LG = 6144
    for tsub in range(16):
        w8('t', 'S', g.cnt['S'])
        tv = tr(ps[0][:, 0:8], fb[0:8, FA + 128 * tsub:FA + 128 * (tsub + 1)],
                cs['ident'][0:8, 0:8], s='T')
        w8('v', 'T', tv)
        lg = fb[:, LG:LG + 8]
        vcp(lg, ps[0][:, 0:8])
        m1n = mc_[:, 53:54]
        vred(m1n, lg, AL.max, negate=True)
        ts(fb[:, LG + 8:LG + 16], lg, m1n, AL.add, s='V')
        ts(fb[:, LG + 16:LG + 24], fb[:, LG + 8:LG + 16], 0.0, AL.is_ge, s='V')
        ts(fb[:, LG + 24:LG + 32], fb[:, LG + 16:LG + 24], g.fc(-1e30), AL.mult, s='V')
        tt(fb[:, LG + 24:LG + 32], fb[:, LG + 8:LG + 16], fb[:, LG + 24:LG + 32], AL.add)
        vred(mc_[:, 54:55], fb[:, LG + 24:LG + 32], AL.max)
        w8('s', 'V', g.cnt['V'])
        act(mc_[:, 55:56], mc_[:, 54:55], AF.Exp, s='S')
        act(fb[:, LG + 32:LG + 40], fb[:, LG + 8:LG + 16], AF.Exp, s='S')
        w8('v', 'S', g.cnt['S'])
        ts(mc_[:, 55:56], mc_[:, 55:56], 1.0, AL.add, s='V')
        vrec(mc_[:, 55:56], mc_[:, 55:56])
        ts(fb[:, LG + 40:LG + 48], fb[:, LG + 8:LG + 16], mc_[:, 54:55], AL.is_ge, s='V')
        tt(fb[:, LG + 32:LG + 40], fb[:, LG + 32:LG + 40], fb[:, LG + 40:LG + 48], AL.mult)
        ts(fb[:, LG:LG + 8], fb[:, LG + 32:LG + 40], mc_[:, 55:56], AL.mult, s='V')
        w8('t', 'V', g.cnt['V'])
        tv = tr(ps[1][0:8, 0:128], fb[:, LG:LG + 8], cs['ident'][:, :], s='T')
        w8('s', 'T', tv)
        act(wb[0:8, CB + 128 * tsub:CB + 128 * (tsub + 1)], ps[1][0:8, 0:128],
            AF.Copy, s='S')
    bar()
    if g.tap == 'l0comb':
        _tap_out(g, [(g.dbg[0:8, 0:T], wb[0:8, CB:CB + T])])
        return True

    for half in range(2):
        bar()
        for eng in ('t', 's', 'v', 'g', 'y'):
            w8(eng, 'O', g.cnt['O'])
        t0 = 1024 * half
        w8('t', 'Wb', b2v)
        for dc in range(8):
            for t2 in range(2):
                bank = ps[(dc * 2 + t2) % 4]
                if dc + t2 > 0:
                    w8('t', 'S', g.cnt['S'])
                mm(bank, wb[0:8, B2 + 128 * dc:B2 + 128 * (dc + 1)],
                   wb[0:8, CB + t0 + 512 * t2:CB + t0 + 512 * (t2 + 1)],
                   True, True, s='T')
                w8('s', 'T', g.cnt['T'])
                act(fb[:, 1024 * dc + 512 * t2:1024 * dc + 512 * (t2 + 1)],
                    bank, AF.Copy, s='S')
        w8('v', 'S', g.cnt['S'])
        wsem = {}
        tm_w1, tm_w2 = {}, {}
        for e in range(8):
            sl = e % 2
            if e >= 2:
                w8('g', 'T', tm_w1[e - 2])
                w8('g', 'T', tm_w2[e - 2])
            elif half == 1:
                w8('g', 'T', g.l0_tm_last)
            wsem[('w1', e)] = g.wgroup(
                [(wb[:, W1s[sl] + 512 * kc:W1s[sl] + 512 * (kc + 1)],
                  g.wt['l0_W1'][e, 128 * kc:128 * (kc + 1), :]) for kc in range(8)],
                'Wa' if sl == 0 else 'Wb')
            wsem[('w2', e)] = g.wgroup(
                [(wb[:, W2s[sl] + 1024 * fc:W2s[sl] + 1024 * (fc + 1)],
                  g.wt['l0_W2'][e, 128 * fc:128 * (fc + 1), :]) for fc in range(4)],
                'Wc' if sl == 0 else 'Wd')
            for t2 in range(2):
                bank = ps[4 + t2]
                w8('t', 'V', g.cnt['V'])
                mm(bank, cs['sel8'][:, 128 * e:128 * (e + 1)],
                   wb[0:8, CB + t0 + 512 * t2:CB + t0 + 512 * (t2 + 1)],
                   True, True, s='T')
                w8('v', 'T', g.cnt['T'])
                vcp(wb[:, CBC + 512 * t2:CBC + 512 * (t2 + 1)], bank)
            w8('t', 'Wa' if sl == 0 else 'Wb', wsem[('w1', e)])
            for fc in range(4):
                for t2 in range(2):
                    bank = ps[4 + 2 + (fc * 2 + t2) % 2]
                    w8('t', 'S', g.cnt['S'])
                    for kc in range(8):
                        mm(bank, wb[:, W1s[sl] + 512 * kc + 128 * fc:W1s[sl] + 512 * kc + 128 * (fc + 1)],
                           xn[:, T * kc + t0 + 512 * t2:T * kc + t0 + 512 * (t2 + 1)],
                           kc == 0, kc == 7, s='T' if kc == 7 else None)
                    w8('s', 'T', g.cnt['T'])
                    act(wb[:, H1 + 1024 * fc + 512 * t2:H1 + 1024 * fc + 512 * (t2 + 1)],
                        bank, AF.Gelu, bias=mc_[:, 20 + 4 * e + fc:21 + 4 * e + fc], s='S')
            tm_w1[e] = g.cnt['T']
            w8('v', 'S', g.cnt['S'])
            for fc in range(4):
                for t2 in range(2):
                    tt(wb[:, H1 + 1024 * fc + 512 * t2:H1 + 1024 * fc + 512 * (t2 + 1)],
                       wb[:, H1 + 1024 * fc + 512 * t2:H1 + 1024 * fc + 512 * (t2 + 1)],
                       wb[:, CBC + 512 * t2:CBC + 512 * (t2 + 1)], AL.mult, s='V')
            hv = g.cnt['V']
            w8('t', 'V', hv)
            w8('t', 'Wc' if sl == 0 else 'Wd', wsem[('w2', e)])
            for dc in range(8):
                for t2 in range(2):
                    bank = ps[(dc * 2 + t2) % 4]
                    w8('t', 'V', g.cnt['V'])
                    for fc in range(4):
                        mm(bank, wb[:, W2s[sl] + 1024 * fc + 128 * dc:W2s[sl] + 1024 * fc + 128 * (dc + 1)],
                           wb[:, H1 + 1024 * fc + 512 * t2:H1 + 1024 * fc + 512 * (t2 + 1)],
                           fc == 0, fc == 3, s='T' if fc == 3 else None)
                    w8('v', 'T', g.cnt['T'])
                    tt(fb[:, 1024 * dc + 512 * t2:1024 * dc + 512 * (t2 + 1)],
                       fb[:, 1024 * dc + 512 * t2:1024 * dc + 512 * (t2 + 1)],
                       bank, AL.add, s='V')
            tm_w2[e] = g.cnt['T']
        g.l0_tm_last = g.cnt['T']
        bar()
        if g.tap == 'l0ffn' and half == 0:
            _tap_out(g, [(g.dbg[128 * dc:128 * (dc + 1), 0:1024],
                          fb[:, 1024 * dc:1024 * (dc + 1)]) for dc in range(8)])
            return True
        w8('v', 'T', g.cnt['T'])
        for dc in range(8):
            for t2 in range(2):
                osl = fb[:, 1024 * dc + 512 * t2:1024 * dc + 512 * (t2 + 1)]
                gsl = wb[:, GG:GG + 512]
                ssl = wb[:, GG + 512:GG + 1024]
                w8('s', 'V', g.cnt['V'])
                w8('s', 'O', g.cnt['O'])
                sv = act(gsl, osl, AF.Silu, s='S')
                w8('y', 'S', sv)
                lv1 = ydma(osl, g.x_fm[128 * dc:128 * (dc + 1),
                                       t0 + 512 * t2:t0 + 512 * (t2 + 1)])
                lv2 = ydma(ssl, g.hs[128 * dc:128 * (dc + 1),
                                     t0 + 512 * t2:t0 + 512 * (t2 + 1)])
                w8('v', 'X', lv2)
                tt(ps[6], gsl, ssl, AL.mult)
                tt(osl, osl, ps[6], AL.add, s='V')
                w8('y', 'V', g.cnt['V'])
                ydma(g.hx1[128 * dc:128 * (dc + 1), t0 + 512 * t2:t0 + 512 * (t2 + 1)],
                     osl, s='O')
    w8('y', 'O', g.cnt['O'])
    bar()
    return False


def emit_pos_bcast(g):
    cs, fb, ps = g.cs, g.fb, g.ps
    lv = g.ydma(fb[0:1, FA:FA + T], g.cst['pos_row'][0:1, :])
    g.w8('t', 'X', lv)
    g.w8('t', 'S', g.cnt['S'])
    for nt in range(4):
        tv = g.mm(ps[nt][:, :], cs['ones32'][0:1, :], fb[0:1, FA + 512 * nt:FA + 512 * (nt + 1)],
                  True, True, s='T')
        g.w8('s', 'T', tv)
        g.act(fb[:, FPOS + 512 * nt:FPOS + 512 * (nt + 1)], ps[nt][:, :], AF.Copy, s='S')
    g.bar()


def emit_l1_attention(g, P):
    cs, fb, wb, xm, ps = g.cs, g.fb, g.wb, g.xm, g.ps
    psA, psB = g.psA, g.psB
    mm, act, tt, ts, vcp, vred, vrec, tr = g.mm, g.act, g.tt, g.ts, g.vcp, g.vred, g.vrec, g.tr
    ydma, gdma, w8, bar = g.ydma, g.gdma, g.w8, g.bar
    mc_ = g.mc_
    QR, KR, VP, WS = 16384, 32768, 0, 49152
    OS = 51200
    NB = 0
    SC1, SC2, MSK, EX = 8192, 10240, 12288, 12416
    src = P['src']
    scale, tau, dsh = P['scale'], P['tau'], P['dsh']

    wvv = g.wgroup([(wb[:, QR + 1024 * kc:QR + 1024 * (kc + 1)],
                     P['wv_moving'][128 * kc:128 * (kc + 1), :]) for kc in range(8)], 'Wa')
    if P.get('vb_row') is not None:
        vbv = g.wgroup([(wb[0:1, WS:WS + 1024], P['vb_row'])], 'Wb')
        w8('t', 'Wb', vbv)
    w8('t', 'Wa', wvv)
    g._ws_tmark = None
    g._examk = {}
    hasvb = P.get('vb_row') is not None
    for mc in range(16):
        for vh in range(2):
            bank = ps[4 + (mc * 2 + vh) % 2]
            w8('t', 'V', g.cnt['V'])
            for kc in range(8):
                mm(bank, src[0][:, src[1] + T * kc + 128 * mc:src[1] + T * kc + 128 * (mc + 1)],
                   wb[:, QR + 1024 * kc + 512 * vh:QR + 1024 * kc + 512 * (vh + 1)],
                   kc == 0, kc == 7,
                   s='T' if (kc == 7 and not hasvb) else None)
            if hasvb:
                mm(bank, cs['ones_bf'][0:1, :], wb[0:1, WS + 512 * vh:WS + 512 * (vh + 1)],
                   False, True, sgc=True, s='T')
            w8('v', 'T', g.cnt['T'])
            vcp(wb[:, VP + 1024 * mc + 512 * vh:VP + 1024 * mc + 512 * (vh + 1)], bank)
    bar()

    if P['shared_tables']:
        _build_tables(g, P, 0)
    for side, Aoff, wsel, wpsel, bsel, bpsel, thsel in (
            ('q', QR, P['wq'], P['wqp'], P.get('qb'), P.get('qpb'), 'theta_q'),
            ('k', KR, P['wk'], P['wkp'], P.get('kb'), P.get('kpb'), 'theta_k')):
        for oc in range(8):
            sems = ('Wa', 'Wb') if side == 'q' else ('Wc', 'Wd')
            if g._ws_tmark is not None:
                w8('g', 'T', g._ws_tmark)
            lv1 = g.wgroup([(wb[:, WS + 256 * kc:WS + 256 * kc + 128],
                             wsel(oc)[128 * kc:128 * (kc + 1), :]) for kc in range(8)], sems[0])
            lv2 = g.wgroup([(wb[:, WS + 256 * kc + 128:WS + 256 * kc + 256],
                             wpsel(oc)[128 * kc:128 * (kc + 1), :]) for kc in range(8)], sems[1])
            if not P['shared_tables']:
                _build_tables(g, P, oc, thsel)
            w8('t', sems[0], lv1)
            w8('t', sems[1], lv2)
            w8('t', 'S', g.cnt['S'])
            for pp, bias in ((0, bsel), (1, bpsel)):
                for nt in range(4):
                    bank = (psA if pp == 0 else psB)[:, 512 * nt:512 * (nt + 1)]
                    for kc in range(8):
                        mm(bank, wb[:, WS + 256 * kc + 128 * pp:WS + 256 * kc + 128 * (pp + 1)],
                           src[0][:, src[1] + T * kc + 512 * nt:src[1] + T * kc + 512 * (nt + 1)],
                           kc == 0, kc == 7, s='T' if kc == 7 else None)
            w8('s', 'T', g.cnt['T'])
            src_is_xm = src[0] is g.xm
            for pp, bias in ((0, bsel), (1, bpsel)):
                pst = psA if pp == 0 else psB
                if src_is_xm:
                    dstc = fb[:, (FA if pp == 0 else FPOS):(FA if pp == 0 else FPOS) + 2048]
                else:
                    dstc = g.xm[:, (SC1 if pp == 0 else SC2):(SC1 if pp == 0 else SC2) + 2048]
                if bias is not None:
                    act(dstc, pst[:, :], AF.Identity, bias=bias(oc), s='S')
                else:
                    act(dstc, pst[:, :], AF.Copy, s='S')
            if g.tap == P.get('tap_tbl') and side == 'q' and oc == 0:
                _tap_out(g, [(g.dbg[0:128, :], g.xm[:, SC1:SC1 + 2048]),
                             (g.dbg[128:256, :], g.xm[:, SC2:SC2 + 2048]),
                             (g.dbg[256:384, :], fb[:, FB_:FB_ + T]),
                             (g.dbg[384:512, :], fb[:, FC:FC + T])])
                return True
            w8('v', 'S', g.cnt['S'])
            if src_is_xm:
                t0 = fb[:, FA:FA + 2048]
                t1 = fb[:, FPOS:FPOS + 2048]
            else:
                t0 = g.xm[:, SC1:SC1 + 2048]
                t1 = g.xm[:, SC2:SC2 + 2048]
            out = wb[:, Aoff + 2048 * oc:Aoff + 2048 * (oc + 1)]
            tt(out, t0, fb[:, FC:FC + T], AL.mult)
            tt(t1, t1, fb[:, FB_:FB_ + T], AL.mult)
            tt(out, out, t1, AL.add, s='V')
            w8('t', 'V', g.cnt['V'])
            g._ws_tmark = g.cnt['T']
    bar()
    if g.tap == P.get('tap_qk'):
        _tap_out(g, [(g.dbg[0:128, 0:T], wb[:, QR:QR + T]),
                     (g.dbg[128:256, 0:T], wb[:, KR:KR + T]),
                     (g.dbg[256:384, 0:1024], wb[:, VP:VP + 1024])])
        return True

    for c in range(8):
        bst = cs['bsumA'] if c % 2 == 0 else cs['bsumB']
        rbase = 64 * (c % 2)
        xtile = NB + 2048 * (c // 2)
        kch = wb[:, KR + 2048 * c:KR + 2048 * (c + 1)]
        qch = wb[:, QR + 2048 * c:QR + 2048 * (c + 1)]
        vred(mc_[:, 20:21], kch, AL.add)
        ts(mc_[:, 20:21], mc_[:, 20:21], 1.0 / T, AL.mult, s='V')
        w8('v', 'T', g.cnt['T'])
        ts(g.xm[:, SC1:SC1 + T], kch, mc_[:, 20:21], AL.subtract, s='V')
        tt(g.xm[:, SC1:SC1 + T], g.xm[:, SC1:SC1 + T], g.xm[:, SC1:SC1 + T], AL.mult, s='V')
        w8('t', 'V', g.cnt['V'])
        for nt in range(4):
            mm(ps[4 + nt], bst[:, :], g.xm[:, SC1 + 512 * nt:SC1 + 512 * (nt + 1)],
               True, True, s='T' if nt == 3 else None)
        w8('v', 'T', g.cnt['T'])
        for nt in range(4):
            vred(mc_[:, 30 + nt:31 + nt], ps[4 + nt], AL.max)
        tt(mc_[:, 30:31], mc_[:, 30:31], mc_[:, 31:32], AL.max)
        tt(mc_[:, 32:33], mc_[:, 32:33], mc_[:, 33:34], AL.max)
        tt(mc_[:, 21:22], mc_[:, 30:31], mc_[:, 32:33], AL.max, s='V')
        w8('s', 'V', g.cnt['V'])
        act(mc_[:, 21:22], mc_[:, 21:22], AF.Sqrt, s='S')
        g.vms(g.xm[:, MSK:MSK + 128], 0.0)
        w8('v', 'S', g.cnt['S'])
        vcp(g.xm[0:64, MSK + rbase:MSK + rbase + 1], mc_[0:64, 20:21])
        vcp(g.xm[64:128, MSK + rbase + 32:MSK + rbase + 33], mc_[64:128, 20:21], s='V')
        w8('t', 'V', g.cnt['V'])
        for nt in range(4):
            mm(ps[nt], g.xm[:, MSK:MSK + 128], qch[:, 512 * nt:512 * (nt + 1)],
               True, True, s='T' if nt == 3 else None)
        tq = g.cnt['T']
        w8('v', 'T', tq)
        tt(g.xm[:, SC2:SC2 + T], qch, qch, AL.mult, s='V')
        w8('t', 'V', g.cnt['V'])
        for nt in range(4):
            mm(ps[4 + nt], bst[:, :], g.xm[:, SC2 + 512 * nt:SC2 + 512 * (nt + 1)],
               True, True, s='T' if nt == 3 else None)
        w8('s', 'T', g.cnt['T'])
        w8('v', 'T', g.cnt['T'])
        for nt in range(4):
            act(fb[:, FA + 512 * nt:FA + 512 * (nt + 1)], ps[4 + nt], AF.Sqrt, s='S')
        w8('v', 'S', g.cnt['S'])
        for nt in range(4):
            sl = slice(512 * nt, 512 * (nt + 1))
            ts(fb[:, FA:FA + T][:, sl], fb[:, FA:FA + T][:, sl], mc_[:, 21:22], AL.mult)
            tt(fb[:, FA:FA + T][:, sl], fb[:, FA:FA + T][:, sl], ps[nt], AL.add)
            for j in range(2):
                rh = rbase + 32 * j
                ts(g.xm[rh:rh + 1, xtile:xtile + T][:, sl],
                   fb[rh:rh + 1, FA:FA + T][:, sl], -1.0, AL.mult, s='V')
        w8('t', 'V', g.cnt['V'])
    bar()

    for qt in range(4):
        for c in range(8):
            rbase = 64 * (c % 2)
            xtile = NB + 2048 * (c // 2)
            for j in range(2):
                h = 2 * c + j
                rh = rbase + 32 * j
                sab = ps[2 + j]
                evq = {}
                for mc in range(17):
                    if mc < 16:
                        bank = ps[mc % 2]
                        if g._examk.get(mc % 2):
                            w8('t', 'S', g._examk[mc % 2])
                        mm(bank, cs['ones_bf'][rh:rh + 1, :],
                           g.xm[rh:rh + 1, xtile + 512 * qt:xtile + 512 * (qt + 1)],
                           True, True, tp=(rh, 0), sgc=True)
                        mm(bank, wb[64 * j:64 * (j + 1), KR + 2048 * c + 128 * mc:KR + 2048 * c + 128 * (mc + 1)],
                           wb[64 * j:64 * (j + 1), QR + 2048 * c + 512 * qt:QR + 2048 * c + 512 * (qt + 1)],
                           False, True, sgc=True, s='T')
                        w8('s', 'T', g.cnt['T'])
                        evq[mc] = act(g.xm[:, EX + 512 * (mc % 2):EX + 512 * (mc % 2 + 1)],
                                      bank, AF.Exp, scale=g.fc(scale / tau), s='S')
                        g._examk[mc % 2] = evq[mc]
                    if mc > 0:
                        pm = mc - 1
                        w8('t', 'S', evq[pm])
                        mm(sab, cs['ones_bf'][:, :],
                           g.xm[:, EX + 512 * (pm % 2):EX + 512 * (pm % 2 + 1)],
                           pm == 0, pm == 15, sgc=True, s='T' if pm == 15 else None)
                w8('s', 'T', g.cnt['T'])
                act(fb[rh:rh + 1, FA:FA + 512], sab[rh:rh + 1, :], AF.Ln, s='S')
                w8('v', 'S', g.cnt['S'])
                ts(fb[rh:rh + 1, FA:FA + 512], fb[rh:rh + 1, FA:FA + 512],
                   g.fc(-tau / scale), AL.mult, g.fc(dsh / scale), AL.add, s='V')
                tt(g.xm[rh:rh + 1, xtile + 512 * qt:xtile + 512 * (qt + 1)],
                   fb[rh:rh + 1, FA:FA + 512],
                   g.xm[rh:rh + 1, xtile + 512 * qt:xtile + 512 * (qt + 1)], AL.add, s='V')
                w8('t', 'V', g.cnt['V'])
    bar()

    for qt in range(4):
        for c in range(8):
            rbase = 64 * (c % 2)
            xtile = NB + 2048 * (c // 2)
            for j in range(2):
                h = 2 * c + j
                rh = rbase + 32 * j
                ob = ps[2 + j]
                sbb = ps[4 + j]
                pb = 64 * (h - (h % 2))
                evq = {}
                for mc in range(17):
                    if mc < 16:
                        bank = ps[mc % 2]
                        if g._examk.get(mc % 2):
                            w8('t', 'S', g._examk[mc % 2])
                        mm(bank, cs['ones_bf'][rh:rh + 1, :],
                           g.xm[rh:rh + 1, xtile + 512 * qt:xtile + 512 * (qt + 1)],
                           True, True, tp=(rh, 0), sgc=True)
                        mm(bank, wb[64 * j:64 * (j + 1), KR + 2048 * c + 128 * mc:KR + 2048 * c + 128 * (mc + 1)],
                           wb[64 * j:64 * (j + 1), QR + 2048 * c + 512 * qt:QR + 2048 * c + 512 * (qt + 1)],
                           False, True, sgc=True, s='T')
                        w8('s', 'T', g.cnt['T'])
                        evq[mc] = act(g.xm[:, EX + 512 * (mc % 2):EX + 512 * (mc % 2 + 1)],
                                      bank, AF.Exp, scale=g.fc(scale), s='S')
                        g._examk[mc % 2] = evq[mc]
                    if mc > 0:
                        pm = mc - 1
                        w8('t', 'S', evq[pm])
                        mm(ob, wb[:, VP + 1024 * pm + pb:VP + 1024 * pm + pb + 128],
                           g.xm[:, EX + 512 * (pm % 2):EX + 512 * (pm % 2 + 1)],
                           pm == 0, pm == 15, sgc=True)
                        mm(sbb, cs['ones_bf'][:, :],
                           g.xm[:, EX + 512 * (pm % 2):EX + 512 * (pm % 2 + 1)],
                           pm == 0, pm == 15, sgc=True, s='T' if pm == 15 else None)
                w8('v', 'T', g.cnt['T'])
                r0 = 64 * (h % 2)
                vrec(fb[:, FA:FA + 512], sbb)
                tt(wb[r0:r0 + 64, OS:OS + 512],
                   ob[r0:r0 + 64, :], fb[r0:r0 + 64, FA:FA + 512], AL.mult, s='V')
                w8('y', 'V', g.cnt['V'])
                ov = ydma(g.ho[64 * h:64 * (h + 1), 512 * qt:512 * (qt + 1)],
                          wb[r0:r0 + 64, OS:OS + 512], s='O')
                g.q['v'].append(lambda e, S=g.sem['O'], V=ov: e.wait_ge(S, V))
                w8('t', 'V', g.cnt['V'])
    bar()
    if g.tap == P.get('tap_o'):
        _tap_out(g, [(g.dbg[:, :], g.ho[:, :])])
        return True

    w8('g', 'O', g.cnt['O'])
    lo_t = g.wgroup([(wb[:, QR + 2048 * hc:QR + 2048 * (hc + 1)],
                      g.ho[128 * hc:128 * (hc + 1), :]) for hc in range(8)], 'Wa')
    wo_t = g.wgroup([(wb[:, KR + 1024 * hc:KR + 1024 * (hc + 1)],
                      P['wo'][128 * hc:128 * (hc + 1), :]) for hc in range(8)], 'Wb')
    w8('t', 'Wa', lo_t)
    w8('t', 'Wb', wo_t)
    ob = P.get('ob')
    for nt in range(4):
        for dc in range(8):
            bank = ps[(nt * 8 + dc) % 2]
            if nt + dc > 0:
                w8('t', 'S', g.cnt['S'])
            for hc in range(8):
                mm(bank, wb[:, KR + 1024 * hc + 128 * dc:KR + 1024 * hc + 128 * (dc + 1)],
                   wb[:, QR + 2048 * hc + 512 * nt:QR + 2048 * hc + 512 * (nt + 1)],
                   hc == 0, hc == 7, s='T' if hc == 7 else None)
            w8('s', 'T', g.cnt['T'])
            if ob is not None:
                ev = act(wb[:, OS:OS + 512], bank, AF.Identity, bias=ob(dc), s='S')
            else:
                ev = act(wb[:, OS:OS + 512], bank, AF.Copy, s='S')
            w8('y', 'S', ev)
            ov = ydma(P['dst'][128 * dc:128 * (dc + 1), 512 * nt:512 * (nt + 1)],
                      wb[:, OS:OS + 512], s='O')
            g.q['s'].append(lambda e, S=g.sem['O'], V=ov: e.wait_ge(S, V))
    bar()
    return False


def _build_tables(g, P, oc, thsel='theta_q'):
    fb, ic = g.fb, g.ic
    th = P[thsel](oc)
    g.w8('v', 'T', g.cnt['T'])
    g.w8('v', 'S', g.cnt['S'])
    for dst, s2 in ((FB_, P['pos_s2_sin']), (FC, P['pos_s2_cos'])):
        g.ts(fb[:, dst:dst + T], fb[:, FPOS:FPOS + T], th, AL.mult, s2, AL.add, s='V')
        g.ts(fb[:, dst:dst + T], fb[:, dst:dst + T], 1.0 / TWO_PI, AL.mult,
             1024.0, AL.add, s='V')
        for nt in range(4):
            sl = slice(dst + 512 * nt, dst + 512 * (nt + 1))
            g.vcp(ic[:, :], fb[:, sl])
            g.vcp(fb[:, FA:FA + 512], ic[:, :])
            g.tt(fb[:, sl], fb[:, sl], fb[:, FA:FA + 512], AL.subtract, s='V')
        g.ts(fb[:, FA:FA + T], fb[:, dst:dst + T], 0.5, AL.is_ge, s='V')
        g.tt(fb[:, dst:dst + T], fb[:, dst:dst + T], fb[:, FA:FA + T],
             AL.subtract, s='V')
        g.ts(fb[:, FA:FA + T], fb[:, dst:dst + T], -0.5, AL.is_lt, s='V')
        g.tt(fb[:, dst:dst + T], fb[:, dst:dst + T], fb[:, FA:FA + T],
             AL.add, s='V')
        g.ts(fb[:, dst:dst + T], fb[:, dst:dst + T], TWO_PI, AL.mult, s='V')
    g.w8('s', 'V', g.cnt['V'])
    g.act(fb[:, FB_:FB_ + T], fb[:, FB_:FB_ + T], AF.Sin, s='S')
    g.act(fb[:, FC:FC + T], fb[:, FC:FC + T], AF.Sin, s='S')
    g.w8('v', 'S', g.cnt['S'])


def emit_l1f_post(g):
    fb, wb, xn = g.fb, g.wb, g.xn
    tt, w8, ydma, gdma = g.tt, g.w8, g.ydma, g.gdma
    for dc in range(8):
        for t2 in range(4):
            sl = slice(512 * t2, 512 * (t2 + 1))
            w8('g', 'V', g.cnt['V'])
            hav = gdma(wb[:, 512:1024], g.ha[128 * dc:128 * (dc + 1), sl], s='Wc')
            w8('v', 'Wc', hav)
            w8('v', 'O', g.cnt['O'])
            tt(fb[:, FA:FA + 512], xn[:, T * dc:T * (dc + 1)][:, sl],
               wb[:, 512:1024], AL.add, s='V')
            w8('y', 'V', g.cnt['V'])
            ov = ydma(g.hx2[128 * dc:128 * (dc + 1), sl], fb[:, FA:FA + 512], s='O')
            g.q['v'].append(lambda e, S=g.sem['O'], V=ov: e.wait_ge(S, V))
    for eng in ('t', 's', 'v'):
        w8(eng, 'O', g.cnt['O'])
    emit_norm(g, g.hx2, 'rms', wcol=g.mc2[:, 56:64], dst=g.xm)


def emit_l1_moe_tail(g):
    cs, fb, wb, xn, xm, ps = g.cs, g.fb, g.wb, g.xn, g.xm, g.ps
    mm, act, tt, ts, vcp, vred, vrec, tr = g.mm, g.act, g.tt, g.ts, g.vcp, g.vred, g.vrec, g.tr
    ydma, gdma, w8, bar = g.ydma, g.gdma, g.w8, g.bar
    mc_, mc2 = g.mc_, g.mc2
    W1, W2, W3 = 0, 16384, 32768
    CB = 49152
    RT = 51200
    H1 = 0
    CBC, ST1, ST2, B3, B2E = 8192, 9216, 10240, 11264, 12288
    wbx = xn
    bv = ydma(mc2[:, 64:192], g.wt['l1f_b1c'][:, :])
    rtv = g.wgroup([(wb[:, RT + 8 * kc:RT + 8 * (kc + 1)],
                     g.wt['l1f_gateT'][128 * kc:128 * (kc + 1), :]) for kc in range(8)], 'Wa')
    w8('t', 'Wa', rtv)
    for nt in range(4):
        bank = ps[4 + nt]
        for kc in range(8):
            mm(bank[0:8, :], wb[:, RT + 8 * kc:RT + 8 * (kc + 1)],
               xm[:, T * kc + 512 * nt:T * kc + 512 * (nt + 1)],
               kc == 0, kc == 7, s='T' if kc == 7 else None)
        w8('s', 'T', g.cnt['T'])
        act(fb[0:8, FA + 512 * nt:FA + 512 * (nt + 1)], bank[0:8, :], AF.Copy, s='S')
    LG = 6144
    for tsub in range(16):
        w8('t', 'S', g.cnt['S'])
        tv = tr(ps[0][:, 0:8], fb[0:8, FA + 128 * tsub:FA + 128 * (tsub + 1)],
                cs['ident'][0:8, 0:8], s='T')
        w8('v', 'T', tv)
        lg = fb[:, LG:LG + 8]
        vcp(lg, ps[0][:, 0:8])
        m1n = mc_[:, 53:54]
        vred(m1n, lg, AL.max, negate=True)
        ts(fb[:, LG + 8:LG + 16], lg, m1n, AL.add, s='V')
        ts(fb[:, LG + 16:LG + 24], fb[:, LG + 8:LG + 16], 0.0, AL.is_ge, s='V')
        ts(fb[:, LG + 24:LG + 32], fb[:, LG + 16:LG + 24], g.fc(-1e30), AL.mult, s='V')
        tt(fb[:, LG + 24:LG + 32], fb[:, LG + 8:LG + 16], fb[:, LG + 24:LG + 32], AL.add)
        vred(mc_[:, 54:55], fb[:, LG + 24:LG + 32], AL.max)
        w8('s', 'V', g.cnt['V'])
        act(mc_[:, 55:56], mc_[:, 54:55], AF.Exp, s='S')
        act(fb[:, LG + 32:LG + 40], fb[:, LG + 8:LG + 16], AF.Exp, s='S')
        w8('v', 'S', g.cnt['S'])
        ts(mc_[:, 55:56], mc_[:, 55:56], 1.0, AL.add, s='V')
        vrec(mc_[:, 55:56], mc_[:, 55:56])
        ts(fb[:, LG + 40:LG + 48], fb[:, LG + 8:LG + 16], mc_[:, 54:55], AL.is_ge, s='V')
        tt(fb[:, LG + 32:LG + 40], fb[:, LG + 32:LG + 40], fb[:, LG + 40:LG + 48], AL.mult)
        ts(fb[:, LG:LG + 8], fb[:, LG + 32:LG + 40], mc_[:, 55:56], AL.mult, s='V')
        w8('t', 'V', g.cnt['V'])
        tv = tr(ps[1][0:8, 0:128], fb[:, LG:LG + 8], cs['ident'][:, :], s='T')
        w8('s', 'T', tv)
        act(wb[0:8, CB + 128 * tsub:CB + 128 * (tsub + 1)], ps[1][0:8, 0:128],
            AF.Copy, s='S')
    bar()
    if g.tap == 'l1comb':
        _tap_out(g, [(g.dbg[0:8, 0:T], wb[0:8, CB:CB + T])])
        return True

    b3v = g.wgroup([(wbx[0:8, B3:B3 + 1024], g.wt['l1f_b3r'][:, :])], 'Wb')
    g.vms(wb[0:1, 51200:51712], 1.0, s='V')
    w8('t', 'V', g.cnt['V'])
    w8('s', 'X', bv)
    tm_w = {}
    for half in range(2):
        bar()
        for eng in ('t', 's', 'v', 'g', 'y'):
            w8(eng, 'O', g.cnt['O'])
        t0 = 1024 * half
        w8('t', 'Wb', b3v)
        for dc in range(8):
            for t2 in range(2):
                bank = ps[(dc * 2 + t2) % 4]
                if dc + t2 > 0:
                    w8('t', 'S', g.cnt['S'])
                mm(bank, wbx[0:8, B3 + 128 * dc:B3 + 128 * (dc + 1)],
                   wb[0:8, CB + t0 + 512 * t2:CB + t0 + 512 * (t2 + 1)],
                   True, True, s='T')
                w8('s', 'T', g.cnt['T'])
                act(fb[:, 1024 * dc + 512 * t2:1024 * dc + 512 * (t2 + 1)],
                    bank, AF.Copy, s='S')
        w8('v', 'S', g.cnt['S'])
        wv_marks = {}

        def load_expert(e2):
            wv_marks[e2] = (
                g.wgroup([(wb[:, W1 + 2048 * kc:W1 + 2048 * (kc + 1)],
                           g.wt['l1f_w1T'][e2, 128 * kc:128 * (kc + 1), :])
                          for kc in range(8)], 'Wa'),
                g.wgroup([(wb[:, W2 + 2048 * kc:W2 + 2048 * (kc + 1)],
                           g.wt['l1f_w2T'][e2, 128 * kc:128 * (kc + 1), :])
                          for kc in range(8)], 'Wc'),
                g.wgroup([(wb[:, W3 + 1024 * fc:W3 + 1024 * (fc + 1)],
                           g.wt['l1f_w3T'][e2, 128 * fc:128 * (fc + 1), :])
                          for fc in range(16)], 'Wd'),
                g.wgroup([(wbx[0:1, B2E + 2048 * (e2 % 2):B2E + 2048 * (e2 % 2) + 2048],
                           g.wt['l1f_b2r'][0:1, 2048 * e2:2048 * (e2 + 1)])], 'Wb'))
        smk = {0: 0, 1: 0}
        vmk = {0: 0, 1: 0}
        hmk = {0: 0, 1: 0}
        omk = {0: 0, 1: 0, 2: 0, 3: 0}
        cbk = 0
        w3mk = 0
        for e in range(8):
            if (e, half) > (0, 0):
                w8('g', 'T', tm_w.get((e, half), g.cnt['T']))
            load_expert(e)
            w1v, w2v, w3v, b2v = wv_marks[e]
            for t2 in range(2):
                bank = ps[4 + t2]
                if cbk:
                    w8('t', 'V', cbk)
                mm(bank, cs['sel8'][:, 128 * e:128 * (e + 1)],
                   wb[0:8, CB + t0 + 512 * t2:CB + t0 + 512 * (t2 + 1)],
                   True, True, s='T')
                w8('v', 'T', g.cnt['T'])
                cbk = vcp(wbx[:, CBC + 512 * t2:CBC + 512 * (t2 + 1)], bank, s='V')
            w8('t', 'Wa', w1v)
            w8('t', 'Wc', w2v)
            w8('t', 'Wb', b2v)
            w8('t', 'Wd', w3v)
            for t2 in range(2):
                for fc in range(16):
                    pp = fc % 2
                    bank = ps[4 + 2 + pp]
                    if smk[pp]:
                        w8('t', 'S', smk[pp])
                    for kc in range(8):
                        mm(bank, wb[:, W1 + 2048 * kc + 128 * fc:W1 + 2048 * kc + 128 * (fc + 1)],
                           xm[:, T * kc + t0 + 512 * t2:T * kc + t0 + 512 * (t2 + 1)],
                           kc == 0, kc == 7, s='T' if kc == 7 else None)
                    tmka = g.cnt['T']
                    bank2 = ps[pp]
                    if hmk[pp]:
                        w8('t', 'V', hmk[pp])
                    for kc in range(8):
                        mm(bank2, wb[:, W2 + 2048 * kc + 128 * fc:W2 + 2048 * kc + 128 * (fc + 1)],
                           xm[:, T * kc + t0 + 512 * t2:T * kc + t0 + 512 * (t2 + 1)],
                           kc == 0, kc == 7, s=None)
                    mm(bank2, wbx[0:1, B2E + 2048 * (e % 2) + 128 * fc:B2E + 2048 * (e % 2) + 128 * (fc + 1)],
                       wb[0:1, 51200:51712],
                       False, True, sgc=True, s='T')
                    tmkb = g.cnt['T']
                    w8('s', 'T', tmka)
                    if hmk[pp]:
                        w8('s', 'V', hmk[pp])
                    smk[pp] = act(wbx[:, ST1 + 512 * pp:ST1 + 512 * (pp + 1)], bank, AF.Silu,
                                  bias=mc2[:, 64 + 16 * e + fc:65 + 16 * e + fc], s='S')
                    w8('v', 'S', smk[pp])
                    w8('v', 'T', tmkb)
                    if fc == 0 and w3mk:
                        w8('v', 'T', w3mk)
                    tt(wbx[:, ST2 + 512 * pp:ST2 + 512 * (pp + 1)],
                       wbx[:, ST1 + 512 * pp:ST1 + 512 * (pp + 1)],
                       wbx[:, CBC + 512 * t2:CBC + 512 * (t2 + 1)], AL.mult)
                    hmk[pp] = tt(xn[:, H1 + 512 * fc:H1 + 512 * (fc + 1)],
                                 wbx[:, ST2 + 512 * pp:ST2 + 512 * (pp + 1)],
                                 bank2, AL.mult, s='V')
                if g.tap == 'l1h' and e == 0 and half == 0 and t2 == 0:
                    g.bar()
                    _tap_out(g, [(g.dbg[128 * fc:128 * (fc + 1), 0:512],
                                  xn[:, H1 + 512 * fc:H1 + 512 * (fc + 1)])
                                 for fc in range(16)])
                    return True
                w8('t', 'V', max(hmk[0], hmk[1]))
                for dc in range(8):
                    bank = ps[(dc + t2) % 4]
                    if omk[(dc + t2) % 4]:
                        w8('t', 'V', omk[(dc + t2) % 4])
                    for fc in range(16):
                        mm(bank, wb[:, W3 + 1024 * fc + 128 * dc:W3 + 1024 * fc + 128 * (dc + 1)],
                           xn[:, H1 + 512 * fc:H1 + 512 * (fc + 1)],
                           fc == 0, fc == 15, s='T' if fc == 15 else None)
                    w8('v', 'T', g.cnt['T'])
                    omk[(dc + t2) % 4] = tt(
                        fb[:, 1024 * dc + 512 * t2:1024 * dc + 512 * (t2 + 1)],
                        fb[:, 1024 * dc + 512 * t2:1024 * dc + 512 * (t2 + 1)],
                        bank, AL.add, s='V')
                w3mk = g.cnt['T']
            tm_w[(e + 1, half) if e < 7 else (0, 1)] = g.cnt['T']
        bar()
        if g.tap == 'l1moe' and half == 0:
            _tap_out(g, [(g.dbg[128 * dc:128 * (dc + 1), 0:1024],
                          fb[:, 1024 * dc:1024 * (dc + 1)]) for dc in range(8)])
            return True
        w8('v', 'T', g.cnt['T'])
        for dc in range(8):
            for t2 in range(2):
                sl = slice(t0 + 512 * t2, t0 + 512 * (t2 + 1))
                osl = fb[:, 1024 * dc + 512 * t2:1024 * dc + 512 * (t2 + 1)]
                w8('g', 'V', g.cnt['V'])
                x2v = gdma(wbx[:, ST1:ST1 + 512], g.hx2[128 * dc:128 * (dc + 1), sl], s='Wc')
                w8('y', 'V', g.cnt['V'])
                sv_ = ydma(wbx[:, ST2:ST2 + 512], g.hs[128 * dc:128 * (dc + 1), sl])
                w8('v', 'Wc', x2v)
                w8('v', 'X', sv_)
                w8('v', 'S', g.cnt['S'])
                tt(ps[6], osl, wbx[:, ST1:ST1 + 512], AL.add, s='V')
                w8('s', 'V', g.cnt['V'])
                gv = act(wbx[:, ST1:ST1 + 512], ps[6], AF.Silu, s='S')
                w8('y', 'V', g.cnt['V'])
                lv1 = ydma(osl, g.hx1[128 * dc:128 * (dc + 1), sl])
                w8('v', 'X', lv1)
                w8('v', 'S', gv)
                tt(ps[7], wbx[:, ST1:ST1 + 512], wbx[:, ST2:ST2 + 512], AL.mult)
                tt(osl, osl, ps[7], AL.add, s='V')
                w8('y', 'V', g.cnt['V'])
                ov = ydma(g.out[128 * dc:128 * (dc + 1), sl], osl, s='O')
                g.q['v'].append(lambda e, S=g.sem['O'], V=ov: e.wait_ge(S, V))
                g.q['s'].append(lambda e, S=g.sem['O'], V=ov: e.wait_ge(S, V))
    w8('y', 'O', g.cnt['O'])
    bar()
    return False


def finish(g):
    for c in range(8):
        g.ydma(g.out[128 * c:128 * (c + 1), :], g.xn[:, T * c:T * (c + 1)], s='O')
    g.w8('y', 'O', g.cnt['O'])


def kernel(**inputs):
    x = np.asarray(inputs['L0_attn_x'])
    CST = _host_consts()
    W = _host_weights(inputs)
    nc = build_program(tap=DEBUG_TAP)
    fvals = np.zeros((128, 64), dtype=np.float32)
    for i, v in enumerate(kernel._fvals):
        fvals[:, i] = v
    in_maps = []
    for c in range(NC_):
        m = {'x_fm': np.ascontiguousarray(x[c].T), 'c_fvals': fvals}
        for k, v in CST.items():
            m['c_' + k] = v
        for k, v in W.items():
            m['w_' + k] = np.ascontiguousarray(v, dtype=np.float32)
        in_maps.append(m)
    res = run_bass_kernel_spmd(nc, in_maps, core_ids=list(range(NC_)))
    outs = [np.asarray(r['out']).T for r in res.results]
    if DEBUG_TAP is not None:
        kernel.dbg = [np.asarray(r['dbg']) for r in res.results]
    return np.stack(outs).astype(np.float32)


# version 09
# speedup vs baseline: 1.2914x; 1.1454x over previous
import numpy as np
import ml_dtypes
import concourse.bass as bass
import concourse.mybir as mybir
from concourse.bass_utils import run_bass_kernel_spmd

F32 = mybir.dt.float32
BF16 = mybir.dt.bfloat16
AF = mybir.ActivationFunctionType
AL = mybir.AluOpType
AX = mybir.AxisListType
BF = ml_dtypes.bfloat16

D, T, NC_ = 1024, 2048, 8
PI = float(np.pi)
TWO_PI = float(2.0 * np.pi)

DEBUG_TAP = None


def _host_consts():
    c = {}
    c['ident'] = np.eye(128, dtype=np.float32)
    c['ones32'] = np.ones((128, 128), dtype=np.float32)
    c['ones_bf'] = np.ones((128, 128), dtype=BF)
    sel = np.zeros((33, 128), dtype=np.float32)
    sel[0, :64] = 1.0
    sel[32, 64:] = 1.0
    c['sel33'] = sel
    b = np.zeros((128, 128), dtype=np.float32)
    for j in range(4):
        b[32 * j:32 * j + 16, 32 * j] = 1.0
    c['bsumL0'] = b.astype(BF)
    a = np.zeros((128, 128), dtype=np.float32)
    a[0:64, 0] = 1.0
    a[64:128, 32] = 1.0
    c['bsumA'] = a.astype(BF)
    a = np.zeros((128, 128), dtype=np.float32)
    a[0:64, 64] = 1.0
    a[64:128, 96] = 1.0
    c['bsumB'] = a.astype(BF)
    c['pos_row'] = np.arange(1, T + 1, dtype=np.float32).reshape(1, T)
    r = np.arange(128)
    exq = np.zeros((128, 8), dtype=np.float32)
    for ch in range(8):
        exq[:, ch] = -(((128 * ch + r) // 2).astype(np.float64) / 512.0)
    c['exq'] = exq
    exk = np.zeros((128, 8), dtype=np.float32)
    for ch in range(8):
        exk[:, ch] = -(((128 * ch + r) // 4).astype(np.float64) / 256.0)
    c['exk'] = exk
    c['exf'] = (-(((r % 64) % 32).astype(np.float64) / 32.0)).astype(np.float32).reshape(128, 1)
    c['sgq'] = np.where(r % 2 == 0, -1.0, 1.0).astype(np.float32).reshape(128, 1)
    c['sgk4'] = np.where(r % 4 < 2, -1.0, 1.0).astype(np.float32).reshape(128, 1)
    c['sgrh'] = np.where(r % 64 < 32, -1.0, 1.0).astype(np.float32).reshape(128, 1)
    c['tenk'] = np.full((128, 1), 10000.0, dtype=np.float32)
    s8 = np.zeros((8, 1024), dtype=np.float32)
    for e in range(8):
        s8[e, 128 * e:128 * (e + 1)] = 1.0
    c['sel8'] = s8.astype(BF)
    return c


def _host_weights(I):
    w = {}
    qp = np.asarray(I['L0_attn_query_proj'])
    kp = np.asarray(I['L0_attn_key_proj'])
    qstr = np.zeros((D, 512), dtype=np.float32)
    kstr = np.zeros((D, 512), dtype=np.float32)
    for h in range(16):
        qstr[:, 32 * h:32 * h + 16] = qp[:, h, :].T
        kstr[:, 32 * h:32 * h + 16] = kp[h, :, :].T
    w['l0_qstr'] = qstr
    w['l0_kstr'] = kstr
    w['l0_vT'] = np.ascontiguousarray(np.asarray(I['L0_attn_value_proj']).reshape(1024, D).T)
    w['l0_opT'] = np.ascontiguousarray(np.asarray(I['L0_attn_output_proj']).reshape(D, 1024).T)
    w['l0_rtr'] = np.asarray(I['L0_ffn_router_w'])
    w['l0_rtb'] = np.asarray(I['L0_ffn_router_b']).reshape(8, 1)
    w['l0_W1'] = np.asarray(I['L0_ffn_W1'])
    w['l0_b1c'] = np.ascontiguousarray(
        np.asarray(I['L0_ffn_b1']).reshape(8, 4, 128).transpose(2, 0, 1).reshape(128, 32))
    w['l0_W2'] = np.asarray(I['L0_ffn_W2'])
    w['l0_b2r'] = np.asarray(I['L0_ffn_b2'])
    Wq = np.asarray(I['L1_attn_Wq'])
    Wk = np.asarray(I['L1_attn_Wk'])
    Wv = np.asarray(I['L1_attn_Wv'])
    rep = np.repeat(np.arange(512), 2)
    permq = np.arange(1024).reshape(-1, 2)[:, ::-1].reshape(-1)
    perm4 = (np.arange(1024).reshape(-1, 4)[:, [2, 3, 0, 1]]).reshape(-1)
    w['l1a_wq'] = Wq
    w['l1a_wqp'] = np.ascontiguousarray(Wq[:, permq])
    Wkr = Wk[:, rep]
    w['l1a_wkr'] = np.ascontiguousarray(Wkr)
    w['l1a_wkrp'] = np.ascontiguousarray(Wkr[:, perm4])
    w['l1a_wvr'] = np.ascontiguousarray(Wv[:, rep])
    w['l1a_wo'] = np.asarray(I['L1_attn_Wo'])
    qkvT = np.ascontiguousarray(np.asarray(I['L1_ffn_qkv_w']).T)
    w['l1f_qkvT'] = qkvT
    permrh = (np.arange(2048).reshape(-1, 64)[:, list(range(32, 64)) + list(range(32))]).reshape(-1)
    w['l1f_qkvTp'] = np.ascontiguousarray(qkvT[:, :2048][:, permrh])
    qkvb = np.asarray(I['L1_ffn_qkv_b'])
    w['l1f_qkvbc'] = np.ascontiguousarray(qkvb.reshape(24, 128).T)
    w['l1f_qkvbcp'] = np.ascontiguousarray(qkvb[:2048][permrh].reshape(16, 128).T)
    w['l1f_outwT'] = np.ascontiguousarray(np.asarray(I['L1_ffn_out_w']).T)
    w['l1f_outbc'] = np.ascontiguousarray(np.asarray(I['L1_ffn_out_b']).reshape(8, 128).T)
    w['l1f_gateT'] = np.ascontiguousarray(np.asarray(I['L1_ffn_gate_w']).T)
    w['l1f_w1T'] = np.ascontiguousarray(np.asarray(I['L1_ffn_w1']).transpose(0, 2, 1))
    w['l1f_w2T'] = np.ascontiguousarray(np.asarray(I['L1_ffn_w2']).transpose(0, 2, 1))
    w['l1f_w3T'] = np.ascontiguousarray(np.asarray(I['L1_ffn_w3']).transpose(0, 2, 1))
    w['l1f_b1c'] = np.ascontiguousarray(
        np.asarray(I['L1_ffn_b1']).reshape(8, 16, 128).transpose(2, 0, 1).reshape(128, 128))
    w['l1f_b2r'] = np.ascontiguousarray(np.asarray(I['L1_ffn_b2']).reshape(1, 8 * 2048))
    w['l1f_b3r'] = np.asarray(I['L1_ffn_b3'])
    w['l1f_vb'] = np.ascontiguousarray(qkvb[2048:3072].reshape(1, 1024))
    w['l1f_n1c'] = np.ascontiguousarray(np.asarray(I['L1_ffn_norm1_w']).reshape(8, 128).T)
    w['l1f_n2c'] = np.ascontiguousarray(np.asarray(I['L1_ffn_norm2_w']).reshape(8, 128).T)
    return w


_WSHAPES = {
    'l0_qstr': (D, 512), 'l0_kstr': (D, 512), 'l0_vT': (D, 1024), 'l0_opT': (1024, D),
    'l0_rtr': (D, 8), 'l0_rtb': (8, 1), 'l0_W1': (8, D, 512), 'l0_b1c': (128, 32),
    'l0_W2': (8, 512, D), 'l0_b2r': (8, D),
    'l1a_wq': (D, D), 'l1a_wqp': (D, D), 'l1a_wkr': (D, D), 'l1a_wkrp': (D, D),
    'l1a_wvr': (D, D), 'l1a_wo': (D, D),
    'l1f_qkvT': (D, 3072), 'l1f_qkvTp': (D, 2048), 'l1f_qkvbc': (128, 24),
    'l1f_qkvbcp': (128, 16), 'l1f_outwT': (D, D), 'l1f_outbc': (128, 8),
    'l1f_gateT': (D, 8), 'l1f_w1T': (8, D, 2048), 'l1f_b1c': (128, 128),
    'l1f_w2T': (8, D, 2048), 'l1f_b2r': (1, 8 * 2048), 'l1f_w3T': (8, 2048, D),
    'l1f_b3r': (8, D), 'l1f_n1c': (128, 8), 'l1f_n2c': (128, 8), 'l1f_vb': (1, 1024),
}


class Ctx:
    pass


def build_program(tap=None):
    nc = bass.Bass('TRN2', target_bir_lowering=False)
    g = Ctx()
    g.nc = nc
    g.tap = tap
    g.x_fm = nc.dram_tensor('x_fm', [D, T], F32, kind='ExternalInput')
    CST = _host_consts()
    g.cst = {k: nc.dram_tensor('c_' + k, list(v.shape), BF16 if v.dtype == BF else F32,
                               kind='ExternalInput') for k, v in CST.items()}
    g.wt = {k: nc.dram_tensor('w_' + k, list(sh), F32, kind='ExternalInput')
            for k, sh in _WSHAPES.items()}
    g.fvals = []
    g.fv_dram = nc.dram_tensor('c_fvals', [128, 64], F32, kind='ExternalInput')
    g.hx1 = nc.dram_tensor('hx1', [D, T], F32)
    g.hx2 = nc.dram_tensor('hx2', [D, T], F32)
    g.hs = nc.dram_tensor('hs', [D, T], BF16)
    g.ho = nc.dram_tensor('ho', [D, T], BF16)
    g.ha = nc.dram_tensor('ha', [D, T], BF16)
    g.out = nc.dram_tensor('out', [D, T], F32, kind='ExternalOutput')
    g.dbg = nc.dram_tensor('dbg', [D, T], F32, kind='ExternalOutput')

    import contextlib
    with contextlib.ExitStack() as st:
        g.xn = st.enter_context(nc.sbuf_tensor('xn', [128, 8 * T], BF16))
        g.xm = st.enter_context(nc.sbuf_tensor('xm', [128, 8 * T], BF16))
        g.wb = st.enter_context(nc.sbuf_tensor('wb', [128, 52 * 1024 - 128], BF16))
        g.fb = st.enter_context(nc.sbuf_tensor('fb', [128, 8 * 1024], F32))
        g.mc_ = st.enter_context(nc.sbuf_tensor('mc_', [128, 64], F32))
        g.mc2 = st.enter_context(nc.sbuf_tensor('mc2', [128, 192], F32))
        g.ic = st.enter_context(nc.sbuf_tensor('ic', [128, 512], mybir.dt.int32))
        g.fv = st.enter_context(nc.sbuf_tensor('fv', [128, 64], F32))

        def fc(v):
            v = float(np.float32(v))
            if v not in g.fvals:
                assert len(g.fvals) < 64
                g.fvals.append(v)
                nc.const_aps.register(F32, v, g.fv[:, len(g.fvals) - 1:len(g.fvals)])
            return v
        g.fc = fc
        g.cs = {k: st.enter_context(nc.sbuf_tensor('s_' + k, list(v.shape),
                                                   BF16 if v.dtype == BF else F32))
                for k, v in CST.items() if k != 'pos_row'}
        g.psA = st.enter_context(nc.psum_tensor('psA', [128, 2048], F32))
        g.psB = st.enter_context(nc.psum_tensor('psB', [128, 2048], F32))
        g.ps = [g.psA[:, 512 * i:512 * (i + 1)] for i in range(4)] + \
               [g.psB[:, 512 * i:512 * (i + 1)] for i in range(4)]
        g.sem = {k: st.enter_context(nc.semaphore('sm' + k))
                 for k in ('W', 'X', 'T', 'S', 'V', 'O', 'Wa', 'Wb', 'Wc', 'Wd', 'D')}
        g.cnt = {k: 0 for k in g.sem}
        g.q = {'t': [], 's': [], 'v': [], 'g': [], 'y': []}
        g.done = False

        emit_all(g)
        kernel._fvals = list(g.fvals)

        with nc.Block() as blk:
            @blk.tensor
            def _(e):
                for fn in g.q['t']:
                    fn(e)

            @blk.scalar
            def _(e):
                for fn in g.q['s']:
                    fn(e)

            @blk.vector
            def _(e):
                for fn in g.q['v']:
                    fn(e)

            @blk.gpsimd
            def _(e):
                for fn in g.q['g']:
                    fn(e)

            @blk.sync
            def _(e):
                for fn in g.q['y']:
                    fn(e)
    return nc


def _mk(g):
    sem, cnt, q = g.sem, g.cnt, g.q

    def w8(eng, s, val):
        q[eng].append(lambda e, S=sem[s], V=val: e.wait_ge(S, V))

    def _run(eng, fn, s=None, n=1):
        if s is None:
            q[eng].append(fn)
            return None
        cnt[s] += n
        v = cnt[s]
        q[eng].append(lambda e, F=fn, S=sem[s], N=n: F(e).then_inc(S, N))
        if eng in ('t', 's', 'v'):
            q[eng].append(lambda e, S=sem[s], V=v: e.wait_ge(S, V))
        return v

    g.dmawait = {}

    def _dma(eng, dst, src, s):
        v = _run(eng, lambda e, d=dst, r=src: e.dma_start(out=d, in_=r), s, 16)
        key = (eng, s)
        if v - g.dmawait.get(key, 0) >= 7 * 16:
            q[eng].append(lambda e, S=sem[s], V=v: e.wait_ge(S, V))
            g.dmawait[key] = v
        return v

    def wgroup(items, s):
        if cnt[s] > 0:
            q['g'].append(lambda e, S=sem[s], V=cnt[s]: e.wait_ge(S, V))
            g.dmawait[('g', s)] = cnt[s]
        v = None
        for dst, srcap in items:
            v = _dma('g', dst, srcap, s)
        return v

    def ydma(dst, src, s='X'):
        return _dma('y', dst, src, s)

    def gdma(dst, src, s='W'):
        return _dma('g', dst, src, s)

    def mm(o, l, r, start, stop, tp=None, sgc=False, s=None):
        return _run('t', lambda e, o=o, l=l, r=r, a=start, b=stop, tp=tp, k=sgc:
                    e.matmul(o, l, r, start=a, stop=b, tile_position=tp,
                             skip_group_check=k), s)

    def tr(o, i, ident, s=None):
        return _run('t', lambda e, o=o, i=i, d=ident: e.transpose(o, i, d), s)

    def act(o, i, f, bias=0.0, scale=1.0, accum=None, s='S'):
        return _run('s', lambda e, o=o, i=i, f=f, b=bias, sc=scale, ac=accum:
                    e.activation(o, i, f, bias=b, scale=sc, accum_out=ac), s)

    def tt(o, a, b, op, s='V'):
        return _run('v', lambda e, o=o, a=a, b=b, op=op: e.tensor_tensor(o, a, b, op), s)

    def ts(o, a, s1, op0, s2=None, op1=None, accum=None, s='V'):
        if op1 is None:
            return _run('v', lambda e, o=o, a=a, s1=s1, ac=accum, op0=op0:
                        e.tensor_scalar(o, a, s1, None, op0, accum_out=ac), s)
        return _run('v', lambda e, o=o, a=a, s1=s1, s2=s2, op0=op0, op1=op1, ac=accum:
                    e.tensor_scalar(o, a, s1, s2, op0, op1, accum_out=ac), s)

    def vcp(o, i, s='V'):
        return _run('v', lambda e, o=o, i=i: e.tensor_copy(o, i), s)

    def vred(o, i, op, negate=None, s='V'):
        return _run('v', lambda e, o=o, i=i, op=op, n=negate:
                    e.tensor_reduce(o, i, AX.X, op, negate=n), s)

    def vrec(o, i, s='V'):
        return _run('v', lambda e, o=o, i=i: e.reciprocal(o, i), s)

    def vms(ap, c, s='V'):
        return _run('v', lambda e, a=ap, c=c: e.memset(a, c), s)

    def bar():
        tm = cnt['T']
        sv = act(g.mc_[:, 62:63], g.mc_[:, 62:63], AF.Copy, s='S')
        w8('v', 'S', sv)
        vv = vcp(g.mc_[:, 63:64], g.mc_[:, 62:63], s='V')
        w8('t', 'S', sv)
        w8('t', 'V', vv)
        w8('s', 'V', vv)
        w8('s', 'T', tm)
        w8('v', 'T', tm)
        for eng in ('g', 'y'):
            w8(eng, 'T', tm)
            w8(eng, 'S', sv)
            w8(eng, 'V', vv)
        return {'T': tm, 'S': sv, 'V': vv}

    g.w8, g.run = w8, _run
    g.wgroup = wgroup
    g.ydma, g.gdma, g.mm, g.tr = ydma, gdma, mm, tr
    g.act, g.tt, g.ts, g.vcp, g.vred, g.vrec, g.vms = act, tt, ts, vcp, vred, vrec, vms
    g.bar = bar


def _tap_out(g, tiles, rows=128):
    for s in ('O', 'X', 'T', 'S', 'V', 'W', 'Wa', 'Wb', 'Wc', 'Wd'):
        g.w8('g', s, g.cnt[s])
    for dst, srcap in tiles:
        g.gdma(dst, srcap, s='D')
    g.w8('g', 'D', g.cnt['D'])
    g.done = True


FPOS, FA, FB_, FC = 0, 2048, 4096, 6144
MLN, MTHQ, MTHK, MTHF, MSC = 0, 1, 9, 17, 18


def emit_all(g):
    _mk(g)
    nc, cs, fb, wb, xn, ps = g.nc, g.cs, g.fb, g.wb, g.xn, g.ps
    mm, act, tt, ts, vcp = g.mm, g.act, g.tt, g.ts, g.vcp
    ydma, w8, bar = g.ydma, g.w8, g.bar
    mc_ = g.mc_

    for k in cs:
        ydma(cs[k][:, :], g.cst[k][:, :])
    ydma(fb[0:1, FA:FA + T], g.cst['pos_row'][0:1, :])
    ydma(g.fv[:, :], g.fv_dram[:, :])
    w8('t', 'X', g.cnt['X'])
    w8('s', 'X', g.cnt['X'])
    w8('v', 'X', g.cnt['X'])

    act(mc_[:, 62:63], cs['tenk'][:, :], AF.Copy, s='S')

    for nt in range(4):
        mm(ps[nt][:, :], cs['ones32'][0:1, :], fb[0:1, FA + 512 * nt:FA + 512 * (nt + 1)],
           True, True, s='T')
    w8('s', 'T', g.cnt['T'])
    for nt in range(4):
        act(fb[:, FPOS + 512 * nt:FPOS + 512 * (nt + 1)], ps[nt][:, :], AF.Copy, s='S')
    act(mc_[:, MLN:MLN + 1], cs['tenk'][:, :], AF.Ln, s='S')
    for c in range(8):
        act(mc_[:, MTHQ + c:MTHQ + c + 1], cs['exq'][:, c:c + 1], AF.Exp,
            scale=mc_[:, MLN:MLN + 1], s='S')
        act(mc_[:, MTHK + c:MTHK + c + 1], cs['exk'][:, c:c + 1], AF.Exp,
            scale=mc_[:, MLN:MLN + 1], s='S')
    act(mc_[:, MTHF:MTHF + 1], cs['exf'][:, :], AF.Exp, scale=mc_[:, MLN:MLN + 1], s='S')
    w8('v', 'S', g.cnt['S'])
    for cols, exc in ((slice(MTHQ, MTHQ + 8), 'exq'), (slice(MTHK, MTHK + 8), 'exk'),
                      (slice(MTHF, MTHF + 1), 'exf')):
        n = cols.stop - cols.start
        ts(mc_[:, 40:40 + n], cs[exc][:, :], mc_[:, MLN:MLN + 1], AL.mult, s='V')
        w8('s', 'V', g.cnt['V'])
        act(mc_[:, 48:48 + n], mc_[:, cols], AF.Ln, s='S')
        w8('v', 'S', g.cnt['S'])
        tt(mc_[:, 40:40 + n], mc_[:, 40:40 + n], mc_[:, 48:48 + n], AL.subtract, s='V')
        ts(mc_[:, 40:40 + n], mc_[:, 40:40 + n], 1.0, AL.add, s='V')
        tt(mc_[:, cols], mc_[:, cols], mc_[:, 40:40 + n], AL.mult, s='V')
    ts(mc_[:, MTHQ:MTHQ + 8], mc_[:, MTHQ:MTHQ + 8], cs['sgq'][:, :], AL.mult, s='V')
    ts(mc_[:, MTHK:MTHK + 8], mc_[:, MTHK:MTHK + 8], cs['sgk4'][:, :], AL.mult, s='V')
    ts(mc_[:, MTHF:MTHF + 1], mc_[:, MTHF:MTHF + 1], cs['sgrh'][:, :], AL.mult, s='V')
    ts(mc_[:, 18:19], mc_[:, MTHF:MTHF + 1], -1.0, AL.mult, s='V')
    ts(mc_[:, 19:20], mc_[:, 18:19], 1.0, AL.mult, PI / 2.0, AL.add, s='V')
    bar()

    if g.tap == 'pos':
        _tap_out(g, [(g.dbg[0:128, 0:T], fb[:, FPOS:FPOS + T]),
                     (g.dbg[128:256, 0:32], mc_[:, 0:32])])
        return

    emit_norm(g, g.x_fm, 'ln', dst=g.xn)
    if g.tap == 'xn0':
        _tap_out(g, [(g.dbg[128 * c:128 * (c + 1), :], xn[:, T * c:T * (c + 1)])
                     for c in range(8)])
        return

    if emit_l0_attn(g):
        return
    if g.tap == 'l0s0':
        _tap_out(g, [(g.dbg[:, :], g.hs[:, :])])
        return

    if emit_l0_ffn_tail(g):
        return
    if g.tap == 'x1':
        _tap_out(g, [(g.dbg[:, :], g.hx1[:, :])])
        return

    emit_norm(g, g.hx1, 'ln', dst=g.xn)
    emit_pos_bcast(g)
    if g.tap == 'xn1':
        _tap_out(g, [(g.dbg[128 * c:128 * (c + 1), :], xn[:, T * c:T * (c + 1)])
                     for c in range(8)])
        return

    P = dict(src=(g.xn, 0), wv_moving=g.wt['l1a_wvr'],
             wq=lambda oc: g.wt['l1a_wq'][:, 128 * oc:128 * (oc + 1)],
             wqp=lambda oc: g.wt['l1a_wqp'][:, 128 * oc:128 * (oc + 1)],
             wk=lambda oc: g.wt['l1a_wkr'][:, 128 * oc:128 * (oc + 1)],
             wkp=lambda oc: g.wt['l1a_wkrp'][:, 128 * oc:128 * (oc + 1)],
             qb=None, qpb=None, kb=None, kpb=None, vb_row=None,
             theta_q=lambda oc: g.mc_[:, MTHQ + oc:MTHQ + oc + 1],
             theta_k=lambda oc: g.mc_[:, MTHK + oc:MTHK + oc + 1],
             pos_s2_sin=0.0, pos_s2_cos=PI / 2.0,
             shared_tables=False, scale=0.5, tau=8.0, dsh=40.0, stab=False,
             wo=g.wt['l1a_wo'], ob=None, dst=g.hs,
             tap_qk='l1aqk', tap_o='l1ao', tap_tbl='l1atbl')
    if emit_l1_attention(g, P):
        return
    if g.tap == 'l1as':
        _tap_out(g, [(g.dbg[:, :], g.hs[:, :])])
        return

    lv = g.ydma(g.mc2[:, 0:24], g.wt['l1f_qkvbc'][:, :])
    lv = g.ydma(g.mc2[:, 24:40], g.wt['l1f_qkvbcp'][:, :])
    lv = g.ydma(g.mc2[:, 40:48], g.wt['l1f_outbc'][:, :])
    lv = g.ydma(g.mc2[:, 48:56], g.wt['l1f_n1c'][:, :])
    lv = g.ydma(g.mc2[:, 56:64], g.wt['l1f_n2c'][:, :])
    g.w8('s', 'X', lv)
    g.w8('v', 'X', lv)
    emit_norm(g, (g.xn, 0), 'rms', wcol=g.mc2[:, 48:56], dst=g.xm)
    if g.tap == 'xn1f':
        _tap_out(g, [(g.dbg[128 * c:128 * (c + 1), :], g.xm[:, T * c:T * (c + 1)])
                     for c in range(8)])
        return
    P = dict(src=(g.xm, 0), wv_moving=g.wt['l1f_qkvT'][:, 2048:3072],
             wq=lambda oc: g.wt['l1f_qkvT'][:, 128 * oc:128 * (oc + 1)],
             wqp=lambda oc: g.wt['l1f_qkvTp'][:, 128 * oc:128 * (oc + 1)],
             wk=lambda oc: g.wt['l1f_qkvT'][:, 1024 + 128 * oc:1024 + 128 * (oc + 1)],
             wkp=lambda oc: g.wt['l1f_qkvTp'][:, 1024 + 128 * oc:1024 + 128 * (oc + 1)],
             qb=lambda oc: g.mc2[:, oc:oc + 1],
             qpb=lambda oc: g.mc2[:, 24 + oc:25 + oc],
             kb=lambda oc: g.mc2[:, 8 + oc:9 + oc],
             kpb=lambda oc: g.mc2[:, 32 + oc:33 + oc],
             vb_row=g.wt['l1f_vb'][0:1, :],
             theta_q=lambda oc: g.mc_[:, MTHF:MTHF + 1],
             theta_k=lambda oc: g.mc_[:, MTHF:MTHF + 1],
             pos_s2_sin=g.mc_[:, 18:19], pos_s2_cos=g.mc_[:, 19:20],
             shared_tables=True, scale=0.125, tau=8.0, dsh=40.0, stab=True,
             wo=g.wt['l1f_outwT'], ob=lambda dc: g.mc2[:, 40 + dc:41 + dc], dst=g.ha,
             tap_qk='l1fqk', tap_o='l1fo', tap_tbl='l1ftbl')
    if emit_l1_attention(g, P):
        return
    emit_l1f_post(g)
    if g.tap == 'xn2':
        _tap_out(g, [(g.dbg[128 * c:128 * (c + 1), :], g.xm[:, T * c:T * (c + 1)])
                     for c in range(8)])
        return

    emit_l1_moe_tail(g)
    if g.tap is None:
        g.w8('y', 'O', g.cnt['O'])


def emit_norm(g, xsrc, kind, wcol=None, dst=None, dst_off=0, stat_src_bf=None):
    cs, fb, wb, ps = g.cs, g.fb, g.wb, g.ps
    mm, act, tt, ts, vcp, vrec = g.mm, g.act, g.tt, g.ts, g.vcp, g.vrec
    ydma, w8, bar = g.ydma, g.w8, g.bar
    from_dram = not isinstance(xsrc, tuple)
    eps = 1e-5 if kind == 'ln' else 1e-6
    ln = kind == 'ln'

    g.w8('y', 'O', g.cnt['O'])
    tmarks = []
    vmark_cast = None
    for c in range(8):
        pp = c % 2
        if from_dram:
            xf = fb[:, FA:FA + T]
            if c > 0:
                w8('y', 'V', vmark_cast)
            lv = ydma(xf, xsrc[128 * c:128 * (c + 1), :])
            w8('v', 'X', lv)
            xb = wb[:, 2048 * pp:2048 * pp + T]
            if c >= 2:
                w8('v', 'T', tmarks[c - 2])
            vmark_cast = vcp(xb, xf, s='V')
        else:
            xb = xsrc[0][:, xsrc[1] + T * c:xsrc[1] + T * (c + 1)]
        sq = wb[:, 4096 + 2048 * pp:4096 + 2048 * pp + T]
        if c >= 2:
            w8('v', 'T', tmarks[c - 2])
        sv = tt(sq, xb, xb, AL.mult, s='V')
        w8('t', 'V', sv)
        for nt in range(4):
            sl = slice(512 * nt, 512 * (nt + 1))
            if ln:
                mm(ps[4 + nt][0:1, :], cs['ones_bf'][:, 0:1], xb[:, sl], c == 0, c == 7)
            mm(ps[nt][0:1, :], cs['ones_bf'][:, 0:1], sq[:, sl], c == 0, c == 7,
               s='T' if nt == 3 else None)
        tmarks.append(g.cnt['T'])
    w8('v', 'T', tmarks[-1])
    w8('s', 'T', tmarks[-1])

    arow = fb[0:1, FA:FA + T]
    brow = fb[0:1, FB_:FB_ + T]
    crow = fb[0:1, FC:FC + T]
    for nt in range(4):
        sl = slice(512 * nt, 512 * (nt + 1))
        ts(arow[:, sl], ps[nt][0:1, :], 1.0 / D, AL.mult)
        if ln:
            ts(crow[:, sl], ps[4 + nt][0:1, :], 1.0 / D, AL.mult)
    if ln:
        tt(brow, crow, crow, AL.mult)
        sv = tt(brow, arow, brow, AL.subtract, s='V')
    else:
        sv = vcp(brow, arow, s='V')
    w8('s', 'V', sv)
    act(brow, brow, AF.Sqrt, bias=g.fc(eps), s='S')
    w8('v', 'S', g.cnt['S'])
    vrec(brow, brow)
    if ln:
        tt(crow, crow, brow, AL.mult)
        sv = ts(crow, crow, -1.0, AL.mult, s='V')
    else:
        sv = vcp(g.mc_[0:1, 61:62], brow[0:1, 0:1], s='V')
    w8('t', 'V', sv)

    for nt in range(4):
        sl = slice(512 * nt, 512 * (nt + 1))
        tv = mm(ps[nt][:, :], cs['ones32'][0:1, :], brow[:, sl], True, True, s='T')
        w8('s', 'T', tv)
        act(fb[:, FB_ + 512 * nt:FB_ + 512 * (nt + 1)], ps[nt][:, :], AF.Copy, s='S')
        if ln:
            tv = mm(ps[4 + nt][:, :], cs['ones32'][0:1, :], crow[:, sl], True, True, s='T')
            w8('s', 'T', tv)
            act(fb[:, FC + 512 * nt:FC + 512 * (nt + 1)], ps[4 + nt][:, :], AF.Copy, s='S')
    w8('v', 'S', g.cnt['S'])

    vbar = g.cnt['V']
    for c in range(8):
        xo = dst[:, dst_off + T * c:dst_off + T * (c + 1)]
        if from_dram:
            xf = fb[:, FA:FA + T]
            w8('y', 'V', vbar if c == 0 else vmark_cast)
            lv = ydma(xf, xsrc[128 * c:128 * (c + 1), :])
            w8('v', 'X', lv)
            vmark_cast = tt(xo, xf, fb[:, FB_:FB_ + T], AL.mult, s='V')
        else:
            xi = xsrc[0][:, xsrc[1] + T * c:xsrc[1] + T * (c + 1)]
            tt(xo, xi, fb[:, FB_:FB_ + T], AL.mult)
        if ln:
            tt(xo, xo, fb[:, FC:FC + T], AL.add, s='V')
        else:
            ts(xo, xo, wcol[:, c:c + 1], AL.mult, s='V')
    bar()


def emit_l0_attn(g):
    cs, fb, wb, xn, xm, ps = g.cs, g.fb, g.wb, g.xn, g.xm, g.ps
    psA, psB = g.psA, g.psB
    mm, act, tt, ts, vcp, vred, vrec, tr = g.mm, g.act, g.tt, g.ts, g.vcp, g.vred, g.vrec, g.tr
    ydma, gdma, w8, bar = g.ydma, g.gdma, g.w8, g.bar
    mc_ = g.mc_
    QS, KS, VP, WA, WB2, WC, SQ, S0 = 0, 8192, 16384, 32768, 36864, 40960, 49152, 51200
    SALL, NEGB, BIAS2 = 6144, 6400, 6656

    wq_t = g.wgroup([(wb[:, WA + 512 * kc:WA + 512 * (kc + 1)],
                      g.wt['l0_qstr'][128 * kc:128 * (kc + 1), :]) for kc in range(8)], 'Wa')
    wk_t = g.wgroup([(wb[:, WB2 + 512 * kc:WB2 + 512 * (kc + 1)],
                      g.wt['l0_kstr'][128 * kc:128 * (kc + 1), :]) for kc in range(8)], 'Wb')
    wv_t = g.wgroup([(wb[:, WC + 1024 * kc:WC + 1024 * (kc + 1)],
                      g.wt['l0_vT'][128 * kc:128 * (kc + 1), :]) for kc in range(8)], 'Wc')

    for side, Woff, Aoff in (('q', WA, QS), ('k', WB2, KS)):
        w8('t', 'Wa' if side == 'q' else 'Wb', wq_t if side == 'q' else wk_t)
        for c4 in range(4):
            for nt in range(4):
                bank = ps[(c4 * 4 + nt) % 2]
                if not (side == 'q' and c4 == 0 and nt < 2):
                    w8('t', 'V', g.cnt['V'])
                for kc in range(8):
                    mm(bank, wb[:, Woff + 512 * kc + 128 * c4:Woff + 512 * kc + 128 * (c4 + 1)],
                       xn[:, T * kc + 512 * nt:T * kc + 512 * (nt + 1)],
                       kc == 0, kc == 7, s='T' if kc == 7 else None)
                w8('v', 'T', g.cnt['T'])
                vcp(wb[:, Aoff + 2048 * c4 + 512 * nt:Aoff + 2048 * c4 + 512 * (nt + 1)], bank)
    w8('t', 'Wc', wv_t)
    for mc in range(16):
        for vh in range(2):
            bank = ps[2 + (mc * 2 + vh) % 2]
            w8('t', 'V', g.cnt['V'])
            for kc in range(8):
                mm(bank, xn[:, T * kc + 128 * mc:T * kc + 128 * (mc + 1)],
                   wb[:, WC + 1024 * kc + 512 * vh:WC + 1024 * kc + 512 * (vh + 1)],
                   kc == 0, kc == 7, s='T' if kc == 7 else None)
            w8('v', 'T', g.cnt['T'])
            vcp(wb[:, VP + 1024 * mc + 512 * vh:VP + 1024 * mc + 512 * (vh + 1)], bank)
    vdone = g.cnt['V']
    w8('g', 'V', vdone)
    wo_t = g.wgroup([(wb[:, WC + 1024 * hc:WC + 1024 * (hc + 1)],
                      g.wt['l0_opT'][128 * hc:128 * (hc + 1), :]) for hc in range(8)], 'Wd')

    for c4 in range(4):
        sq = wb[:, SQ:SQ + T]
        qs = wb[:, QS + 2048 * c4:QS + 2048 * (c4 + 1)]
        if c4 > 0:
            w8('v', 'T', g.cnt['T'])
        tt(sq, qs, qs, AL.mult, s='V')
        w8('t', 'V', g.cnt['V'])
        for nt in range(4):
            if c4 > 0 and nt == 0:
                w8('t', 'V', g.cnt['V'])
            mm(ps[4 + nt], cs['bsumL0'][:, :], sq[:, 512 * nt:512 * (nt + 1)],
               True, True, s='T' if nt == 3 else None)
        w8('v', 'T', g.cnt['T'])
        for nt in range(4):
            vred(mc_[:, 30 + nt:31 + nt], ps[4 + nt], AL.max)
        tt(mc_[:, 30:31], mc_[:, 30:31], mc_[:, 31:32], AL.max)
        tt(mc_[:, 32:33], mc_[:, 32:33], mc_[:, 33:34], AL.max)
        tt(mc_[:, 20 + c4:21 + c4], mc_[:, 30:31], mc_[:, 32:33], AL.max)
        ts(mc_[:, 20 + c4:21 + c4], mc_[:, 20 + c4:21 + c4], 0.0625, AL.mult, s='V')
    for c4 in range(4):
        sq = wb[:, SQ:SQ + T]
        ks = wb[:, KS + 2048 * c4:KS + 2048 * (c4 + 1)]
        w8('v', 'T', g.cnt['T'])
        tt(sq, ks, ks, AL.mult, s='V')
        w8('t', 'V', g.cnt['V'])
        for nt in range(4):
            if nt == 0:
                w8('t', 'S', g.cnt['S'])
            mm(ps[4 + nt], cs['bsumL0'][:, :], sq[:, 512 * nt:512 * (nt + 1)],
               True, True, s='T' if nt == 3 else None)
        w8('s', 'T', g.cnt['T'])
        for nt in range(4):
            act(fb[:, FB_ + 512 * nt:FB_ + 512 * (nt + 1)], ps[4 + nt], AF.Sqrt,
                scale=mc_[:, 20 + c4:21 + c4], s='S')
        w8('t', 'S', g.cnt['S'])
        for mc in range(16):
            tv = tr(ps[mc % 2][:, 0:128], fb[:, FB_ + 128 * mc:FB_ + 128 * (mc + 1)],
                    cs['ident'][:, :], s='T')
            w8('s', 'T', tv)
            for j in range(4):
                act(fb[:, NEGB + 16 * mc + 4 * c4 + j:NEGB + 16 * mc + 4 * c4 + j + 1],
                    ps[mc % 2][:, 32 * j:32 * j + 1], AF.Copy,
                    scale=g.fc(-1.0), bias=g.fc(25.0), s='S')
            w8('t', 'S', g.cnt['S'])
    bar()
    if g.tap == 'l0bnd':
        _tap_out(g, [(g.dbg[0:128, 0:256], fb[:, NEGB:NEGB + 256]),
                     (g.dbg[128:256, 0:1024], wb[:, VP:VP + 1024]),
                     (g.dbg[384:512, 0:2048], wb[:, QS:QS + 2048]),
                     (g.dbg[512:640, 0:2048], wb[:, KS:KS + 2048])])
        return True

    sprev = {}
    vlast = None
    for mc in range(16):
        for q4 in range(4):
            bs = 0 if q4 % 2 == 0 else 4
            c4 = q4
            for nt in range(4):
                for j in range(4):
                    key = bs + j
                    if key in sprev:
                        w8('t', 'S', sprev[key])
                    mm(ps[key],
                       wb[32 * j:32 * j + 16,
                          KS + 2048 * c4 + 128 * mc:KS + 2048 * c4 + 128 * (mc + 1)],
                       wb[32 * j:32 * j + 16,
                          QS + 2048 * c4 + 512 * nt:QS + 2048 * c4 + 512 * (nt + 1)],
                       True, True, tp=(32 * j, 0), s='T' if j == 3 else None)
                w8('s', 'T', g.cnt['T'])
                if nt == 0 and vlast is not None:
                    w8('s', 'V', vlast)
                for j in range(4):
                    h = 4 * q4 + j
                    sprev[bs + j] = act(
                        xm[:, 512 * j:512 * (j + 1)], ps[bs + j], AF.Exp,
                        bias=fb[:, NEGB + 16 * mc + h:NEGB + 16 * mc + h + 1],
                        scale=g.fc(0.25),
                        accum=mc_[:, 36 + 4 * j + nt:37 + 4 * j + nt], s='S')
            w8('v', 'S', g.cnt['S'])
            for j in range(4):
                h = 4 * q4 + j
                vred(fb[:, SALL + 16 * mc + h:SALL + 16 * mc + h + 1],
                     mc_[:, 36 + 4 * j:36 + 4 * j + 4], AL.add, s='V')
            vlast = g.cnt['V']
    w8('s', 'V', g.cnt['V'])
    act(fb[:, BIAS2:BIAS2 + 256], fb[:, SALL:SALL + 256], AF.Ln, s='S')
    w8('v', 'S', g.cnt['S'])
    tt(fb[:, BIAS2:BIAS2 + 256], fb[:, NEGB:NEGB + 256], fb[:, BIAS2:BIAS2 + 256],
       AL.subtract, s='V')
    bar()
    if g.tap == 'l0S':
        _tap_out(g, [(g.dbg[0:128, 0:256], fb[:, BIAS2:BIAS2 + 256]),
                     (g.dbg[128:256, 0:1024], wb[:, VP:VP + 1024]),
                     (g.dbg[256:384, 0:256], fb[:, SALL:SALL + 256])])
        return True

    sprevB = {}
    for q4 in range(4):
        c4 = q4
        for nt in range(4):
            evq = {}
            for mc in range(17):
                if mc < 16:
                    pp2 = mc % 2
                    for j in range(4):
                        if j in sprevB:
                            w8('t', 'S', sprevB[j])
                        mm(ps[4 + j],
                           wb[32 * j:32 * j + 16,
                              KS + 2048 * c4 + 128 * mc:KS + 2048 * c4 + 128 * (mc + 1)],
                           wb[32 * j:32 * j + 16,
                              QS + 2048 * c4 + 512 * nt:QS + 2048 * c4 + 512 * (nt + 1)],
                           True, True, tp=(32 * j, 0), s='T' if j == 3 else None)
                    w8('s', 'T', g.cnt['T'])
                    for j in range(4):
                        h = 4 * q4 + j
                        ev = act(xm[:, 2048 * pp2 + 512 * j:2048 * pp2 + 512 * (j + 1)],
                                 ps[4 + j], AF.Exp,
                                 bias=fb[:, BIAS2 + 16 * mc + h:BIAS2 + 16 * mc + h + 1],
                                 scale=g.fc(0.25), s='S')
                        sprevB[j] = ev
                        evq[(mc, j)] = ev
                if mc > 0:
                    pm = mc - 1
                    pp2p = pm % 2
                    for j in range(4):
                        h = 4 * q4 + j
                        pb = 64 * (h - (h % 2))
                        w8('t', 'S', evq[(pm, j)])
                        mm(ps[j], wb[:, VP + 1024 * pm + pb:VP + 1024 * pm + pb + 128],
                           xm[:, 2048 * pp2p + 512 * j:2048 * pp2p + 512 * (j + 1)],
                           pm == 0, pm == 15, sgc=True,
                           s='T' if (pm == 15 and j == 3) else None)
            w8('s', 'T', g.cnt['T'])
            for j in range(4):
                h = 4 * q4 + j
                r0 = 64 * (h % 2)
                ev = act(wb[r0:r0 + 64, S0:S0 + 512], ps[j][r0:r0 + 64, :], AF.Copy, s='S')
                w8('y', 'S', ev)
                ov = ydma(g.ho[64 * h:64 * (h + 1), 512 * nt:512 * (nt + 1)],
                          wb[r0:r0 + 64, S0:S0 + 512], s='O')
                g.q['s'].append(lambda e, S=g.sem['O'], V=ov: e.wait_ge(S, V))
            w8('t', 'S', g.cnt['S'])
    bar()
    if g.tap == 'l0o':
        _tap_out(g, [(g.dbg[:, :], g.ho[:, :])])
        return True

    w8('g', 'O', g.cnt['O'])
    lo_t = g.wgroup([(wb[:, QS + 2048 * hc:QS + 2048 * (hc + 1)],
                      g.ho[128 * hc:128 * (hc + 1), :]) for hc in range(8)], 'Wa')
    w8('t', 'Wd', wo_t)
    w8('t', 'Wa', lo_t)
    for nt in range(4):
        for dc in range(8):
            bank = ps[(nt * 8 + dc) % 2]
            if nt + dc > 0:
                w8('t', 'S', g.cnt['S'])
            for hc in range(8):
                mm(bank, wb[:, WC + 1024 * hc + 128 * dc:WC + 1024 * hc + 128 * (dc + 1)],
                   wb[:, QS + 2048 * hc + 512 * nt:QS + 2048 * hc + 512 * (nt + 1)],
                   hc == 0, hc == 7, s='T' if hc == 7 else None)
            w8('s', 'T', g.cnt['T'])
            ev = act(wb[:, S0:S0 + 512], bank, AF.Copy, s='S')
            w8('y', 'S', ev)
            ov = ydma(g.hs[128 * dc:128 * (dc + 1), 512 * nt:512 * (nt + 1)],
                      wb[:, S0:S0 + 512], s='O')
            g.q['s'].append(lambda e, S=g.sem['O'], V=ov: e.wait_ge(S, V))
    bar()
    return False


def emit_l0_ffn_tail(g):
    cs, fb, wb, xn, xm, ps = g.cs, g.fb, g.wb, g.xn, g.xm, g.ps
    mm, act, tt, ts, vcp, vred, vrec, tr = g.mm, g.act, g.tt, g.ts, g.vcp, g.vred, g.vrec, g.tr
    ydma, gdma, w8, bar = g.ydma, g.gdma, g.w8, g.bar
    mc_ = g.mc_
    W1s = (0, 4096)
    W2s = (8192, 12288)
    H1, CB, CBC, B2, RT, SB, GG = 16384, 20480, 22528, 23552, 24576, 24704, 25728
    bv = ydma(mc_[:, 20:52], g.wt['l0_b1c'][:, :])
    bv = ydma(mc_[0:8, 52:53], g.wt['l0_rtb'][:, :])
    rtv = g.wgroup([(wb[:, RT + 8 * kc:RT + 8 * (kc + 1)],
                     g.wt['l0_rtr'][128 * kc:128 * (kc + 1), :]) for kc in range(8)], 'Wa')
    b2v = g.wgroup([(wb[0:8, B2:B2 + 1024], g.wt['l0_b2r'][:, :])], 'Wb')

    w8('t', 'Wa', rtv)
    for nt in range(4):
        bank = ps[4 + nt]
        for kc in range(8):
            mm(bank[0:8, :], wb[:, RT + 8 * kc:RT + 8 * (kc + 1)],
               xn[:, T * kc + 512 * nt:T * kc + 512 * (nt + 1)],
               kc == 0, kc == 7, s='T' if kc == 7 else None)
        w8('s', 'T', g.cnt['T'])
        w8('s', 'X', bv)
        act(fb[0:8, FA + 512 * nt:FA + 512 * (nt + 1)], bank[0:8, :], AF.Identity,
            bias=mc_[0:8, 52:53], s='S')
    LG = 6144
    for tsub in range(16):
        w8('t', 'S', g.cnt['S'])
        tv = tr(ps[0][:, 0:8], fb[0:8, FA + 128 * tsub:FA + 128 * (tsub + 1)],
                cs['ident'][0:8, 0:8], s='T')
        w8('v', 'T', tv)
        lg = fb[:, LG:LG + 8]
        vcp(lg, ps[0][:, 0:8])
        m1n = mc_[:, 53:54]
        vred(m1n, lg, AL.max, negate=True)
        ts(fb[:, LG + 8:LG + 16], lg, m1n, AL.add, s='V')
        ts(fb[:, LG + 16:LG + 24], fb[:, LG + 8:LG + 16], 0.0, AL.is_ge, s='V')
        ts(fb[:, LG + 24:LG + 32], fb[:, LG + 16:LG + 24], g.fc(-1e30), AL.mult, s='V')
        tt(fb[:, LG + 24:LG + 32], fb[:, LG + 8:LG + 16], fb[:, LG + 24:LG + 32], AL.add)
        vred(mc_[:, 54:55], fb[:, LG + 24:LG + 32], AL.max)
        w8('s', 'V', g.cnt['V'])
        act(mc_[:, 55:56], mc_[:, 54:55], AF.Exp, s='S')
        act(fb[:, LG + 32:LG + 40], fb[:, LG + 8:LG + 16], AF.Exp, s='S')
        w8('v', 'S', g.cnt['S'])
        ts(mc_[:, 55:56], mc_[:, 55:56], 1.0, AL.add, s='V')
        vrec(mc_[:, 55:56], mc_[:, 55:56])
        ts(fb[:, LG + 40:LG + 48], fb[:, LG + 8:LG + 16], mc_[:, 54:55], AL.is_ge, s='V')
        tt(fb[:, LG + 32:LG + 40], fb[:, LG + 32:LG + 40], fb[:, LG + 40:LG + 48], AL.mult)
        ts(fb[:, LG:LG + 8], fb[:, LG + 32:LG + 40], mc_[:, 55:56], AL.mult, s='V')
        w8('t', 'V', g.cnt['V'])
        tv = tr(ps[1][0:8, 0:128], fb[:, LG:LG + 8], cs['ident'][:, :], s='T')
        w8('s', 'T', tv)
        act(wb[0:8, CB + 128 * tsub:CB + 128 * (tsub + 1)], ps[1][0:8, 0:128],
            AF.Copy, s='S')
    bar()
    if g.tap == 'l0comb':
        _tap_out(g, [(g.dbg[0:8, 0:T], wb[0:8, CB:CB + T])])
        return True

    for half in range(2):
        bar()
        for eng in ('t', 's', 'v', 'g', 'y'):
            w8(eng, 'O', g.cnt['O'])
        t0 = 1024 * half
        w8('t', 'Wb', b2v)
        for dc in range(8):
            for t2 in range(2):
                bank = ps[(dc * 2 + t2) % 4]
                if dc + t2 > 0:
                    w8('t', 'S', g.cnt['S'])
                mm(bank, wb[0:8, B2 + 128 * dc:B2 + 128 * (dc + 1)],
                   wb[0:8, CB + t0 + 512 * t2:CB + t0 + 512 * (t2 + 1)],
                   True, True, s='T')
                w8('s', 'T', g.cnt['T'])
                act(fb[:, 1024 * dc + 512 * t2:1024 * dc + 512 * (t2 + 1)],
                    bank, AF.Copy, s='S')
        w8('v', 'S', g.cnt['S'])
        wsem = {}
        tm_w1, tm_w2 = {}, {}
        for e in range(8):
            sl = e % 2
            if e >= 2:
                w8('g', 'T', tm_w1[e - 2])
                w8('g', 'T', tm_w2[e - 2])
            elif half == 1:
                w8('g', 'T', g.l0_tm_last)
            wsem[('w1', e)] = g.wgroup(
                [(wb[:, W1s[sl] + 512 * kc:W1s[sl] + 512 * (kc + 1)],
                  g.wt['l0_W1'][e, 128 * kc:128 * (kc + 1), :]) for kc in range(8)],
                'Wa' if sl == 0 else 'Wb')
            wsem[('w2', e)] = g.wgroup(
                [(wb[:, W2s[sl] + 1024 * fc:W2s[sl] + 1024 * (fc + 1)],
                  g.wt['l0_W2'][e, 128 * fc:128 * (fc + 1), :]) for fc in range(4)],
                'Wc' if sl == 0 else 'Wd')
            for t2 in range(2):
                bank = ps[4 + t2]
                w8('t', 'V', g.cnt['V'])
                mm(bank, cs['sel8'][:, 128 * e:128 * (e + 1)],
                   wb[0:8, CB + t0 + 512 * t2:CB + t0 + 512 * (t2 + 1)],
                   True, True, s='T')
                w8('v', 'T', g.cnt['T'])
                vcp(wb[:, CBC + 512 * t2:CBC + 512 * (t2 + 1)], bank)
            w8('t', 'Wa' if sl == 0 else 'Wb', wsem[('w1', e)])
            for fc in range(4):
                for t2 in range(2):
                    bank = ps[4 + 2 + (fc * 2 + t2) % 2]
                    w8('t', 'S', g.cnt['S'])
                    for kc in range(8):
                        mm(bank, wb[:, W1s[sl] + 512 * kc + 128 * fc:W1s[sl] + 512 * kc + 128 * (fc + 1)],
                           xn[:, T * kc + t0 + 512 * t2:T * kc + t0 + 512 * (t2 + 1)],
                           kc == 0, kc == 7, s='T' if kc == 7 else None)
                    w8('s', 'T', g.cnt['T'])
                    act(wb[:, H1 + 1024 * fc + 512 * t2:H1 + 1024 * fc + 512 * (t2 + 1)],
                        bank, AF.Gelu, bias=mc_[:, 20 + 4 * e + fc:21 + 4 * e + fc], s='S')
            tm_w1[e] = g.cnt['T']
            w8('v', 'S', g.cnt['S'])
            for fc in range(4):
                for t2 in range(2):
                    tt(wb[:, H1 + 1024 * fc + 512 * t2:H1 + 1024 * fc + 512 * (t2 + 1)],
                       wb[:, H1 + 1024 * fc + 512 * t2:H1 + 1024 * fc + 512 * (t2 + 1)],
                       wb[:, CBC + 512 * t2:CBC + 512 * (t2 + 1)], AL.mult, s='V')
            hv = g.cnt['V']
            w8('t', 'V', hv)
            w8('t', 'Wc' if sl == 0 else 'Wd', wsem[('w2', e)])
            for dc in range(8):
                for t2 in range(2):
                    bank = ps[(dc * 2 + t2) % 4]
                    w8('t', 'V', g.cnt['V'])
                    for fc in range(4):
                        mm(bank, wb[:, W2s[sl] + 1024 * fc + 128 * dc:W2s[sl] + 1024 * fc + 128 * (dc + 1)],
                           wb[:, H1 + 1024 * fc + 512 * t2:H1 + 1024 * fc + 512 * (t2 + 1)],
                           fc == 0, fc == 3, s='T' if fc == 3 else None)
                    w8('v', 'T', g.cnt['T'])
                    tt(fb[:, 1024 * dc + 512 * t2:1024 * dc + 512 * (t2 + 1)],
                       fb[:, 1024 * dc + 512 * t2:1024 * dc + 512 * (t2 + 1)],
                       bank, AL.add, s='V')
            tm_w2[e] = g.cnt['T']
        g.l0_tm_last = g.cnt['T']
        bar()
        if g.tap == 'l0ffn' and half == 0:
            _tap_out(g, [(g.dbg[128 * dc:128 * (dc + 1), 0:1024],
                          fb[:, 1024 * dc:1024 * (dc + 1)]) for dc in range(8)])
            return True
        w8('v', 'T', g.cnt['T'])
        for dc in range(8):
            for t2 in range(2):
                osl = fb[:, 1024 * dc + 512 * t2:1024 * dc + 512 * (t2 + 1)]
                gsl = wb[:, GG:GG + 512]
                ssl = wb[:, GG + 512:GG + 1024]
                w8('s', 'V', g.cnt['V'])
                w8('s', 'O', g.cnt['O'])
                sv = act(gsl, osl, AF.Silu, s='S')
                w8('y', 'S', sv)
                lv1 = ydma(osl, g.x_fm[128 * dc:128 * (dc + 1),
                                       t0 + 512 * t2:t0 + 512 * (t2 + 1)])
                lv2 = ydma(ssl, g.hs[128 * dc:128 * (dc + 1),
                                     t0 + 512 * t2:t0 + 512 * (t2 + 1)])
                w8('v', 'X', lv2)
                tt(ps[6], gsl, ssl, AL.mult)
                tt(osl, osl, ps[6], AL.add, s='V')
                w8('y', 'V', g.cnt['V'])
                ydma(g.hx1[128 * dc:128 * (dc + 1), t0 + 512 * t2:t0 + 512 * (t2 + 1)],
                     osl, s='O')
    w8('y', 'O', g.cnt['O'])
    bar()
    return False


def emit_pos_bcast(g):
    cs, fb, ps = g.cs, g.fb, g.ps
    lv = g.ydma(fb[0:1, FA:FA + T], g.cst['pos_row'][0:1, :])
    g.w8('t', 'X', lv)
    g.w8('t', 'S', g.cnt['S'])
    for nt in range(4):
        tv = g.mm(ps[nt][:, :], cs['ones32'][0:1, :], fb[0:1, FA + 512 * nt:FA + 512 * (nt + 1)],
                  True, True, s='T')
        g.w8('s', 'T', tv)
        g.act(fb[:, FPOS + 512 * nt:FPOS + 512 * (nt + 1)], ps[nt][:, :], AF.Copy, s='S')
    g.bar()


def emit_l1_attention(g, P):
    cs, fb, wb, xm, ps = g.cs, g.fb, g.wb, g.xm, g.ps
    psA, psB = g.psA, g.psB
    mm, act, tt, ts, vcp, vred, vrec, tr = g.mm, g.act, g.tt, g.ts, g.vcp, g.vred, g.vrec, g.tr
    ydma, gdma, w8, bar = g.ydma, g.gdma, g.w8, g.bar
    mc_ = g.mc_
    QR, KR, VP, WS = 16384, 32768, 0, 49152
    OS = 51200
    NB = 0
    SC1, SC2, MSK, EX = 8192, 10240, 12288, 12416
    src = P['src']
    scale, tau, dsh = P['scale'], P['tau'], P['dsh']
    stab = P['stab']

    wvv = g.wgroup([(wb[:, QR + 1024 * kc:QR + 1024 * (kc + 1)],
                     P['wv_moving'][128 * kc:128 * (kc + 1), :]) for kc in range(8)], 'Wa')
    if P.get('vb_row') is not None:
        vbv = g.wgroup([(wb[0:1, WS:WS + 1024], P['vb_row'])], 'Wb')
        w8('t', 'Wb', vbv)
    w8('t', 'Wa', wvv)
    g._ws_tmark = None
    g._examk = {}
    hasvb = P.get('vb_row') is not None
    for mc in range(16):
        for vh in range(2):
            bank = ps[4 + (mc * 2 + vh) % 2]
            w8('t', 'V', g.cnt['V'])
            for kc in range(8):
                mm(bank, src[0][:, src[1] + T * kc + 128 * mc:src[1] + T * kc + 128 * (mc + 1)],
                   wb[:, QR + 1024 * kc + 512 * vh:QR + 1024 * kc + 512 * (vh + 1)],
                   kc == 0, kc == 7,
                   s='T' if (kc == 7 and not hasvb) else None)
            if hasvb:
                mm(bank, cs['ones_bf'][0:1, :], wb[0:1, WS + 512 * vh:WS + 512 * (vh + 1)],
                   False, True, sgc=True, s='T')
            w8('v', 'T', g.cnt['T'])
            vcp(wb[:, VP + 1024 * mc + 512 * vh:VP + 1024 * mc + 512 * (vh + 1)], bank)
    bar()

    if P['shared_tables']:
        _build_tables(g, P, 0)
    for side, Aoff, wsel, wpsel, bsel, bpsel, thsel in (
            ('q', QR, P['wq'], P['wqp'], P.get('qb'), P.get('qpb'), 'theta_q'),
            ('k', KR, P['wk'], P['wkp'], P.get('kb'), P.get('kpb'), 'theta_k')):
        for oc in range(8):
            sems = ('Wa', 'Wb') if side == 'q' else ('Wc', 'Wd')
            if g._ws_tmark is not None:
                w8('g', 'T', g._ws_tmark)
            lv1 = g.wgroup([(wb[:, WS + 256 * kc:WS + 256 * kc + 128],
                             wsel(oc)[128 * kc:128 * (kc + 1), :]) for kc in range(8)], sems[0])
            lv2 = g.wgroup([(wb[:, WS + 256 * kc + 128:WS + 256 * kc + 256],
                             wpsel(oc)[128 * kc:128 * (kc + 1), :]) for kc in range(8)], sems[1])
            if not P['shared_tables']:
                _build_tables(g, P, oc, thsel)
            w8('t', sems[0], lv1)
            w8('t', sems[1], lv2)
            w8('t', 'S', g.cnt['S'])
            for pp, bias in ((0, bsel), (1, bpsel)):
                for nt in range(4):
                    bank = (psA if pp == 0 else psB)[:, 512 * nt:512 * (nt + 1)]
                    for kc in range(8):
                        mm(bank, wb[:, WS + 256 * kc + 128 * pp:WS + 256 * kc + 128 * (pp + 1)],
                           src[0][:, src[1] + T * kc + 512 * nt:src[1] + T * kc + 512 * (nt + 1)],
                           kc == 0, kc == 7, s='T' if kc == 7 else None)
            w8('s', 'T', g.cnt['T'])
            src_is_xm = src[0] is g.xm
            for pp, bias in ((0, bsel), (1, bpsel)):
                pst = psA if pp == 0 else psB
                if src_is_xm:
                    dstc = fb[:, (FA if pp == 0 else FPOS):(FA if pp == 0 else FPOS) + 2048]
                else:
                    dstc = g.xm[:, (SC1 if pp == 0 else SC2):(SC1 if pp == 0 else SC2) + 2048]
                if bias is not None:
                    act(dstc, pst[:, :], AF.Identity, bias=bias(oc), s='S')
                else:
                    act(dstc, pst[:, :], AF.Copy, s='S')
            if g.tap == P.get('tap_tbl') and side == 'q' and oc == 0:
                _tap_out(g, [(g.dbg[0:128, :], g.xm[:, SC1:SC1 + 2048]),
                             (g.dbg[128:256, :], g.xm[:, SC2:SC2 + 2048]),
                             (g.dbg[256:384, :], fb[:, FB_:FB_ + T]),
                             (g.dbg[384:512, :], fb[:, FC:FC + T])])
                return True
            w8('v', 'S', g.cnt['S'])
            if src_is_xm:
                t0 = fb[:, FA:FA + 2048]
                t1 = fb[:, FPOS:FPOS + 2048]
            else:
                t0 = g.xm[:, SC1:SC1 + 2048]
                t1 = g.xm[:, SC2:SC2 + 2048]
            out = wb[:, Aoff + 2048 * oc:Aoff + 2048 * (oc + 1)]
            tt(out, t0, fb[:, FC:FC + T], AL.mult)
            tt(t1, t1, fb[:, FB_:FB_ + T], AL.mult)
            tt(out, out, t1, AL.add, s='V')
            w8('t', 'V', g.cnt['V'])
            g._ws_tmark = g.cnt['T']
    bar()
    if g.tap == P.get('tap_qk'):
        _tap_out(g, [(g.dbg[0:128, 0:T], wb[:, QR:QR + T]),
                     (g.dbg[128:256, 0:T], wb[:, KR:KR + T]),
                     (g.dbg[256:384, 0:1024], wb[:, VP:VP + 1024])])
        return True

    if P['stab']:
        for c in range(8):
            bst = cs['bsumA'] if c % 2 == 0 else cs['bsumB']
            rbase = 64 * (c % 2)
            xtile = NB + 2048 * (c // 2)
            kch = wb[:, KR + 2048 * c:KR + 2048 * (c + 1)]
            qch = wb[:, QR + 2048 * c:QR + 2048 * (c + 1)]
            vred(mc_[:, 20:21], kch, AL.add)
            ts(mc_[:, 20:21], mc_[:, 20:21], 1.0 / T, AL.mult, s='V')
            w8('v', 'T', g.cnt['T'])
            ts(g.xm[:, SC1:SC1 + T], kch, mc_[:, 20:21], AL.subtract, s='V')
            tt(g.xm[:, SC1:SC1 + T], g.xm[:, SC1:SC1 + T], g.xm[:, SC1:SC1 + T], AL.mult, s='V')
            w8('t', 'V', g.cnt['V'])
            for nt in range(4):
                mm(ps[4 + nt], bst[:, :], g.xm[:, SC1 + 512 * nt:SC1 + 512 * (nt + 1)],
                   True, True, s='T' if nt == 3 else None)
            w8('v', 'T', g.cnt['T'])
            for nt in range(4):
                vred(mc_[:, 30 + nt:31 + nt], ps[4 + nt], AL.max)
            tt(mc_[:, 30:31], mc_[:, 30:31], mc_[:, 31:32], AL.max)
            tt(mc_[:, 32:33], mc_[:, 32:33], mc_[:, 33:34], AL.max)
            tt(mc_[:, 21:22], mc_[:, 30:31], mc_[:, 32:33], AL.max, s='V')
            w8('s', 'V', g.cnt['V'])
            act(mc_[:, 21:22], mc_[:, 21:22], AF.Sqrt, s='S')
            g.vms(g.xm[:, MSK:MSK + 128], 0.0)
            w8('v', 'S', g.cnt['S'])
            vcp(g.xm[0:64, MSK + rbase:MSK + rbase + 1], mc_[0:64, 20:21])
            vcp(g.xm[64:128, MSK + rbase + 32:MSK + rbase + 33], mc_[64:128, 20:21], s='V')
            w8('t', 'V', g.cnt['V'])
            for nt in range(4):
                mm(ps[nt], g.xm[:, MSK:MSK + 128], qch[:, 512 * nt:512 * (nt + 1)],
                   True, True, s='T' if nt == 3 else None)
            tq = g.cnt['T']
            w8('v', 'T', tq)
            tt(g.xm[:, SC2:SC2 + T], qch, qch, AL.mult, s='V')
            w8('t', 'V', g.cnt['V'])
            for nt in range(4):
                mm(ps[4 + nt], bst[:, :], g.xm[:, SC2 + 512 * nt:SC2 + 512 * (nt + 1)],
                   True, True, s='T' if nt == 3 else None)
            w8('s', 'T', g.cnt['T'])
            w8('v', 'T', g.cnt['T'])
            for nt in range(4):
                act(fb[:, FA + 512 * nt:FA + 512 * (nt + 1)], ps[4 + nt], AF.Sqrt, s='S')
            w8('v', 'S', g.cnt['S'])
            for nt in range(4):
                sl = slice(512 * nt, 512 * (nt + 1))
                ts(fb[:, FA:FA + T][:, sl], fb[:, FA:FA + T][:, sl], mc_[:, 21:22], AL.mult)
                tt(fb[:, FA:FA + T][:, sl], fb[:, FA:FA + T][:, sl], ps[nt], AL.add)
                for j in range(2):
                    rh = rbase + 32 * j
                    ts(g.xm[rh:rh + 1, xtile:xtile + T][:, sl],
                       fb[rh:rh + 1, FA:FA + T][:, sl], -1.0, AL.mult, s='V')
            w8('t', 'V', g.cnt['V'])
        bar()

        for qt in range(4):
            for c in range(8):
                rbase = 64 * (c % 2)
                xtile = NB + 2048 * (c // 2)
                for j in range(2):
                    h = 2 * c + j
                    rh = rbase + 32 * j
                    sab = ps[2 + j]
                    evq = {}
                    for mc in range(17):
                        if mc < 16:
                            bank = ps[mc % 2]
                            if g._examk.get(mc % 2):
                                w8('t', 'S', g._examk[mc % 2])
                            mm(bank, cs['ones_bf'][rh:rh + 1, :],
                               g.xm[rh:rh + 1, xtile + 512 * qt:xtile + 512 * (qt + 1)],
                               True, True, tp=(rh, 0), sgc=True)
                            mm(bank, wb[64 * j:64 * (j + 1), KR + 2048 * c + 128 * mc:KR + 2048 * c + 128 * (mc + 1)],
                               wb[64 * j:64 * (j + 1), QR + 2048 * c + 512 * qt:QR + 2048 * c + 512 * (qt + 1)],
                               False, True, sgc=True, s='T')
                            w8('s', 'T', g.cnt['T'])
                            evq[mc] = act(g.xm[:, EX + 512 * (mc % 2):EX + 512 * (mc % 2 + 1)],
                                          bank, AF.Exp, scale=g.fc(scale / tau), s='S')
                            g._examk[mc % 2] = evq[mc]
                        if mc > 0:
                            pm = mc - 1
                            w8('t', 'S', evq[pm])
                            mm(sab, cs['ones_bf'][:, :],
                               g.xm[:, EX + 512 * (pm % 2):EX + 512 * (pm % 2 + 1)],
                               pm == 0, pm == 15, sgc=True, s='T' if pm == 15 else None)
                    w8('s', 'T', g.cnt['T'])
                    act(fb[rh:rh + 1, FA:FA + 512], sab[rh:rh + 1, :], AF.Ln, s='S')
                    w8('v', 'S', g.cnt['S'])
                    ts(fb[rh:rh + 1, FA:FA + 512], fb[rh:rh + 1, FA:FA + 512],
                       g.fc(-tau / scale), AL.mult, g.fc(dsh / scale), AL.add, s='V')
                    tt(g.xm[rh:rh + 1, xtile + 512 * qt:xtile + 512 * (qt + 1)],
                       fb[rh:rh + 1, FA:FA + 512],
                       g.xm[rh:rh + 1, xtile + 512 * qt:xtile + 512 * (qt + 1)], AL.add, s='V')
                    w8('t', 'V', g.cnt['V'])
        bar()

    for qt in range(4):
        for c in range(8):
            rbase = 64 * (c % 2)
            xtile = NB + 2048 * (c // 2)
            for j in range(2):
                h = 2 * c + j
                rh = rbase + 32 * j
                ob = ps[2 + j]
                sbb = ps[4 + j]
                pb = 64 * (h - (h % 2))
                evq = {}
                for mc in range(17):
                    if mc < 16:
                        bank = ps[mc % 2]
                        if g._examk.get(mc % 2):
                            w8('t', 'S', g._examk[mc % 2])
                        if stab:
                            mm(bank, cs['ones_bf'][rh:rh + 1, :],
                               g.xm[rh:rh + 1, xtile + 512 * qt:xtile + 512 * (qt + 1)],
                               True, True, tp=(rh, 0), sgc=True)
                        mm(bank, wb[64 * j:64 * (j + 1), KR + 2048 * c + 128 * mc:KR + 2048 * c + 128 * (mc + 1)],
                           wb[64 * j:64 * (j + 1), QR + 2048 * c + 512 * qt:QR + 2048 * c + 512 * (qt + 1)],
                           not stab, True, sgc=True, s='T')
                        w8('s', 'T', g.cnt['T'])
                        evq[mc] = act(g.xm[:, EX + 512 * (mc % 2):EX + 512 * (mc % 2 + 1)],
                                      bank, AF.Exp, scale=g.fc(scale), s='S')
                        g._examk[mc % 2] = evq[mc]
                    if mc > 0:
                        pm = mc - 1
                        w8('t', 'S', evq[pm])
                        mm(ob, wb[:, VP + 1024 * pm + pb:VP + 1024 * pm + pb + 128],
                           g.xm[:, EX + 512 * (pm % 2):EX + 512 * (pm % 2 + 1)],
                           pm == 0, pm == 15, sgc=True)
                        mm(sbb, cs['ones_bf'][:, :],
                           g.xm[:, EX + 512 * (pm % 2):EX + 512 * (pm % 2 + 1)],
                           pm == 0, pm == 15, sgc=True, s='T' if pm == 15 else None)
                w8('v', 'T', g.cnt['T'])
                r0 = 64 * (h % 2)
                vrec(fb[:, FA:FA + 512], sbb)
                tt(wb[r0:r0 + 64, OS:OS + 512],
                   ob[r0:r0 + 64, :], fb[r0:r0 + 64, FA:FA + 512], AL.mult, s='V')
                w8('y', 'V', g.cnt['V'])
                ov = ydma(g.ho[64 * h:64 * (h + 1), 512 * qt:512 * (qt + 1)],
                          wb[r0:r0 + 64, OS:OS + 512], s='O')
                g.q['v'].append(lambda e, S=g.sem['O'], V=ov: e.wait_ge(S, V))
                w8('t', 'V', g.cnt['V'])
    bar()
    if g.tap == P.get('tap_o'):
        _tap_out(g, [(g.dbg[:, :], g.ho[:, :])])
        return True

    w8('g', 'O', g.cnt['O'])
    lo_t = g.wgroup([(wb[:, QR + 2048 * hc:QR + 2048 * (hc + 1)],
                      g.ho[128 * hc:128 * (hc + 1), :]) for hc in range(8)], 'Wa')
    wo_t = g.wgroup([(wb[:, KR + 1024 * hc:KR + 1024 * (hc + 1)],
                      P['wo'][128 * hc:128 * (hc + 1), :]) for hc in range(8)], 'Wb')
    w8('t', 'Wa', lo_t)
    w8('t', 'Wb', wo_t)
    ob = P.get('ob')
    for nt in range(4):
        for dc in range(8):
            bank = ps[(nt * 8 + dc) % 2]
            if nt + dc > 0:
                w8('t', 'S', g.cnt['S'])
            for hc in range(8):
                mm(bank, wb[:, KR + 1024 * hc + 128 * dc:KR + 1024 * hc + 128 * (dc + 1)],
                   wb[:, QR + 2048 * hc + 512 * nt:QR + 2048 * hc + 512 * (nt + 1)],
                   hc == 0, hc == 7, s='T' if hc == 7 else None)
            w8('s', 'T', g.cnt['T'])
            if ob is not None:
                ev = act(wb[:, OS:OS + 512], bank, AF.Identity, bias=ob(dc), s='S')
            else:
                ev = act(wb[:, OS:OS + 512], bank, AF.Copy, s='S')
            w8('y', 'S', ev)
            ov = ydma(P['dst'][128 * dc:128 * (dc + 1), 512 * nt:512 * (nt + 1)],
                      wb[:, OS:OS + 512], s='O')
            g.q['s'].append(lambda e, S=g.sem['O'], V=ov: e.wait_ge(S, V))
    bar()
    return False


def _build_tables(g, P, oc, thsel='theta_q'):
    fb, ic = g.fb, g.ic
    th = P[thsel](oc)
    g.w8('v', 'T', g.cnt['T'])
    g.w8('v', 'S', g.cnt['S'])
    for dst, s2 in ((FB_, P['pos_s2_sin']), (FC, P['pos_s2_cos'])):
        g.ts(fb[:, dst:dst + T], fb[:, FPOS:FPOS + T], th, AL.mult, s2, AL.add, s='V')
        g.ts(fb[:, dst:dst + T], fb[:, dst:dst + T], 1.0 / TWO_PI, AL.mult,
             1024.0, AL.add, s='V')
        for nt in range(4):
            sl = slice(dst + 512 * nt, dst + 512 * (nt + 1))
            g.vcp(ic[:, :], fb[:, sl])
            g.vcp(fb[:, FA:FA + 512], ic[:, :])
            g.tt(fb[:, sl], fb[:, sl], fb[:, FA:FA + 512], AL.subtract, s='V')
        g.ts(fb[:, FA:FA + T], fb[:, dst:dst + T], 0.5, AL.is_ge, s='V')
        g.tt(fb[:, dst:dst + T], fb[:, dst:dst + T], fb[:, FA:FA + T],
             AL.subtract, s='V')
        g.ts(fb[:, FA:FA + T], fb[:, dst:dst + T], -0.5, AL.is_lt, s='V')
        g.tt(fb[:, dst:dst + T], fb[:, dst:dst + T], fb[:, FA:FA + T],
             AL.add, s='V')
        g.ts(fb[:, dst:dst + T], fb[:, dst:dst + T], TWO_PI, AL.mult, s='V')
    g.w8('s', 'V', g.cnt['V'])
    g.act(fb[:, FB_:FB_ + T], fb[:, FB_:FB_ + T], AF.Sin, s='S')
    g.act(fb[:, FC:FC + T], fb[:, FC:FC + T], AF.Sin, s='S')
    g.w8('v', 'S', g.cnt['S'])


def emit_l1f_post(g):
    fb, wb, xn = g.fb, g.wb, g.xn
    tt, w8, ydma, gdma = g.tt, g.w8, g.ydma, g.gdma
    for dc in range(8):
        for t2 in range(4):
            sl = slice(512 * t2, 512 * (t2 + 1))
            w8('g', 'V', g.cnt['V'])
            hav = gdma(wb[:, 512:1024], g.ha[128 * dc:128 * (dc + 1), sl], s='Wc')
            w8('v', 'Wc', hav)
            w8('v', 'O', g.cnt['O'])
            tt(fb[:, FA:FA + 512], xn[:, T * dc:T * (dc + 1)][:, sl],
               wb[:, 512:1024], AL.add, s='V')
            w8('y', 'V', g.cnt['V'])
            ov = ydma(g.hx2[128 * dc:128 * (dc + 1), sl], fb[:, FA:FA + 512], s='O')
            g.q['v'].append(lambda e, S=g.sem['O'], V=ov: e.wait_ge(S, V))
    for eng in ('t', 's', 'v'):
        w8(eng, 'O', g.cnt['O'])
    emit_norm(g, g.hx2, 'rms', wcol=g.mc2[:, 56:64], dst=g.xm)


def emit_l1_moe_tail(g):
    cs, fb, wb, xn, xm, ps = g.cs, g.fb, g.wb, g.xn, g.xm, g.ps
    mm, act, tt, ts, vcp, vred, vrec, tr = g.mm, g.act, g.tt, g.ts, g.vcp, g.vred, g.vrec, g.tr
    ydma, gdma, w8, bar = g.ydma, g.gdma, g.w8, g.bar
    mc_, mc2 = g.mc_, g.mc2
    W1, W2, W3 = 0, 16384, 32768
    CB = 49152
    RT = 51200
    H1 = 0
    CBC, ST1, ST2, B3, B2E = 8192, 9216, 10240, 11264, 12288
    wbx = xn
    bv = ydma(mc2[:, 64:192], g.wt['l1f_b1c'][:, :])
    rtv = g.wgroup([(wb[:, RT + 8 * kc:RT + 8 * (kc + 1)],
                     g.wt['l1f_gateT'][128 * kc:128 * (kc + 1), :]) for kc in range(8)], 'Wa')
    w8('t', 'Wa', rtv)
    for nt in range(4):
        bank = ps[4 + nt]
        for kc in range(8):
            mm(bank[0:8, :], wb[:, RT + 8 * kc:RT + 8 * (kc + 1)],
               xm[:, T * kc + 512 * nt:T * kc + 512 * (nt + 1)],
               kc == 0, kc == 7, s='T' if kc == 7 else None)
        w8('s', 'T', g.cnt['T'])
        act(fb[0:8, FA + 512 * nt:FA + 512 * (nt + 1)], bank[0:8, :], AF.Copy, s='S')
    LG = 6144
    for tsub in range(16):
        w8('t', 'S', g.cnt['S'])
        tv = tr(ps[0][:, 0:8], fb[0:8, FA + 128 * tsub:FA + 128 * (tsub + 1)],
                cs['ident'][0:8, 0:8], s='T')
        w8('v', 'T', tv)
        lg = fb[:, LG:LG + 8]
        vcp(lg, ps[0][:, 0:8])
        m1n = mc_[:, 53:54]
        vred(m1n, lg, AL.max, negate=True)
        ts(fb[:, LG + 8:LG + 16], lg, m1n, AL.add, s='V')
        ts(fb[:, LG + 16:LG + 24], fb[:, LG + 8:LG + 16], 0.0, AL.is_ge, s='V')
        ts(fb[:, LG + 24:LG + 32], fb[:, LG + 16:LG + 24], g.fc(-1e30), AL.mult, s='V')
        tt(fb[:, LG + 24:LG + 32], fb[:, LG + 8:LG + 16], fb[:, LG + 24:LG + 32], AL.add)
        vred(mc_[:, 54:55], fb[:, LG + 24:LG + 32], AL.max)
        w8('s', 'V', g.cnt['V'])
        act(mc_[:, 55:56], mc_[:, 54:55], AF.Exp, s='S')
        act(fb[:, LG + 32:LG + 40], fb[:, LG + 8:LG + 16], AF.Exp, s='S')
        w8('v', 'S', g.cnt['S'])
        ts(mc_[:, 55:56], mc_[:, 55:56], 1.0, AL.add, s='V')
        vrec(mc_[:, 55:56], mc_[:, 55:56])
        ts(fb[:, LG + 40:LG + 48], fb[:, LG + 8:LG + 16], mc_[:, 54:55], AL.is_ge, s='V')
        tt(fb[:, LG + 32:LG + 40], fb[:, LG + 32:LG + 40], fb[:, LG + 40:LG + 48], AL.mult)
        ts(fb[:, LG:LG + 8], fb[:, LG + 32:LG + 40], mc_[:, 55:56], AL.mult, s='V')
        w8('t', 'V', g.cnt['V'])
        tv = tr(ps[1][0:8, 0:128], fb[:, LG:LG + 8], cs['ident'][:, :], s='T')
        w8('s', 'T', tv)
        act(wb[0:8, CB + 128 * tsub:CB + 128 * (tsub + 1)], ps[1][0:8, 0:128],
            AF.Copy, s='S')
    bar()
    if g.tap == 'l1comb':
        _tap_out(g, [(g.dbg[0:8, 0:T], wb[0:8, CB:CB + T])])
        return True

    b3v = g.wgroup([(wbx[0:8, B3:B3 + 1024], g.wt['l1f_b3r'][:, :])], 'Wb')
    g.vms(wb[0:1, 51200:51712], 1.0, s='V')
    w8('t', 'V', g.cnt['V'])
    w8('s', 'X', bv)
    tm_w = {}
    for half in range(2):
        bar()
        for eng in ('t', 's', 'v', 'g', 'y'):
            w8(eng, 'O', g.cnt['O'])
        t0 = 1024 * half
        w8('t', 'Wb', b3v)
        for dc in range(8):
            for t2 in range(2):
                bank = ps[(dc * 2 + t2) % 4]
                if dc + t2 > 0:
                    w8('t', 'S', g.cnt['S'])
                mm(bank, wbx[0:8, B3 + 128 * dc:B3 + 128 * (dc + 1)],
                   wb[0:8, CB + t0 + 512 * t2:CB + t0 + 512 * (t2 + 1)],
                   True, True, s='T')
                w8('s', 'T', g.cnt['T'])
                act(fb[:, 1024 * dc + 512 * t2:1024 * dc + 512 * (t2 + 1)],
                    bank, AF.Copy, s='S')
        w8('v', 'S', g.cnt['S'])
        wv_marks = {}

        def load_expert(e2):
            wv_marks[e2] = (
                g.wgroup([(wb[:, W1 + 2048 * kc:W1 + 2048 * (kc + 1)],
                           g.wt['l1f_w1T'][e2, 128 * kc:128 * (kc + 1), :])
                          for kc in range(8)], 'Wa'),
                g.wgroup([(wb[:, W2 + 2048 * kc:W2 + 2048 * (kc + 1)],
                           g.wt['l1f_w2T'][e2, 128 * kc:128 * (kc + 1), :])
                          for kc in range(8)], 'Wc'),
                g.wgroup([(wb[:, W3 + 1024 * fc:W3 + 1024 * (fc + 1)],
                           g.wt['l1f_w3T'][e2, 128 * fc:128 * (fc + 1), :])
                          for fc in range(16)], 'Wd'),
                g.wgroup([(wbx[0:1, B2E + 2048 * (e2 % 2):B2E + 2048 * (e2 % 2) + 2048],
                           g.wt['l1f_b2r'][0:1, 2048 * e2:2048 * (e2 + 1)])], 'Wb'))
        smk = {0: 0, 1: 0}
        vmk = {0: 0, 1: 0}
        hmk = {0: 0, 1: 0}
        omk = {0: 0, 1: 0, 2: 0, 3: 0}
        cbk = 0
        w3mk = 0
        for e in range(8):
            if (e, half) > (0, 0):
                w8('g', 'T', tm_w.get((e, half), g.cnt['T']))
            load_expert(e)
            w1v, w2v, w3v, b2v = wv_marks[e]
            for t2 in range(2):
                bank = ps[4 + t2]
                if cbk:
                    w8('t', 'V', cbk)
                mm(bank, cs['sel8'][:, 128 * e:128 * (e + 1)],
                   wb[0:8, CB + t0 + 512 * t2:CB + t0 + 512 * (t2 + 1)],
                   True, True, s='T')
                w8('v', 'T', g.cnt['T'])
                cbk = vcp(wbx[:, CBC + 512 * t2:CBC + 512 * (t2 + 1)], bank, s='V')
            w8('t', 'Wa', w1v)
            w8('t', 'Wc', w2v)
            w8('t', 'Wb', b2v)
            w8('t', 'Wd', w3v)
            for t2 in range(2):
                for fc in range(16):
                    pp = fc % 2
                    bank = ps[4 + 2 + pp]
                    if smk[pp]:
                        w8('t', 'S', smk[pp])
                    for kc in range(8):
                        mm(bank, wb[:, W1 + 2048 * kc + 128 * fc:W1 + 2048 * kc + 128 * (fc + 1)],
                           xm[:, T * kc + t0 + 512 * t2:T * kc + t0 + 512 * (t2 + 1)],
                           kc == 0, kc == 7, s='T' if kc == 7 else None)
                    tmka = g.cnt['T']
                    bank2 = ps[pp]
                    if hmk[pp]:
                        w8('t', 'V', hmk[pp])
                    for kc in range(8):
                        mm(bank2, wb[:, W2 + 2048 * kc + 128 * fc:W2 + 2048 * kc + 128 * (fc + 1)],
                           xm[:, T * kc + t0 + 512 * t2:T * kc + t0 + 512 * (t2 + 1)],
                           kc == 0, kc == 7, s=None)
                    mm(bank2, wbx[0:1, B2E + 2048 * (e % 2) + 128 * fc:B2E + 2048 * (e % 2) + 128 * (fc + 1)],
                       wb[0:1, 51200:51712],
                       False, True, sgc=True, s='T')
                    tmkb = g.cnt['T']
                    w8('s', 'T', tmka)
                    if hmk[pp]:
                        w8('s', 'V', hmk[pp])
                    smk[pp] = act(wbx[:, ST1 + 512 * pp:ST1 + 512 * (pp + 1)], bank, AF.Silu,
                                  bias=mc2[:, 64 + 16 * e + fc:65 + 16 * e + fc], s='S')
                    w8('v', 'S', smk[pp])
                    w8('v', 'T', tmkb)
                    if fc == 0 and w3mk:
                        w8('v', 'T', w3mk)
                    tt(wbx[:, ST2 + 512 * pp:ST2 + 512 * (pp + 1)],
                       wbx[:, ST1 + 512 * pp:ST1 + 512 * (pp + 1)],
                       wbx[:, CBC + 512 * t2:CBC + 512 * (t2 + 1)], AL.mult)
                    hmk[pp] = tt(xn[:, H1 + 512 * fc:H1 + 512 * (fc + 1)],
                                 wbx[:, ST2 + 512 * pp:ST2 + 512 * (pp + 1)],
                                 bank2, AL.mult, s='V')
                if g.tap == 'l1h' and e == 0 and half == 0 and t2 == 0:
                    g.bar()
                    _tap_out(g, [(g.dbg[128 * fc:128 * (fc + 1), 0:512],
                                  xn[:, H1 + 512 * fc:H1 + 512 * (fc + 1)])
                                 for fc in range(16)])
                    return True
                w8('t', 'V', max(hmk[0], hmk[1]))
                for dc in range(8):
                    bank = ps[(dc + t2) % 4]
                    if omk[(dc + t2) % 4]:
                        w8('t', 'V', omk[(dc + t2) % 4])
                    for fc in range(16):
                        mm(bank, wb[:, W3 + 1024 * fc + 128 * dc:W3 + 1024 * fc + 128 * (dc + 1)],
                           xn[:, H1 + 512 * fc:H1 + 512 * (fc + 1)],
                           fc == 0, fc == 15, s='T' if fc == 15 else None)
                    w8('v', 'T', g.cnt['T'])
                    omk[(dc + t2) % 4] = tt(
                        fb[:, 1024 * dc + 512 * t2:1024 * dc + 512 * (t2 + 1)],
                        fb[:, 1024 * dc + 512 * t2:1024 * dc + 512 * (t2 + 1)],
                        bank, AL.add, s='V')
                w3mk = g.cnt['T']
            tm_w[(e + 1, half) if e < 7 else (0, 1)] = g.cnt['T']
        bar()
        if g.tap == 'l1moe' and half == 0:
            _tap_out(g, [(g.dbg[128 * dc:128 * (dc + 1), 0:1024],
                          fb[:, 1024 * dc:1024 * (dc + 1)]) for dc in range(8)])
            return True
        w8('v', 'T', g.cnt['T'])
        for dc in range(8):
            for t2 in range(2):
                sl = slice(t0 + 512 * t2, t0 + 512 * (t2 + 1))
                osl = fb[:, 1024 * dc + 512 * t2:1024 * dc + 512 * (t2 + 1)]
                w8('g', 'V', g.cnt['V'])
                x2v = gdma(wbx[:, ST1:ST1 + 512], g.hx2[128 * dc:128 * (dc + 1), sl], s='Wc')
                w8('y', 'V', g.cnt['V'])
                sv_ = ydma(wbx[:, ST2:ST2 + 512], g.hs[128 * dc:128 * (dc + 1), sl])
                w8('v', 'Wc', x2v)
                w8('v', 'X', sv_)
                w8('v', 'S', g.cnt['S'])
                tt(ps[6], osl, wbx[:, ST1:ST1 + 512], AL.add, s='V')
                w8('s', 'V', g.cnt['V'])
                gv = act(wbx[:, ST1:ST1 + 512], ps[6], AF.Silu, s='S')
                w8('y', 'V', g.cnt['V'])
                lv1 = ydma(osl, g.hx1[128 * dc:128 * (dc + 1), sl])
                w8('v', 'X', lv1)
                w8('v', 'S', gv)
                tt(ps[7], wbx[:, ST1:ST1 + 512], wbx[:, ST2:ST2 + 512], AL.mult)
                tt(osl, osl, ps[7], AL.add, s='V')
                w8('y', 'V', g.cnt['V'])
                ov = ydma(g.out[128 * dc:128 * (dc + 1), sl], osl, s='O')
                g.q['v'].append(lambda e, S=g.sem['O'], V=ov: e.wait_ge(S, V))
                g.q['s'].append(lambda e, S=g.sem['O'], V=ov: e.wait_ge(S, V))
    w8('y', 'O', g.cnt['O'])
    bar()
    return False


def finish(g):
    for c in range(8):
        g.ydma(g.out[128 * c:128 * (c + 1), :], g.xn[:, T * c:T * (c + 1)], s='O')
    g.w8('y', 'O', g.cnt['O'])


def kernel(**inputs):
    x = np.asarray(inputs['L0_attn_x'])
    CST = _host_consts()
    W = _host_weights(inputs)
    nc = build_program(tap=DEBUG_TAP)
    fvals = np.zeros((128, 64), dtype=np.float32)
    for i, v in enumerate(kernel._fvals):
        fvals[:, i] = v
    in_maps = []
    for c in range(NC_):
        m = {'x_fm': np.ascontiguousarray(x[c].T), 'c_fvals': fvals}
        for k, v in CST.items():
            m['c_' + k] = v
        for k, v in W.items():
            m['w_' + k] = np.ascontiguousarray(v, dtype=np.float32)
        in_maps.append(m)
    res = run_bass_kernel_spmd(nc, in_maps, core_ids=list(range(NC_)))
    outs = [np.asarray(r['out']).T for r in res.results]
    if DEBUG_TAP is not None:
        kernel.dbg = [np.asarray(r['dbg']) for r in res.results]
    return np.stack(outs).astype(np.float32)


# version 10 (final)
# speedup vs baseline: 1.3496x; 1.0451x over previous
import numpy as np
import ml_dtypes
import concourse.bass as bass
import concourse.mybir as mybir
from concourse.bass_utils import run_bass_kernel_spmd

F32 = mybir.dt.float32
BF16 = mybir.dt.bfloat16
AF = mybir.ActivationFunctionType
AL = mybir.AluOpType
AX = mybir.AxisListType
BF = ml_dtypes.bfloat16

D, T, NC_ = 1024, 2048, 8
PI = float(np.pi)
TWO_PI = float(2.0 * np.pi)

DEBUG_TAP = None


def _host_consts():
    c = {}
    c['ident'] = np.eye(128, dtype=np.float32)
    c['ones32'] = np.ones((128, 128), dtype=np.float32)
    c['ones_bf'] = np.ones((128, 128), dtype=BF)
    sel = np.zeros((33, 128), dtype=np.float32)
    sel[0, :64] = 1.0
    sel[32, 64:] = 1.0
    c['sel33'] = sel
    b = np.zeros((128, 128), dtype=np.float32)
    for j in range(4):
        b[32 * j:32 * j + 16, 32 * j] = 1.0
    c['bsumL0'] = b.astype(BF)
    a = np.zeros((128, 128), dtype=np.float32)
    a[0:64, 0] = 1.0
    a[64:128, 32] = 1.0
    c['bsumA'] = a.astype(BF)
    a = np.zeros((128, 128), dtype=np.float32)
    a[0:64, 64] = 1.0
    a[64:128, 96] = 1.0
    c['bsumB'] = a.astype(BF)
    c['pos_row'] = np.arange(1, T + 1, dtype=np.float32).reshape(1, T)
    r = np.arange(128)
    exq = np.zeros((128, 8), dtype=np.float32)
    for ch in range(8):
        exq[:, ch] = -(((128 * ch + r) // 2).astype(np.float64) / 512.0)
    c['exq'] = exq
    exk = np.zeros((128, 8), dtype=np.float32)
    for ch in range(8):
        exk[:, ch] = -(((128 * ch + r) // 4).astype(np.float64) / 256.0)
    c['exk'] = exk
    c['exf'] = (-(((r % 64) % 32).astype(np.float64) / 32.0)).astype(np.float32).reshape(128, 1)
    c['sgq'] = np.where(r % 2 == 0, -1.0, 1.0).astype(np.float32).reshape(128, 1)
    c['sgk4'] = np.where(r % 4 < 2, -1.0, 1.0).astype(np.float32).reshape(128, 1)
    c['sgrh'] = np.where(r % 64 < 32, -1.0, 1.0).astype(np.float32).reshape(128, 1)
    c['tenk'] = np.full((128, 1), 10000.0, dtype=np.float32)
    s8 = np.zeros((8, 1024), dtype=np.float32)
    for e in range(8):
        s8[e, 128 * e:128 * (e + 1)] = 1.0
    c['sel8'] = s8.astype(BF)
    return c


def _host_weights(I):
    w = {}
    qp = np.asarray(I['L0_attn_query_proj'])
    kp = np.asarray(I['L0_attn_key_proj'])
    qstr = np.zeros((D, 512), dtype=np.float32)
    kstr = np.zeros((D, 512), dtype=np.float32)
    for h in range(16):
        qstr[:, 32 * h:32 * h + 16] = qp[:, h, :].T
        kstr[:, 32 * h:32 * h + 16] = kp[h, :, :].T
    w['l0_qstr'] = qstr
    w['l0_kstr'] = kstr
    w['l0_vT'] = np.ascontiguousarray(np.asarray(I['L0_attn_value_proj']).reshape(1024, D).T)
    w['l0_opT'] = np.ascontiguousarray(np.asarray(I['L0_attn_output_proj']).reshape(D, 1024).T)
    w['l0_rtr'] = np.asarray(I['L0_ffn_router_w'])
    w['l0_rtb'] = np.asarray(I['L0_ffn_router_b']).reshape(8, 1)
    w['l0_W1'] = np.asarray(I['L0_ffn_W1'])
    w['l0_b1c'] = np.ascontiguousarray(
        np.asarray(I['L0_ffn_b1']).reshape(8, 4, 128).transpose(2, 0, 1).reshape(128, 32))
    w['l0_W2'] = np.asarray(I['L0_ffn_W2'])
    w['l0_b2r'] = np.asarray(I['L0_ffn_b2'])
    Wq = np.asarray(I['L1_attn_Wq'])
    Wk = np.asarray(I['L1_attn_Wk'])
    Wv = np.asarray(I['L1_attn_Wv'])
    rep = np.repeat(np.arange(512), 2)
    permq = np.arange(1024).reshape(-1, 2)[:, ::-1].reshape(-1)
    perm4 = (np.arange(1024).reshape(-1, 4)[:, [2, 3, 0, 1]]).reshape(-1)
    w['l1a_wq'] = Wq
    w['l1a_wqp'] = np.ascontiguousarray(Wq[:, permq])
    Wkr = Wk[:, rep]
    w['l1a_wkr'] = np.ascontiguousarray(Wkr)
    w['l1a_wkrp'] = np.ascontiguousarray(Wkr[:, perm4])
    w['l1a_wvr'] = np.ascontiguousarray(Wv[:, rep])
    w['l1a_wo'] = np.asarray(I['L1_attn_Wo'])
    qkvT = np.ascontiguousarray(np.asarray(I['L1_ffn_qkv_w']).T)
    w['l1f_qkvT'] = qkvT
    permrh = (np.arange(2048).reshape(-1, 64)[:, list(range(32, 64)) + list(range(32))]).reshape(-1)
    w['l1f_qkvTp'] = np.ascontiguousarray(qkvT[:, :2048][:, permrh])
    qkvb = np.asarray(I['L1_ffn_qkv_b'])
    w['l1f_qkvbc'] = np.ascontiguousarray(qkvb.reshape(24, 128).T)
    w['l1f_qkvbcp'] = np.ascontiguousarray(qkvb[:2048][permrh].reshape(16, 128).T)
    w['l1f_outwT'] = np.ascontiguousarray(np.asarray(I['L1_ffn_out_w']).T)
    w['l1f_outbc'] = np.ascontiguousarray(np.asarray(I['L1_ffn_out_b']).reshape(8, 128).T)
    w['l1f_gateT'] = np.ascontiguousarray(np.asarray(I['L1_ffn_gate_w']).T)
    w['l1f_w1T'] = np.ascontiguousarray(np.asarray(I['L1_ffn_w1']).transpose(0, 2, 1))
    w['l1f_w2T'] = np.ascontiguousarray(np.asarray(I['L1_ffn_w2']).transpose(0, 2, 1))
    w['l1f_w3T'] = np.ascontiguousarray(np.asarray(I['L1_ffn_w3']).transpose(0, 2, 1))
    w['l1f_b1c'] = np.ascontiguousarray(
        np.asarray(I['L1_ffn_b1']).reshape(8, 16, 128).transpose(2, 0, 1).reshape(128, 128))
    w['l1f_b2r'] = np.ascontiguousarray(np.asarray(I['L1_ffn_b2']).reshape(1, 8 * 2048))
    w['l1f_b3r'] = np.asarray(I['L1_ffn_b3'])
    w['l1f_vb'] = np.ascontiguousarray(qkvb[2048:3072].reshape(1, 1024))
    w['l1f_n1c'] = np.ascontiguousarray(np.asarray(I['L1_ffn_norm1_w']).reshape(8, 128).T)
    w['l1f_n2c'] = np.ascontiguousarray(np.asarray(I['L1_ffn_norm2_w']).reshape(8, 128).T)
    return w


_WSHAPES = {
    'l0_qstr': (D, 512), 'l0_kstr': (D, 512), 'l0_vT': (D, 1024), 'l0_opT': (1024, D),
    'l0_rtr': (D, 8), 'l0_rtb': (8, 1), 'l0_W1': (8, D, 512), 'l0_b1c': (128, 32),
    'l0_W2': (8, 512, D), 'l0_b2r': (8, D),
    'l1a_wq': (D, D), 'l1a_wqp': (D, D), 'l1a_wkr': (D, D), 'l1a_wkrp': (D, D),
    'l1a_wvr': (D, D), 'l1a_wo': (D, D),
    'l1f_qkvT': (D, 3072), 'l1f_qkvTp': (D, 2048), 'l1f_qkvbc': (128, 24),
    'l1f_qkvbcp': (128, 16), 'l1f_outwT': (D, D), 'l1f_outbc': (128, 8),
    'l1f_gateT': (D, 8), 'l1f_w1T': (8, D, 2048), 'l1f_b1c': (128, 128),
    'l1f_w2T': (8, D, 2048), 'l1f_b2r': (1, 8 * 2048), 'l1f_w3T': (8, 2048, D),
    'l1f_b3r': (8, D), 'l1f_n1c': (128, 8), 'l1f_n2c': (128, 8), 'l1f_vb': (1, 1024),
}


class Ctx:
    pass


def build_program(tap=None):
    nc = bass.Bass('TRN2', target_bir_lowering=False)
    g = Ctx()
    g.nc = nc
    g.tap = tap
    g.x_fm = nc.dram_tensor('x_fm', [D, T], F32, kind='ExternalInput')
    CST = _host_consts()
    g.cst = {k: nc.dram_tensor('c_' + k, list(v.shape), BF16 if v.dtype == BF else F32,
                               kind='ExternalInput') for k, v in CST.items()}
    g.wt = {k: nc.dram_tensor('w_' + k, list(sh), F32, kind='ExternalInput')
            for k, sh in _WSHAPES.items()}
    g.fvals = []
    g.fv_dram = nc.dram_tensor('c_fvals', [128, 64], F32, kind='ExternalInput')
    g.hx1 = nc.dram_tensor('hx1', [D, T], F32)
    g.hx2 = nc.dram_tensor('hx2', [D, T], F32)
    g.hs = nc.dram_tensor('hs', [D, T], BF16)
    g.ho = nc.dram_tensor('ho', [D, T], BF16)
    g.ha = nc.dram_tensor('ha', [D, T], BF16)
    g.out = nc.dram_tensor('out', [D, T], F32, kind='ExternalOutput')
    g.dbg = nc.dram_tensor('dbg', [D, T], F32, kind='ExternalOutput')

    import contextlib
    with contextlib.ExitStack() as st:
        g.xn = st.enter_context(nc.sbuf_tensor('xn', [128, 8 * T], BF16))
        g.xm = st.enter_context(nc.sbuf_tensor('xm', [128, 8 * T], BF16))
        g.wb = st.enter_context(nc.sbuf_tensor('wb', [128, 52 * 1024 - 128], BF16))
        g.fb = st.enter_context(nc.sbuf_tensor('fb', [128, 8 * 1024], F32))
        g.mc_ = st.enter_context(nc.sbuf_tensor('mc_', [128, 64], F32))
        g.mc2 = st.enter_context(nc.sbuf_tensor('mc2', [128, 192], F32))
        g.ic = st.enter_context(nc.sbuf_tensor('ic', [128, 512], mybir.dt.int32))
        g.fv = st.enter_context(nc.sbuf_tensor('fv', [128, 64], F32))

        def fc(v):
            v = float(np.float32(v))
            if v not in g.fvals:
                assert len(g.fvals) < 64
                g.fvals.append(v)
                nc.const_aps.register(F32, v, g.fv[:, len(g.fvals) - 1:len(g.fvals)])
            return v
        g.fc = fc
        g.cs = {k: st.enter_context(nc.sbuf_tensor('s_' + k, list(v.shape),
                                                   BF16 if v.dtype == BF else F32))
                for k, v in CST.items() if k != 'pos_row'}
        g.psA = st.enter_context(nc.psum_tensor('psA', [128, 2048], F32))
        g.psB = st.enter_context(nc.psum_tensor('psB', [128, 2048], F32))
        g.ps = [g.psA[:, 512 * i:512 * (i + 1)] for i in range(4)] + \
               [g.psB[:, 512 * i:512 * (i + 1)] for i in range(4)]
        g.sem = {k: st.enter_context(nc.semaphore('sm' + k))
                 for k in ('W', 'X', 'T', 'S', 'V', 'O', 'Wa', 'Wb', 'Wc', 'Wd', 'D')}
        g.cnt = {k: 0 for k in g.sem}
        g.q = {'t': [], 's': [], 'v': [], 'g': [], 'y': []}
        g.done = False

        emit_all(g)
        kernel._fvals = list(g.fvals)

        with nc.Block() as blk:
            @blk.tensor
            def _(e):
                for fn in g.q['t']:
                    fn(e)

            @blk.scalar
            def _(e):
                for fn in g.q['s']:
                    fn(e)

            @blk.vector
            def _(e):
                for fn in g.q['v']:
                    fn(e)

            @blk.gpsimd
            def _(e):
                for fn in g.q['g']:
                    fn(e)

            @blk.sync
            def _(e):
                for fn in g.q['y']:
                    fn(e)
    return nc


def _mk(g):
    sem, cnt, q = g.sem, g.cnt, g.q

    def w8(eng, s, val):
        q[eng].append(lambda e, S=sem[s], V=val: e.wait_ge(S, V))

    def _run(eng, fn, s=None, n=1):
        if s is None:
            q[eng].append(fn)
            return None
        cnt[s] += n
        v = cnt[s]
        q[eng].append(lambda e, F=fn, S=sem[s], N=n: F(e).then_inc(S, N))
        if eng in ('t', 's', 'v'):
            q[eng].append(lambda e, S=sem[s], V=v: e.wait_ge(S, V))
        return v

    g.dmawait = {}

    def _dma(eng, dst, src, s):
        v = _run(eng, lambda e, d=dst, r=src: e.dma_start(out=d, in_=r), s, 16)
        key = (eng, s)
        if v - g.dmawait.get(key, 0) >= 7 * 16:
            q[eng].append(lambda e, S=sem[s], V=v: e.wait_ge(S, V))
            g.dmawait[key] = v
        return v

    def wgroup(items, s):
        if cnt[s] > 0:
            q['g'].append(lambda e, S=sem[s], V=cnt[s]: e.wait_ge(S, V))
            g.dmawait[('g', s)] = cnt[s]
        v = None
        for dst, srcap in items:
            v = _dma('g', dst, srcap, s)
        return v

    def ydma(dst, src, s='X'):
        return _dma('y', dst, src, s)

    def gdma(dst, src, s='W'):
        return _dma('g', dst, src, s)

    def mm(o, l, r, start, stop, tp=None, sgc=False, s=None):
        return _run('t', lambda e, o=o, l=l, r=r, a=start, b=stop, tp=tp, k=sgc:
                    e.matmul(o, l, r, start=a, stop=b, tile_position=tp,
                             skip_group_check=k), s)

    def tr(o, i, ident, s=None):
        return _run('t', lambda e, o=o, i=i, d=ident: e.transpose(o, i, d), s)

    def act(o, i, f, bias=0.0, scale=1.0, accum=None, s='S'):
        return _run('s', lambda e, o=o, i=i, f=f, b=bias, sc=scale, ac=accum:
                    e.activation(o, i, f, bias=b, scale=sc, accum_out=ac), s)

    def tt(o, a, b, op, s='V'):
        return _run('v', lambda e, o=o, a=a, b=b, op=op: e.tensor_tensor(o, a, b, op), s)

    def ts(o, a, s1, op0, s2=None, op1=None, accum=None, s='V'):
        if op1 is None:
            return _run('v', lambda e, o=o, a=a, s1=s1, ac=accum, op0=op0:
                        e.tensor_scalar(o, a, s1, None, op0, accum_out=ac), s)
        return _run('v', lambda e, o=o, a=a, s1=s1, s2=s2, op0=op0, op1=op1, ac=accum:
                    e.tensor_scalar(o, a, s1, s2, op0, op1, accum_out=ac), s)

    def vcp(o, i, s='V'):
        return _run('v', lambda e, o=o, i=i: e.tensor_copy(o, i), s)

    def vred(o, i, op, negate=None, s='V'):
        return _run('v', lambda e, o=o, i=i, op=op, n=negate:
                    e.tensor_reduce(o, i, AX.X, op, negate=n), s)

    def vrec(o, i, s='V'):
        return _run('v', lambda e, o=o, i=i: e.reciprocal(o, i), s)

    def vms(ap, c, s='V'):
        return _run('v', lambda e, a=ap, c=c: e.memset(a, c), s)

    def bar():
        tm = cnt['T']
        sv = act(g.mc_[:, 62:63], g.mc_[:, 62:63], AF.Copy, s='S')
        w8('v', 'S', sv)
        vv = vcp(g.mc_[:, 63:64], g.mc_[:, 62:63], s='V')
        w8('t', 'S', sv)
        w8('t', 'V', vv)
        w8('s', 'V', vv)
        w8('s', 'T', tm)
        w8('v', 'T', tm)
        for eng in ('g', 'y'):
            w8(eng, 'T', tm)
            w8(eng, 'S', sv)
            w8(eng, 'V', vv)
        return {'T': tm, 'S': sv, 'V': vv}

    g.w8, g.run = w8, _run
    g.wgroup = wgroup
    g.ydma, g.gdma, g.mm, g.tr = ydma, gdma, mm, tr
    g.act, g.tt, g.ts, g.vcp, g.vred, g.vrec, g.vms = act, tt, ts, vcp, vred, vrec, vms
    g.bar = bar


def _tap_out(g, tiles, rows=128):
    for s in ('O', 'X', 'T', 'S', 'V', 'W', 'Wa', 'Wb', 'Wc', 'Wd'):
        g.w8('g', s, g.cnt[s])
    for dst, srcap in tiles:
        g.gdma(dst, srcap, s='D')
    g.w8('g', 'D', g.cnt['D'])
    g.done = True


FPOS, FA, FB_, FC = 0, 2048, 4096, 6144
MLN, MTHQ, MTHK, MTHF, MSC = 0, 1, 9, 17, 18


def emit_all(g):
    _mk(g)
    nc, cs, fb, wb, xn, ps = g.nc, g.cs, g.fb, g.wb, g.xn, g.ps
    mm, act, tt, ts, vcp = g.mm, g.act, g.tt, g.ts, g.vcp
    ydma, w8, bar = g.ydma, g.w8, g.bar
    mc_ = g.mc_

    for k in cs:
        ydma(cs[k][:, :], g.cst[k][:, :])
    ydma(fb[0:1, FA:FA + T], g.cst['pos_row'][0:1, :])
    ydma(g.fv[:, :], g.fv_dram[:, :])
    w8('t', 'X', g.cnt['X'])
    w8('s', 'X', g.cnt['X'])
    w8('v', 'X', g.cnt['X'])

    act(mc_[:, 62:63], cs['tenk'][:, :], AF.Copy, s='S')

    for nt in range(4):
        mm(ps[nt][:, :], cs['ones32'][0:1, :], fb[0:1, FA + 512 * nt:FA + 512 * (nt + 1)],
           True, True, s='T')
    w8('s', 'T', g.cnt['T'])
    for nt in range(4):
        act(fb[:, FPOS + 512 * nt:FPOS + 512 * (nt + 1)], ps[nt][:, :], AF.Copy, s='S')
    act(mc_[:, MLN:MLN + 1], cs['tenk'][:, :], AF.Ln, s='S')
    for c in range(8):
        act(mc_[:, MTHQ + c:MTHQ + c + 1], cs['exq'][:, c:c + 1], AF.Exp,
            scale=mc_[:, MLN:MLN + 1], s='S')
        act(mc_[:, MTHK + c:MTHK + c + 1], cs['exk'][:, c:c + 1], AF.Exp,
            scale=mc_[:, MLN:MLN + 1], s='S')
    act(mc_[:, MTHF:MTHF + 1], cs['exf'][:, :], AF.Exp, scale=mc_[:, MLN:MLN + 1], s='S')
    w8('v', 'S', g.cnt['S'])
    for cols, exc in ((slice(MTHQ, MTHQ + 8), 'exq'), (slice(MTHK, MTHK + 8), 'exk'),
                      (slice(MTHF, MTHF + 1), 'exf')):
        n = cols.stop - cols.start
        ts(mc_[:, 40:40 + n], cs[exc][:, :], mc_[:, MLN:MLN + 1], AL.mult, s='V')
        w8('s', 'V', g.cnt['V'])
        act(mc_[:, 48:48 + n], mc_[:, cols], AF.Ln, s='S')
        w8('v', 'S', g.cnt['S'])
        tt(mc_[:, 40:40 + n], mc_[:, 40:40 + n], mc_[:, 48:48 + n], AL.subtract, s='V')
        ts(mc_[:, 40:40 + n], mc_[:, 40:40 + n], 1.0, AL.add, s='V')
        tt(mc_[:, cols], mc_[:, cols], mc_[:, 40:40 + n], AL.mult, s='V')
    ts(mc_[:, MTHQ:MTHQ + 8], mc_[:, MTHQ:MTHQ + 8], cs['sgq'][:, :], AL.mult, s='V')
    ts(mc_[:, MTHK:MTHK + 8], mc_[:, MTHK:MTHK + 8], cs['sgk4'][:, :], AL.mult, s='V')
    ts(mc_[:, MTHF:MTHF + 1], mc_[:, MTHF:MTHF + 1], cs['sgrh'][:, :], AL.mult, s='V')
    ts(mc_[:, 18:19], mc_[:, MTHF:MTHF + 1], -1.0, AL.mult, s='V')
    ts(mc_[:, 19:20], mc_[:, 18:19], 1.0, AL.mult, PI / 2.0, AL.add, s='V')
    bar()

    if g.tap == 'pos':
        _tap_out(g, [(g.dbg[0:128, 0:T], fb[:, FPOS:FPOS + T]),
                     (g.dbg[128:256, 0:32], mc_[:, 0:32])])
        return

    emit_norm(g, g.x_fm, 'ln', dst=g.xn)
    if g.tap == 'xn0':
        _tap_out(g, [(g.dbg[128 * c:128 * (c + 1), :], xn[:, T * c:T * (c + 1)])
                     for c in range(8)])
        return

    if emit_l0_attn(g):
        return
    if g.tap == 'l0s0':
        _tap_out(g, [(g.dbg[:, :], g.hs[:, :])])
        return

    if emit_l0_ffn_tail(g):
        return
    if g.tap == 'x1':
        _tap_out(g, [(g.dbg[:, :], g.hx1[:, :])])
        return

    emit_norm(g, g.hx1, 'ln', dst=g.xn)
    emit_pos_bcast(g)
    if g.tap == 'xn1':
        _tap_out(g, [(g.dbg[128 * c:128 * (c + 1), :], xn[:, T * c:T * (c + 1)])
                     for c in range(8)])
        return

    P = dict(src=(g.xn, 0), wv_moving=g.wt['l1a_wvr'],
             wq=lambda oc: g.wt['l1a_wq'][:, 128 * oc:128 * (oc + 1)],
             wqp=lambda oc: g.wt['l1a_wqp'][:, 128 * oc:128 * (oc + 1)],
             wk=lambda oc: g.wt['l1a_wkr'][:, 128 * oc:128 * (oc + 1)],
             wkp=lambda oc: g.wt['l1a_wkrp'][:, 128 * oc:128 * (oc + 1)],
             qb=None, qpb=None, kb=None, kpb=None, vb_row=None,
             theta_q=lambda oc: g.mc_[:, MTHQ + oc:MTHQ + oc + 1],
             theta_k=lambda oc: g.mc_[:, MTHK + oc:MTHK + oc + 1],
             pos_s2_sin=0.0, pos_s2_cos=PI / 2.0,
             shared_tables=False, scale=0.5, tau=8.0, dsh=40.0, stab=False,
             wo=g.wt['l1a_wo'], ob=None, dst=g.hs,
             tap_qk='l1aqk', tap_o='l1ao', tap_tbl='l1atbl')
    if emit_l1_attention(g, P):
        return
    if g.tap == 'l1as':
        _tap_out(g, [(g.dbg[:, :], g.hs[:, :])])
        return

    lv = g.ydma(g.mc2[:, 0:24], g.wt['l1f_qkvbc'][:, :])
    lv = g.ydma(g.mc2[:, 24:40], g.wt['l1f_qkvbcp'][:, :])
    lv = g.ydma(g.mc2[:, 40:48], g.wt['l1f_outbc'][:, :])
    lv = g.ydma(g.mc2[:, 48:56], g.wt['l1f_n1c'][:, :])
    lv = g.ydma(g.mc2[:, 56:64], g.wt['l1f_n2c'][:, :])
    g.w8('s', 'X', lv)
    g.w8('v', 'X', lv)
    emit_norm(g, (g.xn, 0), 'rms', wcol=g.mc2[:, 48:56], dst=g.xm)
    if g.tap == 'xn1f':
        _tap_out(g, [(g.dbg[128 * c:128 * (c + 1), :], g.xm[:, T * c:T * (c + 1)])
                     for c in range(8)])
        return
    P = dict(src=(g.xm, 0), wv_moving=g.wt['l1f_qkvT'][:, 2048:3072],
             wq=lambda oc: g.wt['l1f_qkvT'][:, 128 * oc:128 * (oc + 1)],
             wqp=lambda oc: g.wt['l1f_qkvTp'][:, 128 * oc:128 * (oc + 1)],
             wk=lambda oc: g.wt['l1f_qkvT'][:, 1024 + 128 * oc:1024 + 128 * (oc + 1)],
             wkp=lambda oc: g.wt['l1f_qkvTp'][:, 1024 + 128 * oc:1024 + 128 * (oc + 1)],
             qb=lambda oc: g.mc2[:, oc:oc + 1],
             qpb=lambda oc: g.mc2[:, 24 + oc:25 + oc],
             kb=lambda oc: g.mc2[:, 8 + oc:9 + oc],
             kpb=lambda oc: g.mc2[:, 32 + oc:33 + oc],
             vb_row=g.wt['l1f_vb'][0:1, :],
             theta_q=lambda oc: g.mc_[:, MTHF:MTHF + 1],
             theta_k=lambda oc: g.mc_[:, MTHF:MTHF + 1],
             pos_s2_sin=g.mc_[:, 18:19], pos_s2_cos=g.mc_[:, 19:20],
             shared_tables=True, scale=0.125, tau=8.0, dsh=40.0, stab=True,
             wo=g.wt['l1f_outwT'], ob=lambda dc: g.mc2[:, 40 + dc:41 + dc], dst=g.ha,
             tap_qk='l1fqk', tap_o='l1fo', tap_tbl='l1ftbl')
    if emit_l1_attention(g, P):
        return
    emit_l1f_post(g)
    if g.tap == 'xn2':
        _tap_out(g, [(g.dbg[128 * c:128 * (c + 1), :], g.xm[:, T * c:T * (c + 1)])
                     for c in range(8)])
        return

    emit_l1_moe_tail(g)
    if g.tap is None:
        g.w8('y', 'O', g.cnt['O'])


def emit_norm(g, xsrc, kind, wcol=None, dst=None, dst_off=0, stat_src_bf=None):
    cs, fb, wb, ps = g.cs, g.fb, g.wb, g.ps
    mm, act, tt, ts, vcp, vrec = g.mm, g.act, g.tt, g.ts, g.vcp, g.vrec
    ydma, w8, bar = g.ydma, g.w8, g.bar
    from_dram = not isinstance(xsrc, tuple)
    eps = 1e-5 if kind == 'ln' else 1e-6
    ln = kind == 'ln'

    g.w8('y', 'O', g.cnt['O'])
    tmarks = []
    vmark_cast = None
    for c in range(8):
        pp = c % 2
        if from_dram:
            xf = fb[:, FA:FA + T]
            if c > 0:
                w8('y', 'V', vmark_cast)
            lv = ydma(xf, xsrc[128 * c:128 * (c + 1), :])
            w8('v', 'X', lv)
            xb = wb[:, 2048 * pp:2048 * pp + T]
            if c >= 2:
                w8('v', 'T', tmarks[c - 2])
            vmark_cast = vcp(xb, xf, s='V')
        else:
            xb = xsrc[0][:, xsrc[1] + T * c:xsrc[1] + T * (c + 1)]
        sq = wb[:, 4096 + 2048 * pp:4096 + 2048 * pp + T]
        if c >= 2:
            w8('v', 'T', tmarks[c - 2])
        sv = tt(sq, xb, xb, AL.mult, s='V')
        w8('t', 'V', sv)
        for nt in range(4):
            sl = slice(512 * nt, 512 * (nt + 1))
            if ln:
                mm(ps[4 + nt][0:1, :], cs['ones_bf'][:, 0:1], xb[:, sl], c == 0, c == 7)
            mm(ps[nt][0:1, :], cs['ones_bf'][:, 0:1], sq[:, sl], c == 0, c == 7,
               s='T' if nt == 3 else None)
        tmarks.append(g.cnt['T'])
    w8('v', 'T', tmarks[-1])
    w8('s', 'T', tmarks[-1])

    arow = fb[0:1, FA:FA + T]
    brow = fb[0:1, FB_:FB_ + T]
    crow = fb[0:1, FC:FC + T]
    for nt in range(4):
        sl = slice(512 * nt, 512 * (nt + 1))
        ts(arow[:, sl], ps[nt][0:1, :], 1.0 / D, AL.mult)
        if ln:
            ts(crow[:, sl], ps[4 + nt][0:1, :], 1.0 / D, AL.mult)
    if ln:
        tt(brow, crow, crow, AL.mult)
        sv = tt(brow, arow, brow, AL.subtract, s='V')
    else:
        sv = vcp(brow, arow, s='V')
    w8('s', 'V', sv)
    act(brow, brow, AF.Sqrt, bias=g.fc(eps), s='S')
    w8('v', 'S', g.cnt['S'])
    vrec(brow, brow)
    if ln:
        tt(crow, crow, brow, AL.mult)
        sv = ts(crow, crow, -1.0, AL.mult, s='V')
    else:
        sv = vcp(g.mc_[0:1, 61:62], brow[0:1, 0:1], s='V')
    w8('t', 'V', sv)

    for nt in range(4):
        sl = slice(512 * nt, 512 * (nt + 1))
        tv = mm(ps[nt][:, :], cs['ones32'][0:1, :], brow[:, sl], True, True, s='T')
        w8('s', 'T', tv)
        act(fb[:, FB_ + 512 * nt:FB_ + 512 * (nt + 1)], ps[nt][:, :], AF.Copy, s='S')
        if ln:
            tv = mm(ps[4 + nt][:, :], cs['ones32'][0:1, :], crow[:, sl], True, True, s='T')
            w8('s', 'T', tv)
            act(fb[:, FC + 512 * nt:FC + 512 * (nt + 1)], ps[4 + nt][:, :], AF.Copy, s='S')
    w8('v', 'S', g.cnt['S'])

    vbar = g.cnt['V']
    for c in range(8):
        xo = dst[:, dst_off + T * c:dst_off + T * (c + 1)]
        if from_dram:
            xf = fb[:, FA:FA + T]
            w8('y', 'V', vbar if c == 0 else vmark_cast)
            lv = ydma(xf, xsrc[128 * c:128 * (c + 1), :])
            w8('v', 'X', lv)
            vmark_cast = tt(xo, xf, fb[:, FB_:FB_ + T], AL.mult, s='V')
        else:
            xi = xsrc[0][:, xsrc[1] + T * c:xsrc[1] + T * (c + 1)]
            tt(xo, xi, fb[:, FB_:FB_ + T], AL.mult)
        if ln:
            tt(xo, xo, fb[:, FC:FC + T], AL.add, s='V')
        else:
            ts(xo, xo, wcol[:, c:c + 1], AL.mult, s='V')
    bar()


def emit_l0_attn(g):
    cs, fb, wb, xn, xm, ps = g.cs, g.fb, g.wb, g.xn, g.xm, g.ps
    psA, psB = g.psA, g.psB
    mm, act, tt, ts, vcp, vred, vrec, tr = g.mm, g.act, g.tt, g.ts, g.vcp, g.vred, g.vrec, g.tr
    ydma, gdma, w8, bar = g.ydma, g.gdma, g.w8, g.bar
    mc_ = g.mc_
    QS, KS, VP, WA, WB2, WC, SQ, S0 = 0, 8192, 16384, 32768, 36864, 40960, 49152, 51200
    SALL, NEGB, BIAS2 = 6144, 6400, 6656

    wq_t = g.wgroup([(wb[:, WA + 512 * kc:WA + 512 * (kc + 1)],
                      g.wt['l0_qstr'][128 * kc:128 * (kc + 1), :]) for kc in range(8)], 'Wa')
    wk_t = g.wgroup([(wb[:, WB2 + 512 * kc:WB2 + 512 * (kc + 1)],
                      g.wt['l0_kstr'][128 * kc:128 * (kc + 1), :]) for kc in range(8)], 'Wb')
    wv_t = g.wgroup([(wb[:, WC + 1024 * kc:WC + 1024 * (kc + 1)],
                      g.wt['l0_vT'][128 * kc:128 * (kc + 1), :]) for kc in range(8)], 'Wc')

    for side, Woff, Aoff in (('q', WA, QS), ('k', WB2, KS)):
        w8('t', 'Wa' if side == 'q' else 'Wb', wq_t if side == 'q' else wk_t)
        for c4 in range(4):
            for nt in range(4):
                bank = ps[(c4 * 4 + nt) % 2]
                if not (side == 'q' and c4 == 0 and nt < 2):
                    w8('t', 'V', g.cnt['V'])
                for kc in range(8):
                    mm(bank, wb[:, Woff + 512 * kc + 128 * c4:Woff + 512 * kc + 128 * (c4 + 1)],
                       xn[:, T * kc + 512 * nt:T * kc + 512 * (nt + 1)],
                       kc == 0, kc == 7, s='T' if kc == 7 else None)
                w8('v', 'T', g.cnt['T'])
                vcp(wb[:, Aoff + 2048 * c4 + 512 * nt:Aoff + 2048 * c4 + 512 * (nt + 1)], bank)
    w8('t', 'Wc', wv_t)
    for mc in range(16):
        for vh in range(2):
            bank = ps[2 + (mc * 2 + vh) % 2]
            w8('t', 'V', g.cnt['V'])
            for kc in range(8):
                mm(bank, xn[:, T * kc + 128 * mc:T * kc + 128 * (mc + 1)],
                   wb[:, WC + 1024 * kc + 512 * vh:WC + 1024 * kc + 512 * (vh + 1)],
                   kc == 0, kc == 7, s='T' if kc == 7 else None)
            w8('v', 'T', g.cnt['T'])
            vcp(wb[:, VP + 1024 * mc + 512 * vh:VP + 1024 * mc + 512 * (vh + 1)], bank)
    vdone = g.cnt['V']
    w8('g', 'V', vdone)
    wo_t = g.wgroup([(wb[:, WC + 1024 * hc:WC + 1024 * (hc + 1)],
                      g.wt['l0_opT'][128 * hc:128 * (hc + 1), :]) for hc in range(8)], 'Wd')

    for c4 in range(4):
        sq = wb[:, SQ:SQ + T]
        qs = wb[:, QS + 2048 * c4:QS + 2048 * (c4 + 1)]
        if c4 > 0:
            w8('v', 'T', g.cnt['T'])
        tt(sq, qs, qs, AL.mult, s='V')
        w8('t', 'V', g.cnt['V'])
        for nt in range(4):
            if c4 > 0 and nt == 0:
                w8('t', 'V', g.cnt['V'])
            mm(ps[4 + nt], cs['bsumL0'][:, :], sq[:, 512 * nt:512 * (nt + 1)],
               True, True, s='T' if nt == 3 else None)
        w8('v', 'T', g.cnt['T'])
        for nt in range(4):
            vred(mc_[:, 30 + nt:31 + nt], ps[4 + nt], AL.max)
        tt(mc_[:, 30:31], mc_[:, 30:31], mc_[:, 31:32], AL.max)
        tt(mc_[:, 32:33], mc_[:, 32:33], mc_[:, 33:34], AL.max)
        tt(mc_[:, 20 + c4:21 + c4], mc_[:, 30:31], mc_[:, 32:33], AL.max)
        ts(mc_[:, 20 + c4:21 + c4], mc_[:, 20 + c4:21 + c4], 0.0625, AL.mult, s='V')
    for c4 in range(4):
        sq = wb[:, SQ:SQ + T]
        ks = wb[:, KS + 2048 * c4:KS + 2048 * (c4 + 1)]
        w8('v', 'T', g.cnt['T'])
        tt(sq, ks, ks, AL.mult, s='V')
        w8('t', 'V', g.cnt['V'])
        for nt in range(4):
            if nt == 0:
                w8('t', 'S', g.cnt['S'])
            mm(ps[4 + nt], cs['bsumL0'][:, :], sq[:, 512 * nt:512 * (nt + 1)],
               True, True, s='T' if nt == 3 else None)
        w8('s', 'T', g.cnt['T'])
        for nt in range(4):
            act(fb[:, FB_ + 512 * nt:FB_ + 512 * (nt + 1)], ps[4 + nt], AF.Sqrt,
                scale=mc_[:, 20 + c4:21 + c4], s='S')
        w8('t', 'S', g.cnt['S'])
        for mc in range(16):
            tv = tr(ps[mc % 2][:, 0:128], fb[:, FB_ + 128 * mc:FB_ + 128 * (mc + 1)],
                    cs['ident'][:, :], s='T')
            w8('s', 'T', tv)
            for j in range(4):
                act(fb[:, NEGB + 16 * mc + 4 * c4 + j:NEGB + 16 * mc + 4 * c4 + j + 1],
                    ps[mc % 2][:, 32 * j:32 * j + 1], AF.Copy,
                    scale=g.fc(-1.0), bias=g.fc(25.0), s='S')
            w8('t', 'S', g.cnt['S'])
    bar()
    if g.tap == 'l0bnd':
        _tap_out(g, [(g.dbg[0:128, 0:256], fb[:, NEGB:NEGB + 256]),
                     (g.dbg[128:256, 0:1024], wb[:, VP:VP + 1024]),
                     (g.dbg[384:512, 0:2048], wb[:, QS:QS + 2048]),
                     (g.dbg[512:640, 0:2048], wb[:, KS:KS + 2048])])
        return True

    sprev = {}
    vlast = None
    for mc in range(16):
        for q4 in range(4):
            bs = 0 if q4 % 2 == 0 else 4
            c4 = q4
            for nt in range(4):
                for j in range(4):
                    key = bs + j
                    if key in sprev:
                        w8('t', 'S', sprev[key])
                    mm(ps[key],
                       wb[32 * j:32 * j + 16,
                          KS + 2048 * c4 + 128 * mc:KS + 2048 * c4 + 128 * (mc + 1)],
                       wb[32 * j:32 * j + 16,
                          QS + 2048 * c4 + 512 * nt:QS + 2048 * c4 + 512 * (nt + 1)],
                       True, True, tp=(32 * j, 0), s='T' if j == 3 else None)
                w8('s', 'T', g.cnt['T'])
                if nt == 0 and vlast is not None:
                    w8('s', 'V', vlast)
                for j in range(4):
                    h = 4 * q4 + j
                    sprev[bs + j] = act(
                        xm[:, 512 * j:512 * (j + 1)], ps[bs + j], AF.Exp,
                        bias=fb[:, NEGB + 16 * mc + h:NEGB + 16 * mc + h + 1],
                        scale=g.fc(0.25),
                        accum=mc_[:, 36 + 4 * j + nt:37 + 4 * j + nt], s='S')
            w8('v', 'S', g.cnt['S'])
            for j in range(4):
                h = 4 * q4 + j
                vred(fb[:, SALL + 16 * mc + h:SALL + 16 * mc + h + 1],
                     mc_[:, 36 + 4 * j:36 + 4 * j + 4], AL.add, s='V')
            vlast = g.cnt['V']
    w8('s', 'V', g.cnt['V'])
    act(fb[:, BIAS2:BIAS2 + 256], fb[:, SALL:SALL + 256], AF.Ln, s='S')
    w8('v', 'S', g.cnt['S'])
    tt(fb[:, BIAS2:BIAS2 + 256], fb[:, NEGB:NEGB + 256], fb[:, BIAS2:BIAS2 + 256],
       AL.subtract, s='V')
    bar()
    if g.tap == 'l0S':
        _tap_out(g, [(g.dbg[0:128, 0:256], fb[:, BIAS2:BIAS2 + 256]),
                     (g.dbg[128:256, 0:1024], wb[:, VP:VP + 1024]),
                     (g.dbg[256:384, 0:256], fb[:, SALL:SALL + 256])])
        return True

    sprevB = {}
    for q4 in range(4):
        c4 = q4
        for nt in range(4):
            evq = {}
            for mc in range(17):
                if mc < 16:
                    pp2 = mc % 2
                    for j in range(4):
                        if j in sprevB:
                            w8('t', 'S', sprevB[j])
                        mm(ps[4 + j],
                           wb[32 * j:32 * j + 16,
                              KS + 2048 * c4 + 128 * mc:KS + 2048 * c4 + 128 * (mc + 1)],
                           wb[32 * j:32 * j + 16,
                              QS + 2048 * c4 + 512 * nt:QS + 2048 * c4 + 512 * (nt + 1)],
                           True, True, tp=(32 * j, 0), s='T' if j == 3 else None)
                    w8('s', 'T', g.cnt['T'])
                    for j in range(4):
                        h = 4 * q4 + j
                        ev = act(xm[:, 2048 * pp2 + 512 * j:2048 * pp2 + 512 * (j + 1)],
                                 ps[4 + j], AF.Exp,
                                 bias=fb[:, BIAS2 + 16 * mc + h:BIAS2 + 16 * mc + h + 1],
                                 scale=g.fc(0.25), s='S')
                        sprevB[j] = ev
                        evq[(mc, j)] = ev
                if mc > 0:
                    pm = mc - 1
                    pp2p = pm % 2
                    for j in range(4):
                        h = 4 * q4 + j
                        pb = 64 * (h - (h % 2))
                        w8('t', 'S', evq[(pm, j)])
                        mm(ps[j], wb[:, VP + 1024 * pm + pb:VP + 1024 * pm + pb + 128],
                           xm[:, 2048 * pp2p + 512 * j:2048 * pp2p + 512 * (j + 1)],
                           pm == 0, pm == 15, sgc=True,
                           s='T' if (pm == 15 and j == 3) else None)
            w8('s', 'T', g.cnt['T'])
            for j in range(4):
                h = 4 * q4 + j
                r0 = 64 * (h % 2)
                ev = act(wb[r0:r0 + 64, S0:S0 + 512], ps[j][r0:r0 + 64, :], AF.Copy, s='S')
                w8('y', 'S', ev)
                ov = ydma(g.ho[64 * h:64 * (h + 1), 512 * nt:512 * (nt + 1)],
                          wb[r0:r0 + 64, S0:S0 + 512], s='O')
                g.q['s'].append(lambda e, S=g.sem['O'], V=ov: e.wait_ge(S, V))
            w8('t', 'S', g.cnt['S'])
    bar()
    if g.tap == 'l0o':
        _tap_out(g, [(g.dbg[:, :], g.ho[:, :])])
        return True

    w8('g', 'O', g.cnt['O'])
    lo_t = g.wgroup([(wb[:, QS + 2048 * hc:QS + 2048 * (hc + 1)],
                      g.ho[128 * hc:128 * (hc + 1), :]) for hc in range(8)], 'Wa')
    w8('t', 'Wd', wo_t)
    w8('t', 'Wa', lo_t)
    for nt in range(4):
        for dc in range(8):
            bank = ps[(nt * 8 + dc) % 2]
            if nt + dc > 0:
                w8('t', 'S', g.cnt['S'])
            for hc in range(8):
                mm(bank, wb[:, WC + 1024 * hc + 128 * dc:WC + 1024 * hc + 128 * (dc + 1)],
                   wb[:, QS + 2048 * hc + 512 * nt:QS + 2048 * hc + 512 * (nt + 1)],
                   hc == 0, hc == 7, s='T' if hc == 7 else None)
            w8('s', 'T', g.cnt['T'])
            ev = act(wb[:, S0:S0 + 512], bank, AF.Copy, s='S')
            w8('y', 'S', ev)
            ov = ydma(g.hs[128 * dc:128 * (dc + 1), 512 * nt:512 * (nt + 1)],
                      wb[:, S0:S0 + 512], s='O')
            g.q['s'].append(lambda e, S=g.sem['O'], V=ov: e.wait_ge(S, V))
    bar()
    return False


def emit_l0_ffn_tail(g):
    cs, fb, wb, xn, xm, ps = g.cs, g.fb, g.wb, g.xn, g.xm, g.ps
    mm, act, tt, ts, vcp, vred, vrec, tr = g.mm, g.act, g.tt, g.ts, g.vcp, g.vred, g.vrec, g.tr
    ydma, gdma, w8, bar = g.ydma, g.gdma, g.w8, g.bar
    mc_ = g.mc_
    W1s = (0, 4096)
    W2s = (8192, 12288)
    H1, CB, CBC, B2, RT, SB, GG = 16384, 20480, 22528, 23552, 24576, 24704, 25728
    bv = ydma(mc_[:, 20:52], g.wt['l0_b1c'][:, :])
    bv = ydma(mc_[0:8, 52:53], g.wt['l0_rtb'][:, :])
    rtv = g.wgroup([(wb[:, RT + 8 * kc:RT + 8 * (kc + 1)],
                     g.wt['l0_rtr'][128 * kc:128 * (kc + 1), :]) for kc in range(8)], 'Wa')
    b2v = g.wgroup([(wb[0:8, B2:B2 + 1024], g.wt['l0_b2r'][:, :])], 'Wb')

    w8('t', 'Wa', rtv)
    for nt in range(4):
        bank = ps[4 + nt]
        for kc in range(8):
            mm(bank[0:8, :], wb[:, RT + 8 * kc:RT + 8 * (kc + 1)],
               xn[:, T * kc + 512 * nt:T * kc + 512 * (nt + 1)],
               kc == 0, kc == 7, s='T' if kc == 7 else None)
        w8('s', 'T', g.cnt['T'])
        w8('s', 'X', bv)
        act(fb[0:8, FA + 512 * nt:FA + 512 * (nt + 1)], bank[0:8, :], AF.Identity,
            bias=mc_[0:8, 52:53], s='S')
    LG = 6144
    for tsub in range(16):
        w8('t', 'S', g.cnt['S'])
        tv = tr(ps[0][:, 0:8], fb[0:8, FA + 128 * tsub:FA + 128 * (tsub + 1)],
                cs['ident'][0:8, 0:8], s='T')
        w8('v', 'T', tv)
        lg = fb[:, LG:LG + 8]
        vcp(lg, ps[0][:, 0:8])
        m1n = mc_[:, 53:54]
        vred(m1n, lg, AL.max, negate=True)
        ts(fb[:, LG + 8:LG + 16], lg, m1n, AL.add, s='V')
        ts(fb[:, LG + 16:LG + 24], fb[:, LG + 8:LG + 16], 0.0, AL.is_ge, s='V')
        ts(fb[:, LG + 24:LG + 32], fb[:, LG + 16:LG + 24], g.fc(-1e30), AL.mult, s='V')
        tt(fb[:, LG + 24:LG + 32], fb[:, LG + 8:LG + 16], fb[:, LG + 24:LG + 32], AL.add)
        vred(mc_[:, 54:55], fb[:, LG + 24:LG + 32], AL.max)
        w8('s', 'V', g.cnt['V'])
        act(mc_[:, 55:56], mc_[:, 54:55], AF.Exp, s='S')
        act(fb[:, LG + 32:LG + 40], fb[:, LG + 8:LG + 16], AF.Exp, s='S')
        w8('v', 'S', g.cnt['S'])
        ts(mc_[:, 55:56], mc_[:, 55:56], 1.0, AL.add, s='V')
        vrec(mc_[:, 55:56], mc_[:, 55:56])
        ts(fb[:, LG + 40:LG + 48], fb[:, LG + 8:LG + 16], mc_[:, 54:55], AL.is_ge, s='V')
        tt(fb[:, LG + 32:LG + 40], fb[:, LG + 32:LG + 40], fb[:, LG + 40:LG + 48], AL.mult)
        ts(fb[:, LG:LG + 8], fb[:, LG + 32:LG + 40], mc_[:, 55:56], AL.mult, s='V')
        w8('t', 'V', g.cnt['V'])
        tv = tr(ps[1][0:8, 0:128], fb[:, LG:LG + 8], cs['ident'][:, :], s='T')
        w8('s', 'T', tv)
        act(wb[0:8, CB + 128 * tsub:CB + 128 * (tsub + 1)], ps[1][0:8, 0:128],
            AF.Copy, s='S')
    bar()
    if g.tap == 'l0comb':
        _tap_out(g, [(g.dbg[0:8, 0:T], wb[0:8, CB:CB + T])])
        return True

    for half in range(2):
        bar()
        for eng in ('t', 's', 'v', 'g', 'y'):
            w8(eng, 'O', g.cnt['O'])
        t0 = 1024 * half
        w8('t', 'Wb', b2v)
        for dc in range(8):
            for t2 in range(2):
                bank = ps[(dc * 2 + t2) % 4]
                if dc + t2 > 0:
                    w8('t', 'S', g.cnt['S'])
                mm(bank, wb[0:8, B2 + 128 * dc:B2 + 128 * (dc + 1)],
                   wb[0:8, CB + t0 + 512 * t2:CB + t0 + 512 * (t2 + 1)],
                   True, True, s='T')
                w8('s', 'T', g.cnt['T'])
                act(fb[:, 1024 * dc + 512 * t2:1024 * dc + 512 * (t2 + 1)],
                    bank, AF.Copy, s='S')
        w8('v', 'S', g.cnt['S'])
        wsem = {}
        tm_w1, tm_w2 = {}, {}
        for e in range(8):
            sl = e % 2
            if e >= 2:
                w8('g', 'T', tm_w1[e - 2])
                w8('g', 'T', tm_w2[e - 2])
            elif half == 1:
                w8('g', 'T', g.l0_tm_last)
            wsem[('w1', e)] = g.wgroup(
                [(wb[:, W1s[sl] + 512 * kc:W1s[sl] + 512 * (kc + 1)],
                  g.wt['l0_W1'][e, 128 * kc:128 * (kc + 1), :]) for kc in range(8)],
                'Wa' if sl == 0 else 'Wb')
            wsem[('w2', e)] = g.wgroup(
                [(wb[:, W2s[sl] + 1024 * fc:W2s[sl] + 1024 * (fc + 1)],
                  g.wt['l0_W2'][e, 128 * fc:128 * (fc + 1), :]) for fc in range(4)],
                'Wc' if sl == 0 else 'Wd')
            for t2 in range(2):
                bank = ps[4 + t2]
                w8('t', 'V', g.cnt['V'])
                mm(bank, cs['sel8'][:, 128 * e:128 * (e + 1)],
                   wb[0:8, CB + t0 + 512 * t2:CB + t0 + 512 * (t2 + 1)],
                   True, True, s='T')
                w8('v', 'T', g.cnt['T'])
                vcp(wb[:, CBC + 512 * t2:CBC + 512 * (t2 + 1)], bank)
            w8('t', 'Wa' if sl == 0 else 'Wb', wsem[('w1', e)])
            for fc in range(4):
                for t2 in range(2):
                    bank = ps[4 + 2 + (fc * 2 + t2) % 2]
                    w8('t', 'S', g.cnt['S'])
                    for kc in range(8):
                        mm(bank, wb[:, W1s[sl] + 512 * kc + 128 * fc:W1s[sl] + 512 * kc + 128 * (fc + 1)],
                           xn[:, T * kc + t0 + 512 * t2:T * kc + t0 + 512 * (t2 + 1)],
                           kc == 0, kc == 7, s='T' if kc == 7 else None)
                    w8('s', 'T', g.cnt['T'])
                    act(wb[:, H1 + 1024 * fc + 512 * t2:H1 + 1024 * fc + 512 * (t2 + 1)],
                        bank, AF.Gelu, bias=mc_[:, 20 + 4 * e + fc:21 + 4 * e + fc], s='S')
            tm_w1[e] = g.cnt['T']
            w8('v', 'S', g.cnt['S'])
            for fc in range(4):
                for t2 in range(2):
                    tt(wb[:, H1 + 1024 * fc + 512 * t2:H1 + 1024 * fc + 512 * (t2 + 1)],
                       wb[:, H1 + 1024 * fc + 512 * t2:H1 + 1024 * fc + 512 * (t2 + 1)],
                       wb[:, CBC + 512 * t2:CBC + 512 * (t2 + 1)], AL.mult, s='V')
            hv = g.cnt['V']
            w8('t', 'V', hv)
            w8('t', 'Wc' if sl == 0 else 'Wd', wsem[('w2', e)])
            for dc in range(8):
                for t2 in range(2):
                    bank = ps[(dc * 2 + t2) % 4]
                    w8('t', 'V', g.cnt['V'])
                    for fc in range(4):
                        mm(bank, wb[:, W2s[sl] + 1024 * fc + 128 * dc:W2s[sl] + 1024 * fc + 128 * (dc + 1)],
                           wb[:, H1 + 1024 * fc + 512 * t2:H1 + 1024 * fc + 512 * (t2 + 1)],
                           fc == 0, fc == 3, s='T' if fc == 3 else None)
                    w8('v', 'T', g.cnt['T'])
                    tt(fb[:, 1024 * dc + 512 * t2:1024 * dc + 512 * (t2 + 1)],
                       fb[:, 1024 * dc + 512 * t2:1024 * dc + 512 * (t2 + 1)],
                       bank, AL.add, s='V')
            tm_w2[e] = g.cnt['T']
        g.l0_tm_last = g.cnt['T']
        bar()
        if g.tap == 'l0ffn' and half == 0:
            _tap_out(g, [(g.dbg[128 * dc:128 * (dc + 1), 0:1024],
                          fb[:, 1024 * dc:1024 * (dc + 1)]) for dc in range(8)])
            return True
        w8('v', 'T', g.cnt['T'])
        for dc in range(8):
            for t2 in range(2):
                osl = fb[:, 1024 * dc + 512 * t2:1024 * dc + 512 * (t2 + 1)]
                gsl = wb[:, GG:GG + 512]
                ssl = wb[:, GG + 512:GG + 1024]
                w8('s', 'V', g.cnt['V'])
                w8('s', 'O', g.cnt['O'])
                sv = act(gsl, osl, AF.Silu, s='S')
                w8('y', 'S', sv)
                lv1 = ydma(osl, g.x_fm[128 * dc:128 * (dc + 1),
                                       t0 + 512 * t2:t0 + 512 * (t2 + 1)])
                lv2 = ydma(ssl, g.hs[128 * dc:128 * (dc + 1),
                                     t0 + 512 * t2:t0 + 512 * (t2 + 1)])
                w8('v', 'X', lv2)
                tt(ps[6], gsl, ssl, AL.mult)
                tt(osl, osl, ps[6], AL.add, s='V')
                w8('y', 'V', g.cnt['V'])
                ydma(g.hx1[128 * dc:128 * (dc + 1), t0 + 512 * t2:t0 + 512 * (t2 + 1)],
                     osl, s='O')
    w8('y', 'O', g.cnt['O'])
    bar()
    return False


def emit_pos_bcast(g):
    cs, fb, ps = g.cs, g.fb, g.ps
    lv = g.ydma(fb[0:1, FA:FA + T], g.cst['pos_row'][0:1, :])
    g.w8('t', 'X', lv)
    g.w8('t', 'S', g.cnt['S'])
    for nt in range(4):
        tv = g.mm(ps[nt][:, :], cs['ones32'][0:1, :], fb[0:1, FA + 512 * nt:FA + 512 * (nt + 1)],
                  True, True, s='T')
        g.w8('s', 'T', tv)
        g.act(fb[:, FPOS + 512 * nt:FPOS + 512 * (nt + 1)], ps[nt][:, :], AF.Copy, s='S')
    g.bar()


def emit_l1_attention(g, P):
    cs, fb, wb, xm, ps = g.cs, g.fb, g.wb, g.xm, g.ps
    psA, psB = g.psA, g.psB
    mm, act, tt, ts, vcp, vred, vrec, tr = g.mm, g.act, g.tt, g.ts, g.vcp, g.vred, g.vrec, g.tr
    ydma, gdma, w8, bar = g.ydma, g.gdma, g.w8, g.bar
    mc_ = g.mc_
    QR, KR, VP, WS = 16384, 32768, 0, 49152
    OS = 51200
    NB = 0
    SC1, SC2, MSK, EX = 8192, 10240, 12288, 12416
    src = P['src']
    scale, tau, dsh = P['scale'], P['tau'], P['dsh']
    stab = P['stab']

    wvv = g.wgroup([(wb[:, QR + 1024 * kc:QR + 1024 * (kc + 1)],
                     P['wv_moving'][128 * kc:128 * (kc + 1), :]) for kc in range(8)], 'Wa')
    if P.get('vb_row') is not None:
        vbv = g.wgroup([(wb[0:1, WS:WS + 1024], P['vb_row'])], 'Wb')
        w8('t', 'Wb', vbv)
    w8('t', 'Wa', wvv)
    g._ws_tmark = None
    g._examk = {}
    hasvb = P.get('vb_row') is not None
    for mc in range(16):
        for vh in range(2):
            bank = ps[4 + (mc * 2 + vh) % 2]
            w8('t', 'V', g.cnt['V'])
            for kc in range(8):
                mm(bank, src[0][:, src[1] + T * kc + 128 * mc:src[1] + T * kc + 128 * (mc + 1)],
                   wb[:, QR + 1024 * kc + 512 * vh:QR + 1024 * kc + 512 * (vh + 1)],
                   kc == 0, kc == 7,
                   s='T' if (kc == 7 and not hasvb) else None)
            if hasvb:
                mm(bank, cs['ones_bf'][0:1, :], wb[0:1, WS + 512 * vh:WS + 512 * (vh + 1)],
                   False, True, sgc=True, s='T')
            w8('v', 'T', g.cnt['T'])
            vcp(wb[:, VP + 1024 * mc + 512 * vh:VP + 1024 * mc + 512 * (vh + 1)], bank)
    bar()

    if P['shared_tables']:
        _build_tables(g, P, 0)
    for side, Aoff, wsel, wpsel, bsel, bpsel, thsel in (
            ('q', QR, P['wq'], P['wqp'], P.get('qb'), P.get('qpb'), 'theta_q'),
            ('k', KR, P['wk'], P['wkp'], P.get('kb'), P.get('kpb'), 'theta_k')):
        for oc in range(8):
            sems = ('Wa', 'Wb') if side == 'q' else ('Wc', 'Wd')
            if g._ws_tmark is not None:
                w8('g', 'T', g._ws_tmark)
            lv1 = g.wgroup([(wb[:, WS + 256 * kc:WS + 256 * kc + 128],
                             wsel(oc)[128 * kc:128 * (kc + 1), :]) for kc in range(8)], sems[0])
            lv2 = g.wgroup([(wb[:, WS + 256 * kc + 128:WS + 256 * kc + 256],
                             wpsel(oc)[128 * kc:128 * (kc + 1), :]) for kc in range(8)], sems[1])
            if not P['shared_tables']:
                _build_tables(g, P, oc, thsel)
            w8('t', sems[0], lv1)
            w8('t', sems[1], lv2)
            w8('t', 'S', g.cnt['S'])
            for pp, bias in ((0, bsel), (1, bpsel)):
                for nt in range(4):
                    bank = (psA if pp == 0 else psB)[:, 512 * nt:512 * (nt + 1)]
                    for kc in range(8):
                        mm(bank, wb[:, WS + 256 * kc + 128 * pp:WS + 256 * kc + 128 * (pp + 1)],
                           src[0][:, src[1] + T * kc + 512 * nt:src[1] + T * kc + 512 * (nt + 1)],
                           kc == 0, kc == 7, s='T' if kc == 7 else None)
            w8('s', 'T', g.cnt['T'])
            src_is_xm = src[0] is g.xm
            for pp, bias in ((0, bsel), (1, bpsel)):
                pst = psA if pp == 0 else psB
                if src_is_xm:
                    dstc = fb[:, (FA if pp == 0 else FPOS):(FA if pp == 0 else FPOS) + 2048]
                else:
                    dstc = g.xm[:, (SC1 if pp == 0 else SC2):(SC1 if pp == 0 else SC2) + 2048]
                if bias is not None:
                    act(dstc, pst[:, :], AF.Identity, bias=bias(oc), s='S')
                else:
                    act(dstc, pst[:, :], AF.Copy, s='S')
            if g.tap == P.get('tap_tbl') and side == 'q' and oc == 0:
                _tap_out(g, [(g.dbg[0:128, :], g.xm[:, SC1:SC1 + 2048]),
                             (g.dbg[128:256, :], g.xm[:, SC2:SC2 + 2048]),
                             (g.dbg[256:384, :], fb[:, FB_:FB_ + T]),
                             (g.dbg[384:512, :], fb[:, FC:FC + T])])
                return True
            w8('v', 'S', g.cnt['S'])
            if src_is_xm:
                t0 = fb[:, FA:FA + 2048]
                t1 = fb[:, FPOS:FPOS + 2048]
            else:
                t0 = g.xm[:, SC1:SC1 + 2048]
                t1 = g.xm[:, SC2:SC2 + 2048]
            out = wb[:, Aoff + 2048 * oc:Aoff + 2048 * (oc + 1)]
            tt(out, t0, fb[:, FC:FC + T], AL.mult)
            tt(t1, t1, fb[:, FB_:FB_ + T], AL.mult)
            tt(out, out, t1, AL.add, s='V')
            w8('t', 'V', g.cnt['V'])
            g._ws_tmark = g.cnt['T']
    bar()
    if g.tap == P.get('tap_qk'):
        _tap_out(g, [(g.dbg[0:128, 0:T], wb[:, QR:QR + T]),
                     (g.dbg[128:256, 0:T], wb[:, KR:KR + T]),
                     (g.dbg[256:384, 0:1024], wb[:, VP:VP + 1024])])
        return True

    if P['stab']:
        for qt in range(4):
            for c in range(8):
                rbase = 64 * (c % 2)
                xtile = NB + 2048 * (c // 2)
                for j in range(2):
                    h = 2 * c + j
                    rh = rbase + 32 * j
                    sab = ps[2 + j]
                    evq = {}
                    for mc in range(17):
                        if mc < 16:
                            bank = ps[mc % 2]
                            if g._examk.get(mc % 2):
                                w8('t', 'S', g._examk[mc % 2])
                            mm(bank, wb[64 * j:64 * (j + 1), KR + 2048 * c + 128 * mc:KR + 2048 * c + 128 * (mc + 1)],
                               wb[64 * j:64 * (j + 1), QR + 2048 * c + 512 * qt:QR + 2048 * c + 512 * (qt + 1)],
                               True, True, sgc=True, s='T')
                            w8('s', 'T', g.cnt['T'])
                            evq[mc] = act(g.xm[:, EX + 512 * (mc % 2):EX + 512 * (mc % 2 + 1)],
                                          bank, AF.Exp, scale=g.fc(scale / tau), s='S')
                            g._examk[mc % 2] = evq[mc]
                        if mc > 0:
                            pm = mc - 1
                            w8('t', 'S', evq[pm])
                            mm(sab, cs['ones_bf'][:, :],
                               g.xm[:, EX + 512 * (pm % 2):EX + 512 * (pm % 2 + 1)],
                               pm == 0, pm == 15, sgc=True, s='T' if pm == 15 else None)
                    w8('s', 'T', g.cnt['T'])
                    act(fb[rh:rh + 1, FA:FA + 512], sab[rh:rh + 1, :], AF.Ln, s='S')
                    w8('v', 'S', g.cnt['S'])
                    ts(g.xm[rh:rh + 1, xtile + 512 * qt:xtile + 512 * (qt + 1)],
                       fb[rh:rh + 1, FA:FA + 512],
                       g.fc(-tau / scale), AL.mult, g.fc(dsh / scale), AL.add, s='V')
                    w8('t', 'V', g.cnt['V'])
        bar()

    for qt in range(4):
        for c in range(8):
            rbase = 64 * (c % 2)
            xtile = NB + 2048 * (c // 2)
            for j in range(2):
                h = 2 * c + j
                rh = rbase + 32 * j
                ob = ps[2 + j]
                sbb = ps[4 + j]
                pb = 64 * (h - (h % 2))
                evq = {}
                for mc in range(17):
                    if mc < 16:
                        bank = ps[mc % 2]
                        if g._examk.get(mc % 2):
                            w8('t', 'S', g._examk[mc % 2])
                        if stab:
                            mm(bank, cs['ones_bf'][rh:rh + 1, :],
                               g.xm[rh:rh + 1, xtile + 512 * qt:xtile + 512 * (qt + 1)],
                               True, True, tp=(rh, 0), sgc=True)
                        mm(bank, wb[64 * j:64 * (j + 1), KR + 2048 * c + 128 * mc:KR + 2048 * c + 128 * (mc + 1)],
                           wb[64 * j:64 * (j + 1), QR + 2048 * c + 512 * qt:QR + 2048 * c + 512 * (qt + 1)],
                           not stab, True, sgc=True, s='T')
                        w8('s', 'T', g.cnt['T'])
                        evq[mc] = act(g.xm[:, EX + 512 * (mc % 2):EX + 512 * (mc % 2 + 1)],
                                      bank, AF.Exp, scale=g.fc(scale), s='S')
                        g._examk[mc % 2] = evq[mc]
                    if mc > 0:
                        pm = mc - 1
                        w8('t', 'S', evq[pm])
                        mm(ob, wb[:, VP + 1024 * pm + pb:VP + 1024 * pm + pb + 128],
                           g.xm[:, EX + 512 * (pm % 2):EX + 512 * (pm % 2 + 1)],
                           pm == 0, pm == 15, sgc=True)
                        mm(sbb, cs['ones_bf'][:, :],
                           g.xm[:, EX + 512 * (pm % 2):EX + 512 * (pm % 2 + 1)],
                           pm == 0, pm == 15, sgc=True, s='T' if pm == 15 else None)
                w8('v', 'T', g.cnt['T'])
                r0 = 64 * (h % 2)
                vrec(fb[:, FA:FA + 512], sbb)
                tt(wb[r0:r0 + 64, OS:OS + 512],
                   ob[r0:r0 + 64, :], fb[r0:r0 + 64, FA:FA + 512], AL.mult, s='V')
                w8('y', 'V', g.cnt['V'])
                ov = ydma(g.ho[64 * h:64 * (h + 1), 512 * qt:512 * (qt + 1)],
                          wb[r0:r0 + 64, OS:OS + 512], s='O')
                g.q['v'].append(lambda e, S=g.sem['O'], V=ov: e.wait_ge(S, V))
                w8('t', 'V', g.cnt['V'])
    bar()
    if g.tap == P.get('tap_o'):
        _tap_out(g, [(g.dbg[:, :], g.ho[:, :])])
        return True

    w8('g', 'O', g.cnt['O'])
    lo_t = g.wgroup([(wb[:, QR + 2048 * hc:QR + 2048 * (hc + 1)],
                      g.ho[128 * hc:128 * (hc + 1), :]) for hc in range(8)], 'Wa')
    wo_t = g.wgroup([(wb[:, KR + 1024 * hc:KR + 1024 * (hc + 1)],
                      P['wo'][128 * hc:128 * (hc + 1), :]) for hc in range(8)], 'Wb')
    w8('t', 'Wa', lo_t)
    w8('t', 'Wb', wo_t)
    ob = P.get('ob')
    for nt in range(4):
        for dc in range(8):
            bank = ps[(nt * 8 + dc) % 2]
            if nt + dc > 0:
                w8('t', 'S', g.cnt['S'])
            for hc in range(8):
                mm(bank, wb[:, KR + 1024 * hc + 128 * dc:KR + 1024 * hc + 128 * (dc + 1)],
                   wb[:, QR + 2048 * hc + 512 * nt:QR + 2048 * hc + 512 * (nt + 1)],
                   hc == 0, hc == 7, s='T' if hc == 7 else None)
            w8('s', 'T', g.cnt['T'])
            if ob is not None:
                ev = act(wb[:, OS:OS + 512], bank, AF.Identity, bias=ob(dc), s='S')
            else:
                ev = act(wb[:, OS:OS + 512], bank, AF.Copy, s='S')
            w8('y', 'S', ev)
            ov = ydma(P['dst'][128 * dc:128 * (dc + 1), 512 * nt:512 * (nt + 1)],
                      wb[:, OS:OS + 512], s='O')
            g.q['s'].append(lambda e, S=g.sem['O'], V=ov: e.wait_ge(S, V))
    bar()
    return False


def _build_tables(g, P, oc, thsel='theta_q'):
    fb, ic = g.fb, g.ic
    th = P[thsel](oc)
    g.w8('v', 'T', g.cnt['T'])
    g.w8('v', 'S', g.cnt['S'])
    for dst, s2 in ((FB_, P['pos_s2_sin']), (FC, P['pos_s2_cos'])):
        g.ts(fb[:, dst:dst + T], fb[:, FPOS:FPOS + T], th, AL.mult, s2, AL.add, s='V')
        g.ts(fb[:, dst:dst + T], fb[:, dst:dst + T], 1.0 / TWO_PI, AL.mult,
             1024.0, AL.add, s='V')
        for nt in range(4):
            sl = slice(dst + 512 * nt, dst + 512 * (nt + 1))
            g.vcp(ic[:, :], fb[:, sl])
            g.vcp(fb[:, FA:FA + 512], ic[:, :])
            g.tt(fb[:, sl], fb[:, sl], fb[:, FA:FA + 512], AL.subtract, s='V')
        g.ts(fb[:, FA:FA + T], fb[:, dst:dst + T], 0.5, AL.is_ge, s='V')
        g.tt(fb[:, dst:dst + T], fb[:, dst:dst + T], fb[:, FA:FA + T],
             AL.subtract, s='V')
        g.ts(fb[:, FA:FA + T], fb[:, dst:dst + T], -0.5, AL.is_lt, s='V')
        g.tt(fb[:, dst:dst + T], fb[:, dst:dst + T], fb[:, FA:FA + T],
             AL.add, s='V')
        g.ts(fb[:, dst:dst + T], fb[:, dst:dst + T], TWO_PI, AL.mult, s='V')
    g.w8('s', 'V', g.cnt['V'])
    g.act(fb[:, FB_:FB_ + T], fb[:, FB_:FB_ + T], AF.Sin, s='S')
    g.act(fb[:, FC:FC + T], fb[:, FC:FC + T], AF.Sin, s='S')
    g.w8('v', 'S', g.cnt['S'])


def emit_l1f_post(g):
    fb, wb, xn = g.fb, g.wb, g.xn
    tt, w8, ydma, gdma = g.tt, g.w8, g.ydma, g.gdma
    for dc in range(8):
        for t2 in range(4):
            sl = slice(512 * t2, 512 * (t2 + 1))
            w8('g', 'V', g.cnt['V'])
            hav = gdma(wb[:, 512:1024], g.ha[128 * dc:128 * (dc + 1), sl], s='Wc')
            w8('v', 'Wc', hav)
            w8('v', 'O', g.cnt['O'])
            tt(fb[:, FA:FA + 512], xn[:, T * dc:T * (dc + 1)][:, sl],
               wb[:, 512:1024], AL.add, s='V')
            w8('y', 'V', g.cnt['V'])
            ov = ydma(g.hx2[128 * dc:128 * (dc + 1), sl], fb[:, FA:FA + 512], s='O')
            g.q['v'].append(lambda e, S=g.sem['O'], V=ov: e.wait_ge(S, V))
    for eng in ('t', 's', 'v'):
        w8(eng, 'O', g.cnt['O'])
    emit_norm(g, g.hx2, 'rms', wcol=g.mc2[:, 56:64], dst=g.xm)


def emit_l1_moe_tail(g):
    cs, fb, wb, xn, xm, ps = g.cs, g.fb, g.wb, g.xn, g.xm, g.ps
    mm, act, tt, ts, vcp, vred, vrec, tr = g.mm, g.act, g.tt, g.ts, g.vcp, g.vred, g.vrec, g.tr
    ydma, gdma, w8, bar = g.ydma, g.gdma, g.w8, g.bar
    mc_, mc2 = g.mc_, g.mc2
    W1, W2, W3 = 0, 16384, 32768
    CB = 49152
    RT = 51200
    H1 = 0
    CBC, ST1, ST2, B3, B2E = 8192, 9216, 10240, 11264, 12288
    wbx = xn
    bv = ydma(mc2[:, 64:192], g.wt['l1f_b1c'][:, :])
    rtv = g.wgroup([(wb[:, RT + 8 * kc:RT + 8 * (kc + 1)],
                     g.wt['l1f_gateT'][128 * kc:128 * (kc + 1), :]) for kc in range(8)], 'Wa')
    w8('t', 'Wa', rtv)
    for nt in range(4):
        bank = ps[4 + nt]
        for kc in range(8):
            mm(bank[0:8, :], wb[:, RT + 8 * kc:RT + 8 * (kc + 1)],
               xm[:, T * kc + 512 * nt:T * kc + 512 * (nt + 1)],
               kc == 0, kc == 7, s='T' if kc == 7 else None)
        w8('s', 'T', g.cnt['T'])
        act(fb[0:8, FA + 512 * nt:FA + 512 * (nt + 1)], bank[0:8, :], AF.Copy, s='S')
    LG = 6144
    for tsub in range(16):
        w8('t', 'S', g.cnt['S'])
        tv = tr(ps[0][:, 0:8], fb[0:8, FA + 128 * tsub:FA + 128 * (tsub + 1)],
                cs['ident'][0:8, 0:8], s='T')
        w8('v', 'T', tv)
        lg = fb[:, LG:LG + 8]
        vcp(lg, ps[0][:, 0:8])
        m1n = mc_[:, 53:54]
        vred(m1n, lg, AL.max, negate=True)
        ts(fb[:, LG + 8:LG + 16], lg, m1n, AL.add, s='V')
        ts(fb[:, LG + 16:LG + 24], fb[:, LG + 8:LG + 16], 0.0, AL.is_ge, s='V')
        ts(fb[:, LG + 24:LG + 32], fb[:, LG + 16:LG + 24], g.fc(-1e30), AL.mult, s='V')
        tt(fb[:, LG + 24:LG + 32], fb[:, LG + 8:LG + 16], fb[:, LG + 24:LG + 32], AL.add)
        vred(mc_[:, 54:55], fb[:, LG + 24:LG + 32], AL.max)
        w8('s', 'V', g.cnt['V'])
        act(mc_[:, 55:56], mc_[:, 54:55], AF.Exp, s='S')
        act(fb[:, LG + 32:LG + 40], fb[:, LG + 8:LG + 16], AF.Exp, s='S')
        w8('v', 'S', g.cnt['S'])
        ts(mc_[:, 55:56], mc_[:, 55:56], 1.0, AL.add, s='V')
        vrec(mc_[:, 55:56], mc_[:, 55:56])
        ts(fb[:, LG + 40:LG + 48], fb[:, LG + 8:LG + 16], mc_[:, 54:55], AL.is_ge, s='V')
        tt(fb[:, LG + 32:LG + 40], fb[:, LG + 32:LG + 40], fb[:, LG + 40:LG + 48], AL.mult)
        ts(fb[:, LG:LG + 8], fb[:, LG + 32:LG + 40], mc_[:, 55:56], AL.mult, s='V')
        w8('t', 'V', g.cnt['V'])
        tv = tr(ps[1][0:8, 0:128], fb[:, LG:LG + 8], cs['ident'][:, :], s='T')
        w8('s', 'T', tv)
        act(wb[0:8, CB + 128 * tsub:CB + 128 * (tsub + 1)], ps[1][0:8, 0:128],
            AF.Copy, s='S')
    bar()
    if g.tap == 'l1comb':
        _tap_out(g, [(g.dbg[0:8, 0:T], wb[0:8, CB:CB + T])])
        return True

    b3v = g.wgroup([(wbx[0:8, B3:B3 + 1024], g.wt['l1f_b3r'][:, :])], 'Wb')
    g.vms(wb[0:1, 51200:51712], 1.0, s='V')
    w8('t', 'V', g.cnt['V'])
    w8('s', 'X', bv)
    tm_w = {}
    for half in range(2):
        bar()
        for eng in ('t', 's', 'v', 'g', 'y'):
            w8(eng, 'O', g.cnt['O'])
        t0 = 1024 * half
        w8('t', 'Wb', b3v)
        for dc in range(8):
            for t2 in range(2):
                bank = ps[(dc * 2 + t2) % 4]
                if dc + t2 > 0:
                    w8('t', 'S', g.cnt['S'])
                mm(bank, wbx[0:8, B3 + 128 * dc:B3 + 128 * (dc + 1)],
                   wb[0:8, CB + t0 + 512 * t2:CB + t0 + 512 * (t2 + 1)],
                   True, True, s='T')
                w8('s', 'T', g.cnt['T'])
                act(fb[:, 1024 * dc + 512 * t2:1024 * dc + 512 * (t2 + 1)],
                    bank, AF.Copy, s='S')
        w8('v', 'S', g.cnt['S'])
        wv_marks = {}

        def load_expert(e2):
            wv_marks[e2] = (
                g.wgroup([(wb[:, W1 + 2048 * kc:W1 + 2048 * (kc + 1)],
                           g.wt['l1f_w1T'][e2, 128 * kc:128 * (kc + 1), :])
                          for kc in range(8)], 'Wa'),
                g.wgroup([(wb[:, W2 + 2048 * kc:W2 + 2048 * (kc + 1)],
                           g.wt['l1f_w2T'][e2, 128 * kc:128 * (kc + 1), :])
                          for kc in range(8)], 'Wc'),
                g.wgroup([(wb[:, W3 + 1024 * fc:W3 + 1024 * (fc + 1)],
                           g.wt['l1f_w3T'][e2, 128 * fc:128 * (fc + 1), :])
                          for fc in range(16)], 'Wd'),
                g.wgroup([(wbx[0:1, B2E + 2048 * (e2 % 2):B2E + 2048 * (e2 % 2) + 2048],
                           g.wt['l1f_b2r'][0:1, 2048 * e2:2048 * (e2 + 1)])], 'Wb'))
        smk = {0: 0, 1: 0}
        vmk = {0: 0, 1: 0}
        hmk = {0: 0, 1: 0}
        omk = {0: 0, 1: 0, 2: 0, 3: 0}
        cbk = 0
        w3mk = 0
        for e in range(8):
            if (e, half) > (0, 0):
                w8('g', 'T', tm_w.get((e, half), g.cnt['T']))
            load_expert(e)
            w1v, w2v, w3v, b2v = wv_marks[e]
            for t2 in range(2):
                bank = ps[4 + t2]
                if cbk:
                    w8('t', 'V', cbk)
                mm(bank, cs['sel8'][:, 128 * e:128 * (e + 1)],
                   wb[0:8, CB + t0 + 512 * t2:CB + t0 + 512 * (t2 + 1)],
                   True, True, s='T')
                w8('v', 'T', g.cnt['T'])
                cbk = vcp(wbx[:, CBC + 512 * t2:CBC + 512 * (t2 + 1)], bank, s='V')
            w8('t', 'Wa', w1v)
            w8('t', 'Wc', w2v)
            w8('t', 'Wb', b2v)
            w8('t', 'Wd', w3v)
            for t2 in range(2):
                for fc in range(16):
                    pp = fc % 2
                    bank = ps[4 + 2 + pp]
                    if smk[pp]:
                        w8('t', 'S', smk[pp])
                    for kc in range(8):
                        mm(bank, wb[:, W1 + 2048 * kc + 128 * fc:W1 + 2048 * kc + 128 * (fc + 1)],
                           xm[:, T * kc + t0 + 512 * t2:T * kc + t0 + 512 * (t2 + 1)],
                           kc == 0, kc == 7, s='T' if kc == 7 else None)
                    tmka = g.cnt['T']
                    bank2 = ps[pp]
                    if hmk[pp]:
                        w8('t', 'V', hmk[pp])
                    for kc in range(8):
                        mm(bank2, wb[:, W2 + 2048 * kc + 128 * fc:W2 + 2048 * kc + 128 * (fc + 1)],
                           xm[:, T * kc + t0 + 512 * t2:T * kc + t0 + 512 * (t2 + 1)],
                           kc == 0, kc == 7, s=None)
                    mm(bank2, wbx[0:1, B2E + 2048 * (e % 2) + 128 * fc:B2E + 2048 * (e % 2) + 128 * (fc + 1)],
                       wb[0:1, 51200:51712],
                       False, True, sgc=True, s='T')
                    tmkb = g.cnt['T']
                    w8('s', 'T', tmka)
                    if hmk[pp]:
                        w8('s', 'V', hmk[pp])
                    smk[pp] = act(wbx[:, ST1 + 512 * pp:ST1 + 512 * (pp + 1)], bank, AF.Silu,
                                  bias=mc2[:, 64 + 16 * e + fc:65 + 16 * e + fc], s='S')
                    w8('v', 'S', smk[pp])
                    w8('v', 'T', tmkb)
                    if fc == 0 and w3mk:
                        w8('v', 'T', w3mk)
                    tt(wbx[:, ST2 + 512 * pp:ST2 + 512 * (pp + 1)],
                       wbx[:, ST1 + 512 * pp:ST1 + 512 * (pp + 1)],
                       wbx[:, CBC + 512 * t2:CBC + 512 * (t2 + 1)], AL.mult)
                    hmk[pp] = tt(xn[:, H1 + 512 * fc:H1 + 512 * (fc + 1)],
                                 wbx[:, ST2 + 512 * pp:ST2 + 512 * (pp + 1)],
                                 bank2, AL.mult, s='V')
                if g.tap == 'l1h' and e == 0 and half == 0 and t2 == 0:
                    g.bar()
                    _tap_out(g, [(g.dbg[128 * fc:128 * (fc + 1), 0:512],
                                  xn[:, H1 + 512 * fc:H1 + 512 * (fc + 1)])
                                 for fc in range(16)])
                    return True
                w8('t', 'V', max(hmk[0], hmk[1]))
                for dc in range(8):
                    bank = ps[(dc + t2) % 4]
                    if omk[(dc + t2) % 4]:
                        w8('t', 'V', omk[(dc + t2) % 4])
                    for fc in range(16):
                        mm(bank, wb[:, W3 + 1024 * fc + 128 * dc:W3 + 1024 * fc + 128 * (dc + 1)],
                           xn[:, H1 + 512 * fc:H1 + 512 * (fc + 1)],
                           fc == 0, fc == 15, s='T' if fc == 15 else None)
                    w8('v', 'T', g.cnt['T'])
                    omk[(dc + t2) % 4] = tt(
                        fb[:, 1024 * dc + 512 * t2:1024 * dc + 512 * (t2 + 1)],
                        fb[:, 1024 * dc + 512 * t2:1024 * dc + 512 * (t2 + 1)],
                        bank, AL.add, s='V')
                w3mk = g.cnt['T']
            tm_w[(e + 1, half) if e < 7 else (0, 1)] = g.cnt['T']
        bar()
        if g.tap == 'l1moe' and half == 0:
            _tap_out(g, [(g.dbg[128 * dc:128 * (dc + 1), 0:1024],
                          fb[:, 1024 * dc:1024 * (dc + 1)]) for dc in range(8)])
            return True
        w8('v', 'T', g.cnt['T'])
        for dc in range(8):
            for t2 in range(2):
                sl = slice(t0 + 512 * t2, t0 + 512 * (t2 + 1))
                osl = fb[:, 1024 * dc + 512 * t2:1024 * dc + 512 * (t2 + 1)]
                w8('g', 'V', g.cnt['V'])
                x2v = gdma(wbx[:, ST1:ST1 + 512], g.hx2[128 * dc:128 * (dc + 1), sl], s='Wc')
                w8('y', 'V', g.cnt['V'])
                sv_ = ydma(wbx[:, ST2:ST2 + 512], g.hs[128 * dc:128 * (dc + 1), sl])
                w8('v', 'Wc', x2v)
                w8('v', 'X', sv_)
                w8('v', 'S', g.cnt['S'])
                tt(ps[6], osl, wbx[:, ST1:ST1 + 512], AL.add, s='V')
                w8('s', 'V', g.cnt['V'])
                gv = act(wbx[:, ST1:ST1 + 512], ps[6], AF.Silu, s='S')
                w8('y', 'V', g.cnt['V'])
                lv1 = ydma(osl, g.hx1[128 * dc:128 * (dc + 1), sl])
                w8('v', 'X', lv1)
                w8('v', 'S', gv)
                tt(ps[7], wbx[:, ST1:ST1 + 512], wbx[:, ST2:ST2 + 512], AL.mult)
                tt(osl, osl, ps[7], AL.add, s='V')
                w8('y', 'V', g.cnt['V'])
                ov = ydma(g.out[128 * dc:128 * (dc + 1), sl], osl, s='O')
                g.q['v'].append(lambda e, S=g.sem['O'], V=ov: e.wait_ge(S, V))
                g.q['s'].append(lambda e, S=g.sem['O'], V=ov: e.wait_ge(S, V))
    w8('y', 'O', g.cnt['O'])
    bar()
    return False


def finish(g):
    for c in range(8):
        g.ydma(g.out[128 * c:128 * (c + 1), :], g.xn[:, T * c:T * (c + 1)], s='O')
    g.w8('y', 'O', g.cnt['O'])


def kernel(**inputs):
    x = np.asarray(inputs['L0_attn_x'])
    CST = _host_consts()
    W = _host_weights(inputs)
    nc = build_program(tap=DEBUG_TAP)
    fvals = np.zeros((128, 64), dtype=np.float32)
    for i, v in enumerate(kernel._fvals):
        fvals[:, i] = v
    in_maps = []
    for c in range(NC_):
        m = {'x_fm': np.ascontiguousarray(x[c].T), 'c_fvals': fvals}
        for k, v in CST.items():
            m['c_' + k] = v
        for k, v in W.items():
            m['w_' + k] = np.ascontiguousarray(v, dtype=np.float32)
        in_maps.append(m)
    res = run_bass_kernel_spmd(nc, in_maps, core_ids=list(range(NC_)))
    outs = [np.asarray(r['out']).T for r in res.results]
    if DEBUG_TAP is not None:
        kernel.dbg = [np.asarray(r['dbg']) for r in res.results]
    return np.stack(outs).astype(np.float32)
